# version 01
# baseline (speedup 1.0000x reference)
import numpy as np
import ml_dtypes
import concourse.bass as bass
import concourse.mybir as mybir
from concourse.bass_utils import run_bass_kernel_spmd
from contextlib import ExitStack

F32 = mybir.dt.float32
BF16 = mybir.dt.bfloat16
U32 = mybir.dt.uint32
AX = mybir.AxisListType.X
ALU = mybir.AluOpType
AF = mybir.ActivationFunctionType
BF = ml_dtypes.bfloat16

NCORES = 8
B, S, D = 4, 4096, 2048
H, DH = 16, 128
NOPE = ROPE = 64
QP = 1024
KVP = 1365
KVPP = 1408
INTER = 5632
EPS = 1e-5
SCALE = 1.0 / float(np.sqrt(np.float32(128.0)))
NEG = -1e9
T = B * S
TC = T // NCORES

ENGS = ("sync", "gpsimd", "tensor", "vector", "scalar")


DMA_POOL = 8


class Prog:

    def __init__(self, nc, ctx):
        self.nc = nc
        self.ops = []
        self.last_writer = {}
        self.readers = {}
        self.eng_count = {e: 0 for e in ENGS}
        self.sems = {e: ctx.enter_context(nc.semaphore(f"se_{e}")) for e in ENGS}
        self.dpool = {e: [ctx.enter_context(nc.semaphore(f"sd_{e}{i}"))
                          for i in range(DMA_POOL)] for e in ("sync", "gpsimd", "scalar")}
        self.dcount = {e: [0] * DMA_POOL for e in ("sync", "gpsimd", "scalar")}
        self.drr = {e: 0 for e in ("sync", "gpsimd", "scalar")}

    def op(self, engine, fn, reads=(), writes=(), dma=False, after=()):
        deps = set(after)
        for k in reads:
            if k in self.last_writer:
                deps.add(self.last_writer[k])
            if k.startswith("pb"):
                for r in self.readers.get(k, ()):
                    if self.ops[r][0] != engine:
                        deps.add(r)
        for k in writes:
            if k in self.last_writer:
                deps.add(self.last_writer[k])
            for r in self.readers.get(k, ()):
                deps.add(r)
        i = len(self.ops)
        if dma:
            idx = self.drr[engine] % DMA_POOL
            self.drr[engine] += 1
            before = self.dcount[engine][idx]
            self.dcount[engine][idx] = before + 16
            info = (idx, before)
            pos = 0
        else:
            info = None
            self.eng_count[engine] += 1
            pos = self.eng_count[engine]
        self.ops.append((engine, fn, deps, info, pos))
        for k in reads:
            self.readers.setdefault(k, []).append(i)
        for k in writes:
            self.last_writer[k] = i
            self.readers[k] = []
        return i

    def emit(self):
        nc = self.nc
        with nc.Block() as block:
            for ename in ENGS:
                self._emit_engine(block, ename)

    def _emit_engine(self, block, ename):
        prog = self

        def body(eng):
            wm = {}

            def wait(sem, val):
                if val > wm.get(id(sem), 0):
                    eng.wait_ge(sem, val)
                    wm[id(sem)] = val

            for i, (e, fn, deps, info, pos) in enumerate(prog.ops):
                if e != ename:
                    continue
                for d in sorted(deps):
                    de, dfn, ddeps, dinfo, dpos = prog.ops[d]
                    if dinfo is not None:
                        didx, dbefore = dinfo
                        wait(prog.dpool[de][didx], dbefore + 16)
                    else:
                        if de == ename and ename == "tensor" and info is None:
                            continue
                        wait(prog.sems[de], dpos)
                if info is not None:
                    idx, before = info
                    if before > 0:
                        wait(prog.dpool[ename][idx], before)
                    inst = fn(eng)
                    inst.then_inc(prog.dpool[ename][idx], 16)
                else:
                    inst = fn(eng)
                    inst.then_inc(prog.sems[ename], 1)
            if ename == "sync":
                for q, pool in prog.dpool.items():
                    for idx, sem in enumerate(pool):
                        if prog.dcount[q][idx] > 0:
                            wait(sem, prog.dcount[q][idx])

        getattr(block, ename)(body)


def _bf(a):
    return np.ascontiguousarray(a.astype(BF))


def _f32(a):
    return np.ascontiguousarray(a.astype(np.float32))


def _rope_tables_np():
    freqs = (1.0 / (np.float32(10000.0) **
                    (np.arange(0, 128, 2, dtype=np.float32) / np.float32(128.0))))
    pos = np.arange(S, dtype=np.float32)
    emb = pos[:, None] * freqs[None, :32]
    cos = np.cos(emb.astype(np.float32)).astype(np.float32)
    sin = np.sin(emb.astype(np.float32)).astype(np.float32)
    return cos.T.copy(), sin.T.copy()


def q_pack_perm():
    perm = np.zeros(2048, np.int64)
    for j in range(8):
        for half in range(2):
            h = 2 * j + half
            perm[j * 128 + half * 64: j * 128 + half * 64 + 64] = h * 128 + np.arange(64)
    for g in range(4):
        for hh in range(4):
            h = 4 * g + hh
            perm[(8 + g) * 128 + hh * 32:(8 + g) * 128 + hh * 32 + 32] = \
                h * 128 + 64 + 2 * np.arange(32)
    for g in range(4):
        for hh in range(4):
            h = 4 * g + hh
            perm[(12 + g) * 128 + hh * 32:(12 + g) * 128 + hh * 32 + 32] = \
                h * 128 + 64 + 2 * np.arange(32) + 1
    return perm


def q_head_rows(h):
    j, half = h // 2, h % 2
    g, hh = h // 4, h % 4
    return (j * 128 + half * 64,
            (8 + g) * 128 + hh * 32,
            (12 + g) * 128 + hh * 32)


def stage_inputs(ins):
    st = {}
    x = ins["L0_attn_x"]
    xf = x.reshape(T, D).T
    st["x_f"] = _f32(xf)

    st["wq_down"] = _f32(ins["L0_attn_Wq_down"])
    wkv = ins["L0_attn_Wkv_down"]
    lora = np.zeros((D, KVPP), np.float32)
    lora[:, :KVP] = wkv[:, :KVP]
    st["wkv_down_lora"] = _f32(lora)
    wr = wkv[:, KVP:]
    wr2 = np.concatenate([wr[:, 0::2], wr[:, 1::2]], axis=1)
    st["wkv_down_rope"] = _f32(wr2)

    gq = ins["L0_attn_gq"]
    st["gq_pp"] = _f32(gq.reshape(8, 128).T)
    gkv = np.zeros(KVPP, np.float32)
    gkv[:KVP] = ins["L0_attn_gkv"]
    st["gkv_pp"] = _f32(gkv.reshape(11, 128).T)

    st["wq_up"] = _f32(ins["L0_attn_Wq_up"][:, q_pack_perm()])
    wkvu = ins["L0_attn_Wkv_up"]
    wkvu_p = np.zeros((KVPP, 3072), np.float32)
    wkvu_p[:KVP] = wkvu
    per_head = wkvu_p.reshape(KVPP, H, 192)
    st["wkv_up_k"] = _f32(per_head[:, :, :64].reshape(KVPP, H * 64))
    st["wkv_up_v"] = _f32(per_head[:, :, 64:].reshape(KVPP, H * 128))
    st["wo"] = _f32(ins["L0_attn_Wo"])

    cosT, sinT = _rope_tables_np()
    st["cos1"] = cosT
    st["sin1"] = sinT
    st["cos4"] = _f32(np.tile(cosT, (4, 1)))
    st["sin4"] = _f32(np.tile(sinT, (4, 1)))

    for L in ("L0", "L1"):
        gw = ins[f"{L}_ffn_gate_w"]
        uw = ins[f"{L}_ffn_up_w"]
        dw = ins[f"{L}_ffn_down_w"]
        st[f"{L}_gws"] = _f32(gw.reshape(44, 128, 16, 128).transpose(0, 3, 2, 1)
                              .reshape(44, 128, 2048))
        st[f"{L}_uws"] = _f32(uw.reshape(44, 128, 16, 128).transpose(0, 3, 2, 1)
                              .reshape(44, 128, 2048))
        st[f"{L}_dws"] = _f32(dw.reshape(16, 128, 44, 128).transpose(0, 3, 2, 1)
                              .reshape(16, 128, 5632))
        st[f"{L}_gate_s"] = _f32(ins[f"{L}_ffn_gate_s"].reshape(44, 128).T)
        st[f"{L}_up_s"] = _f32(ins[f"{L}_ffn_up_s"].reshape(44, 128).T)
        st[f"{L}_down_s"] = _f32(ins[f"{L}_ffn_down_s"].reshape(16, 128).T)

    st["wq1"] = _f32(ins["L1_attn_Wq"])
    st["wk1"] = _f32(ins["L1_attn_Wk"])
    st["wv1"] = _f32(ins["L1_attn_Wv"])
    st["bq1"] = _f32(ins["L1_attn_bq"].reshape(1, 2048))
    st["bk1"] = _f32(ins["L1_attn_bk"].reshape(1, 2048))
    st["bv1"] = _f32(ins["L1_attn_bv"].reshape(1, 2048))

    st["ident"] = np.eye(128).astype(BF)
    st["ones_row"] = np.ones((1, 128), np.float32)
    st["ones_col"] = np.ones((128, 1), np.float32)
    st["ones_row_bf"] = np.ones((1, 128), BF)
    st["ones_col_bf"] = np.ones((128, 1), BF)
    kk, qq = np.meshgrid(np.arange(128), np.arange(128), indexing="ij")
    st["maskT"] = _f32(np.where(kk > qq, NEG, 0.0))
    me = np.zeros((128, 4, 512), np.float32)
    kk2 = np.arange(128)[:, None]
    qq2 = np.arange(512)[None, :]
    for j in range(4):
        me[:, j, :] = np.where(j * 128 + kk2 > qq2, NEG, 0.0)
    st["maskE_E"] = _f32(me.transpose(0, 1, 2).reshape(128, 2048))
    st["iota64_row"] = _f32(np.tile(np.arange(64, dtype=np.float32)[None, :], (128, 1)))
    st["iota64_pp"] = _f32(np.arange(64, dtype=np.float32).reshape(64, 1))
    ml = np.zeros((128, 384), np.float32)
    ml[64:128, 0:128] = NEG
    ml[0:64, 256:384] = NEG
    st["maskL"] = _f32(ml)
    return st


def build_A():
    nc = bass.Bass("TRN2", target_bir_lowering=False)
    x = nc.dram_tensor("x", [D, TC], F32, kind="ExternalInput").ap()
    w = nc.dram_tensor("w", [D, QP], F32, kind="ExternalInput").ap()
    gq = nc.dram_tensor("gq", [128, 8], F32, kind="ExternalInput").ap()
    ones_col = nc.dram_tensor("ones_col", [128, 1], BF16, kind="ExternalInput").ap()
    ones_row = nc.dram_tensor("ones_row", [1, 128], F32, kind="ExternalInput").ap()
    cqn = nc.dram_tensor("cqn", [QP, TC], BF16, kind="ExternalOutput").ap()

    NT = TC // 512
    with ExitStack() as ctx:
        xsb = ctx.enter_context(nc.sbuf_tensor("xsb", [128, 16 * TC], BF16))
        wsb = ctx.enter_context(nc.sbuf_tensor("wsb", [128, 16 * QP], BF16))
        gqs = ctx.enter_context(nc.sbuf_tensor("gqs", [128, 8], F32))
        onc = ctx.enter_context(nc.sbuf_tensor("onc", [128, 1], BF16))
        onr = ctx.enter_context(nc.sbuf_tensor("onr", [1, 128], F32))
        craw = ctx.enter_context(nc.sbuf_tensor("craw", [128, 8 * 512], F32))
        sqs = ctx.enter_context(nc.sbuf_tensor("sqs", [128, 512], BF16))
        rowbuf = ctx.enter_context(nc.sbuf_tensor("rowbuf", [1, 3 * 512], F32))
        outsb = ctx.enter_context(nc.sbuf_tensor("outsb", [128, 8 * 512], BF16))
        pb = [ctx.enter_context(nc.psum_tensor(f"pb{i}", [128, 512], F32))
              for i in range(8)]
        p = Prog(nc, ctx)

        for k in range(16):
            p.op("gpsimd", lambda e, k=k: e.dma_start(
                out=xsb[:, k * TC:(k + 1) * TC], in_=x[k * 128:(k + 1) * 128, :]),
                writes=[f"x{k}"], dma=True)
            p.op("gpsimd", lambda e, k=k: e.dma_start(
                out=wsb[:, k * QP:(k + 1) * QP], in_=w[k * 128:(k + 1) * 128, :]),
                writes=[f"w{k}"], dma=True)
        p.op("sync", lambda e: e.dma_start(out=gqs[:], in_=gq), writes=["gq"], dma=True)
        p.op("sync", lambda e: e.dma_start(out=onc[:], in_=ones_col), writes=["onc"], dma=True)
        p.op("sync", lambda e: e.dma_start(out=onr[:], in_=ones_row), writes=["onr"], dma=True)

        for tt in range(NT):
            t0 = tt * 512
            for m in range(8):
                bank = m % 4
                for k in range(16):
                    p.op("tensor", lambda e, k=k, m=m, t0=t0, bank=bank: e.matmul(
                        pb[bank][:, :],
                        wsb[:, k * QP + m * 128: k * QP + (m + 1) * 128],
                        xsb[:, k * TC + t0: k * TC + t0 + 512],
                        start=(k == 0), stop=(k == 15)),
                        reads=[f"x{k}", f"w{k}"],
                        writes=[f"pb{bank}"] if k in (0, 15) else [])
                cp = p.op("vector", lambda e, m=m, bank=bank: e.tensor_copy(
                    craw[:, m * 512:(m + 1) * 512], pb[bank][:, :]),
                    reads=[f"pb{bank}"], writes=[f"craw{m}"])
                p.op("scalar", lambda e, bank=bank: e.activation(
                    sqs[:, :], pb[bank][:, :], AF.Square),
                    reads=[f"pb{bank}"], writes=["sqs"], after=[cp])
                p.op("tensor", lambda e, m=m: e.matmul(
                    pb[4][0:1, :], onc[:, 0:1], sqs[:, :],
                    start=(m == 0), stop=(m == 7)),
                    reads=["sqs", "onc"],
                    writes=["pb4"] if m in (0, 7) else [])
            p.op("vector", lambda e: e.tensor_scalar(
                rowbuf[0:1, 0:512], pb[4][0:1, :], 1.0 / QP, EPS, ALU.mult, ALU.add),
                reads=["pb4"], writes=["row0"])
            p.op("scalar", lambda e: e.activation(
                rowbuf[0:1, 512:1024], rowbuf[0:1, 0:512], AF.Ln),
                reads=["row0"], writes=["row1"])
            p.op("scalar", lambda e: e.activation(
                rowbuf[0:1, 1024:1536], rowbuf[0:1, 512:1024], AF.Exp, scale=-0.5),
                reads=["row1"], writes=["row2"])
            p.op("tensor", lambda e: e.matmul(
                pb[5][:, :], onr[0:1, 0:128], rowbuf[0:1, 1024:1536],
                start=True, stop=True),
                reads=["row2", "onr", "pb5"], writes=["pb5"])
            for m in range(8):
                p.op("vector", lambda e, m=m: e.scalar_tensor_tensor(
                    outsb[:, m * 512:(m + 1) * 512],
                    craw[:, m * 512:(m + 1) * 512],
                    gqs[:, m:m + 1], pb[5][:, :], ALU.mult, ALU.mult),
                    reads=[f"craw{m}", "gq", "pb5"], writes=[f"out{m}"])
                p.op("sync", lambda e, m=m, t0=t0: e.dma_start(
                    out=cqn[m * 128:(m + 1) * 128, t0:t0 + 512],
                    in_=outsb[:, m * 512:(m + 1) * 512]),
                    reads=[f"out{m}"], dma=True)
        p.emit()
    return nc


def build_B():
    nc = bass.Bass("TRN2", target_bir_lowering=False)
    x = nc.dram_tensor("x", [D, TC], F32, kind="ExternalInput").ap()
    wl = nc.dram_tensor("wl", [D, KVPP], F32, kind="ExternalInput").ap()
    wr = nc.dram_tensor("wr", [D, 64], F32, kind="ExternalInput").ap()
    gkv = nc.dram_tensor("gkv", [128, 11], F32, kind="ExternalInput").ap()
    cs = nc.dram_tensor("cs", [64, TC], F32, kind="ExternalInput").ap()
    sc = nc.dram_tensor("sc", [64, TC], F32, kind="ExternalInput").ap()
    ones_col = nc.dram_tensor("ones_col", [128, 1], BF16, kind="ExternalInput").ap()
    ones_row = nc.dram_tensor("ones_row", [1, 128], F32, kind="ExternalInput").ap()
    ckvn = nc.dram_tensor("ckvn", [KVPP, TC], BF16, kind="ExternalOutput").ap()
    kroped = nc.dram_tensor("kroped", [64, TC], BF16, kind="ExternalOutput").ap()

    NT = TC // 512
    with ExitStack() as ctx:
        xsb = ctx.enter_context(nc.sbuf_tensor("xsb", [128, 16 * TC], BF16))
        wsb = ctx.enter_context(nc.sbuf_tensor("wsb", [128, 16 * KVPP], BF16))
        wrs = ctx.enter_context(nc.sbuf_tensor("wrs", [128, 16 * 64], BF16))
        gks = ctx.enter_context(nc.sbuf_tensor("gks", [128, 11], F32))
        css = ctx.enter_context(nc.sbuf_tensor("css", [64, TC], F32))
        scs = ctx.enter_context(nc.sbuf_tensor("scs", [64, TC], F32))
        onc = ctx.enter_context(nc.sbuf_tensor("onc", [128, 1], BF16))
        onr = ctx.enter_context(nc.sbuf_tensor("onr", [1, 128], F32))
        craw = ctx.enter_context(nc.sbuf_tensor("craw", [128, 11 * 512], F32))
        kraw = ctx.enter_context(nc.sbuf_tensor("kraw", [64, TC], F32))
        ksc1 = ctx.enter_context(nc.sbuf_tensor("ksc1", [64, TC], F32))
        ksc2 = ctx.enter_context(nc.sbuf_tensor("ksc2", [64, TC], F32))
        kout = ctx.enter_context(nc.sbuf_tensor("kout", [64, TC], BF16))
        sqs = ctx.enter_context(nc.sbuf_tensor("sqs", [128, 512], BF16))
        rowbuf = ctx.enter_context(nc.sbuf_tensor("rowbuf", [1, 3 * 512], F32))
        outsb = ctx.enter_context(nc.sbuf_tensor("outsb", [128, 11 * 512], BF16))
        pb = [ctx.enter_context(nc.psum_tensor(f"pb{i}", [128, 512], F32))
              for i in range(8)]
        p = Prog(nc, ctx)

        for k in range(16):
            p.op("gpsimd", lambda e, k=k: e.dma_start(
                out=xsb[:, k * TC:(k + 1) * TC], in_=x[k * 128:(k + 1) * 128, :]),
                writes=[f"x{k}"], dma=True)
            p.op("gpsimd", lambda e, k=k: e.dma_start(
                out=wsb[:, k * KVPP:(k + 1) * KVPP], in_=wl[k * 128:(k + 1) * 128, :]),
                writes=[f"w{k}"], dma=True)
            p.op("gpsimd", lambda e, k=k: e.dma_start(
                out=wrs[:, k * 64:(k + 1) * 64], in_=wr[k * 128:(k + 1) * 128, :]),
                writes=[f"wr{k}"], dma=True)
        p.op("sync", lambda e: e.dma_start(out=gks[:], in_=gkv), writes=["gkv"], dma=True)
        p.op("sync", lambda e: e.dma_start(out=css[:], in_=cs), writes=["cs"], dma=True)
        p.op("sync", lambda e: e.dma_start(out=scs[:], in_=sc), writes=["sc"], dma=True)
        p.op("sync", lambda e: e.dma_start(out=onc[:], in_=ones_col), writes=["onc"], dma=True)
        p.op("sync", lambda e: e.dma_start(out=onr[:], in_=ones_row), writes=["onr"], dma=True)

        for tt in range(NT):
            t0 = tt * 512
            for m in range(11):
                bank = m % 4
                for k in range(16):
                    p.op("tensor", lambda e, k=k, m=m, t0=t0, bank=bank: e.matmul(
                        pb[bank][:, :],
                        wsb[:, k * KVPP + m * 128: k * KVPP + (m + 1) * 128],
                        xsb[:, k * TC + t0: k * TC + t0 + 512],
                        start=(k == 0), stop=(k == 15)),
                        reads=[f"x{k}", f"w{k}"],
                        writes=[f"pb{bank}"] if k in (0, 15) else [])
                cp = p.op("vector", lambda e, m=m, bank=bank: e.tensor_copy(
                    craw[:, m * 512:(m + 1) * 512], pb[bank][:, :]),
                    reads=[f"pb{bank}"], writes=[f"craw{m}"])
                p.op("scalar", lambda e, bank=bank: e.activation(
                    sqs[:, :], pb[bank][:, :], AF.Square),
                    reads=[f"pb{bank}"], writes=["sqs"], after=[cp])
                p.op("tensor", lambda e, m=m: e.matmul(
                    pb[4][0:1, :], onc[:, 0:1], sqs[:, :],
                    start=(m == 0), stop=(m == 10)),
                    reads=["sqs", "onc"],
                    writes=["pb4"] if m in (0, 10) else [])
            for k in range(16):
                p.op("tensor", lambda e, k=k, t0=t0: e.matmul(
                    pb[5][0:64, :],
                    wrs[:, k * 64:(k + 1) * 64],
                    xsb[:, k * TC + t0: k * TC + t0 + 512],
                    start=(k == 0), stop=(k == 15)),
                    reads=[f"x{k}", f"wr{k}"],
                    writes=["pb5"] if k in (0, 15) else [])
            p.op("vector", lambda e, t0=t0: e.tensor_copy(
                kraw[:, t0:t0 + 512], pb[5][0:64, :]),
                reads=["pb5"], writes=["kraw"])
            p.op("vector", lambda e: e.tensor_scalar(
                rowbuf[0:1, 0:512], pb[4][0:1, :], 1.0 / KVP, EPS, ALU.mult, ALU.add),
                reads=["pb4"], writes=["row0"])
            p.op("scalar", lambda e: e.activation(
                rowbuf[0:1, 512:1024], rowbuf[0:1, 0:512], AF.Ln),
                reads=["row0"], writes=["row1"])
            p.op("scalar", lambda e: e.activation(
                rowbuf[0:1, 1024:1536], rowbuf[0:1, 512:1024], AF.Exp, scale=-0.5),
                reads=["row1"], writes=["row2"])
            p.op("tensor", lambda e: e.matmul(
                pb[6][:, :], onr[0:1, 0:128], rowbuf[0:1, 1024:1536],
                start=True, stop=True),
                reads=["row2", "onr"], writes=["pb6"])
            for m in range(11):
                p.op("vector", lambda e, m=m: e.scalar_tensor_tensor(
                    outsb[:, m * 512:(m + 1) * 512],
                    craw[:, m * 512:(m + 1) * 512],
                    gks[:, m:m + 1], pb[6][:, :], ALU.mult, ALU.mult),
                    reads=[f"craw{m}", "gkv", "pb6"], writes=[f"out{m}"])
                p.op("sync", lambda e, m=m, t0=t0: e.dma_start(
                    out=ckvn[m * 128:(m + 1) * 128, t0:t0 + 512],
                    in_=outsb[:, m * 512:(m + 1) * 512]),
                    reads=[f"out{m}"], dma=True)
        p.op("vector", lambda e: e.tensor_mul(ksc1[0:32, :], kraw[0:32, :], css[0:32, :]),
             reads=["kraw", "cs"], writes=["m1"])
        p.op("vector", lambda e: e.tensor_mul(ksc1[32:64, :], kraw[32:64, :], css[32:64, :]),
             reads=["kraw", "cs"], writes=["m2"])
        p.op("vector", lambda e: e.tensor_mul(ksc2[0:32, :], kraw[0:32, :], scs[0:32, :]),
             reads=["kraw", "sc"], writes=["m3"])
        p.op("vector", lambda e: e.tensor_mul(ksc2[32:64, :], kraw[32:64, :], scs[32:64, :]),
             reads=["kraw", "sc"], writes=["m4"])
        p.op("vector", lambda e: e.tensor_copy(kraw[0:32, :], ksc1[32:64, :]),
             reads=["m2"], writes=["m2c", "kraw"])
        p.op("vector", lambda e: e.tensor_sub(kout[0:32, :], ksc1[0:32, :], kraw[0:32, :]),
             reads=["m1", "m2c"], writes=["ne"])
        p.op("vector", lambda e: e.tensor_copy(kraw[32:64, :], ksc2[0:32, :]),
             reads=["m3"], writes=["m3c", "kraw"])
        p.op("vector", lambda e: e.tensor_add(kout[32:64, :], ksc2[32:64, :], kraw[32:64, :]),
             reads=["m4", "m3c"], writes=["no"])
        p.op("sync", lambda e: e.dma_start(out=kroped, in_=kout[0:64, :]),
             reads=["ne", "no"], dma=True)
        p.emit()
    return nc


def build_C():
    nc = bass.Bass("TRN2", target_bir_lowering=False)
    cq = nc.dram_tensor("cq", [QP, TC], BF16, kind="ExternalInput").ap()
    w = nc.dram_tensor("w", [QP, D], F32, kind="ExternalInput").ap()
    cos4 = nc.dram_tensor("cos4", [128, TC], F32, kind="ExternalInput").ap()
    sin4 = nc.dram_tensor("sin4", [128, TC], F32, kind="ExternalInput").ap()
    q0 = nc.dram_tensor("q0", [D, TC], BF16, kind="ExternalOutput").ap()

    NT = TC // 512
    with ExitStack() as ctx:
        csb = ctx.enter_context(nc.sbuf_tensor("csb", [128, 8 * TC], BF16))
        wsb = ctx.enter_context(nc.sbuf_tensor("wsb", [128, 8 * D], BF16))
        c4 = ctx.enter_context(nc.sbuf_tensor("c4", [128, TC], F32))
        s4 = ctx.enter_context(nc.sbuf_tensor("s4", [128, TC], F32))
        rraw = ctx.enter_context(nc.sbuf_tensor("rraw", [128, 8 * 512], F32))
        rsc1 = ctx.enter_context(nc.sbuf_tensor("rsc1", [128, 512], F32))
        rsc2 = ctx.enter_context(nc.sbuf_tensor("rsc2", [128, 512], F32))
        outsb = ctx.enter_context(nc.sbuf_tensor("outsb", [128, 16 * 512], BF16))
        pb = [ctx.enter_context(nc.psum_tensor(f"pb{i}", [128, 512], F32))
              for i in range(8)]
        p = Prog(nc, ctx)

        for k in range(8):
            p.op("sync", lambda e, k=k: e.dma_start(
                out=csb[:, k * TC:(k + 1) * TC], in_=cq[k * 128:(k + 1) * 128, :]),
                writes=[f"c{k}"], dma=True)
            p.op("gpsimd", lambda e, k=k: e.dma_start(
                out=wsb[:, k * D:(k + 1) * D], in_=w[k * 128:(k + 1) * 128, :]),
                writes=[f"w{k}"], dma=True)
        p.op("sync", lambda e: e.dma_start(out=c4[:], in_=cos4), writes=["cos"], dma=True)
        p.op("sync", lambda e: e.dma_start(out=s4[:], in_=sin4), writes=["sin"], dma=True)

        for tt in range(NT):
            t0 = tt * 512
            for m in range(16):
                bank = m % 4
                for k in range(8):
                    p.op("tensor", lambda e, k=k, m=m, t0=t0, bank=bank: e.matmul(
                        pb[bank][:, :],
                        wsb[:, k * D + m * 128: k * D + (m + 1) * 128],
                        csb[:, k * TC + t0: k * TC + t0 + 512],
                        start=(k == 0), stop=(k == 7)),
                        reads=[f"c{k}", f"w{k}"],
                        writes=[f"pb{bank}"] if k in (0, 7) else [])
                if m < 8:
                    p.op("vector", lambda e, m=m, bank=bank: e.tensor_copy(
                        outsb[:, m * 512:(m + 1) * 512], pb[bank][:, :]),
                        reads=[f"pb{bank}"], writes=[f"out{m}"])
                else:
                    p.op("vector", lambda e, m=m, bank=bank: e.tensor_copy(
                        rraw[:, (m - 8) * 512:(m - 7) * 512], pb[bank][:, :]),
                        reads=[f"pb{bank}"], writes=[f"rr{m - 8}"])
            for g in range(4):
                ek, ok = f"rr{g}", f"rr{4 + g}"
                E = rraw[:, g * 512:(g + 1) * 512]
                O = rraw[:, (4 + g) * 512:(5 + g) * 512]
                p.op("vector", lambda e, E=E, t0=t0: e.tensor_mul(
                    rsc1[:, :], E, c4[:, t0:t0 + 512]),
                    reads=[ek, "cos"], writes=["rs1"])
                p.op("vector", lambda e, O=O, t0=t0: e.tensor_mul(
                    rsc2[:, :], O, s4[:, t0:t0 + 512]),
                    reads=[ok, "sin"], writes=["rs2"])
                p.op("vector", lambda e, g=g: e.tensor_sub(
                    outsb[:, (8 + g) * 512:(9 + g) * 512], rsc1[:, :], rsc2[:, :]),
                    reads=["rs1", "rs2"], writes=[f"out{8 + g}"])
                p.op("vector", lambda e, O=O, t0=t0: e.tensor_mul(
                    rsc1[:, :], O, c4[:, t0:t0 + 512]),
                    reads=[ok, "cos"], writes=["rs1"])
                p.op("vector", lambda e, E=E, t0=t0: e.tensor_mul(
                    rsc2[:, :], E, s4[:, t0:t0 + 512]),
                    reads=[ek, "sin"], writes=["rs2"])
                p.op("vector", lambda e, g=g: e.tensor_add(
                    outsb[:, (12 + g) * 512:(13 + g) * 512], rsc1[:, :], rsc2[:, :]),
                    reads=["rs1", "rs2"], writes=[f"out{12 + g}"])
            for m in range(16):
                p.op("sync", lambda e, m=m, t0=t0: e.dma_start(
                    out=q0[m * 128:(m + 1) * 128, t0:t0 + 512],
                    in_=outsb[:, m * 512:(m + 1) * 512]),
                    reads=[f"out{m}"], dma=True)
        p.emit()
    return nc


def build_D():
    nc = bass.Bass("TRN2", target_bir_lowering=False)
    ckv = nc.dram_tensor("ckv", [KVPP, TC], BF16, kind="ExternalInput").ap()
    wk = nc.dram_tensor("wk", [KVPP, QP], F32, kind="ExternalInput").ap()
    wv = nc.dram_tensor("wv", [KVPP, D], F32, kind="ExternalInput").ap()
    knope = nc.dram_tensor("knope", [QP, TC], BF16, kind="ExternalOutput").ap()
    vt = nc.dram_tensor("vt", [TC, D], BF16, kind="ExternalOutput").ap()

    NT = TC // 512
    with ExitStack() as ctx:
        csb = ctx.enter_context(nc.sbuf_tensor("csb", [128, 11 * TC], BF16))
        wks = ctx.enter_context(nc.sbuf_tensor("wks", [128, 11 * QP], BF16))
        wvs = ctx.enter_context(nc.sbuf_tensor("wvs", [128, 11 * D], BF16))
        outsb = ctx.enter_context(nc.sbuf_tensor("outsb", [128, 8 * 512], BF16))
        vosb = ctx.enter_context(nc.sbuf_tensor("vosb", [128, 4 * 512], BF16))
        pb = [ctx.enter_context(nc.psum_tensor(f"pb{i}", [128, 512], F32))
              for i in range(8)]
        p = Prog(nc, ctx)

        for k in range(11):
            p.op("sync", lambda e, k=k: e.dma_start(
                out=csb[:, k * TC:(k + 1) * TC], in_=ckv[k * 128:(k + 1) * 128, :]),
                writes=[f"c{k}"], dma=True)
            p.op("gpsimd", lambda e, k=k: e.dma_start(
                out=wks[:, k * QP:(k + 1) * QP], in_=wk[k * 128:(k + 1) * 128, :]),
                writes=[f"wk{k}"], dma=True)
            p.op("gpsimd", lambda e, k=k: e.dma_start(
                out=wvs[:, k * D:(k + 1) * D], in_=wv[k * 128:(k + 1) * 128, :]),
                writes=[f"wv{k}"], dma=True)

        for tt in range(NT):
            t0 = tt * 512
            for m in range(8):
                bank = m % 4
                for k in range(11):
                    p.op("tensor", lambda e, k=k, m=m, t0=t0, bank=bank: e.matmul(
                        pb[bank][:, :],
                        wks[:, k * QP + m * 128: k * QP + (m + 1) * 128],
                        csb[:, k * TC + t0: k * TC + t0 + 512],
                        start=(k == 0), stop=(k == 10)),
                        reads=[f"c{k}", f"wk{k}"],
                        writes=[f"pb{bank}"] if k in (0, 10) else [])
                p.op("vector", lambda e, m=m, bank=bank: e.tensor_copy(
                    outsb[:, m * 512:(m + 1) * 512], pb[bank][:, :]),
                    reads=[f"pb{bank}"], writes=[f"out{m}"])
                p.op("sync", lambda e, m=m, t0=t0: e.dma_start(
                    out=knope[m * 128:(m + 1) * 128, t0:t0 + 512],
                    in_=outsb[:, m * 512:(m + 1) * 512]),
                    reads=[f"out{m}"], dma=True)

        for ttile in range(TC // 128):
            for v4 in range(4):
                bank = 4 + (v4 % 2) * 2 + (ttile % 2)
                for k in range(11):
                    p.op("tensor", lambda e, k=k, ttile=ttile, v4=v4, bank=bank: e.matmul(
                        pb[bank][:, :],
                        csb[:, k * TC + ttile * 128: k * TC + (ttile + 1) * 128],
                        wvs[:, k * D + v4 * 512: k * D + (v4 + 1) * 512],
                        start=(k == 0), stop=(k == 10)),
                        reads=[f"c{k}", f"wv{k}"],
                        writes=[f"pb{bank}"] if k in (0, 10) else [])
                p.op("vector", lambda e, v4=v4, bank=bank: e.tensor_copy(
                    vosb[:, v4 * 512:(v4 + 1) * 512], pb[bank][:, :]),
                    reads=[f"pb{bank}"], writes=[f"vo{v4}"])
                p.op("sync", lambda e, ttile=ttile, v4=v4: e.dma_start(
                    out=vt[ttile * 128:(ttile + 1) * 128, v4 * 512:(v4 + 1) * 512],
                    in_=vosb[:, v4 * 512:(v4 + 1) * 512]),
                    reads=[f"vo{v4}"], dma=True)
        p.emit()
    return nc


def _bound_c(p, pb, q_key, q_sb, sq_sb, onc, scratch_row, slot, n_chunks=8):
    for c8 in range(n_chunks):
        p.op("scalar", lambda e, c8=c8: e.activation(
            sq_sb[:, 0:512], q_sb[:, c8 * 512:(c8 + 1) * 512], AF.Square),
            reads=[q_key], writes=["bsq"])
        p.op("tensor", lambda e: e.matmul(
            pb[6][0:1, 0:512], onc[:, 0:1], sq_sb[:, 0:512],
            start=True, stop=True),
            reads=["bsq", "onc"], writes=["pb6"])
        p.op("vector", lambda e, c8=c8, slot=slot: e.reduce_max(
            scratch_row[0:1, 16 + slot * 8 + c8:17 + slot * 8 + c8],
            pb[6][0:1, 0:512], axis=AX),
            reads=["pb6"], writes=[f"m8_{slot}_{c8}"])
    p.op("vector", lambda e, slot=slot: e.reduce_max(
        scratch_row[0:1, slot:slot + 1],
        scratch_row[0:1, 16 + slot * 8:16 + slot * 8 + 8], axis=AX),
        reads=[f"m8_{slot}_{c8}" for c8 in range(n_chunks)], writes=[f"mx{slot}"])


def build_E():
    nc = bass.Bass("TRN2", target_bir_lowering=False)
    qd = nc.dram_tensor("q", [8, 128, S], BF16, kind="ExternalInput").ap()
    kd = nc.dram_tensor("k", [8, 128, S], BF16, kind="ExternalInput").ap()
    vd = nc.dram_tensor("v", [8, S, 128], BF16, kind="ExternalInput").ap()
    maskE = nc.dram_tensor("maskE", [128, 2048], F32, kind="ExternalInput").ap()
    ones_col = nc.dram_tensor("ones_col", [128, 1], BF16, kind="ExternalInput").ap()
    ones_row = nc.dram_tensor("ones_row", [1, 128], F32, kind="ExternalInput").ap()
    ctxd = nc.dram_tensor("ctx", [8, 128, S], BF16, kind="ExternalOutput").ap()

    with ExitStack() as ctx:
        q_sb = ctx.enter_context(nc.sbuf_tensor("q_sb", [128, S], BF16))
        k_sb = ctx.enter_context(nc.sbuf_tensor("k_sb", [128, S], BF16))
        vt_sb = ctx.enter_context(nc.sbuf_tensor("vt_sb", [128, S], BF16))
        msk = ctx.enter_context(nc.sbuf_tensor("msk", [128, 2048], F32))
        onc = ctx.enter_context(nc.sbuf_tensor("onc", [128, 1], BF16))
        onr = ctx.enter_context(nc.sbuf_tensor("onr", [1, 128], F32))
        sq_sb = ctx.enter_context(nc.sbuf_tensor("sq_sb", [128, 512], BF16))
        srow = ctx.enter_context(nc.sbuf_tensor("srow", [1, 64], F32))
        negc = ctx.enter_context(nc.sbuf_tensor("negc", [128, 1], F32))
        probs = ctx.enter_context(nc.sbuf_tensor("probs", [128, 2 * 512], BF16))
        rbc = ctx.enter_context(nc.sbuf_tensor("rbc", [128, 512], F32))
        rrow = ctx.enter_context(nc.sbuf_tensor("rrow", [1, 512], F32))
        ctxo = ctx.enter_context(nc.sbuf_tensor("ctxo", [128, 512], BF16))
        pb = [ctx.enter_context(nc.psum_tensor(f"pb{i}", [128, 512], F32))
              for i in range(8)]
        p = Prog(nc, ctx)

        p.op("sync", lambda e: e.dma_start(out=msk[:], in_=maskE), writes=["msk"], dma=True)
        p.op("sync", lambda e: e.dma_start(out=onc[:], in_=ones_col), writes=["onc"], dma=True)
        p.op("sync", lambda e: e.dma_start(out=onr[:], in_=ones_row), writes=["onr"], dma=True)

        for i in range(8):
            p.op("sync", lambda e, i=i: e.dma_start(out=q_sb[:], in_=qd[i]),
                 writes=["q"], dma=True)
            p.op("sync", lambda e, i=i: e.dma_start(out=k_sb[:], in_=kd[i]),
                 writes=["k"], dma=True)
            for g in range(8):
                p.op("sync", lambda e, i=i, g=g: e.dma_start(
                    out=vt_sb[:, g * 512:(g + 1) * 512].rearrange(
                        "p (a d) -> p a d", d=128),
                    in_=vd[i, g * 512:(g + 1) * 512, :].rearrange(
                        "(a p) d -> p a d", p=128)),
                    writes=["v"], dma=True)

            _bound_c(p, pb, "q", q_sb, sq_sb, onc, srow, 0)
            _bound_c(p, pb, "k", k_sb, sq_sb, onc, srow, 1)
            p.op("vector", lambda e: e.tensor_mul(
                srow[0:1, 2:3], srow[0:1, 0:1], srow[0:1, 1:2]),
                reads=["mx0", "mx1"], writes=["prod"])
            p.op("scalar", lambda e: e.activation(
                srow[0:1, 3:4], srow[0:1, 2:3], AF.Ln),
                reads=["prod"], writes=["lnp"])
            p.op("scalar", lambda e: e.activation(
                srow[0:1, 4:5], srow[0:1, 3:4], AF.Exp, scale=0.5),
                reads=["lnp"], writes=["sqp"])
            p.op("vector", lambda e: e.tensor_scalar_mul(
                srow[0:1, 5:6], srow[0:1, 4:5], -SCALE),
                reads=["sqp"], writes=["negc_row"])
            p.op("tensor", lambda e: e.matmul(
                pb[6][:, 0:1], onr[0:1, 0:128], srow[0:1, 5:6],
                start=True, stop=True),
                reads=["negc_row", "onr"], writes=["pb6"])
            p.op("vector", lambda e: e.tensor_copy(negc[:, 0:1], pb[6][:, 0:1]),
                 reads=["pb6"], writes=["negc"])

            for sl in range(8):
                nkb = 4 * sl + 4
                for kb in range(nkb):
                    b0 = kb % 2
                    p.op("tensor", lambda e, sl=sl, kb=kb, b0=b0: e.matmul(
                        pb[b0][:, :],
                        k_sb[:, kb * 128:(kb + 1) * 128],
                        q_sb[:, sl * 512:(sl + 1) * 512],
                        start=True, stop=True),
                        reads=["k", "q"], writes=[f"pb{b0}"])
                    j = kb - 4 * sl
                    if j >= 0:
                        p.op("vector", lambda e, j=j, b0=b0: e.tensor_add(
                            pb[b0][:, :], pb[b0][:, :], msk[:, j * 512:(j + 1) * 512]),
                            reads=[f"pb{b0}", "msk"], writes=[f"pb{b0}"])
                    p.op("scalar", lambda e, b0=b0: e.activation(
                        probs[:, b0 * 512:(b0 + 1) * 512], pb[b0][:, :], AF.Exp,
                        bias=negc[:, 0:1], scale=SCALE),
                        reads=[f"pb{b0}", "negc"], writes=[f"probs{b0}"])
                    p.op("tensor", lambda e, kb=kb, b0=b0, nkb=nkb: e.matmul(
                        pb[4][0:1, :], onc[:, 0:1], probs[:, b0 * 512:(b0 + 1) * 512],
                        start=(kb == 0), stop=(kb == nkb - 1)),
                        reads=[f"probs{b0}", "onc"],
                        writes=["pb4"] if kb in (0, nkb - 1) else [])
                    p.op("tensor", lambda e, kb=kb, b0=b0, nkb=nkb: e.matmul(
                        pb[5][:, :], vt_sb[:, kb * 128:(kb + 1) * 128],
                        probs[:, b0 * 512:(b0 + 1) * 512],
                        start=(kb == 0), stop=(kb == nkb - 1)),
                        reads=[f"probs{b0}", "v"],
                        writes=["pb5"] if kb in (0, nkb - 1) else [])
                p.op("vector", lambda e: e.reciprocal(rrow[0:1, :], pb[4][0:1, :]),
                     reads=["pb4"], writes=["rrow"])
                p.op("tensor", lambda e: e.matmul(
                    pb[7][:, :], onr[0:1, 0:128], rrow[0:1, :],
                    start=True, stop=True),
                    reads=["rrow", "onr"], writes=["pb7"])
                p.op("vector", lambda e: e.tensor_copy(rbc[:, :], pb[7][:, :]),
                     reads=["pb7"], writes=["rbc"])
                p.op("vector", lambda e: e.tensor_mul(ctxo[:, :], pb[5][:, :], rbc[:, :]),
                     reads=["pb5", "rbc"], writes=["ctxo"])
                p.op("sync", lambda e, i=i, sl=sl: e.dma_start(
                    out=ctxd[i, :, sl * 512:(sl + 1) * 512], in_=ctxo[:, :]),
                    reads=["ctxo"], dma=True)
        p.emit()
    return nc


def _ln_rows(p, pb, srow, sum_key, ssq_key, dim, base):
    A = srow[0:1, base * 512:(base + 1) * 512]
    Bm = srow[0:1, (base + 1) * 512:(base + 2) * 512]
    C = srow[0:1, (base + 2) * 512:(base + 3) * 512]
    p.op("vector", lambda e: e.tensor_scalar(A, pb[6][0:1, :], 1.0 / dim, None, ALU.mult),
         reads=[sum_key], writes=["lnA"])
    p.op("vector", lambda e: e.tensor_mul(Bm, A, A), reads=["lnA"], writes=["lnB"])
    p.op("vector", lambda e: e.tensor_scalar(C, pb[7][0:1, :], 1.0 / dim, EPS,
                                             ALU.mult, ALU.add),
         reads=[ssq_key], writes=["lnC"])
    p.op("vector", lambda e: e.tensor_sub(C, C, Bm), reads=["lnB", "lnC"], writes=["lnC"])
    p.op("scalar", lambda e: e.activation(C, C, AF.Ln), reads=["lnC"], writes=["lnC"])
    p.op("scalar", lambda e: e.activation(C, C, AF.Exp, scale=-0.5),
         reads=["lnC"], writes=["lnC"])
    p.op("vector", lambda e: e.tensor_mul(Bm, A, C), reads=["lnA", "lnC"], writes=["lnB"])
    p.op("vector", lambda e: e.tensor_scalar_mul(Bm, Bm, -1.0),
         reads=["lnB"], writes=["lnB"])
    return C, Bm


def build_F():
    nc = bass.Bass("TRN2", target_bir_lowering=False)
    ctxf = nc.dram_tensor("ctxf", [D, TC], BF16, kind="ExternalInput").ap()
    wo = nc.dram_tensor("wo", [D, D], F32, kind="ExternalInput").ap()
    xf = nc.dram_tensor("xf", [D, TC], F32, kind="ExternalInput").ap()
    ones_colb = nc.dram_tensor("ones_colb", [128, 1], BF16, kind="ExternalInput").ap()
    ones_colf = nc.dram_tensor("ones_colf", [128, 1], F32, kind="ExternalInput").ap()
    ones_row = nc.dram_tensor("ones_row", [1, 128], F32, kind="ExternalInput").ap()
    x1f = nc.dram_tensor("x1f", [D, TC], F32, kind="ExternalOutput").ap()

    NT = TC // 512
    with ExitStack() as ctx:
        csb = ctx.enter_context(nc.sbuf_tensor("csb", [128, 16 * TC], BF16))
        wsb = ctx.enter_context(nc.sbuf_tensor("wsb", [128, 16 * D], BF16))
        oncb = ctx.enter_context(nc.sbuf_tensor("oncb", [128, 1], BF16))
        oncf = ctx.enter_context(nc.sbuf_tensor("oncf", [128, 1], F32))
        onr = ctx.enter_context(nc.sbuf_tensor("onr", [1, 128], F32))
        xc = ctx.enter_context(nc.sbuf_tensor("xc", [128, 2 * 512], F32))
        u1 = ctx.enter_context(nc.sbuf_tensor("u1", [128, 16 * 512], F32))
        sq = ctx.enter_context(nc.sbuf_tensor("sq", [128, 512], BF16))
        srow = ctx.enter_context(nc.sbuf_tensor("srow", [1, 6 * 512], F32))
        t1 = ctx.enter_context(nc.sbuf_tensor("t1", [128, 512], F32))
        outs = ctx.enter_context(nc.sbuf_tensor("outs", [128, 512], F32))
        pb = [ctx.enter_context(nc.psum_tensor(f"pb{i}", [128, 512], F32))
              for i in range(8)]
        p = Prog(nc, ctx)

        for k in range(16):
            p.op("sync", lambda e, k=k: e.dma_start(
                out=csb[:, k * TC:(k + 1) * TC], in_=ctxf[k * 128:(k + 1) * 128, :]),
                writes=[f"c{k}"], dma=True)
            p.op("gpsimd", lambda e, k=k: e.dma_start(
                out=wsb[:, k * D:(k + 1) * D], in_=wo[k * 128:(k + 1) * 128, :]),
                writes=[f"w{k}"], dma=True)
        p.op("sync", lambda e: e.dma_start(out=oncb[:], in_=ones_colb), writes=["oncb"], dma=True)
        p.op("sync", lambda e: e.dma_start(out=oncf[:], in_=ones_colf), writes=["oncf"], dma=True)
        p.op("sync", lambda e: e.dma_start(out=onr[:], in_=ones_row), writes=["onr"], dma=True)

        for tt in range(NT):
            t0 = tt * 512
            for m in range(16):
                bank = m % 4
                xb = m % 2
                p.op("sync", lambda e, m=m, t0=t0, xb=xb: e.dma_start(
                    out=xc[:, xb * 512:(xb + 1) * 512],
                    in_=xf[m * 128:(m + 1) * 128, t0:t0 + 512]),
                    writes=[f"xc{xb}"], dma=True)
                for k in range(16):
                    p.op("tensor", lambda e, k=k, m=m, t0=t0, bank=bank: e.matmul(
                        pb[bank][:, :],
                        wsb[:, k * D + m * 128: k * D + (m + 1) * 128],
                        csb[:, k * TC + t0: k * TC + t0 + 512],
                        start=(k == 0), stop=(k == 15)),
                        reads=[f"c{k}", f"w{k}"],
                        writes=[f"pb{bank}"] if k in (0, 15) else [])
                cp = p.op("vector", lambda e, m=m, bank=bank, xb=xb: e.tensor_add(
                    u1[:, m * 512:(m + 1) * 512], pb[bank][:, :], xc[:, xb * 512:(xb + 1) * 512]),
                    reads=[f"pb{bank}", f"xc{xb}"], writes=[f"u1_{m}"])
                p.op("tensor", lambda e, m=m: e.matmul(
                    pb[6][0:1, :], oncf[:, 0:1], u1[:, m * 512:(m + 1) * 512],
                    start=(m == 0), stop=(m == 15)),
                    reads=[f"u1_{m}", "oncf"],
                    writes=["pb6"] if m in (0, 15) else [])
                p.op("scalar", lambda e, m=m: e.activation(
                    sq[:, :], u1[:, m * 512:(m + 1) * 512], AF.Square),
                    reads=[f"u1_{m}"], writes=["sq"])
                p.op("tensor", lambda e, m=m: e.matmul(
                    pb[7][0:1, :], oncb[:, 0:1], sq[:, :],
                    start=(m == 0), stop=(m == 15)),
                    reads=["sq", "oncb"],
                    writes=["pb7"] if m in (0, 15) else [])
            R, Bm = _ln_rows(p, pb, srow, "pb6", "pb7", D, 0)
            p.op("tensor", lambda e: e.matmul(
                pb[4][:, :], onr[0:1, 0:128], R, start=True, stop=True),
                reads=["lnC", "onr"], writes=["pb4"])
            p.op("tensor", lambda e: e.matmul(
                pb[5][:, :], onr[0:1, 0:128], Bm, start=True, stop=True),
                reads=["lnB", "onr"], writes=["pb5"])
            for m in range(16):
                p.op("vector", lambda e, m=m: e.tensor_mul(
                    t1[:, :], u1[:, m * 512:(m + 1) * 512], pb[4][:, :]),
                    reads=[f"u1_{m}", "pb4"], writes=["t1"])
                p.op("vector", lambda e: e.tensor_add(outs[:, :], t1[:, :], pb[5][:, :]),
                     reads=["t1", "pb5"], writes=["outs"])
                p.op("sync", lambda e, m=m, t0=t0: e.dma_start(
                    out=x1f[m * 128:(m + 1) * 128, t0:t0 + 512], in_=outs[:, :]),
                    reads=["outs"], dma=True)
        p.emit()
    return nc


def build_G():
    nc = bass.Bass("TRN2", target_bir_lowering=False)
    xinf = nc.dram_tensor("xinf", [D, TC], F32, kind="ExternalInput").ap()
    gws = nc.dram_tensor("gws", [44, 128, 2048], F32, kind="ExternalInput").ap()
    uws = nc.dram_tensor("uws", [44, 128, 2048], F32, kind="ExternalInput").ap()
    dws = nc.dram_tensor("dws", [16, 128, 5632], F32, kind="ExternalInput").ap()
    gs = nc.dram_tensor("gs", [128, 44], F32, kind="ExternalInput").ap()
    us = nc.dram_tensor("us", [128, 44], F32, kind="ExternalInput").ap()
    ds = nc.dram_tensor("ds", [128, 16], F32, kind="ExternalInput").ap()
    ones_colb = nc.dram_tensor("ones_colb", [128, 1], BF16, kind="ExternalInput").ap()
    ones_colf = nc.dram_tensor("ones_colf", [128, 1], F32, kind="ExternalInput").ap()
    ones_row = nc.dram_tensor("ones_row", [1, 128], F32, kind="ExternalInput").ap()
    xof = nc.dram_tensor("xof", [D, TC], F32, kind="ExternalOutput").ap()
    u2d = nc.dram_tensor("u2d", [D, TC], F32, kind="Internal").ap()

    TH = TC // 2
    with ExitStack() as ctx:
        xb_sb = ctx.enter_context(nc.sbuf_tensor("xb_sb", [128, 16 * TH], BF16))
        gw = ctx.enter_context(nc.sbuf_tensor("gw", [128, 2 * 2048], BF16))
        uw = ctx.enter_context(nc.sbuf_tensor("uw", [128, 2 * 2048], BF16))
        dw = ctx.enter_context(nc.sbuf_tensor("dw", [128, 2 * 5632], BF16))
        gssb = ctx.enter_context(nc.sbuf_tensor("gssb", [128, 44], F32))
        ussb = ctx.enter_context(nc.sbuf_tensor("ussb", [128, 44], F32))
        dssb = ctx.enter_context(nc.sbuf_tensor("dssb", [128, 16], F32))
        oncb = ctx.enter_context(nc.sbuf_tensor("oncb", [128, 1], BF16))
        oncf = ctx.enter_context(nc.sbuf_tensor("oncf", [128, 1], F32))
        onr = ctx.enter_context(nc.sbuf_tensor("onr", [1, 128], F32))
        g_sb = ctx.enter_context(nc.sbuf_tensor("g_sb", [128, 1024], F32))
        u_sb = ctx.enter_context(nc.sbuf_tensor("u_sb", [128, 1024], F32))
        hdn = ctx.enter_context(nc.sbuf_tensor("hdn", [128, 44 * 1024], BF16))
        xc = ctx.enter_context(nc.sbuf_tensor("xc", [128, 2 * 512], F32))
        u2 = ctx.enter_context(nc.sbuf_tensor("u2", [128, 2 * 512], F32))
        sq = ctx.enter_context(nc.sbuf_tensor("sq", [128, 512], BF16))
        srow = ctx.enter_context(nc.sbuf_tensor("srow", [1, 12 * 512], F32))
        t1 = ctx.enter_context(nc.sbuf_tensor("t1", [128, 512], F32))
        outs = ctx.enter_context(nc.sbuf_tensor("outs", [128, 512], F32))
        pb = [ctx.enter_context(nc.psum_tensor(f"pb{i}", [128, 512], F32))
              for i in range(8)]
        p = Prog(nc, ctx)

        p.op("sync", lambda e: e.dma_start(out=gssb[:], in_=gs), writes=["gs"], dma=True)
        p.op("sync", lambda e: e.dma_start(out=ussb[:], in_=us), writes=["us"], dma=True)
        p.op("sync", lambda e: e.dma_start(out=dssb[:], in_=ds), writes=["ds"], dma=True)
        p.op("sync", lambda e: e.dma_start(out=oncb[:], in_=ones_colb), writes=["oncb"], dma=True)
        p.op("sync", lambda e: e.dma_start(out=oncf[:], in_=ones_colf), writes=["oncf"], dma=True)
        p.op("sync", lambda e: e.dma_start(out=onr[:], in_=ones_row), writes=["onr"], dma=True)

        for half in range(2):
            h0 = half * TH
            for k in range(16):
                p.op("gpsimd", lambda e, k=k, h0=h0: e.dma_start(
                    out=xb_sb[:, k * TH:(k + 1) * TH],
                    in_=xinf[k * 128:(k + 1) * 128, h0:h0 + TH]),
                    writes=[f"xb{k}"], dma=True)
            for i in range(44):
                wsl = i % 2
                p.op("gpsimd", lambda e, i=i, wsl=wsl: e.dma_start(
                    out=gw[:, wsl * 2048:(wsl + 1) * 2048], in_=gws[i]),
                    writes=[f"gw{wsl}"], dma=True)
                p.op("gpsimd", lambda e, i=i, wsl=wsl: e.dma_start(
                    out=uw[:, wsl * 2048:(wsl + 1) * 2048], in_=uws[i]),
                    writes=[f"uw{wsl}"], dma=True)
                for k in range(16):
                    for t2 in range(2):
                        p.op("tensor", lambda e, k=k, t2=t2, wsl=wsl: e.matmul(
                            pb[t2][:, :],
                            gw[:, wsl * 2048 + k * 128: wsl * 2048 + (k + 1) * 128],
                            xb_sb[:, k * TH + t2 * 512: k * TH + t2 * 512 + 512],
                            start=(k == 0), stop=(k == 15)),
                            reads=[f"gw{wsl}", f"xb{k}"],
                            writes=[f"pb{t2}"] if k in (0, 15) else [])
                for k in range(16):
                    for t2 in range(2):
                        p.op("tensor", lambda e, k=k, t2=t2, wsl=wsl: e.matmul(
                            pb[2 + t2][:, :],
                            uw[:, wsl * 2048 + k * 128: wsl * 2048 + (k + 1) * 128],
                            xb_sb[:, k * TH + t2 * 512: k * TH + t2 * 512 + 512],
                            start=(k == 0), stop=(k == 15)),
                            reads=[f"uw{wsl}", f"xb{k}"],
                            writes=[f"pb{2 + t2}"] if k in (0, 15) else [])
                for t2 in range(2):
                    p.op("scalar", lambda e, i=i, t2=t2: e.activation(
                        g_sb[:, t2 * 512:(t2 + 1) * 512], pb[t2][:, :], AF.Silu,
                        scale=gssb[:, i:i + 1]),
                        reads=[f"pb{t2}", "gs"], writes=[f"g{t2}"])
                    p.op("vector", lambda e, i=i, t2=t2: e.tensor_scalar_mul(
                        u_sb[:, t2 * 512:(t2 + 1) * 512], pb[2 + t2][:, :],
                        ussb[:, i:i + 1]),
                        reads=[f"pb{2 + t2}", "us"], writes=[f"u{t2}"])
                p.op("vector", lambda e, i=i: e.tensor_mul(
                    hdn[:, i * 1024:(i + 1) * 1024], g_sb[:, :], u_sb[:, :]),
                    reads=["g0", "g1", "u0", "u1"], writes=[f"hdn{i}"])
            for f in range(16):
                wsl = f % 2
                p.op("gpsimd", lambda e, f=f, wsl=wsl: e.dma_start(
                    out=dw[:, wsl * 5632:(wsl + 1) * 5632], in_=dws[f]),
                    writes=[f"dw{wsl}"], dma=True)
                for t2 in range(2):
                    p.op("sync", lambda e, f=f, t2=t2, h0=h0: e.dma_start(
                        out=xc[:, t2 * 512:(t2 + 1) * 512],
                        in_=xinf[f * 128:(f + 1) * 128, h0 + t2 * 512:h0 + t2 * 512 + 512]),
                        writes=[f"xc{t2}"], dma=True)
                for ik in range(44):
                    for t2 in range(2):
                        p.op("tensor", lambda e, ik=ik, t2=t2, wsl=wsl: e.matmul(
                            pb[t2][:, :],
                            dw[:, wsl * 5632 + ik * 128: wsl * 5632 + (ik + 1) * 128],
                            hdn[:, ik * 1024 + t2 * 512: ik * 1024 + t2 * 512 + 512],
                            start=(ik == 0), stop=(ik == 43)),
                            reads=[f"dw{wsl}", f"hdn{ik}"],
                            writes=[f"pb{t2}"] if ik in (0, 43) else [])
                for t2 in range(2):
                    cp = p.op("vector", lambda e, f=f, t2=t2: e.scalar_tensor_tensor(
                        u2[:, t2 * 512:(t2 + 1) * 512], pb[t2][:, :],
                        dssb[:, f:f + 1], xc[:, t2 * 512:(t2 + 1) * 512],
                        ALU.mult, ALU.add),
                        reads=[f"pb{t2}", "ds", f"xc{t2}"], writes=[f"u2_{t2}"])
                    p.op("sync", lambda e, f=f, t2=t2, h0=h0: e.dma_start(
                        out=u2d[f * 128:(f + 1) * 128, h0 + t2 * 512:h0 + t2 * 512 + 512],
                        in_=u2[:, t2 * 512:(t2 + 1) * 512]),
                        reads=[f"u2_{t2}"], dma=True)
                    p.op("tensor", lambda e, f=f, t2=t2: e.matmul(
                        pb[4 + t2][0:1, :], oncf[:, 0:1], u2[:, t2 * 512:(t2 + 1) * 512],
                        start=(f == 0), stop=(f == 15)),
                        reads=[f"u2_{t2}", "oncf"],
                        writes=[f"pb{4 + t2}"] if f in (0, 15) else [])
                    p.op("scalar", lambda e, t2=t2: e.activation(
                        sq[:, :], u2[:, t2 * 512:(t2 + 1) * 512], AF.Square),
                        reads=[f"u2_{t2}"], writes=["sq"])
                    p.op("tensor", lambda e, f=f, t2=t2: e.matmul(
                        pb[6 + t2][0:1, :], oncb[:, 0:1], sq[:, :],
                        start=(f == 0), stop=(f == 15)),
                        reads=["sq", "oncb"],
                        writes=[f"pb{6 + t2}"] if f in (0, 15) else [])
            for t2 in range(2):
                base = 6 * t2
                A = srow[0:1, base * 512:(base + 1) * 512]
                Bm = srow[0:1, (base + 1) * 512:(base + 2) * 512]
                C = srow[0:1, (base + 2) * 512:(base + 3) * 512]
                p.op("vector", lambda e, A=A, t2=t2: e.tensor_scalar(
                    A, pb[4 + t2][0:1, :], 1.0 / D, None, ALU.mult),
                    reads=[f"pb{4 + t2}"], writes=[f"lnA{t2}"])
                p.op("vector", lambda e, A=A, Bm=Bm: e.tensor_mul(Bm, A, A),
                     reads=[f"lnA{t2}"], writes=[f"lnB{t2}"])
                p.op("vector", lambda e, C=C, t2=t2: e.tensor_scalar(
                    C, pb[6 + t2][0:1, :], 1.0 / D, EPS, ALU.mult, ALU.add),
                    reads=[f"pb{6 + t2}"], writes=[f"lnC{t2}"])
                p.op("vector", lambda e, C=C, Bm=Bm: e.tensor_sub(C, C, Bm),
                     reads=[f"lnB{t2}", f"lnC{t2}"], writes=[f"lnC{t2}"])
                p.op("scalar", lambda e, C=C: e.activation(C, C, AF.Ln),
                     reads=[f"lnC{t2}"], writes=[f"lnC{t2}"])
                p.op("scalar", lambda e, C=C: e.activation(C, C, AF.Exp, scale=-0.5),
                     reads=[f"lnC{t2}"], writes=[f"lnC{t2}"])
                p.op("vector", lambda e, A=A, Bm=Bm, C=C: e.tensor_mul(Bm, A, C),
                     reads=[f"lnA{t2}", f"lnC{t2}"], writes=[f"lnB{t2}"])
                p.op("vector", lambda e, Bm=Bm: e.tensor_scalar_mul(Bm, Bm, -1.0),
                     reads=[f"lnB{t2}"], writes=[f"lnB{t2}"])
                p.op("tensor", lambda e, C=C, t2=t2: e.matmul(
                    pb[t2][:, :], onr[0:1, 0:128], C, start=True, stop=True),
                    reads=[f"lnC{t2}", "onr"], writes=[f"pb{t2}"])
                p.op("tensor", lambda e, Bm=Bm, t2=t2: e.matmul(
                    pb[2 + t2][:, :], onr[0:1, 0:128], Bm, start=True, stop=True),
                    reads=[f"lnB{t2}", "onr"], writes=[f"pb{2 + t2}"])
                for f in range(16):
                    p.op("sync", lambda e, f=f, t2=t2, h0=h0: e.dma_start(
                        out=u2[:, t2 * 512:(t2 + 1) * 512],
                        in_=u2d[f * 128:(f + 1) * 128, h0 + t2 * 512:h0 + t2 * 512 + 512]),
                        writes=[f"u2_{t2}"], dma=True)
                    p.op("vector", lambda e, t2=t2: e.tensor_mul(
                        t1[:, :], u2[:, t2 * 512:(t2 + 1) * 512], pb[t2][:, :]),
                        reads=[f"u2_{t2}", f"pb{t2}"], writes=["t1"])
                    p.op("vector", lambda e, t2=t2: e.tensor_add(
                        outs[:, :], t1[:, :], pb[2 + t2][:, :]),
                        reads=["t1", f"pb{2 + t2}"], writes=["outs"])
                    p.op("sync", lambda e, f=f, t2=t2, h0=h0: e.dma_start(
                        out=xof[f * 128:(f + 1) * 128, h0 + t2 * 512:h0 + t2 * 512 + 512],
                        in_=outs[:, :]),
                        reads=["outs"], dma=True)
        p.emit()
    return nc


def build_H():
    nc = bass.Bass("TRN2", target_bir_lowering=False)
    xf = nc.dram_tensor("xf", [D, TC], F32, kind="ExternalInput").ap()
    wq = nc.dram_tensor("wq", [D, D], F32, kind="ExternalInput").ap()
    wk = nc.dram_tensor("wk", [D, D], F32, kind="ExternalInput").ap()
    wv = nc.dram_tensor("wv", [D, D], F32, kind="ExternalInput").ap()
    bq = nc.dram_tensor("bq", [1, D], F32, kind="ExternalInput").ap()
    bk = nc.dram_tensor("bk", [1, D], F32, kind="ExternalInput").ap()
    bv = nc.dram_tensor("bv", [1, D], F32, kind="ExternalInput").ap()
    ones512 = nc.dram_tensor("ones512", [1, 512], F32, kind="ExternalInput").ap()
    ones_row = nc.dram_tensor("ones_row", [1, 128], F32, kind="ExternalInput").ap()
    q1 = nc.dram_tensor("q1", [D, TC], BF16, kind="ExternalOutput").ap()
    k1 = nc.dram_tensor("k1", [D, TC], BF16, kind="ExternalOutput").ap()
    v1t = nc.dram_tensor("v1t", [TC, D], BF16, kind="ExternalOutput").ap()

    NT = TC // 512
    with ExitStack() as ctx:
        xsb = ctx.enter_context(nc.sbuf_tensor("xsb", [128, 16 * TC], BF16))
        wsb = ctx.enter_context(nc.sbuf_tensor("wsb", [128, 16 * D], BF16))
        bsb = ctx.enter_context(nc.sbuf_tensor("bsb", [1, 3 * D], F32))
        on5 = ctx.enter_context(nc.sbuf_tensor("on5", [1, 512], F32))
        onr = ctx.enter_context(nc.sbuf_tensor("onr", [1, 128], F32))
        outsb = ctx.enter_context(nc.sbuf_tensor("outsb", [128, 4 * 512], BF16))
        pb = [ctx.enter_context(nc.psum_tensor(f"pb{i}", [128, 512], F32))
              for i in range(8)]
        p = Prog(nc, ctx)

        for k in range(16):
            p.op("gpsimd", lambda e, k=k: e.dma_start(
                out=xsb[:, k * TC:(k + 1) * TC], in_=xf[k * 128:(k + 1) * 128, :]),
                writes=[f"x{k}"], dma=True)
        p.op("sync", lambda e: e.dma_start(out=bsb[0:1, 0:D], in_=bq), writes=["bq"], dma=True)
        p.op("sync", lambda e: e.dma_start(out=bsb[0:1, D:2 * D], in_=bk), writes=["bk"], dma=True)
        p.op("sync", lambda e: e.dma_start(out=bsb[0:1, 2 * D:3 * D], in_=bv), writes=["bv"], dma=True)
        p.op("sync", lambda e: e.dma_start(out=on5[:], in_=ones512), writes=["on5"], dma=True)
        p.op("sync", lambda e: e.dma_start(out=onr[:], in_=ones_row), writes=["onr"], dma=True)

        for wi, (wd, bkey, boff, outd) in enumerate(
                [(wq, "bq", 0, q1), (wk, "bk", D, k1)]):
            for k in range(16):
                p.op("gpsimd", lambda e, k=k, wd=wd: e.dma_start(
                    out=wsb[:, k * D:(k + 1) * D], in_=wd[k * 128:(k + 1) * 128, :]),
                    writes=[f"w{k}"], dma=True)
            for tt in range(NT):
                t0 = tt * 512
                for m in range(16):
                    bank = m % 4
                    p.op("tensor", lambda e, m=m, bank=bank, boff=boff: e.matmul(
                        pb[bank][:, :], bsb[0:1, boff + m * 128: boff + (m + 1) * 128],
                        on5[0:1, :], start=True, stop=False),
                        reads=[bkey, "on5"], writes=[f"pb{bank}"])
                    for k in range(16):
                        p.op("tensor", lambda e, k=k, m=m, t0=t0, bank=bank: e.matmul(
                            pb[bank][:, :],
                            wsb[:, k * D + m * 128: k * D + (m + 1) * 128],
                            xsb[:, k * TC + t0: k * TC + t0 + 512],
                            start=False, stop=(k == 15)),
                            reads=[f"x{k}", f"w{k}"],
                            writes=[f"pb{bank}"] if k == 15 else [])
                    p.op("vector", lambda e, m=m, bank=bank: e.tensor_copy(
                        outsb[:, bank * 512:(bank + 1) * 512], pb[bank][:, :]),
                        reads=[f"pb{bank}"], writes=[f"o{bank}"])
                    p.op("sync", lambda e, m=m, t0=t0, bank=bank, outd=outd: e.dma_start(
                        out=outd[m * 128:(m + 1) * 128, t0:t0 + 512],
                        in_=outsb[:, bank * 512:(bank + 1) * 512]),
                        reads=[f"o{bank}"], dma=True)
        for k in range(16):
            p.op("gpsimd", lambda e, k=k: e.dma_start(
                out=wsb[:, k * D:(k + 1) * D], in_=wv[k * 128:(k + 1) * 128, :]),
                writes=[f"w{k}"], dma=True)
        for ttile in range(TC // 128):
            for v4 in range(4):
                bank = 4 + v4 % 2 + 2 * (ttile % 2)
                p.op("tensor", lambda e, v4=v4, bank=bank: e.matmul(
                    pb[bank][:, :], onr[0:1, 0:128],
                    bsb[0:1, 2 * D + v4 * 512: 2 * D + (v4 + 1) * 512],
                    start=True, stop=False),
                    reads=["bv", "onr"], writes=[f"pb{bank}"])
                for k in range(16):
                    p.op("tensor", lambda e, k=k, ttile=ttile, v4=v4, bank=bank: e.matmul(
                        pb[bank][:, :],
                        xsb[:, k * TC + ttile * 128: k * TC + (ttile + 1) * 128],
                        wsb[:, k * D + v4 * 512: k * D + (v4 + 1) * 512],
                        start=False, stop=(k == 15)),
                        reads=[f"x{k}", f"w{k}"],
                        writes=[f"pb{bank}"] if k == 15 else [])
                p.op("vector", lambda e, v4=v4, bank=bank: e.tensor_copy(
                    outsb[:, v4 * 512:(v4 + 1) * 512], pb[bank][:, :]),
                    reads=[f"pb{bank}"], writes=[f"vo{v4}"])
                p.op("sync", lambda e, ttile=ttile, v4=v4: e.dma_start(
                    out=v1t[ttile * 128:(ttile + 1) * 128, v4 * 512:(v4 + 1) * 512],
                    in_=outsb[:, v4 * 512:(v4 + 1) * 512]),
                    reads=[f"vo{v4}"], dma=True)
        p.emit()
    return nc


def build_I():
    nc = bass.Bass("TRN2", target_bir_lowering=False)
    qd = nc.dram_tensor("q", [8, 128, S], BF16, kind="ExternalInput").ap()
    kd = nc.dram_tensor("k", [8, 128, S], BF16, kind="ExternalInput").ap()
    vd = nc.dram_tensor("v", [8, S, 128], BF16, kind="ExternalInput").ap()
    maskL = nc.dram_tensor("maskL", [128, 384], F32, kind="ExternalInput").ap()
    io_row = nc.dram_tensor("io_row", [128, 64], F32, kind="ExternalInput").ap()
    io_pp = nc.dram_tensor("io_pp", [64, 1], F32, kind="ExternalInput").ap()
    identd = nc.dram_tensor("identd", [128, 128], BF16, kind="ExternalInput").ap()
    ones_col = nc.dram_tensor("ones_col", [128, 1], BF16, kind="ExternalInput").ap()
    ones_row = nc.dram_tensor("ones_row", [1, 128], F32, kind="ExternalInput").ap()
    ctxd = nc.dram_tensor("ctx", [8, 128, S], BF16, kind="ExternalOutput").ap()
    rankm = nc.dram_tensor("rankm", [8, S], F32, kind="Internal").ap()
    maskm = nc.dram_tensor("maskm", [8, S], F32, kind="Internal").ap()

    with ExitStack() as ctx:
        q_sb = ctx.enter_context(nc.sbuf_tensor("q_sb", [128, S], BF16))
        k_sb = ctx.enter_context(nc.sbuf_tensor("k_sb", [128, S], BF16))
        vt_sb = ctx.enter_context(nc.sbuf_tensor("vt_sb", [128, S], BF16))
        msk = ctx.enter_context(nc.sbuf_tensor("msk", [128, 384], F32))
        iorow = ctx.enter_context(nc.sbuf_tensor("iorow", [128, 64], F32))
        iopp = ctx.enter_context(nc.sbuf_tensor("iopp", [64, 1], F32))
        ident = ctx.enter_context(nc.sbuf_tensor("ident", [128, 128], BF16))
        onc = ctx.enter_context(nc.sbuf_tensor("onc", [128, 1], BF16))
        onr = ctx.enter_context(nc.sbuf_tensor("onr", [1, 128], F32))
        sq_sb = ctx.enter_context(nc.sbuf_tensor("sq_sb", [128, 512], BF16))
        srow = ctx.enter_context(nc.sbuf_tensor("srow", [1, 64], F32))
        negc = ctx.enter_context(nc.sbuf_tensor("negc", [128, 1], F32))
        nrow = ctx.enter_context(nc.sbuf_tensor("nrow", [1, S], F32))
        na = ctx.enter_context(nc.sbuf_tensor("na", [8, S], F32))
        nb = ctx.enter_context(nc.sbuf_tensor("nb", [8, S], F32))
        mx8 = ctx.enter_context(nc.sbuf_tensor("mx8", [8, 8], F32))
        probs = ctx.enter_context(nc.sbuf_tensor("probs", [128, 384], BF16))
        lctx = ctx.enter_context(nc.sbuf_tensor("lctx", [128, S], BF16))
        rloc = ctx.enter_context(nc.sbuf_tensor("rloc", [1, S], F32))
        P_sb = ctx.enter_context(nc.sbuf_tensor("P_sb", [128, 64], BF16))
        qt_sb = ctx.enter_context(nc.sbuf_tensor("qt_sb", [128, 128], BF16))
        qgT_sb = ctx.enter_context(nc.sbuf_tensor("qgT_sb", [64, 128], BF16))
        qg_sb = ctx.enter_context(nc.sbuf_tensor("qg_sb", [128, 64], BF16))
        gp_sb = ctx.enter_context(nc.sbuf_tensor("gp_sb", [128, 64], BF16))
        rg_pp = ctx.enter_context(nc.sbuf_tensor("rg_pp", [64, 1], F32))
        gctx_sb = ctx.enter_context(nc.sbuf_tensor("gctx_sb", [64, 128], BF16))
        rr_sb = ctx.enter_context(nc.sbuf_tensor("rr_sb", [1, 512], F32))
        mr_sb = ctx.enter_context(nc.sbuf_tensor("mr_sb", [1, 512], F32))
        cbrow = ctx.enter_context(nc.sbuf_tensor("cbrow", [1, 512], F32))
        cb_sb = ctx.enter_context(nc.sbuf_tensor("cb_sb", [128, 512], F32))
        pi_sb = ctx.enter_context(nc.sbuf_tensor("pi_sb", [64, 512], BF16))
        rkcol = ctx.enter_context(nc.sbuf_tensor("rkcol", [128, 1], F32))
        t2_sb = ctx.enter_context(nc.sbuf_tensor("t2_sb", [128, 512], F32))
        ctxo = ctx.enter_context(nc.sbuf_tensor("ctxo", [128, 512], BF16))
        rb_sb = ctx.enter_context(nc.sbuf_tensor("rb_sb", [128, 128], F32))
        pb = [ctx.enter_context(nc.psum_tensor(f"pb{i}", [128, 512], F32))
              for i in range(8)]
        pbb6 = pb[6].bitcast(BF16)
        p = Prog(nc, ctx)

        for (t, dsrc, key) in [(msk, maskL, "msk"), (iorow, io_row, "iorow"),
                               (iopp, io_pp, "iopp"), (ident, identd, "ident"),
                               (onc, ones_col, "onc"), (onr, ones_row, "onr")]:
            p.op("sync", lambda e, t=t, dsrc=dsrc: e.dma_start(out=t[:], in_=dsrc),
                 writes=[key], dma=True)

        for i in range(8):
            p.op("sync", lambda e, i=i: e.dma_start(out=q_sb[:], in_=qd[i]),
                 writes=["q"], dma=True)
            for c8 in range(8):
                p.op("scalar", lambda e, c8=c8: e.activation(
                    sq_sb[:, :], q_sb[:, c8 * 512:(c8 + 1) * 512], AF.Square),
                    reads=["q"], writes=["sq"])
                p.op("tensor", lambda e: e.matmul(
                    pb[0][0:1, :], onc[:, 0:1], sq_sb[:, :], start=True, stop=True),
                    reads=["sq", "onc"], writes=["pb0"])
                p.op("vector", lambda e, c8=c8: e.tensor_copy(
                    nrow[0:1, c8 * 512:(c8 + 1) * 512], pb[0][0:1, :]),
                    reads=["pb0"], writes=["nrow"])
            p.op("sync", lambda e, i=i: e.dma_start(
                out=na[i:i + 1, :], in_=nrow[0:1, :]),
                reads=["nrow"], writes=["na"], dma=True)
        bufs = [na, nb]
        for r in range(8):
            cur, nxt = bufs[r % 2], bufs[(r + 1) % 2]
            ck, nk = ("na" if r % 2 == 0 else "nb"), ("nb" if r % 2 == 0 else "na")
            p.op("vector", lambda e, cur=cur: e.max(out=mx8[:, :], in_=cur[:, :]),
                 reads=[ck], writes=["mx8"])
            if r == 7:
                p.op("vector", lambda e: e.memset(mx8[:, 6:8], -1.0),
                     reads=[], writes=["mx8"])
            p.op("vector", lambda e, cur=cur, nxt=nxt: e.match_replace(
                out=nxt[:, :], in_to_replace=mx8[:, :], in_values=cur[:, :],
                imm_value=-1.0),
                reads=[ck, "mx8"], writes=[nk])
        p.op("vector", lambda e: e.tensor_scalar(
            nb[:, :], na[:, :], -0.5, None, ALU.is_le),
            reads=["na"], writes=["nb"])
        p.op("vector", lambda e: e.memset(nb[:, 0:1], 1.0), reads=[], writes=["nb"])
        p.op("vector", lambda e: e.memset(nb[:, S - 1:S], 1.0), reads=[], writes=["nb"])
        p.op("vector", lambda e: e.tensor_tensor_scan(
            na[:, :], nb[:, :], nb[:, :], 0.0, ALU.add, ALU.bypass),
            reads=["nb"], writes=["na"])
        p.op("vector", lambda e: e.tensor_sub(na[:, :], na[:, :], nb[:, :]),
             reads=["na", "nb"], writes=["na"])
        p.op("sync", lambda e: e.dma_start(out=maskm, in_=nb[:, :]),
             reads=["nb"], writes=["maskmD"], dma=True)
        p.op("vector", lambda e: e.tensor_scalar(
            nb[:, :], nb[:, :], -1e9, 1e9, ALU.mult, ALU.add),
            reads=["nb"], writes=["nb"])
        p.op("vector", lambda e: e.tensor_add(na[:, :], na[:, :], nb[:, :]),
             reads=["na", "nb"], writes=["na"])
        p.op("sync", lambda e: e.dma_start(out=rankm, in_=na[:, :]),
             reads=["na"], writes=["rankmD"], dma=True)

        for i in range(8):
            p.op("sync", lambda e, i=i: e.dma_start(out=q_sb[:], in_=qd[i]),
                 writes=["q"], dma=True)
            p.op("sync", lambda e, i=i: e.dma_start(out=k_sb[:], in_=kd[i]),
                 writes=["k"], dma=True)
            for g in range(8):
                p.op("sync", lambda e, i=i, g=g: e.dma_start(
                    out=vt_sb[:, g * 512:(g + 1) * 512].rearrange(
                        "p (a d) -> p a d", d=128),
                    in_=vd[i, g * 512:(g + 1) * 512, :].rearrange(
                        "(a p) d -> p a d", p=128)),
                    writes=["v"], dma=True)
            _bound_c(p, pb, "q", q_sb, sq_sb, onc, srow, 0)
            _bound_c(p, pb, "k", k_sb, sq_sb, onc, srow, 1)
            p.op("vector", lambda e: e.tensor_mul(
                srow[0:1, 2:3], srow[0:1, 0:1], srow[0:1, 1:2]),
                reads=["mx0", "mx1"], writes=["prod"])
            p.op("scalar", lambda e: e.activation(
                srow[0:1, 3:4], srow[0:1, 2:3], AF.Ln), reads=["prod"], writes=["lnp"])
            p.op("scalar", lambda e: e.activation(
                srow[0:1, 4:5], srow[0:1, 3:4], AF.Exp, scale=0.5),
                reads=["lnp"], writes=["sqp"])
            p.op("vector", lambda e: e.tensor_scalar_mul(
                srow[0:1, 5:6], srow[0:1, 4:5], -SCALE),
                reads=["sqp"], writes=["negc_row"])
            p.op("tensor", lambda e: e.matmul(
                pb[6][:, 0:1], onr[0:1, 0:128], srow[0:1, 5:6], start=True, stop=True),
                reads=["negc_row", "onr"], writes=["pb6"])
            p.op("vector", lambda e: e.tensor_copy(negc[:, 0:1], pb[6][:, 0:1]),
                 reads=["pb6"], writes=["negc"])

            NB = 32
            for t in range(NB):
                qlo = max(0, (t - 1) * 128)
                qhi = min(S, (t + 2) * 128)
                Nq = qhi - qlo
                mlo = 128 if t == 0 else 0
                p.op("tensor", lambda e, t=t, qlo=qlo, Nq=Nq: e.matmul(
                    pb[0][:, 0:Nq], k_sb[:, t * 128:(t + 1) * 128],
                    q_sb[:, qlo:qlo + Nq], start=True, stop=True),
                    reads=["k", "q"], writes=["pb0"])
                p.op("vector", lambda e, Nq=Nq, mlo=mlo: e.tensor_add(
                    pb[0][:, 0:Nq], pb[0][:, 0:Nq], msk[:, mlo:mlo + Nq]),
                    reads=["pb0", "msk"], writes=["pb0"])
                p.op("scalar", lambda e, Nq=Nq: e.activation(
                    probs[:, 0:Nq], pb[0][:, 0:Nq], AF.Exp,
                    bias=negc[:, 0:1], scale=SCALE),
                    reads=["pb0", "negc"], writes=["probs"])
                for cg in (-1, 0, 1):
                    qb = t + cg
                    if qb < 0 or qb >= NB:
                        continue
                    col = qb * 128 - qlo
                    first = (t == max(0, qb - 1))
                    last = (t == min(NB - 1, qb + 1))
                    cbank = 2 if qb % 3 == 0 else (3 if qb % 3 == 1 else 6)
                    dbank = 1 if qb % 3 == 0 else (4 if qb % 3 == 1 else 5)
                    p.op("tensor", lambda e, col=col, dbank=dbank, first=first,
                         last=last: e.matmul(
                        pb[dbank][0:1, 0:128], onc[:, 0:1], probs[:, col:col + 128],
                        start=first, stop=last),
                        reads=["probs", "onc"],
                        writes=[f"pb{dbank}"] if (first or last) else [])
                    p.op("tensor", lambda e, t=t, col=col, cbank=cbank, first=first,
                         last=last: e.matmul(
                        pb[cbank][:, 0:128], vt_sb[:, t * 128:(t + 1) * 128],
                        probs[:, col:col + 128], start=first, stop=last),
                        reads=["probs", "v"],
                        writes=[f"pb{cbank}"] if (first or last) else [])
                done = [t - 1] if t < NB - 1 else [t - 1, t]
                for qb in done:
                    if qb < 0:
                        continue
                    cbank = 2 if qb % 3 == 0 else (3 if qb % 3 == 1 else 6)
                    dbank = 1 if qb % 3 == 0 else (4 if qb % 3 == 1 else 5)
                    p.op("vector", lambda e, qb=qb, dbank=dbank: e.reciprocal(
                        rloc[0:1, qb * 128:(qb + 1) * 128], pb[dbank][0:1, 0:128]),
                        reads=[f"pb{dbank}"], writes=["rloc"])
                    p.op("vector", lambda e, qb=qb, cbank=cbank: e.tensor_copy(
                        lctx[:, qb * 128:(qb + 1) * 128], pb[cbank][:, 0:128]),
                        reads=[f"pb{cbank}"], writes=["lctx"])

            for c in range(32):
                p.op("sync", lambda e, i=i, c=c: e.dma_start(
                    out=rkcol[:, 0:1], in_=rankm[i:i + 1, c * 128:(c + 1) * 128]
                    .rearrange("o p -> p o")),
                    reads=["rankmD"], writes=["rkcol"], dma=True)
                p.op("vector", lambda e: e.tensor_scalar(
                    P_sb[:, :], iorow[:, :], rkcol[:, 0:1], None, ALU.is_equal),
                    reads=["iorow", "rkcol"], writes=["P"])
                p.op("tensor", lambda e, c=c: e.transpose(
                    pbb6[:, 0:128], q_sb[:, c * 128:(c + 1) * 128], ident[:, :]),
                    reads=["q", "ident"], writes=["pb6"])
                p.op("vector", lambda e: e.tensor_copy(qt_sb[:, :], pbb6[:, 0:128]),
                     reads=["pb6"], writes=["qt"])
                p.op("tensor", lambda e, c=c: e.matmul(
                    pb[0][0:64, 0:128], P_sb[:, :], qt_sb[:, :],
                    start=(c == 0), stop=(c == 31)),
                    reads=["P", "qt"],
                    writes=["pb0"] if c in (0, 31) else [])
            p.op("vector", lambda e: e.tensor_copy(qgT_sb[:, :], pb[0][0:64, 0:128]),
                 reads=["pb0"], writes=["qgT"])
            p.op("tensor", lambda e: e.transpose(
                pbb6[:, 0:64], qgT_sb[:, :], ident[0:64, 0:64]),
                reads=["qgT", "ident"], writes=["pb6"])
            p.op("vector", lambda e: e.tensor_copy(qg_sb[:, :], pbb6[:, 0:64]),
                 reads=["pb6"], writes=["qg"])
            for kb in range(32):
                p.op("tensor", lambda e, kb=kb: e.matmul(
                    pb[1][:, 0:64], k_sb[:, kb * 128:(kb + 1) * 128], qg_sb[:, :],
                    start=True, stop=True),
                    reads=["k", "qg"], writes=["pb1"])
                p.op("scalar", lambda e: e.activation(
                    gp_sb[:, :], pb[1][:, 0:64], AF.Exp,
                    bias=negc[:, 0:1], scale=SCALE),
                    reads=["pb1", "negc"], writes=["gp"])
                p.op("tensor", lambda e, kb=kb: e.matmul(
                    pb[2][0:1, 0:64], onc[:, 0:1], gp_sb[:, :],
                    start=(kb == 0), stop=(kb == 31)),
                    reads=["gp", "onc"],
                    writes=["pb2"] if kb in (0, 31) else [])
                p.op("tensor", lambda e, kb=kb: e.matmul(
                    pb[3][0:64, 0:128], gp_sb[:, :], vt_sb[:, kb * 128:(kb + 1) * 128],
                    start=(kb == 0), stop=(kb == 31)),
                    reads=["gp", "v"],
                    writes=["pb3"] if kb in (0, 31) else [])
            p.op("vector", lambda e: e.reciprocal(srow[0:1, 0:64], pb[2][0:1, 0:64]),
                 reads=["pb2"], writes=["rgrow"])
            p.op("tensor", lambda e: e.matmul(
                pb[5][0:64, 0:1], srow[0:1, 0:64], onr[0:1, 0:1],
                start=True, stop=True),
                reads=["rgrow", "onr"], writes=["pb5"])
            p.op("vector", lambda e: e.tensor_copy(rg_pp[:, 0:1], pb[5][0:64, 0:1]),
                 reads=["pb5"], writes=["rg"])
            p.op("vector", lambda e: e.tensor_scalar_mul(
                gctx_sb[:, :], pb[3][0:64, 0:128], rg_pp[:, 0:1]),
                reads=["pb3", "rg"], writes=["gctx"])
            for c8 in range(8):
                p.op("sync", lambda e, i=i, c8=c8: e.dma_start(
                    out=rr_sb[0:1, :], in_=rankm[i:i + 1, c8 * 512:(c8 + 1) * 512]),
                    reads=["rankmD"], writes=["rr"], dma=True)
                p.op("sync", lambda e, i=i, c8=c8: e.dma_start(
                    out=mr_sb[0:1, :], in_=maskm[i:i + 1, c8 * 512:(c8 + 1) * 512]),
                    reads=["maskmD"], writes=["mr"], dma=True)
                p.op("vector", lambda e: e.tensor_scalar(
                    cbrow[0:1, :], mr_sb[0:1, :], -1.0, 1.0, ALU.mult, ALU.add),
                    reads=["mr"], writes=["cbrow"])
                p.op("vector", lambda e, c8=c8: e.tensor_mul(
                    cbrow[0:1, :], cbrow[0:1, :], rloc[0:1, c8 * 512:(c8 + 1) * 512]),
                    reads=["cbrow", "rloc"], writes=["cbrow"])
                p.op("tensor", lambda e: e.matmul(
                    pb[5][:, :], onr[0:1, 0:128], cbrow[0:1, :], start=True, stop=True),
                    reads=["cbrow", "onr"], writes=["pb5"])
                p.op("vector", lambda e: e.tensor_copy(cb_sb[:, :], pb[5][:, :]),
                     reads=["pb5"], writes=["cb"])
                p.op("tensor", lambda e: e.matmul(
                    pb[7][0:64, :], onr[0:1, 0:64], rr_sb[0:1, :], start=True, stop=True),
                    reads=["rr", "onr"], writes=["pb7"])
                p.op("vector", lambda e: e.tensor_scalar(
                    pi_sb[:, :], pb[7][0:64, :], iopp[:, 0:1], None, ALU.is_equal),
                    reads=["pb7", "iopp"], writes=["pi"])
                p.op("tensor", lambda e: e.matmul(
                    pb[4][:, :], gctx_sb[:, :], pi_sb[:, :], start=True, stop=True),
                    reads=["gctx", "pi"], writes=["pb4"])
                p.op("vector", lambda e, c8=c8: e.tensor_mul(
                    t2_sb[:, :], lctx[:, c8 * 512:(c8 + 1) * 512], cb_sb[:, :]),
                    reads=["lctx", "cb"], writes=["t2"])
                p.op("vector", lambda e: e.tensor_add(ctxo[:, :], t2_sb[:, :], pb[4][:, :]),
                     reads=["t2", "pb4"], writes=["ctxo"])
                p.op("sync", lambda e, i=i, c8=c8: e.dma_start(
                    out=ctxd[i, :, c8 * 512:(c8 + 1) * 512], in_=ctxo[:, :]),
                    reads=["ctxo"], dma=True)
        p.emit()
    return nc


def build_J():
    nc = bass.Bass("TRN2", target_bir_lowering=False)
    a1f = nc.dram_tensor("a1f", [D, TC], BF16, kind="ExternalInput").ap()
    x2f = nc.dram_tensor("x2f", [D, TC], F32, kind="ExternalInput").ap()
    ones_colb = nc.dram_tensor("ones_colb", [128, 1], BF16, kind="ExternalInput").ap()
    ones_colf = nc.dram_tensor("ones_colf", [128, 1], F32, kind="ExternalInput").ap()
    ones_row = nc.dram_tensor("ones_row", [1, 128], F32, kind="ExternalInput").ap()
    x3f = nc.dram_tensor("x3f", [D, TC], F32, kind="ExternalOutput").ap()

    NT = TC // 512
    with ExitStack() as ctx:
        ac = ctx.enter_context(nc.sbuf_tensor("ac", [128, 2 * 512], F32))
        xc = ctx.enter_context(nc.sbuf_tensor("xc", [128, 2 * 512], F32))
        u1 = ctx.enter_context(nc.sbuf_tensor("u1", [128, 16 * 512], F32))
        sq = ctx.enter_context(nc.sbuf_tensor("sq", [128, 512], BF16))
        srow = ctx.enter_context(nc.sbuf_tensor("srow", [1, 6 * 512], F32))
        t1 = ctx.enter_context(nc.sbuf_tensor("t1", [128, 512], F32))
        outs = ctx.enter_context(nc.sbuf_tensor("outs", [128, 512], F32))
        oncb = ctx.enter_context(nc.sbuf_tensor("oncb", [128, 1], BF16))
        oncf = ctx.enter_context(nc.sbuf_tensor("oncf", [128, 1], F32))
        onr = ctx.enter_context(nc.sbuf_tensor("onr", [1, 128], F32))
        pb = [ctx.enter_context(nc.psum_tensor(f"pb{i}", [128, 512], F32))
              for i in range(8)]
        p = Prog(nc, ctx)

        p.op("sync", lambda e: e.dma_start(out=oncb[:], in_=ones_colb), writes=["oncb"], dma=True)
        p.op("sync", lambda e: e.dma_start(out=oncf[:], in_=ones_colf), writes=["oncf"], dma=True)
        p.op("sync", lambda e: e.dma_start(out=onr[:], in_=ones_row), writes=["onr"], dma=True)

        for tt in range(NT):
            t0 = tt * 512
            for m in range(16):
                xb = m % 2
                p.op("gpsimd", lambda e, m=m, t0=t0, xb=xb: e.dma_start(
                    out=ac[:, xb * 512:(xb + 1) * 512],
                    in_=a1f[m * 128:(m + 1) * 128, t0:t0 + 512]),
                    writes=[f"ac{xb}"], dma=True)
                p.op("sync", lambda e, m=m, t0=t0, xb=xb: e.dma_start(
                    out=xc[:, xb * 512:(xb + 1) * 512],
                    in_=x2f[m * 128:(m + 1) * 128, t0:t0 + 512]),
                    writes=[f"xc{xb}"], dma=True)
                p.op("vector", lambda e, m=m, xb=xb: e.tensor_add(
                    u1[:, m * 512:(m + 1) * 512], ac[:, xb * 512:(xb + 1) * 512],
                    xc[:, xb * 512:(xb + 1) * 512]),
                    reads=[f"ac{xb}", f"xc{xb}"], writes=[f"u1_{m}"])
                p.op("tensor", lambda e, m=m: e.matmul(
                    pb[6][0:1, :], oncf[:, 0:1], u1[:, m * 512:(m + 1) * 512],
                    start=(m == 0), stop=(m == 15)),
                    reads=[f"u1_{m}", "oncf"],
                    writes=["pb6"] if m in (0, 15) else [])
                p.op("scalar", lambda e, m=m: e.activation(
                    sq[:, :], u1[:, m * 512:(m + 1) * 512], AF.Square),
                    reads=[f"u1_{m}"], writes=["sq"])
                p.op("tensor", lambda e, m=m: e.matmul(
                    pb[7][0:1, :], oncb[:, 0:1], sq[:, :],
                    start=(m == 0), stop=(m == 15)),
                    reads=["sq", "oncb"],
                    writes=["pb7"] if m in (0, 15) else [])
            R, Bm = _ln_rows(p, pb, srow, "pb6", "pb7", D, 0)
            p.op("tensor", lambda e: e.matmul(
                pb[4][:, :], onr[0:1, 0:128], R, start=True, stop=True),
                reads=["lnC", "onr"], writes=["pb4"])
            p.op("tensor", lambda e: e.matmul(
                pb[5][:, :], onr[0:1, 0:128], Bm, start=True, stop=True),
                reads=["lnB", "onr"], writes=["pb5"])
            for m in range(16):
                p.op("vector", lambda e, m=m: e.tensor_mul(
                    t1[:, :], u1[:, m * 512:(m + 1) * 512], pb[4][:, :]),
                    reads=[f"u1_{m}", "pb4"], writes=["t1"])
                p.op("vector", lambda e: e.tensor_add(outs[:, :], t1[:, :], pb[5][:, :]),
                     reads=["t1", "pb5"], writes=["outs"])
                p.op("sync", lambda e, m=m, t0=t0: e.dma_start(
                    out=x3f[m * 128:(m + 1) * 128, t0:t0 + 512], in_=outs[:, :]),
                    reads=["outs"], dma=True)
        p.emit()
    return nc


_BUILD_CACHE = {}


def _get(name):
    if name not in _BUILD_CACHE:
        _BUILD_CACHE[name] = globals()[f"build_{name}"]()
    return _BUILD_CACHE[name]


def _run(name, maps):
    nc = _get(name)
    return run_bass_kernel_spmd(nc, maps, core_ids=list(range(NCORES))).results


def kernel(**ins):
    st = stage_inputs(ins)
    ones512 = np.ones((1, 512), np.float32)
    cosT, sinT = st["cos1"], st["sin1"]

    def csl(c):
        return slice(c * TC, (c + 1) * TC)

    def psl(c):
        p0 = (c % 2) * 2048
        return slice(p0, p0 + 2048)

    r = _run("A", [dict(x=np.ascontiguousarray(st["x_f"][:, csl(c)]),
                        w=st["wq_down"], gq=st["gq_pp"],
                        ones_col=st["ones_col_bf"], ones_row=st["ones_row"])
                   for c in range(8)])
    cqn = np.concatenate([np.asarray(x["cqn"]) for x in r], 1)

    maps = []
    for c in range(8):
        ps = psl(c)
        cs = np.ascontiguousarray(np.concatenate([cosT[:, ps], sinT[:, ps]], 0))
        sc = np.ascontiguousarray(np.concatenate([sinT[:, ps], cosT[:, ps]], 0))
        maps.append(dict(x=np.ascontiguousarray(st["x_f"][:, csl(c)]),
                         wl=st["wkv_down_lora"], wr=st["wkv_down_rope"],
                         gkv=st["gkv_pp"], cs=cs, sc=sc,
                         ones_col=st["ones_col_bf"], ones_row=st["ones_row"]))
    r = _run("B", maps)
    ckvn = np.concatenate([np.asarray(x["ckvn"]) for x in r], 1)
    kroped = np.concatenate([np.asarray(x["kroped"]) for x in r], 1)

    r = _run("C", [dict(cq=np.ascontiguousarray(cqn[:, csl(c)]), w=st["wq_up"],
                        cos4=np.ascontiguousarray(st["cos4"][:, psl(c)]),
                        sin4=np.ascontiguousarray(st["sin4"][:, psl(c)]))
                   for c in range(8)])
    q0 = np.concatenate([np.asarray(x["q0"]) for x in r], 1)

    r = _run("D", [dict(ckv=np.ascontiguousarray(ckvn[:, csl(c)]),
                        wk=st["wkv_up_k"], wv=st["wkv_up_v"])
                   for c in range(8)])
    knope = np.concatenate([np.asarray(x["knope"]) for x in r], 1)
    vt = np.concatenate([np.asarray(x["vt"]) for x in r], 0)

    maps = []
    for c in range(8):
        q = np.zeros((8, 128, S), BF)
        k = np.zeros((8, 128, S), BF)
        v = np.zeros((8, S, 128), BF)
        for j in range(8):
            g = c * 8 + j
            b, h = g // 16, g % 16
            bs = slice(b * S, (b + 1) * S)
            r0, r1, r2 = q_head_rows(h)
            q[j, 0:64] = q0[r0:r0 + 64, bs]
            q[j, 64:96] = q0[r1:r1 + 32, bs]
            q[j, 96:128] = q0[r2:r2 + 32, bs]
            k[j, 0:64] = knope[h * 64:(h + 1) * 64, bs]
            k[j, 64:128] = kroped[:, bs]
            v[j] = vt[bs, h * 128:(h + 1) * 128]
        maps.append(dict(q=q, k=k, v=v, maskE=st["maskE_E"],
                         ones_col=st["ones_col_bf"], ones_row=st["ones_row"]))
    r = _run("E", maps)
    ctxf = np.zeros((D, T), BF)
    for c in range(8):
        got = np.asarray(r[c]["ctx"])
        for j in range(8):
            g = c * 8 + j
            b, h = g // 16, g % 16
            ctxf[h * 128:(h + 1) * 128, b * S:(b + 1) * S] = got[j]

    r = _run("F", [dict(ctxf=np.ascontiguousarray(ctxf[:, csl(c)]), wo=st["wo"],
                        xf=np.ascontiguousarray(st["x_f"][:, csl(c)]),
                        ones_colb=st["ones_col_bf"], ones_colf=st["ones_col"],
                        ones_row=st["ones_row"])
                   for c in range(8)])
    x1 = np.concatenate([np.asarray(x["x1f"]) for x in r], 1)

    r = _run("G", [dict(xinf=np.ascontiguousarray(x1[:, csl(c)]),
                        gws=st["L0_gws"], uws=st["L0_uws"], dws=st["L0_dws"],
                        gs=st["L0_gate_s"], us=st["L0_up_s"], ds=st["L0_down_s"],
                        ones_colb=st["ones_col_bf"], ones_colf=st["ones_col"],
                        ones_row=st["ones_row"])
                   for c in range(8)])
    x2 = np.concatenate([np.asarray(x["xof"]) for x in r], 1)

    r = _run("H", [dict(xf=np.ascontiguousarray(x2[:, csl(c)]),
                        wq=st["wq1"], wk=st["wk1"], wv=st["wv1"],
                        bq=st["bq1"], bk=st["bk1"], bv=st["bv1"],
                        ones512=ones512, ones_row=st["ones_row"])
                   for c in range(8)])
    q1 = np.concatenate([np.asarray(x["q1"]) for x in r], 1)
    k1 = np.concatenate([np.asarray(x["k1"]) for x in r], 1)
    v1t = np.concatenate([np.asarray(x["v1t"]) for x in r], 0)

    maps = []
    for c in range(8):
        q = np.zeros((8, 128, S), BF)
        k = np.zeros((8, 128, S), BF)
        v = np.zeros((8, S, 128), BF)
        for j in range(8):
            g = c * 8 + j
            b, h = g // 16, g % 16
            bs = slice(b * S, (b + 1) * S)
            q[j] = q1[h * 128:(h + 1) * 128, bs]
            k[j] = k1[h * 128:(h + 1) * 128, bs]
            v[j] = v1t[bs, h * 128:(h + 1) * 128]
        maps.append(dict(q=q, k=k, v=v, maskL=st["maskL"],
                         io_row=st["iota64_row"], io_pp=st["iota64_pp"],
                         identd=st["ident"], ones_col=st["ones_col_bf"],
                         ones_row=st["ones_row"]))
    r = _run("I", maps)
    a1f = np.zeros((D, T), BF)
    for c in range(8):
        got = np.asarray(r[c]["ctx"])
        for j in range(8):
            g = c * 8 + j
            b, h = g // 16, g % 16
            a1f[h * 128:(h + 1) * 128, b * S:(b + 1) * S] = got[j]

    r = _run("J", [dict(a1f=np.ascontiguousarray(a1f[:, csl(c)]),
                        x2f=np.ascontiguousarray(x2[:, csl(c)]),
                        ones_colb=st["ones_col_bf"], ones_colf=st["ones_col"],
                        ones_row=st["ones_row"])
                   for c in range(8)])
    x3 = np.concatenate([np.asarray(x["x3f"]) for x in r], 1)

    r = _run("G", [dict(xinf=np.ascontiguousarray(x3[:, csl(c)]),
                        gws=st["L1_gws"], uws=st["L1_uws"], dws=st["L1_dws"],
                        gs=st["L1_gate_s"], us=st["L1_up_s"], ds=st["L1_down_s"],
                        ones_colb=st["ones_col_bf"], ones_colf=st["ones_col"],
                        ones_row=st["ones_row"])
                   for c in range(8)])
    x4 = np.concatenate([np.asarray(x["xof"]) for x in r], 1)

    return np.ascontiguousarray(x4.T.reshape(B, S, D)).astype(np.float32)


# version 02
# speedup vs baseline: 1.2002x; 1.2002x over previous
import numpy as np
import ml_dtypes
import concourse.bass as bass
import concourse.mybir as mybir
from concourse.bass import IndirectOffsetOnAxis
from concourse.bass_utils import run_bass_kernel_spmd
from contextlib import ExitStack

F32 = mybir.dt.float32
BF16 = mybir.dt.bfloat16
U32 = mybir.dt.uint32
AX = mybir.AxisListType.X
ALU = mybir.AluOpType
AF = mybir.ActivationFunctionType
BF = ml_dtypes.bfloat16

NCORES = 8
B, S, D = 4, 4096, 2048
H, DH = 16, 128
QP = 1024
KVP = 1365
KVPP = 1408
INTER = 5632
EPS = 1e-5
SCALE = 1.0 / float(np.sqrt(np.float32(128.0)))
NEG = -1e9
T = B * S
TC = 2048
PAIRS = [[0, 1], [2, 3], [4, 5], [6, 7]]

ENGS = ("sync", "gpsimd", "tensor", "vector", "scalar")
DMA_POOL = 8


class Prog:

    def __init__(self, nc, ctx):
        self.nc = nc
        self.ops = []
        self.last_writer = {}
        self.readers = {}
        self.eng_count = {e: 0 for e in ENGS}
        self.sems = {e: ctx.enter_context(nc.semaphore(f"se_{e}")) for e in ENGS}
        self.dpool = {e: [ctx.enter_context(nc.semaphore(f"sd_{e}{i}"))
                          for i in range(DMA_POOL)] for e in ("sync", "gpsimd", "scalar")}
        self.dcount = {e: [0] * DMA_POOL for e in ("sync", "gpsimd", "scalar")}
        self.drr = {e: 0 for e in ("sync", "gpsimd", "scalar")}
        self.ccsems = [ctx.enter_context(nc.semaphore(f"scc{i}")) for i in range(8)]
        self.ncc = 0
        self.fence_deps = ()

    def op(self, engine, fn, reads=(), writes=(), dma=False, after=()):
        deps = set(after)
        deps.update(self.fence_deps)
        for k in reads:
            if k in self.last_writer:
                deps.add(self.last_writer[k])
            if k.startswith("pb"):
                for rd in self.readers.get(k, ()):
                    if self.ops[rd][0] != engine:
                        deps.add(rd)
        for k in writes:
            if k in self.last_writer:
                deps.add(self.last_writer[k])
            for rd in self.readers.get(k, ()):
                deps.add(rd)
        i = len(self.ops)
        if dma:
            idx = self.drr[engine] % DMA_POOL
            self.drr[engine] += 1
            before = self.dcount[engine][idx]
            self.dcount[engine][idx] = before + 16
            info = ("dma", idx, before)
            pos = 0
        else:
            info = None
            self.eng_count[engine] += 1
            pos = self.eng_count[engine]
        self.ops.append((engine, fn, deps, info, pos))
        for k in reads:
            self.readers.setdefault(k, []).append(i)
        for k in writes:
            self.last_writer[k] = i
            self.readers[k] = []
        return i

    def cc(self, fn):
        sem = self.ccsems[self.ncc]
        self.ncc += 1
        deps = set(self.fence_deps)
        i = len(self.ops)
        self.ops.append(("gpsimd", fn, deps, ("cc", sem), 0))
        return i

    def fence(self):
        deps = []
        last_eng = {}
        last_dma = {}
        last_cc = []
        for i, (e, fn, d, info, pos) in enumerate(self.ops):
            if info is None:
                last_eng[e] = i
            elif info[0] == "dma":
                last_dma[(e, info[1])] = i
            else:
                last_cc.append(i)
        deps.extend(last_eng.values())
        deps.extend(last_dma.values())
        deps.extend(last_cc[-4:])
        self.fence_deps = tuple(deps)

    def emit(self):
        nc = self.nc
        with nc.Block() as block:
            for ename in ENGS:
                self._emit_engine(block, ename)

    def _emit_engine(self, block, ename):
        prog = self

        def body(eng):
            wm = {}

            def wait(sem, val):
                if val > wm.get(id(sem), 0):
                    eng.wait_ge(sem, val)
                    wm[id(sem)] = val

            def do_waits(deps):
                for d in sorted(deps):
                    de, dfn, ddeps, dinfo, dpos = prog.ops[d]
                    if dinfo is not None and dinfo[0] == "dma":
                        wait(prog.dpool[de][dinfo[1]], dinfo[2] + 16)
                    elif dinfo is not None and dinfo[0] == "cc":
                        wait(dinfo[1], 1)
                    else:
                        if de == ename and ename == "tensor":
                            continue
                        wait(prog.sems[de], dpos)

            for i, (e, fn, deps, info, pos) in enumerate(prog.ops):
                if e != ename:
                    continue
                do_waits(deps)
                if info is not None and info[0] == "dma":
                    idx, before = info[1], info[2]
                    if before > 0:
                        wait(prog.dpool[ename][idx], before)
                    inst = fn(eng)
                    inst.then_inc(prog.dpool[ename][idx], 16)
                elif info is not None and info[0] == "cc":
                    inst = fn(eng)
                    inst.then_inc(info[1])
                else:
                    inst = fn(eng)
                    inst.then_inc(prog.sems[ename], 1)
            if ename == "sync":
                for q, pool in prog.dpool.items():
                    for idx, sem in enumerate(pool):
                        if prog.dcount[q][idx] > 0:
                            wait(sem, prog.dcount[q][idx])
                for i in range(prog.ncc):
                    wait(prog.ccsems[i], 1)

        getattr(block, ename)(body)


class Arena:

    def __init__(self, nc, ctx, cols_f32):
        self.t = ctx.enter_context(nc.sbuf_tensor("arena", [128, cols_f32], F32))
        self.cap = cols_f32
        self.off = 0
        self.peak = 0

    def reset(self, to=0):
        self.off = to

    def f32(self, parts, cols):
        o = self.off
        self.off += cols
        self.peak = max(self.peak, self.off)
        assert self.off <= self.cap, f"arena overflow {self.off}>{self.cap}"
        return self.t[0:parts, o:o + cols]

    def bf16(self, parts, cols):
        cf = (cols + 1) // 2
        o = self.off
        self.off += cf
        self.peak = max(self.peak, self.off)
        assert self.off <= self.cap, f"arena overflow {self.off}>{self.cap}"
        return self.t.bitcast(BF16)[0:parts, 2 * o:2 * o + cols]


class Rows:

    def __init__(self, nc, ctx, cols):
        self.t = ctx.enter_context(nc.sbuf_tensor("rows", [1, cols], F32))
        self.cap = cols
        self.off = 0

    def reset(self):
        self.off = 0

    def f32(self, cols):
        o = self.off
        self.off += cols
        assert self.off <= self.cap, "rows overflow"
        return self.t[0:1, o:o + cols]


def _f32(a):
    return np.ascontiguousarray(np.asarray(a, dtype=np.float32))


def _rope_tables_np():
    freqs = (1.0 / (np.float32(10000.0) **
                    (np.arange(0, 128, 2, dtype=np.float32) / np.float32(128.0))))
    pos = np.arange(S, dtype=np.float32)
    emb = pos[:, None] * freqs[None, :32]
    return np.cos(emb).astype(np.float32).T.copy(), \
        np.sin(emb).astype(np.float32).T.copy()


def q_pack_perm8(r):
    perm = np.zeros(1024, np.int64)
    for c in range(4):
        for half in range(2):
            j = 2 * c + half
            h = 8 * r + j
            perm[c * 128 + half * 64: c * 128 + half * 64 + 64] = \
                h * 128 + np.arange(64)
    for g in range(2):
        for hh in range(4):
            j = 4 * g + hh
            h = 8 * r + j
            perm[(4 + g) * 128 + hh * 32:(4 + g) * 128 + hh * 32 + 32] = \
                h * 128 + 64 + 2 * np.arange(32)
    for g in range(2):
        for hh in range(4):
            j = 4 * g + hh
            h = 8 * r + j
            perm[(6 + g) * 128 + hh * 32:(6 + g) * 128 + hh * 32 + 32] = \
                h * 128 + 64 + 2 * np.arange(32) + 1
    return perm


def q_head_rows8(j):
    return ((j // 2) * 128 + (j % 2) * 64,
            (4 + j // 4) * 128 + (j % 4) * 32,
            (6 + j // 4) * 128 + (j % 4) * 32)


def stage_inputs(ins):
    sh = {}
    x = ins["L0_attn_x"]
    xf = _f32(x.reshape(T, D).T)

    sh["wq_down"] = _f32(ins["L0_attn_Wq_down"])
    wkv = np.asarray(ins["L0_attn_Wkv_down"], np.float32)
    lora = np.zeros((D, KVPP), np.float32)
    lora[:, :KVP] = wkv[:, :KVP]
    sh["wkv_lora"] = _f32(lora)
    wr = wkv[:, KVP:]
    sh["wkv_rope"] = _f32(np.concatenate([wr[:, 0::2], wr[:, 1::2]], axis=1))

    sh["gq"] = _f32(np.asarray(ins["L0_attn_gq"]).reshape(8, 128).T)
    gkv = np.zeros(KVPP, np.float32)
    gkv[:KVP] = np.asarray(ins["L0_attn_gkv"])
    sh["gkv"] = _f32(gkv.reshape(11, 128).T)
    sh["wo"] = _f32(ins["L0_attn_Wo"])

    cosT, sinT = _rope_tables_np()
    sh["cos4"] = _f32(np.tile(cosT, (4, 1)))
    sh["sin4"] = _f32(np.tile(sinT, (4, 1)))

    wkvu = np.asarray(ins["L0_attn_Wkv_up"], np.float32)
    wkvu_p = np.zeros((KVPP, 3072), np.float32)
    wkvu_p[:KVP] = wkvu
    per_head = wkvu_p.reshape(KVPP, H, 192)

    for L in ("L0", "L1"):
        gw = np.asarray(ins[f"{L}_ffn_gate_w"], np.float32)
        uw = np.asarray(ins[f"{L}_ffn_up_w"], np.float32)
        dw = np.asarray(ins[f"{L}_ffn_down_w"], np.float32)
        sh[f"{L}_gws"] = _f32(gw.reshape(44, 128, 16, 128).transpose(0, 3, 2, 1)
                              .reshape(44, 128, 2048))
        sh[f"{L}_uws"] = _f32(uw.reshape(44, 128, 16, 128).transpose(0, 3, 2, 1)
                              .reshape(44, 128, 2048))
        sh[f"{L}_dws"] = _f32(dw.reshape(16, 128, 44, 128).transpose(0, 3, 2, 1)
                              .reshape(16, 128, 5632))
        sh[f"{L}_gs"] = _f32(np.asarray(ins[f"{L}_ffn_gate_s"]).reshape(44, 128).T)
        sh[f"{L}_us"] = _f32(np.asarray(ins[f"{L}_ffn_up_s"]).reshape(44, 128).T)
        sh[f"{L}_ds"] = _f32(np.asarray(ins[f"{L}_ffn_down_s"]).reshape(16, 128).T)

    sh["ones_colb"] = np.ones((128, 1), BF)
    sh["ones_colf"] = np.ones((128, 1), np.float32)
    sh["ones_row"] = np.ones((1, 128), np.float32)
    sh["ones512"] = np.ones((1, 512), np.float32)
    sh["identd"] = np.eye(128).astype(BF)
    kk2 = np.arange(128)[:, None]
    qq2 = np.arange(512)[None, :]
    me = np.zeros((128, 4, 512), np.float32)
    for jj in range(4):
        me[:, jj, :] = np.where(jj * 128 + kk2 > qq2, NEG, 0.0)
    sh["maskE"] = _f32(me.reshape(128, 2048))
    ml = np.zeros((128, 384), np.float32)
    ml[64:128, 0:128] = NEG
    ml[0:64, 256:384] = NEG
    sh["maskL"] = _f32(ml)
    sh["io_row"] = _f32(np.tile(np.arange(64, dtype=np.float32)[None, :], (128, 1)))
    sh["io_pp"] = _f32(np.arange(64, dtype=np.float32).reshape(64, 1))

    wq1 = np.asarray(ins["L1_attn_Wq"], np.float32)
    wk1 = np.asarray(ins["L1_attn_Wk"], np.float32)
    wv1 = np.asarray(ins["L1_attn_Wv"], np.float32)
    bq1 = np.asarray(ins["L1_attn_bq"], np.float32)
    bk1 = np.asarray(ins["L1_attn_bk"], np.float32)
    bv1 = np.asarray(ins["L1_attn_bv"], np.float32)

    i128 = np.arange(128, dtype=np.uint32)[:, None]
    cores = []
    for c in range(NCORES):
        b, r = c // 2, c % 2
        pc = {}
        pc["x_f"] = _f32(xf[:, c * TC:(c + 1) * TC])
        pc["wq_up"] = _f32(np.asarray(ins["L0_attn_Wq_up"], np.float32)[:, q_pack_perm8(r)])
        hs = slice(8 * r, 8 * r + 8)
        pc["wkv_up_k"] = _f32(per_head[:, hs, :64].reshape(KVPP, 512))
        pc["wkv_up_v"] = _f32(per_head[:, hs, 64:].reshape(KVPP, 1024))
        ps = slice(r * 2048, r * 2048 + 2048)
        pc["cs_own"] = _f32(np.concatenate([cosT[:, ps], sinT[:, ps]], 0))
        pc["sc_own"] = _f32(np.concatenate([sinT[:, ps], cosT[:, ps]], 0))
        cs = slice(1024 * r, 1024 * r + 1024)
        pc["wq1"] = _f32(wq1[:, cs])
        pc["wk1"] = _f32(wk1[:, cs])
        pc["wv1"] = _f32(wv1[:, cs])
        pc["bq1"] = _f32(bq1[cs].reshape(1, 1024))
        pc["bk1"] = _f32(bk1[cs].reshape(1, 1024))
        pc["bv1"] = _f32(bv1[cs].reshape(1, 1024))
        pc["idx_fcq"] = (r * 1024 + np.arange(8, dtype=np.uint32)[None, :] * 128
                         + i128).astype(np.uint32)
        pc["idx_fckv"] = (r * 1408 + np.arange(11, dtype=np.uint32)[None, :] * 128
                          + i128).astype(np.uint32)
        pc["idx_fkr"] = (r * 64 + np.arange(64, dtype=np.uint32)[:, None]
                         ).astype(np.uint32)
        pc["idx_h"] = ((8 * r + np.arange(8, dtype=np.uint32)[None, :]) * 128
                       + i128).astype(np.uint32)
        pc["idx_t"] = (r * 2048 + np.arange(16, dtype=np.uint32)[None, :] * 128
                       + i128).astype(np.uint32)
        pc["idx_c"] = (2 * (np.arange(16, dtype=np.uint32)[None, :] * 128 + i128)
                       + r).astype(np.uint32)
        cores.append(pc)
    return sh, cores


def em_AB(p, t, C, ar, rw):
    xsb = ar.bf16(128, 16 * TC)
    for k in range(16):
        p.op("gpsimd", lambda e, k=k: e.dma_start(
            out=xsb[:, k * TC:(k + 1) * TC], in_=t["x_f"][k * 128:(k + 1) * 128, :]),
            writes=[f"x{k}"], dma=True)
    a0 = ar.off
    wsb = ar.bf16(128, 16 * QP)
    craw = ar.f32(128, 8 * 512)
    sqs = ar.bf16(128, 512)
    outsb = ar.bf16(128, 8 * 512)
    rowbuf = rw.f32(3 * 512)
    for k in range(16):
        p.op("gpsimd", lambda e, k=k: e.dma_start(
            out=wsb[:, k * QP:(k + 1) * QP], in_=t["wq_down"][k * 128:(k + 1) * 128, :]),
            writes=[f"A:w{k}"], dma=True)
    NT = TC // 512
    for tt in range(NT):
        t0 = tt * 512
        for m in range(8):
            bank = m % 4
            for k in range(16):
                p.op("tensor", lambda e, k=k, m=m, t0=t0, bank=bank: e.matmul(
                    p.pb[bank][:, :],
                    wsb[:, k * QP + m * 128: k * QP + (m + 1) * 128],
                    xsb[:, k * TC + t0: k * TC + t0 + 512],
                    start=(k == 0), stop=(k == 15)),
                    reads=[f"x{k}", f"A:w{k}"],
                    writes=[f"pb{bank}"] if k in (0, 15) else [])
            cp = p.op("vector", lambda e, m=m, bank=bank: e.tensor_copy(
                craw[:, m * 512:(m + 1) * 512], p.pb[bank][:, :]),
                reads=[f"pb{bank}"], writes=[f"A:craw{m}"])
            p.op("scalar", lambda e, bank=bank: e.activation(
                sqs[:, :], p.pb[bank][:, :], AF.Square),
                reads=[f"pb{bank}"], writes=["A:sqs"], after=[cp])
            p.op("tensor", lambda e, m=m: e.matmul(
                p.pb[4][0:1, :], C["onc"][:, 0:1], sqs[:, :],
                start=(m == 0), stop=(m == 7)),
                reads=["A:sqs", "onc"],
                writes=["pb4"] if m in (0, 7) else [])
        p.op("vector", lambda e: e.tensor_scalar(
            rowbuf[0:1, 0:512], p.pb[4][0:1, :], 1.0 / QP, EPS, ALU.mult, ALU.add),
            reads=["pb4"], writes=["A:row0"])
        p.op("scalar", lambda e: e.activation(
            rowbuf[0:1, 512:1024], rowbuf[0:1, 0:512], AF.Ln),
            reads=["A:row0"], writes=["A:row1"])
        p.op("scalar", lambda e: e.activation(
            rowbuf[0:1, 1024:1536], rowbuf[0:1, 512:1024], AF.Exp, scale=-0.5),
            reads=["A:row1"], writes=["A:row2"])
        p.op("tensor", lambda e: e.matmul(
            p.pb[5][:, :], C["onr"][0:1, 0:128], rowbuf[0:1, 1024:1536],
            start=True, stop=True),
            reads=["A:row2", "onr"], writes=["pb5"])
        for m in range(8):
            p.op("vector", lambda e, m=m: e.scalar_tensor_tensor(
                outsb[:, m * 512:(m + 1) * 512],
                craw[:, m * 512:(m + 1) * 512],
                C["gq"][:, m:m + 1], p.pb[5][:, :], ALU.mult, ALU.mult),
                reads=[f"A:craw{m}", "gq", "pb5"], writes=[f"A:out{m}"])
            p.op("gpsimd", lambda e, m=m, t0=t0: e.indirect_dma_start(
                t["cq_sh"][:, :],
                IndirectOffsetOnAxis(ap=C["idx_fcq"][:, m:m + 1], axis=0),
                outsb[:, m * 512:(m + 1) * 512], None, element_offset=t0),
                reads=[f"A:out{m}", "idx"], dma=True)
    p.fence()
    ar.reset(a0)
    wsb = ar.bf16(128, 16 * KVPP)
    wrs = ar.bf16(128, 16 * 64)
    css = ar.f32(64, 2048)
    scs = ar.f32(64, 2048)
    craw = ar.f32(128, 11 * 512)
    kraw = ar.f32(64, 2048)
    ksc1 = ar.f32(64, 1024)
    ksc2 = ar.f32(64, 1024)
    kout = ar.bf16(64, 2048)
    sqs = ar.bf16(128, 512)
    outsb = ar.bf16(128, 11 * 512)
    for k in range(16):
        p.op("gpsimd", lambda e, k=k: e.dma_start(
            out=wsb[:, k * KVPP:(k + 1) * KVPP],
            in_=t["wkv_lora"][k * 128:(k + 1) * 128, :]),
            writes=[f"B:w{k}"], dma=True)
        p.op("gpsimd", lambda e, k=k: e.dma_start(
            out=wrs[:, k * 64:(k + 1) * 64],
            in_=t["wkv_rope"][k * 128:(k + 1) * 128, :]),
            writes=[f"B:wr{k}"], dma=True)
    p.op("sync", lambda e: e.dma_start(out=css[:, :], in_=t["cs_own"]),
         writes=["B:cs"], dma=True)
    p.op("sync", lambda e: e.dma_start(out=scs[:, :], in_=t["sc_own"]),
         writes=["B:sc"], dma=True)
    rowbuf = rw.f32(3 * 512)
    for tt in range(NT):
        t0 = tt * 512
        for m in range(11):
            bank = m % 4
            for k in range(16):
                p.op("tensor", lambda e, k=k, m=m, t0=t0, bank=bank: e.matmul(
                    p.pb[bank][:, :],
                    wsb[:, k * KVPP + m * 128: k * KVPP + (m + 1) * 128],
                    xsb[:, k * TC + t0: k * TC + t0 + 512],
                    start=(k == 0), stop=(k == 15)),
                    reads=[f"x{k}", f"B:w{k}"],
                    writes=[f"pb{bank}"] if k in (0, 15) else [])
            cp = p.op("vector", lambda e, m=m, bank=bank: e.tensor_copy(
                craw[:, m * 512:(m + 1) * 512], p.pb[bank][:, :]),
                reads=[f"pb{bank}"], writes=[f"B:craw{m}"])
            p.op("scalar", lambda e, bank=bank: e.activation(
                sqs[:, :], p.pb[bank][:, :], AF.Square),
                reads=[f"pb{bank}"], writes=["B:sqs"], after=[cp])
            p.op("tensor", lambda e, m=m: e.matmul(
                p.pb[4][0:1, :], C["onc"][:, 0:1], sqs[:, :],
                start=(m == 0), stop=(m == 10)),
                reads=["B:sqs", "onc"],
                writes=["pb4"] if m in (0, 10) else [])
        for k in range(16):
            p.op("tensor", lambda e, k=k, t0=t0: e.matmul(
                p.pb[5][0:64, :],
                wrs[:, k * 64:(k + 1) * 64],
                xsb[:, k * TC + t0: k * TC + t0 + 512],
                start=(k == 0), stop=(k == 15)),
                reads=[f"x{k}", f"B:wr{k}"],
                writes=["pb5"] if k in (0, 15) else [])
        p.op("vector", lambda e, t0=t0: e.tensor_copy(
            kraw[:, t0:t0 + 512], p.pb[5][0:64, :]),
            reads=["pb5"], writes=["B:kraw"])
        p.op("vector", lambda e: e.tensor_scalar(
            rowbuf[0:1, 0:512], p.pb[4][0:1, :], 1.0 / KVP, EPS, ALU.mult, ALU.add),
            reads=["pb4"], writes=["B:row0"])
        p.op("scalar", lambda e: e.activation(
            rowbuf[0:1, 512:1024], rowbuf[0:1, 0:512], AF.Ln),
            reads=["B:row0"], writes=["B:row1"])
        p.op("scalar", lambda e: e.activation(
            rowbuf[0:1, 1024:1536], rowbuf[0:1, 512:1024], AF.Exp, scale=-0.5),
            reads=["B:row1"], writes=["B:row2"])
        p.op("tensor", lambda e: e.matmul(
            p.pb[6][:, :], C["onr"][0:1, 0:128], rowbuf[0:1, 1024:1536],
            start=True, stop=True),
            reads=["B:row2", "onr"], writes=["pb6"])
        for m in range(11):
            p.op("vector", lambda e, m=m: e.scalar_tensor_tensor(
                outsb[:, m * 512:(m + 1) * 512],
                craw[:, m * 512:(m + 1) * 512],
                C["gkv"][:, m:m + 1], p.pb[6][:, :], ALU.mult, ALU.mult),
                reads=[f"B:craw{m}", "gkv", "pb6"], writes=[f"B:out{m}"])
            p.op("gpsimd", lambda e, m=m, t0=t0: e.indirect_dma_start(
                t["ckv_sh"][:, :],
                IndirectOffsetOnAxis(ap=C["idx_fckv"][:, m:m + 1], axis=0),
                outsb[:, m * 512:(m + 1) * 512], None, element_offset=t0),
                reads=[f"B:out{m}", "idx"], dma=True)
    for hh in range(2):
        c0 = hh * 1024
        cs_ = slice(c0, c0 + 1024)
        p.op("vector", lambda e, cs_=cs_: e.tensor_mul(
            ksc1[0:32, :], kraw[0:32, cs_], css[0:32, cs_]),
            reads=["B:kraw", "B:cs"], writes=["B:ksc1a"])
        p.op("vector", lambda e, cs_=cs_: e.tensor_mul(
            ksc1[32:64, :], kraw[32:64, cs_], css[32:64, cs_]),
            reads=["B:kraw", "B:cs"], writes=["B:ksc1b"])
        p.op("vector", lambda e, cs_=cs_: e.tensor_mul(
            ksc2[0:32, :], kraw[0:32, cs_], scs[0:32, cs_]),
            reads=["B:kraw", "B:sc"], writes=["B:ksc2a"])
        p.op("vector", lambda e, cs_=cs_: e.tensor_mul(
            ksc2[32:64, :], kraw[32:64, cs_], scs[32:64, cs_]),
            reads=["B:kraw", "B:sc"], writes=["B:ksc2b"])
        p.op("vector", lambda e, cs_=cs_: e.tensor_copy(kraw[0:32, cs_], ksc1[32:64, :]),
             reads=["B:ksc1b"], writes=["B:kraw"])
        p.op("vector", lambda e, cs_=cs_: e.tensor_sub(
            kout[0:32, cs_], ksc1[0:32, :], kraw[0:32, cs_]),
            reads=["B:ksc1a", "B:kraw"], writes=[f"B:ne{hh}"])
        p.op("vector", lambda e, cs_=cs_: e.tensor_copy(kraw[32:64, cs_], ksc2[0:32, :]),
             reads=["B:ksc2a"], writes=["B:kraw"])
        p.op("vector", lambda e, cs_=cs_: e.tensor_add(
            kout[32:64, cs_], ksc2[32:64, :], kraw[32:64, cs_]),
            reads=["B:ksc2b", "B:kraw"], writes=[f"B:no{hh}"])
    p.op("gpsimd", lambda e: e.indirect_dma_start(
        t["kr_sh"][:, :],
        IndirectOffsetOnAxis(ap=C["idx_fkr"][0:64, 0:1], axis=0),
        kout[0:64, :], None),
        reads=["B:ne0", "B:no0", "B:ne1", "B:no1", "idx"], dma=True)


def em_C(p, t, C, ar, rw):
    csb = ar.bf16(128, 8 * S)
    wsb = ar.bf16(128, 8 * QP)
    c4 = ar.f32(128, S)
    s4 = ar.f32(128, S)
    rraw = ar.f32(128, 4 * 512)
    rsc1 = ar.f32(128, 512)
    rsc2 = ar.f32(128, 512)
    outsb = ar.bf16(128, 8 * 512)
    for k in range(8):
        for pp in range(2):
            p.op("sync", lambda e, k=k, pp=pp: e.dma_start(
                out=csb[:, k * S + pp * 2048: k * S + pp * 2048 + 2048],
                in_=t["cq_sh"][pp * 1024 + k * 128: pp * 1024 + (k + 1) * 128, :]),
                writes=[f"C:c{k}"], dma=True)
        p.op("gpsimd", lambda e, k=k: e.dma_start(
            out=wsb[:, k * QP:(k + 1) * QP],
            in_=t["wq_up"][k * 128:(k + 1) * 128, :]),
            writes=[f"C:w{k}"], dma=True)
    p.op("sync", lambda e: e.dma_start(out=c4[:, :], in_=t["cos4"]),
         writes=["C:cos"], dma=True)
    p.op("sync", lambda e: e.dma_start(out=s4[:, :], in_=t["sin4"]),
         writes=["C:sin"], dma=True)
    for tt in range(S // 512):
        t0 = tt * 512
        for m in range(8):
            bank = m % 4
            for k in range(8):
                p.op("tensor", lambda e, k=k, m=m, t0=t0, bank=bank: e.matmul(
                    p.pb[bank][:, :],
                    wsb[:, k * QP + m * 128: k * QP + (m + 1) * 128],
                    csb[:, k * S + t0: k * S + t0 + 512],
                    start=(k == 0), stop=(k == 7)),
                    reads=[f"C:c{k}", f"C:w{k}"],
                    writes=[f"pb{bank}"] if k in (0, 7) else [])
            if m < 4:
                p.op("vector", lambda e, m=m, bank=bank: e.tensor_copy(
                    outsb[:, m * 512:(m + 1) * 512], p.pb[bank][:, :]),
                    reads=[f"pb{bank}"], writes=[f"C:out{m}"])
            else:
                p.op("vector", lambda e, m=m, bank=bank: e.tensor_copy(
                    rraw[:, (m - 4) * 512:(m - 3) * 512], p.pb[bank][:, :]),
                    reads=[f"pb{bank}"], writes=[f"C:rr{m - 4}"])
        for g in range(2):
            ek, ok = f"C:rr{g}", f"C:rr{2 + g}"
            E = rraw[:, g * 512:(g + 1) * 512]
            O = rraw[:, (2 + g) * 512:(3 + g) * 512]
            p.op("vector", lambda e, E=E, t0=t0: e.tensor_mul(
                rsc1[:, :], E, c4[:, t0:t0 + 512]),
                reads=[ek, "C:cos"], writes=["C:rs1"])
            p.op("vector", lambda e, O=O, t0=t0: e.tensor_mul(
                rsc2[:, :], O, s4[:, t0:t0 + 512]),
                reads=[ok, "C:sin"], writes=["C:rs2"])
            p.op("vector", lambda e, g=g: e.tensor_sub(
                outsb[:, (4 + g) * 512:(5 + g) * 512], rsc1[:, :], rsc2[:, :]),
                reads=["C:rs1", "C:rs2"], writes=[f"C:out{4 + g}"])
            p.op("vector", lambda e, O=O, t0=t0: e.tensor_mul(
                rsc1[:, :], O, c4[:, t0:t0 + 512]),
                reads=[ok, "C:cos"], writes=["C:rs1"])
            p.op("vector", lambda e, E=E, t0=t0: e.tensor_mul(
                rsc2[:, :], E, s4[:, t0:t0 + 512]),
                reads=[ek, "C:sin"], writes=["C:rs2"])
            p.op("vector", lambda e, g=g: e.tensor_add(
                outsb[:, (6 + g) * 512:(7 + g) * 512], rsc1[:, :], rsc2[:, :]),
                reads=["C:rs1", "C:rs2"], writes=[f"C:out{6 + g}"])
        for m in range(8):
            p.op("sync", lambda e, m=m, t0=t0: e.dma_start(
                out=t["q0"][m * 128:(m + 1) * 128, t0:t0 + 512],
                in_=outsb[:, m * 512:(m + 1) * 512]),
                reads=[f"C:out{m}"], dma=True)


def em_D(p, t, C, ar, rw):
    csb = ar.bf16(128, 11 * S)
    wks = ar.bf16(128, 11 * 512)
    wvs = ar.bf16(128, 11 * 1024)
    outsb = ar.bf16(128, 4 * 512)
    vosb = ar.bf16(128, 2 * 512)
    for k in range(11):
        for pp in range(2):
            p.op("sync", lambda e, k=k, pp=pp: e.dma_start(
                out=csb[:, k * S + pp * 2048: k * S + pp * 2048 + 2048],
                in_=t["ckv_sh"][pp * 1408 + k * 128: pp * 1408 + (k + 1) * 128, :]),
                writes=[f"D:c{k}"], dma=True)
        p.op("gpsimd", lambda e, k=k: e.dma_start(
            out=wks[:, k * 512:(k + 1) * 512],
            in_=t["wkv_up_k"][k * 128:(k + 1) * 128, :]),
            writes=[f"D:wk{k}"], dma=True)
        p.op("gpsimd", lambda e, k=k: e.dma_start(
            out=wvs[:, k * 1024:(k + 1) * 1024],
            in_=t["wkv_up_v"][k * 128:(k + 1) * 128, :]),
            writes=[f"D:wv{k}"], dma=True)
    for tt in range(S // 512):
        t0 = tt * 512
        for m in range(4):
            bank = m % 4
            for k in range(11):
                p.op("tensor", lambda e, k=k, m=m, t0=t0, bank=bank: e.matmul(
                    p.pb[bank][:, :],
                    wks[:, k * 512 + m * 128: k * 512 + (m + 1) * 128],
                    csb[:, k * S + t0: k * S + t0 + 512],
                    start=(k == 0), stop=(k == 10)),
                    reads=[f"D:c{k}", f"D:wk{k}"],
                    writes=[f"pb{bank}"] if k in (0, 10) else [])
            p.op("vector", lambda e, m=m, bank=bank: e.tensor_copy(
                outsb[:, m * 512:(m + 1) * 512], p.pb[bank][:, :]),
                reads=[f"pb{bank}"], writes=[f"D:out{m}"])
            p.op("sync", lambda e, m=m, t0=t0: e.dma_start(
                out=t["knope"][m * 128:(m + 1) * 128, t0:t0 + 512],
                in_=outsb[:, m * 512:(m + 1) * 512]),
                reads=[f"D:out{m}"], dma=True)
    for ttile in range(S // 128):
        for v4 in range(2):
            bank = 4 + (2 * ttile + v4) % 4
            for k in range(11):
                p.op("tensor", lambda e, k=k, ttile=ttile, v4=v4, bank=bank: e.matmul(
                    p.pb[bank][:, :],
                    csb[:, k * S + ttile * 128: k * S + (ttile + 1) * 128],
                    wvs[:, k * 1024 + v4 * 512: k * 1024 + (v4 + 1) * 512],
                    start=(k == 0), stop=(k == 10)),
                    reads=[f"D:c{k}", f"D:wv{k}"],
                    writes=[f"pb{bank}"] if k in (0, 10) else [])
            p.op("vector", lambda e, v4=v4, bank=bank: e.tensor_copy(
                vosb[:, v4 * 512:(v4 + 1) * 512], p.pb[bank][:, :]),
                reads=[f"pb{bank}"], writes=[f"D:vo{v4}"])
            p.op("sync", lambda e, ttile=ttile, v4=v4: e.dma_start(
                out=t["vt"][ttile * 128:(ttile + 1) * 128, v4 * 512:(v4 + 1) * 512],
                in_=vosb[:, v4 * 512:(v4 + 1) * 512]),
                reads=[f"D:vo{v4}"], dma=True)


def _bound_c(p, q_key, q_sb, sq_sb, onc, srow, slot, pf):
    for c8 in range(8):
        p.op("scalar", lambda e, c8=c8: e.activation(
            sq_sb[:, 0:512], q_sb[:, c8 * 512:(c8 + 1) * 512], AF.Square),
            reads=[q_key], writes=[f"{pf}:bsq"])
        p.op("tensor", lambda e: e.matmul(
            p.pb[6][0:1, 0:512], onc[:, 0:1], sq_sb[:, 0:512],
            start=True, stop=True),
            reads=[f"{pf}:bsq", "onc"], writes=["pb6"])
        p.op("vector", lambda e, c8=c8, slot=slot: e.reduce_max(
            srow[0:1, 16 + slot * 8 + c8:17 + slot * 8 + c8],
            p.pb[6][0:1, 0:512], axis=AX),
            reads=["pb6"], writes=[f"{pf}:m8_{slot}_{c8}"])
    p.op("vector", lambda e, slot=slot: e.reduce_max(
        srow[0:1, slot:slot + 1],
        srow[0:1, 16 + slot * 8:16 + slot * 8 + 8], axis=AX),
        reads=[f"{pf}:m8_{slot}_{c8}" for c8 in range(8)], writes=[f"{pf}:mx{slot}"])


def _negc_from_bounds(p, C, srow, negc, pf):
    p.op("vector", lambda e: e.tensor_mul(
        srow[0:1, 2:3], srow[0:1, 0:1], srow[0:1, 1:2]),
        reads=[f"{pf}:mx0", f"{pf}:mx1"], writes=[f"{pf}:prod"])
    p.op("scalar", lambda e: e.activation(
        srow[0:1, 3:4], srow[0:1, 2:3], AF.Ln),
        reads=[f"{pf}:prod"], writes=[f"{pf}:lnp"])
    p.op("scalar", lambda e: e.activation(
        srow[0:1, 4:5], srow[0:1, 3:4], AF.Exp, scale=0.5),
        reads=[f"{pf}:lnp"], writes=[f"{pf}:sqp"])
    p.op("vector", lambda e: e.tensor_scalar_mul(
        srow[0:1, 5:6], srow[0:1, 4:5], -SCALE),
        reads=[f"{pf}:sqp"], writes=[f"{pf}:negc_row"])
    p.op("tensor", lambda e: e.matmul(
        p.pb[6][:, 0:1], C["onr"][0:1, 0:128], srow[0:1, 5:6],
        start=True, stop=True),
        reads=[f"{pf}:negc_row", "onr"], writes=["pb6"])
    p.op("vector", lambda e: e.tensor_copy(negc[:, 0:1], p.pb[6][:, 0:1]),
         reads=["pb6"], writes=[f"{pf}:negc"])


def em_E(p, t, C, ar, rw):
    q_sb = ar.bf16(128, S)
    k_sb = ar.bf16(128, S)
    vt_sb = ar.bf16(128, S)
    msk = ar.f32(128, 2048)
    sq_sb = ar.bf16(128, 512)
    negc = ar.f32(128, 1)
    probs = ar.bf16(128, 2 * 512)
    rbc = ar.f32(128, 512)
    ctxo = ar.bf16(128, 512)
    srow = rw.f32(64)
    rrow = rw.f32(512)
    p.op("sync", lambda e: e.dma_start(out=msk[:, :], in_=t["maskE"]),
         writes=["E:msk"], dma=True)
    for j in range(8):
        nof, evf, odf = q_head_rows8(j)
        p.op("sync", lambda e, nof=nof: e.dma_start(
            out=q_sb[0:64, :], in_=t["q0"][nof:nof + 64, :]),
            writes=["E:q"], dma=True)
        p.op("sync", lambda e, evf=evf: e.dma_start(
            out=q_sb[64:96, :], in_=t["q0"][evf:evf + 32, :]),
            writes=["E:q"], dma=True)
        p.op("sync", lambda e, odf=odf: e.dma_start(
            out=q_sb[96:128, :], in_=t["q0"][odf:odf + 32, :]),
            writes=["E:q"], dma=True)
        p.op("sync", lambda e, j=j: e.dma_start(
            out=k_sb[0:64, :], in_=t["knope"][j * 64:(j + 1) * 64, :]),
            writes=["E:k"], dma=True)
        for pp in range(2):
            p.op("sync", lambda e, pp=pp: e.dma_start(
                out=k_sb[64:96, pp * 2048:(pp + 1) * 2048],
                in_=t["kr_sh"][pp * 64: pp * 64 + 32, :]),
                writes=["E:k"], dma=True)
            p.op("sync", lambda e, pp=pp: e.dma_start(
                out=k_sb[96:128, pp * 2048:(pp + 1) * 2048],
                in_=t["kr_sh"][pp * 64 + 32: pp * 64 + 64, :]),
                writes=["E:k"], dma=True)
        for g in range(8):
            p.op("sync", lambda e, j=j, g=g: e.dma_start(
                out=vt_sb[:, g * 512:(g + 1) * 512].rearrange(
                    "p (a d) -> p a d", d=128),
                in_=t["vt"][g * 512:(g + 1) * 512, j * 128:(j + 1) * 128].rearrange(
                    "(a p) d -> p a d", p=128)),
                writes=["E:v"], dma=True)
        _bound_c(p, "E:q", q_sb, sq_sb, C["onc"], srow, 0, "E")
        _bound_c(p, "E:k", k_sb, sq_sb, C["onc"], srow, 1, "E")
        _negc_from_bounds(p, C, srow, negc, "E")
        for sl in range(8):
            nkb = 4 * sl + 4
            for kb in range(nkb):
                b0 = kb % 2
                p.op("tensor", lambda e, sl=sl, kb=kb, b0=b0: e.matmul(
                    p.pb[b0][:, :],
                    k_sb[:, kb * 128:(kb + 1) * 128],
                    q_sb[:, sl * 512:(sl + 1) * 512],
                    start=True, stop=True),
                    reads=["E:k", "E:q"], writes=[f"pb{b0}"])
                jj = kb - 4 * sl
                if jj >= 0:
                    p.op("vector", lambda e, jj=jj, b0=b0: e.tensor_add(
                        p.pb[b0][:, :], p.pb[b0][:, :], msk[:, jj * 512:(jj + 1) * 512]),
                        reads=[f"pb{b0}", "E:msk"], writes=[f"pb{b0}"])
                p.op("scalar", lambda e, b0=b0: e.activation(
                    probs[:, b0 * 512:(b0 + 1) * 512], p.pb[b0][:, :], AF.Exp,
                    bias=negc[:, 0:1], scale=SCALE),
                    reads=[f"pb{b0}", "E:negc"], writes=[f"E:probs{b0}"])
                p.op("tensor", lambda e, kb=kb, b0=b0, nkb=nkb: e.matmul(
                    p.pb[4][0:1, :], C["onc"][:, 0:1], probs[:, b0 * 512:(b0 + 1) * 512],
                    start=(kb == 0), stop=(kb == nkb - 1)),
                    reads=[f"E:probs{b0}", "onc"],
                    writes=["pb4"] if kb in (0, nkb - 1) else [])
                p.op("tensor", lambda e, kb=kb, b0=b0, nkb=nkb: e.matmul(
                    p.pb[5][:, :], vt_sb[:, kb * 128:(kb + 1) * 128],
                    probs[:, b0 * 512:(b0 + 1) * 512],
                    start=(kb == 0), stop=(kb == nkb - 1)),
                    reads=[f"E:probs{b0}", "E:v"],
                    writes=["pb5"] if kb in (0, nkb - 1) else [])
            p.op("vector", lambda e: e.reciprocal(rrow[0:1, :], p.pb[4][0:1, :]),
                 reads=["pb4"], writes=["E:rrow"])
            p.op("tensor", lambda e: e.matmul(
                p.pb[7][:, :], C["onr"][0:1, 0:128], rrow[0:1, :],
                start=True, stop=True),
                reads=["E:rrow", "onr"], writes=["pb7"])
            p.op("vector", lambda e: e.tensor_copy(rbc[:, :], p.pb[7][:, :]),
                 reads=["pb7"], writes=["E:rbc"])
            p.op("vector", lambda e: e.tensor_mul(ctxo[:, :], p.pb[5][:, :], rbc[:, :]),
                 reads=["pb5", "E:rbc"], writes=["E:ctxo"])
            p.op("gpsimd", lambda e, j=j, sl=sl: e.indirect_dma_start(
                t["ctx0_sh"][:, :],
                IndirectOffsetOnAxis(ap=C["idx_h"][:, j:j + 1], axis=0),
                ctxo[:, :], None, element_offset=sl * 512),
                reads=["E:ctxo", "idx"], dma=True)


def _ln_apply(p, C, pf, u1, nchunk, dim, outfn, srow, t1, outs):
    A_ = srow[0:1, 0:512]
    Bm = srow[0:1, 512:1024]
    Cc = srow[0:1, 1024:1536]
    p.op("vector", lambda e: e.tensor_scalar(A_, p.pb[6][0:1, :], 1.0 / dim, None, ALU.mult),
         reads=["pb6"], writes=[f"{pf}:lnA"])
    p.op("vector", lambda e: e.tensor_mul(Bm, A_, A_), reads=[f"{pf}:lnA"],
         writes=[f"{pf}:lnB"])
    p.op("vector", lambda e: e.tensor_scalar(Cc, p.pb[7][0:1, :], 1.0 / dim, EPS,
                                             ALU.mult, ALU.add),
         reads=["pb7"], writes=[f"{pf}:lnC"])
    p.op("vector", lambda e: e.tensor_sub(Cc, Cc, Bm),
         reads=[f"{pf}:lnB", f"{pf}:lnC"], writes=[f"{pf}:lnC"])
    p.op("scalar", lambda e: e.activation(Cc, Cc, AF.Ln),
         reads=[f"{pf}:lnC"], writes=[f"{pf}:lnC"])
    p.op("scalar", lambda e: e.activation(Cc, Cc, AF.Exp, scale=-0.5),
         reads=[f"{pf}:lnC"], writes=[f"{pf}:lnC"])
    p.op("vector", lambda e: e.tensor_mul(Bm, A_, Cc),
         reads=[f"{pf}:lnA", f"{pf}:lnC"], writes=[f"{pf}:lnB"])
    p.op("vector", lambda e: e.tensor_scalar_mul(Bm, Bm, -1.0),
         reads=[f"{pf}:lnB"], writes=[f"{pf}:lnB"])
    p.op("tensor", lambda e: e.matmul(
        p.pb[4][:, :], C["onr"][0:1, 0:128], Cc, start=True, stop=True),
        reads=[f"{pf}:lnC", "onr"], writes=["pb4"])
    p.op("tensor", lambda e: e.matmul(
        p.pb[5][:, :], C["onr"][0:1, 0:128], Bm, start=True, stop=True),
        reads=[f"{pf}:lnB", "onr"], writes=["pb5"])
    for m in range(nchunk):
        p.op("vector", lambda e, m=m: e.tensor_mul(
            t1[:, :], u1[:, m * 512:(m + 1) * 512], p.pb[4][:, :]),
            reads=[f"{pf}:u1_{m}", "pb4"], writes=[f"{pf}:t1"])
        p.op("vector", lambda e: e.tensor_add(outs[:, :], t1[:, :], p.pb[5][:, :]),
             reads=[f"{pf}:t1", "pb5"], writes=[f"{pf}:outs"])
        outfn(m, outs)


def em_F(p, t, C, ar, rw):
    ctx0v = t["ctx0_sh"].rearrange("h (p w) -> (h p) w", p=2)
    csb = ar.bf16(128, 16 * TC)
    wsb = ar.bf16(128, 16 * D)
    xc = ar.f32(128, 2 * 512)
    u1 = ar.f32(128, 16 * 512)
    sq = ar.bf16(128, 512)
    t1 = ar.f32(128, 512)
    outs = ar.f32(128, 512)
    srow = rw.f32(3 * 512)
    for k in range(16):
        p.op("gpsimd", lambda e, k=k: e.indirect_dma_start(
            csb[:, k * TC:(k + 1) * TC], None,
            ctx0v[:, :], IndirectOffsetOnAxis(ap=C["idx_c"][:, k:k + 1], axis=0)),
            writes=[f"F:c{k}"], reads=["idx"], dma=True)
        p.op("gpsimd", lambda e, k=k: e.dma_start(
            out=wsb[:, k * D:(k + 1) * D], in_=t["wo"][k * 128:(k + 1) * 128, :]),
            writes=[f"F:w{k}"], dma=True)
    for tt in range(TC // 512):
        t0 = tt * 512
        for m in range(16):
            bank = m % 4
            xb = m % 2
            p.op("sync", lambda e, m=m, t0=t0, xb=xb: e.dma_start(
                out=xc[:, xb * 512:(xb + 1) * 512],
                in_=t["x_f"][m * 128:(m + 1) * 128, t0:t0 + 512]),
                writes=[f"F:xc{xb}"], dma=True)
            for k in range(16):
                p.op("tensor", lambda e, k=k, m=m, t0=t0, bank=bank: e.matmul(
                    p.pb[bank][:, :],
                    wsb[:, k * D + m * 128: k * D + (m + 1) * 128],
                    csb[:, k * TC + t0: k * TC + t0 + 512],
                    start=(k == 0), stop=(k == 15)),
                    reads=[f"F:c{k}", f"F:w{k}"],
                    writes=[f"pb{bank}"] if k in (0, 15) else [])
            p.op("vector", lambda e, m=m, bank=bank, xb=xb: e.tensor_add(
                u1[:, m * 512:(m + 1) * 512], p.pb[bank][:, :],
                xc[:, xb * 512:(xb + 1) * 512]),
                reads=[f"pb{bank}", f"F:xc{xb}"], writes=[f"F:u1_{m}"])
            p.op("tensor", lambda e, m=m: e.matmul(
                p.pb[6][0:1, :], C["oncf"][:, 0:1], u1[:, m * 512:(m + 1) * 512],
                start=(m == 0), stop=(m == 15)),
                reads=[f"F:u1_{m}", "oncf"],
                writes=["pb6"] if m in (0, 15) else [])
            p.op("scalar", lambda e, m=m: e.activation(
                sq[:, :], u1[:, m * 512:(m + 1) * 512], AF.Square),
                reads=[f"F:u1_{m}"], writes=["F:sq"])
            p.op("tensor", lambda e, m=m: e.matmul(
                p.pb[7][0:1, :], C["onc"][:, 0:1], sq[:, :],
                start=(m == 0), stop=(m == 15)),
                reads=["F:sq", "onc"],
                writes=["pb7"] if m in (0, 15) else [])

        def outfn(m, o, t0=t0):
            p.op("sync", lambda e, m=m, t0=t0, o=o: e.dma_start(
                out=t["x1"][m * 128:(m + 1) * 128, t0:t0 + 512], in_=o[:, :]),
                reads=["F:outs"], dma=True)
        _ln_apply(p, C, "F", u1, 16, D, outfn, srow, t1, outs)


def em_G(p, t, C, ar, rw, L, xin, xout, scat):
    pf = f"G{L}"
    TH = TC // 2
    xb_sb = ar.bf16(128, 16 * TH)
    gw = ar.bf16(128, 2 * 2048)
    uw = ar.bf16(128, 2 * 2048)
    dw = ar.bf16(128, 2 * 5632)
    gssb = ar.f32(128, 44)
    ussb = ar.f32(128, 44)
    dssb = ar.f32(128, 16)
    g_sb = ar.f32(128, 1024)
    u_sb = ar.f32(128, 1024)
    hdn = ar.bf16(128, 44 * 1024)
    xc = ar.f32(128, 2 * 512)
    u2 = ar.f32(128, 2 * 512)
    sq = ar.bf16(128, 512)
    t1 = ar.f32(128, 512)
    outs = ar.f32(128, 512)
    outs_bf = ar.bf16(128, 512)
    srow = rw.f32(6 * 512)
    p.op("sync", lambda e: e.dma_start(out=gssb[:, :], in_=t[f"{L}_gs"]),
         writes=[f"{pf}:gs"], dma=True)
    p.op("sync", lambda e: e.dma_start(out=ussb[:, :], in_=t[f"{L}_us"]),
         writes=[f"{pf}:us"], dma=True)
    p.op("sync", lambda e: e.dma_start(out=dssb[:, :], in_=t[f"{L}_ds"]),
         writes=[f"{pf}:ds"], dma=True)
    for half in range(2):
        h0 = half * TH
        for k in range(16):
            p.op("gpsimd", lambda e, k=k, h0=h0: e.dma_start(
                out=xb_sb[:, k * TH:(k + 1) * TH],
                in_=xin[k * 128:(k + 1) * 128, h0:h0 + TH]),
                writes=[f"{pf}:xb{k}"], dma=True)
        for i in range(44):
            wsl = i % 2
            p.op("gpsimd", lambda e, i=i, wsl=wsl: e.dma_start(
                out=gw[:, wsl * 2048:(wsl + 1) * 2048], in_=t[f"{L}_gws"][i]),
                writes=[f"{pf}:gw{wsl}"], dma=True)
            p.op("gpsimd", lambda e, i=i, wsl=wsl: e.dma_start(
                out=uw[:, wsl * 2048:(wsl + 1) * 2048], in_=t[f"{L}_uws"][i]),
                writes=[f"{pf}:uw{wsl}"], dma=True)
            for k in range(16):
                for t2 in range(2):
                    p.op("tensor", lambda e, k=k, t2=t2, wsl=wsl: e.matmul(
                        p.pb[t2][:, :],
                        gw[:, wsl * 2048 + k * 128: wsl * 2048 + (k + 1) * 128],
                        xb_sb[:, k * TH + t2 * 512: k * TH + t2 * 512 + 512],
                        start=(k == 0), stop=(k == 15)),
                        reads=[f"{pf}:gw{wsl}", f"{pf}:xb{k}"],
                        writes=[f"pb{t2}"] if k in (0, 15) else [])
            for k in range(16):
                for t2 in range(2):
                    p.op("tensor", lambda e, k=k, t2=t2, wsl=wsl: e.matmul(
                        p.pb[2 + t2][:, :],
                        uw[:, wsl * 2048 + k * 128: wsl * 2048 + (k + 1) * 128],
                        xb_sb[:, k * TH + t2 * 512: k * TH + t2 * 512 + 512],
                        start=(k == 0), stop=(k == 15)),
                        reads=[f"{pf}:uw{wsl}", f"{pf}:xb{k}"],
                        writes=[f"pb{2 + t2}"] if k in (0, 15) else [])
            for t2 in range(2):
                p.op("scalar", lambda e, i=i, t2=t2: e.activation(
                    g_sb[:, t2 * 512:(t2 + 1) * 512], p.pb[t2][:, :], AF.Silu,
                    scale=gssb[:, i:i + 1]),
                    reads=[f"pb{t2}", f"{pf}:gs"], writes=[f"{pf}:g{t2}"])
                p.op("vector", lambda e, i=i, t2=t2: e.tensor_scalar_mul(
                    u_sb[:, t2 * 512:(t2 + 1) * 512], p.pb[2 + t2][:, :],
                    ussb[:, i:i + 1]),
                    reads=[f"pb{2 + t2}", f"{pf}:us"], writes=[f"{pf}:u{t2}"])
            p.op("vector", lambda e, i=i: e.tensor_mul(
                hdn[:, i * 1024:(i + 1) * 1024], g_sb[:, :], u_sb[:, :]),
                reads=[f"{pf}:g0", f"{pf}:g1", f"{pf}:u0", f"{pf}:u1"],
                writes=[f"{pf}:hdn{i}"])
        for f in range(16):
            wsl = f % 2
            p.op("gpsimd", lambda e, f=f, wsl=wsl: e.dma_start(
                out=dw[:, wsl * 5632:(wsl + 1) * 5632], in_=t[f"{L}_dws"][f]),
                writes=[f"{pf}:dw{wsl}"], dma=True)
            for t2 in range(2):
                p.op("sync", lambda e, f=f, t2=t2, h0=h0: e.dma_start(
                    out=xc[:, t2 * 512:(t2 + 1) * 512],
                    in_=xin[f * 128:(f + 1) * 128, h0 + t2 * 512:h0 + t2 * 512 + 512]),
                    writes=[f"{pf}:xc{t2}"], dma=True)
            for ik in range(44):
                for t2 in range(2):
                    p.op("tensor", lambda e, ik=ik, t2=t2, wsl=wsl: e.matmul(
                        p.pb[t2][:, :],
                        dw[:, wsl * 5632 + ik * 128: wsl * 5632 + (ik + 1) * 128],
                        hdn[:, ik * 1024 + t2 * 512: ik * 1024 + t2 * 512 + 512],
                        start=(ik == 0), stop=(ik == 43)),
                        reads=[f"{pf}:dw{wsl}", f"{pf}:hdn{ik}"],
                        writes=[f"pb{t2}"] if ik in (0, 43) else [])
            for t2 in range(2):
                p.op("vector", lambda e, f=f, t2=t2: e.scalar_tensor_tensor(
                    u2[:, t2 * 512:(t2 + 1) * 512], p.pb[t2][:, :],
                    dssb[:, f:f + 1], xc[:, t2 * 512:(t2 + 1) * 512],
                    ALU.mult, ALU.add),
                    reads=[f"pb{t2}", f"{pf}:ds", f"{pf}:xc{t2}"],
                    writes=[f"{pf}:u2_{t2}"])
                p.op("sync", lambda e, f=f, t2=t2, h0=h0: e.dma_start(
                    out=t["u2d"][f * 128:(f + 1) * 128, h0 + t2 * 512:h0 + t2 * 512 + 512],
                    in_=u2[:, t2 * 512:(t2 + 1) * 512]),
                    reads=[f"{pf}:u2_{t2}"], dma=True)
                p.op("tensor", lambda e, f=f, t2=t2: e.matmul(
                    p.pb[4 + t2][0:1, :], C["oncf"][:, 0:1], u2[:, t2 * 512:(t2 + 1) * 512],
                    start=(f == 0), stop=(f == 15)),
                    reads=[f"{pf}:u2_{t2}", "oncf"],
                    writes=[f"pb{4 + t2}"] if f in (0, 15) else [])
                p.op("scalar", lambda e, t2=t2: e.activation(
                    sq[:, :], u2[:, t2 * 512:(t2 + 1) * 512], AF.Square),
                    reads=[f"{pf}:u2_{t2}"], writes=[f"{pf}:sq"])
                p.op("tensor", lambda e, f=f, t2=t2: e.matmul(
                    p.pb[6 + t2][0:1, :], C["onc"][:, 0:1], sq[:, :],
                    start=(f == 0), stop=(f == 15)),
                    reads=[f"{pf}:sq", "onc"],
                    writes=[f"pb{6 + t2}"] if f in (0, 15) else [])
        for t2 in range(2):
            base = 3 * t2
            A_ = srow[0:1, base * 512:(base + 1) * 512]
            Bm = srow[0:1, (base + 1) * 512:(base + 2) * 512]
            Cc = srow[0:1, (base + 2) * 512:(base + 3) * 512]
            p.op("vector", lambda e, A_=A_, t2=t2: e.tensor_scalar(
                A_, p.pb[4 + t2][0:1, :], 1.0 / D, None, ALU.mult),
                reads=[f"pb{4 + t2}"], writes=[f"{pf}:lnA{t2}"])
            p.op("vector", lambda e, A_=A_, Bm=Bm: e.tensor_mul(Bm, A_, A_),
                 reads=[f"{pf}:lnA{t2}"], writes=[f"{pf}:lnB{t2}"])
            p.op("vector", lambda e, Cc=Cc, t2=t2: e.tensor_scalar(
                Cc, p.pb[6 + t2][0:1, :], 1.0 / D, EPS, ALU.mult, ALU.add),
                reads=[f"pb{6 + t2}"], writes=[f"{pf}:lnC{t2}"])
            p.op("vector", lambda e, Cc=Cc, Bm=Bm: e.tensor_sub(Cc, Cc, Bm),
                 reads=[f"{pf}:lnB{t2}", f"{pf}:lnC{t2}"], writes=[f"{pf}:lnC{t2}"])
            p.op("scalar", lambda e, Cc=Cc: e.activation(Cc, Cc, AF.Ln),
                 reads=[f"{pf}:lnC{t2}"], writes=[f"{pf}:lnC{t2}"])
            p.op("scalar", lambda e, Cc=Cc: e.activation(Cc, Cc, AF.Exp, scale=-0.5),
                 reads=[f"{pf}:lnC{t2}"], writes=[f"{pf}:lnC{t2}"])
            p.op("vector", lambda e, A_=A_, Bm=Bm, Cc=Cc: e.tensor_mul(Bm, A_, Cc),
                 reads=[f"{pf}:lnA{t2}", f"{pf}:lnC{t2}"], writes=[f"{pf}:lnB{t2}"])
            p.op("vector", lambda e, Bm=Bm: e.tensor_scalar_mul(Bm, Bm, -1.0),
                 reads=[f"{pf}:lnB{t2}"], writes=[f"{pf}:lnB{t2}"])
            p.op("tensor", lambda e, Cc=Cc, t2=t2: e.matmul(
                p.pb[t2][:, :], C["onr"][0:1, 0:128], Cc, start=True, stop=True),
                reads=[f"{pf}:lnC{t2}", "onr"], writes=[f"pb{t2}"])
            p.op("tensor", lambda e, Bm=Bm, t2=t2: e.matmul(
                p.pb[2 + t2][:, :], C["onr"][0:1, 0:128], Bm, start=True, stop=True),
                reads=[f"{pf}:lnB{t2}", "onr"], writes=[f"pb{2 + t2}"])
            for f in range(16):
                p.op("sync", lambda e, f=f, t2=t2, h0=h0: e.dma_start(
                    out=u2[:, t2 * 512:(t2 + 1) * 512],
                    in_=t["u2d"][f * 128:(f + 1) * 128, h0 + t2 * 512:h0 + t2 * 512 + 512]),
                    writes=[f"{pf}:u2_{t2}"], dma=True)
                p.op("vector", lambda e, t2=t2: e.tensor_mul(
                    t1[:, :], u2[:, t2 * 512:(t2 + 1) * 512], p.pb[t2][:, :]),
                    reads=[f"{pf}:u2_{t2}", f"pb{t2}"], writes=[f"{pf}:t1"])
                p.op("vector", lambda e, t2=t2: e.tensor_add(
                    outs[:, :], t1[:, :], p.pb[2 + t2][:, :]),
                    reads=[f"{pf}:t1", f"pb{2 + t2}"], writes=[f"{pf}:outs"])
                p.op("sync", lambda e, f=f, t2=t2, h0=h0: e.dma_start(
                    out=xout[f * 128:(f + 1) * 128, h0 + t2 * 512:h0 + t2 * 512 + 512],
                    in_=outs[:, :]),
                    reads=[f"{pf}:outs"], dma=True)
                if scat:
                    p.op("vector", lambda e: e.tensor_copy(outs_bf[:, :], outs[:, :]),
                         reads=[f"{pf}:outs"], writes=[f"{pf}:obf"])
                    p.op("gpsimd", lambda e, f=f, t2=t2, h0=h0: e.indirect_dma_start(
                        t["x2_sh"][:, :],
                        IndirectOffsetOnAxis(ap=C["idx_t"][:, f:f + 1], axis=0),
                        outs_bf[:, :], None, element_offset=h0 + t2 * 512),
                        reads=[f"{pf}:obf", "idx"], dma=True)


def em_H(p, t, C, ar, rw):
    xsb = ar.bf16(128, 16 * S)
    wsb = ar.bf16(128, 16 * 1024)
    outsb = ar.bf16(128, 4 * 512)
    bsb = rw.f32(3 * 1024)
    on5 = rw.f32(512)
    p.op("sync", lambda e: e.dma_start(out=on5[0:1, :], in_=t["ones512"]),
         writes=["H:on5"], dma=True)
    for k in range(16):
        for pp in range(2):
            p.op("sync", lambda e, k=k, pp=pp: e.dma_start(
                out=xsb[:, k * S + pp * 2048: k * S + pp * 2048 + 2048],
                in_=t["x2_sh"][pp * 2048 + k * 128: pp * 2048 + (k + 1) * 128, :]),
                writes=[f"H:x{k}"], dma=True)
    for nm, boff in (("bq1", 0), ("bk1", 1024), ("bv1", 2048)):
        p.op("sync", lambda e, nm=nm, boff=boff: e.dma_start(
            out=bsb[0:1, boff:boff + 1024], in_=t[nm]),
            writes=[f"H:{nm}"], dma=True)
    for wnm, bnm, boff, outd in (("wq1", "bq1", 0, "q1"), ("wk1", "bk1", 1024, "k1")):
        for k in range(16):
            p.op("gpsimd", lambda e, k=k, wnm=wnm: e.dma_start(
                out=wsb[:, k * 1024:(k + 1) * 1024],
                in_=t[wnm][k * 128:(k + 1) * 128, :]),
                writes=[f"H:w{k}"], dma=True)
        for tt in range(S // 512):
            t0 = tt * 512
            for m in range(8):
                bank = m % 4
                p.op("tensor", lambda e, m=m, bank=bank, boff=boff: e.matmul(
                    p.pb[bank][:, :], bsb[0:1, boff + m * 128: boff + (m + 1) * 128],
                    on5[0:1, :], start=True, stop=False),
                    reads=[f"H:{bnm}", "H:on5"], writes=[f"pb{bank}"])
                for k in range(16):
                    p.op("tensor", lambda e, k=k, m=m, t0=t0, bank=bank: e.matmul(
                        p.pb[bank][:, :],
                        wsb[:, k * 1024 + m * 128: k * 1024 + (m + 1) * 128],
                        xsb[:, k * S + t0: k * S + t0 + 512],
                        start=False, stop=(k == 15)),
                        reads=[f"H:x{k}", f"H:w{k}"],
                        writes=[f"pb{bank}"] if k == 15 else [])
                p.op("vector", lambda e, bank=bank: e.tensor_copy(
                    outsb[:, bank * 512:(bank + 1) * 512], p.pb[bank][:, :]),
                    reads=[f"pb{bank}"], writes=[f"H:o{bank}"])
                p.op("sync", lambda e, m=m, t0=t0, bank=bank, outd=outd: e.dma_start(
                    out=t[outd][m * 128:(m + 1) * 128, t0:t0 + 512],
                    in_=outsb[:, bank * 512:(bank + 1) * 512]),
                    reads=[f"H:o{bank}"], dma=True)
    for k in range(16):
        p.op("gpsimd", lambda e, k=k: e.dma_start(
            out=wsb[:, k * 1024:(k + 1) * 1024],
            in_=t["wv1"][k * 128:(k + 1) * 128, :]),
            writes=[f"H:w{k}"], dma=True)
    for ttile in range(S // 128):
        for v4 in range(2):
            bank = 4 + (2 * ttile + v4) % 4
            p.op("tensor", lambda e, v4=v4, bank=bank: e.matmul(
                p.pb[bank][:, :], C["onr"][0:1, 0:128],
                bsb[0:1, 2048 + v4 * 512: 2048 + (v4 + 1) * 512],
                start=True, stop=False),
                reads=["H:bv1", "onr"], writes=[f"pb{bank}"])
            for k in range(16):
                p.op("tensor", lambda e, k=k, ttile=ttile, v4=v4, bank=bank: e.matmul(
                    p.pb[bank][:, :],
                    xsb[:, k * S + ttile * 128: k * S + (ttile + 1) * 128],
                    wsb[:, k * 1024 + v4 * 512: k * 1024 + (v4 + 1) * 512],
                    start=False, stop=(k == 15)),
                    reads=[f"H:x{k}", f"H:w{k}"],
                    writes=[f"pb{bank}"] if k == 15 else [])
            p.op("vector", lambda e, v4=v4, bank=bank: e.tensor_copy(
                outsb[:, v4 * 512:(v4 + 1) * 512], p.pb[bank][:, :]),
                reads=[f"pb{bank}"], writes=[f"H:o{v4}"])
            p.op("sync", lambda e, ttile=ttile, v4=v4: e.dma_start(
                out=t["v1t"][ttile * 128:(ttile + 1) * 128, v4 * 512:(v4 + 1) * 512],
                in_=outsb[:, v4 * 512:(v4 + 1) * 512]),
                reads=[f"H:o{v4}"], dma=True)


def em_I(p, t, C, ar, rw):
    q_sb = ar.bf16(128, S)
    k_sb = ar.bf16(128, S)
    vt_sb = ar.bf16(128, S)
    msk = ar.f32(128, 384)
    iorow = ar.f32(128, 64)
    iopp = ar.f32(64, 1)
    ident = ar.bf16(128, 128)
    sq_sb = ar.bf16(128, 512)
    negc = ar.f32(128, 1)
    na = ar.f32(8, S)
    nb = ar.f32(8, S)
    mx8 = ar.f32(8, 8)
    probs = ar.bf16(128, 384)
    lctx = ar.bf16(128, S)
    P_sb = ar.bf16(128, 64)
    qt_sb = ar.bf16(128, 128)
    qgT_sb = ar.bf16(64, 128)
    qg_sb = ar.bf16(128, 64)
    gp_sb = ar.bf16(128, 64)
    rg_pp = ar.f32(64, 1)
    gctx_sb = ar.bf16(64, 128)
    cb_sb = ar.f32(128, 512)
    pi_sb = ar.bf16(64, 512)
    rkcol = ar.f32(128, 1)
    t2_sb = ar.f32(128, 512)
    ctxo = ar.bf16(128, 512)
    nrow = ar.f32(1, S)
    rloc = ar.f32(1, S)
    srow = rw.f32(64)
    rgrow = rw.f32(64)
    rr_sb = rw.f32(512)
    mr_sb = rw.f32(512)
    cbrow = rw.f32(512)
    pbb6 = p.pb[6].bitcast(BF16)
    for (tl, dsrc, key) in [(msk, "maskL", "I:msk"), (iorow, "io_row", "I:iorow"),
                            (iopp, "io_pp", "I:iopp"), (ident, "identd", "I:ident")]:
        p.op("sync", lambda e, tl=tl, dsrc=dsrc: e.dma_start(out=tl[:, :], in_=t[dsrc]),
             writes=[key], dma=True)
    for j in range(8):
        p.op("sync", lambda e, j=j: e.dma_start(
            out=q_sb[:, :], in_=t["q1"][j * 128:(j + 1) * 128, :]),
            writes=["I:q"], dma=True)
        for c8 in range(8):
            p.op("scalar", lambda e, c8=c8: e.activation(
                sq_sb[:, :], q_sb[:, c8 * 512:(c8 + 1) * 512], AF.Square),
                reads=["I:q"], writes=["I:sq"])
            p.op("tensor", lambda e: e.matmul(
                p.pb[0][0:1, :], C["onc"][:, 0:1], sq_sb[:, :], start=True, stop=True),
                reads=["I:sq", "onc"], writes=["pb0"])
            p.op("vector", lambda e, c8=c8: e.tensor_copy(
                nrow[0:1, c8 * 512:(c8 + 1) * 512], p.pb[0][0:1, :]),
                reads=["pb0"], writes=["I:nrow"])
        p.op("sync", lambda e, j=j: e.dma_start(
            out=t["rankm"][j:j + 1, :], in_=nrow[0:1, :]),
            reads=["I:nrow"], writes=["I:na_d"], dma=True)
    p.op("sync", lambda e: e.dma_start(out=na[:, :], in_=t["rankm"]),
         reads=["I:na_d"], writes=["I:na"], dma=True)
    bufs = [na, nb]
    for rnd in range(8):
        cur, nxt = bufs[rnd % 2], bufs[(rnd + 1) % 2]
        ck, nk = ("I:na" if rnd % 2 == 0 else "I:nb"), ("I:nb" if rnd % 2 == 0 else "I:na")
        p.op("vector", lambda e, cur=cur: e.max(out=mx8[:, :], in_=cur[:, :]),
             reads=[ck], writes=["I:mx8"])
        if rnd == 7:
            p.op("vector", lambda e: e.memset(mx8[:, 6:8], -1.0),
                 reads=[], writes=["I:mx8"])
        p.op("vector", lambda e, cur=cur, nxt=nxt: e.match_replace(
            out=nxt[:, :], in_to_replace=mx8[:, :], in_values=cur[:, :],
            imm_value=-1.0),
            reads=[ck, "I:mx8"], writes=[nk])
    p.op("vector", lambda e: e.tensor_scalar(
        nb[:, :], na[:, :], -0.5, None, ALU.is_le),
        reads=["I:na"], writes=["I:nb"])
    p.op("vector", lambda e: e.memset(nb[:, 0:1], 1.0), reads=[], writes=["I:nb"])
    p.op("vector", lambda e: e.memset(nb[:, S - 1:S], 1.0), reads=[], writes=["I:nb"])
    p.op("vector", lambda e: e.tensor_tensor_scan(
        na[:, :], nb[:, :], nb[:, :], 0.0, ALU.add, ALU.bypass),
        reads=["I:nb"], writes=["I:na"])
    p.op("vector", lambda e: e.tensor_sub(na[:, :], na[:, :], nb[:, :]),
         reads=["I:na", "I:nb"], writes=["I:na"])
    p.op("sync", lambda e: e.dma_start(out=t["maskm"], in_=nb[:, :]),
         reads=["I:nb"], writes=["I:maskmD"], dma=True)
    p.op("vector", lambda e: e.tensor_scalar(
        nb[:, :], nb[:, :], -1e9, 1e9, ALU.mult, ALU.add),
        reads=["I:nb"], writes=["I:nb"])
    p.op("vector", lambda e: e.tensor_add(na[:, :], na[:, :], nb[:, :]),
         reads=["I:na", "I:nb"], writes=["I:na"])
    p.op("sync", lambda e: e.dma_start(out=t["rankm"], in_=na[:, :]),
         reads=["I:na"], writes=["I:rankmD"], dma=True)
    for j in range(8):
        p.op("sync", lambda e, j=j: e.dma_start(
            out=q_sb[:, :], in_=t["q1"][j * 128:(j + 1) * 128, :]),
            writes=["I:q"], dma=True)
        p.op("sync", lambda e, j=j: e.dma_start(
            out=k_sb[:, :], in_=t["k1"][j * 128:(j + 1) * 128, :]),
            writes=["I:k"], dma=True)
        for g in range(8):
            p.op("sync", lambda e, j=j, g=g: e.dma_start(
                out=vt_sb[:, g * 512:(g + 1) * 512].rearrange(
                    "p (a d) -> p a d", d=128),
                in_=t["v1t"][g * 512:(g + 1) * 512, j * 128:(j + 1) * 128].rearrange(
                    "(a p) d -> p a d", p=128)),
                writes=["I:v"], dma=True)
        _bound_c(p, "I:q", q_sb, sq_sb, C["onc"], srow, 0, "I")
        _bound_c(p, "I:k", k_sb, sq_sb, C["onc"], srow, 1, "I")
        _negc_from_bounds(p, C, srow, negc, "I")
        NB = 32
        for tb in range(NB):
            qlo = max(0, (tb - 1) * 128)
            qhi = min(S, (tb + 2) * 128)
            Nq = qhi - qlo
            mlo = 128 if tb == 0 else 0
            p.op("tensor", lambda e, tb=tb, qlo=qlo, Nq=Nq: e.matmul(
                p.pb[0][:, 0:Nq], k_sb[:, tb * 128:(tb + 1) * 128],
                q_sb[:, qlo:qlo + Nq], start=True, stop=True),
                reads=["I:k", "I:q"], writes=["pb0"])
            p.op("vector", lambda e, Nq=Nq, mlo=mlo: e.tensor_add(
                p.pb[0][:, 0:Nq], p.pb[0][:, 0:Nq], msk[:, mlo:mlo + Nq]),
                reads=["pb0", "I:msk"], writes=["pb0"])
            p.op("scalar", lambda e, Nq=Nq: e.activation(
                probs[:, 0:Nq], p.pb[0][:, 0:Nq], AF.Exp,
                bias=negc[:, 0:1], scale=SCALE),
                reads=["pb0", "I:negc"], writes=["I:probs"])
            for cg in (-1, 0, 1):
                qb = tb + cg
                if qb < 0 or qb >= NB:
                    continue
                col = qb * 128 - qlo
                first = (tb == max(0, qb - 1))
                last = (tb == min(NB - 1, qb + 1))
                cbank = 2 if qb % 3 == 0 else (3 if qb % 3 == 1 else 6)
                dbank = 1 if qb % 3 == 0 else (4 if qb % 3 == 1 else 5)
                p.op("tensor", lambda e, col=col, dbank=dbank, first=first,
                     last=last: e.matmul(
                    p.pb[dbank][0:1, 0:128], C["onc"][:, 0:1], probs[:, col:col + 128],
                    start=first, stop=last),
                    reads=["I:probs", "onc"],
                    writes=[f"pb{dbank}"] if (first or last) else [])
                p.op("tensor", lambda e, tb=tb, col=col, cbank=cbank, first=first,
                     last=last: e.matmul(
                    p.pb[cbank][:, 0:128], vt_sb[:, tb * 128:(tb + 1) * 128],
                    probs[:, col:col + 128], start=first, stop=last),
                    reads=["I:probs", "I:v"],
                    writes=[f"pb{cbank}"] if (first or last) else [])
            done = [tb - 1] if tb < NB - 1 else [tb - 1, tb]
            for qb in done:
                if qb < 0:
                    continue
                cbank = 2 if qb % 3 == 0 else (3 if qb % 3 == 1 else 6)
                dbank = 1 if qb % 3 == 0 else (4 if qb % 3 == 1 else 5)
                p.op("vector", lambda e, qb=qb, dbank=dbank: e.reciprocal(
                    rloc[0:1, qb * 128:(qb + 1) * 128], p.pb[dbank][0:1, 0:128]),
                    reads=[f"pb{dbank}"], writes=["I:rloc"])
                p.op("vector", lambda e, qb=qb, cbank=cbank: e.tensor_copy(
                    lctx[:, qb * 128:(qb + 1) * 128], p.pb[cbank][:, 0:128]),
                    reads=[f"pb{cbank}"], writes=["I:lctx"])
        for cc in range(32):
            p.op("sync", lambda e, j=j, cc=cc: e.dma_start(
                out=rkcol[:, 0:1], in_=t["rankm"][j:j + 1, cc * 128:(cc + 1) * 128]
                .rearrange("o p -> p o")),
                reads=["I:rankmD"], writes=["I:rkcol"], dma=True)
            p.op("vector", lambda e: e.tensor_scalar(
                P_sb[:, :], iorow[:, :], rkcol[:, 0:1], None, ALU.is_equal),
                reads=["I:iorow", "I:rkcol"], writes=["I:P"])
            p.op("tensor", lambda e, cc=cc: e.transpose(
                pbb6[:, 0:128], q_sb[:, cc * 128:(cc + 1) * 128], ident[:, :]),
                reads=["I:q", "I:ident"], writes=["pb6"])
            p.op("vector", lambda e: e.tensor_copy(qt_sb[:, :], pbb6[:, 0:128]),
                 reads=["pb6"], writes=["I:qt"])
            p.op("tensor", lambda e, cc=cc: e.matmul(
                p.pb[0][0:64, 0:128], P_sb[:, :], qt_sb[:, :],
                start=(cc == 0), stop=(cc == 31)),
                reads=["I:P", "I:qt"],
                writes=["pb0"] if cc in (0, 31) else [])
        p.op("vector", lambda e: e.tensor_copy(qgT_sb[:, :], p.pb[0][0:64, 0:128]),
             reads=["pb0"], writes=["I:qgT"])
        p.op("tensor", lambda e: e.transpose(
            pbb6[:, 0:64], qgT_sb[:, :], ident[0:64, 0:64]),
            reads=["I:qgT", "I:ident"], writes=["pb6"])
        p.op("vector", lambda e: e.tensor_copy(qg_sb[:, :], pbb6[:, 0:64]),
             reads=["pb6"], writes=["I:qg"])
        for kb in range(32):
            p.op("tensor", lambda e, kb=kb: e.matmul(
                p.pb[1][:, 0:64], k_sb[:, kb * 128:(kb + 1) * 128], qg_sb[:, :],
                start=True, stop=True),
                reads=["I:k", "I:qg"], writes=["pb1"])
            p.op("scalar", lambda e: e.activation(
                gp_sb[:, :], p.pb[1][:, 0:64], AF.Exp,
                bias=negc[:, 0:1], scale=SCALE),
                reads=["pb1", "I:negc"], writes=["I:gp"])
            p.op("tensor", lambda e, kb=kb: e.matmul(
                p.pb[2][0:1, 0:64], C["onc"][:, 0:1], gp_sb[:, :],
                start=(kb == 0), stop=(kb == 31)),
                reads=["I:gp", "onc"],
                writes=["pb2"] if kb in (0, 31) else [])
            p.op("tensor", lambda e, kb=kb: e.matmul(
                p.pb[3][0:64, 0:128], gp_sb[:, :], vt_sb[:, kb * 128:(kb + 1) * 128],
                start=(kb == 0), stop=(kb == 31)),
                reads=["I:gp", "I:v"],
                writes=["pb3"] if kb in (0, 31) else [])
        p.op("vector", lambda e: e.reciprocal(rgrow[0:1, 0:64], p.pb[2][0:1, 0:64]),
             reads=["pb2"], writes=["I:rgrow"])
        p.op("tensor", lambda e: e.matmul(
            p.pb[5][0:64, 0:1], rgrow[0:1, 0:64], C["onr"][0:1, 0:1],
            start=True, stop=True),
            reads=["I:rgrow", "onr"], writes=["pb5"])
        p.op("vector", lambda e: e.tensor_copy(rg_pp[:, 0:1], p.pb[5][0:64, 0:1]),
             reads=["pb5"], writes=["I:rg"])
        p.op("vector", lambda e: e.tensor_scalar_mul(
            gctx_sb[:, :], p.pb[3][0:64, 0:128], rg_pp[:, 0:1]),
            reads=["pb3", "I:rg"], writes=["I:gctx"])
        for c8 in range(8):
            p.op("sync", lambda e, j=j, c8=c8: e.dma_start(
                out=rr_sb[0:1, :], in_=t["rankm"][j:j + 1, c8 * 512:(c8 + 1) * 512]),
                reads=["I:rankmD"], writes=["I:rr"], dma=True)
            p.op("sync", lambda e, j=j, c8=c8: e.dma_start(
                out=mr_sb[0:1, :], in_=t["maskm"][j:j + 1, c8 * 512:(c8 + 1) * 512]),
                reads=["I:maskmD"], writes=["I:mr"], dma=True)
            p.op("vector", lambda e: e.tensor_scalar(
                cbrow[0:1, :], mr_sb[0:1, :], -1.0, 1.0, ALU.mult, ALU.add),
                reads=["I:mr"], writes=["I:cbrow"])
            p.op("vector", lambda e, c8=c8: e.tensor_mul(
                cbrow[0:1, :], cbrow[0:1, :], rloc[0:1, c8 * 512:(c8 + 1) * 512]),
                reads=["I:cbrow", "I:rloc"], writes=["I:cbrow"])
            p.op("tensor", lambda e: e.matmul(
                p.pb[5][:, :], C["onr"][0:1, 0:128], cbrow[0:1, :], start=True, stop=True),
                reads=["I:cbrow", "onr"], writes=["pb5"])
            p.op("vector", lambda e: e.tensor_copy(cb_sb[:, :], p.pb[5][:, :]),
                 reads=["pb5"], writes=["I:cb"])
            p.op("tensor", lambda e: e.matmul(
                p.pb[7][0:64, :], C["onr"][0:1, 0:64], rr_sb[0:1, :],
                start=True, stop=True),
                reads=["I:rr", "onr"], writes=["pb7"])
            p.op("vector", lambda e: e.tensor_scalar(
                pi_sb[:, :], p.pb[7][0:64, :], iopp[:, 0:1], None, ALU.is_equal),
                reads=["pb7", "I:iopp"], writes=["I:pi"])
            p.op("tensor", lambda e: e.matmul(
                p.pb[4][:, :], gctx_sb[:, :], pi_sb[:, :], start=True, stop=True),
                reads=["I:gctx", "I:pi"], writes=["pb4"])
            p.op("vector", lambda e, c8=c8: e.tensor_mul(
                t2_sb[:, :], lctx[:, c8 * 512:(c8 + 1) * 512], cb_sb[:, :]),
                reads=["I:lctx", "I:cb"], writes=["I:t2"])
            p.op("vector", lambda e: e.tensor_add(ctxo[:, :], t2_sb[:, :], p.pb[4][:, :]),
                 reads=["I:t2", "pb4"], writes=["I:ctxo"])
            p.op("gpsimd", lambda e, j=j, c8=c8: e.indirect_dma_start(
                t["ctx1_sh"][:, :],
                IndirectOffsetOnAxis(ap=C["idx_h"][:, j:j + 1], axis=0),
                ctxo[:, :], None, element_offset=c8 * 512),
                reads=["I:ctxo", "idx"], dma=True)


def em_J(p, t, C, ar, rw):
    ctx1v = t["ctx1_sh"].rearrange("h (p w) -> (h p) w", p=2)
    a1sb = ar.bf16(128, 16 * TC)
    xc = ar.f32(128, 2 * 512)
    u1 = ar.f32(128, 16 * 512)
    sq = ar.bf16(128, 512)
    t1 = ar.f32(128, 512)
    outs = ar.f32(128, 512)
    srow = rw.f32(3 * 512)
    for k in range(16):
        p.op("gpsimd", lambda e, k=k: e.indirect_dma_start(
            a1sb[:, k * TC:(k + 1) * TC], None,
            ctx1v[:, :], IndirectOffsetOnAxis(ap=C["idx_c"][:, k:k + 1], axis=0)),
            writes=[f"J:a{k}"], reads=["idx"], dma=True)
    for tt in range(TC // 512):
        t0 = tt * 512
        for m in range(16):
            xb = m % 2
            p.op("sync", lambda e, m=m, t0=t0, xb=xb: e.dma_start(
                out=xc[:, xb * 512:(xb + 1) * 512],
                in_=t["x2"][m * 128:(m + 1) * 128, t0:t0 + 512]),
                writes=[f"J:xc{xb}"], dma=True)
            p.op("vector", lambda e, m=m, t0=t0: e.tensor_copy(
                u1[:, m * 512:(m + 1) * 512],
                a1sb[:, m * TC + t0: m * TC + t0 + 512]),
                reads=[f"J:a{m}"], writes=[f"J:u1_{m}"])
            p.op("vector", lambda e, m=m, xb=xb: e.tensor_add(
                u1[:, m * 512:(m + 1) * 512],
                u1[:, m * 512:(m + 1) * 512],
                xc[:, xb * 512:(xb + 1) * 512]),
                reads=[f"J:u1_{m}", f"J:xc{xb}"], writes=[f"J:u1_{m}"])
            p.op("tensor", lambda e, m=m: e.matmul(
                p.pb[6][0:1, :], C["oncf"][:, 0:1], u1[:, m * 512:(m + 1) * 512],
                start=(m == 0), stop=(m == 15)),
                reads=[f"J:u1_{m}", "oncf"],
                writes=["pb6"] if m in (0, 15) else [])
            p.op("scalar", lambda e, m=m: e.activation(
                sq[:, :], u1[:, m * 512:(m + 1) * 512], AF.Square),
                reads=[f"J:u1_{m}"], writes=["J:sq"])
            p.op("tensor", lambda e, m=m: e.matmul(
                p.pb[7][0:1, :], C["onc"][:, 0:1], sq[:, :],
                start=(m == 0), stop=(m == 15)),
                reads=["J:sq", "onc"],
                writes=["pb7"] if m in (0, 15) else [])

        def outfn(m, o, t0=t0):
            p.op("sync", lambda e, m=m, t0=t0, o=o: e.dma_start(
                out=t["x3"][m * 128:(m + 1) * 128, t0:t0 + 512], in_=o[:, :]),
                reads=["J:outs"], dma=True)
        _ln_apply(p, C, "J", u1, 16, D, outfn, srow, t1, outs)


PHASES_ALL = ("AB", "C", "D", "E", "F", "G0", "H", "I", "J", "G1")


def build_fused(upto="G1", dbg=False):
    nc = bass.Bass("TRN2", target_bir_lowering=False)
    phases = PHASES_ALL[:PHASES_ALL.index(upto) + 1]
    t = {}

    def ein(name, shape, dt=F32):
        t[name] = nc.dram_tensor(name, shape, dt, kind="ExternalInput").ap()

    ein("x_f", [D, TC]); ein("wq_down", [D, QP]); ein("gq", [128, 8])
    ein("wkv_lora", [D, KVPP]); ein("wkv_rope", [D, 64]); ein("gkv", [128, 11])
    ein("wq_up", [QP, 1024]); ein("wkv_up_k", [KVPP, 512]); ein("wkv_up_v", [KVPP, 1024])
    ein("wo", [D, D])
    ein("cs_own", [64, TC]); ein("sc_own", [64, TC])
    ein("cos4", [128, S]); ein("sin4", [128, S])
    for L in ("L0", "L1"):
        ein(f"{L}_gws", [44, 128, 2048]); ein(f"{L}_uws", [44, 128, 2048])
        ein(f"{L}_dws", [16, 128, 5632])
        ein(f"{L}_gs", [128, 44]); ein(f"{L}_us", [128, 44]); ein(f"{L}_ds", [128, 16])
    ein("wq1", [D, 1024]); ein("wk1", [D, 1024]); ein("wv1", [D, 1024])
    ein("bq1", [1, 1024]); ein("bk1", [1, 1024]); ein("bv1", [1, 1024])
    ein("maskE", [128, 2048]); ein("maskL", [128, 384])
    ein("io_row", [128, 64]); ein("io_pp", [64, 1]); ein("identd", [128, 128], BF16)
    ein("ones_colb", [128, 1], BF16); ein("ones_colf", [128, 1])
    ein("ones_row", [1, 128]); ein("ones512", [1, 512])
    ein("idx_fcq", [128, 8], U32); ein("idx_fckv", [128, 11], U32)
    ein("idx_fkr", [64, 1], U32); ein("idx_h", [128, 8], U32)
    ein("idx_t", [128, 16], U32); ein("idx_c", [128, 16], U32)

    def sh_t(name, shape, dt=BF16):
        t[name] = nc.dram_tensor(name, shape, dt, kind="Internal",
                                 addr_space="Shared").ap()

    sh_t("cq_sh", [2 * 1024, 2048]); sh_t("ckv_sh", [2 * 1408, 2048])
    sh_t("kr_sh", [2 * 64, 2048])
    sh_t("ctx0_sh", [2048, S]); sh_t("x2_sh", [2 * 2048, 2048])
    sh_t("ctx1_sh", [2048, S])

    ikind = "ExternalOutput" if dbg else "Internal"

    def loc(name, shape, dt, kind):
        t[name] = nc.dram_tensor(name, shape, dt, kind=kind).ap()

    loc("q0", [1024, S], BF16, ikind)
    loc("knope", [512, S], BF16, ikind)
    loc("vt", [S, 1024], BF16, ikind)
    loc("x1", [D, TC], F32, ikind)
    loc("x2", [D, TC], F32, ikind)
    loc("x3", [D, TC], F32, ikind)
    loc("q1", [1024, S], BF16, ikind)
    loc("k1", [1024, S], BF16, ikind)
    loc("v1t", [S, 1024], BF16, ikind)
    loc("u2d", [D, TC], F32, "Internal")
    loc("rankm", [8, S], F32, "Internal")
    loc("maskm", [8, S], F32, "Internal")
    loc("bar_in", [1, 128], F32, "Internal")
    for i in range(4):
        loc(f"bar_out{i}", [1, 128], F32, "Internal")
    t["xout"] = nc.dram_tensor("xout", [D, TC], F32, kind="ExternalOutput").ap()

    with ExitStack() as ctx:
        C = {}
        C["onc"] = ctx.enter_context(nc.sbuf_tensor("c_onc", [128, 1], BF16))
        C["oncf"] = ctx.enter_context(nc.sbuf_tensor("c_oncf", [128, 1], F32))
        C["onr"] = ctx.enter_context(nc.sbuf_tensor("c_onr", [1, 128], F32))
        C["gq"] = ctx.enter_context(nc.sbuf_tensor("c_gq", [128, 8], F32))
        C["gkv"] = ctx.enter_context(nc.sbuf_tensor("c_gkv", [128, 11], F32))
        C["idx_fcq"] = ctx.enter_context(nc.sbuf_tensor("c_ifcq", [128, 8], U32))
        C["idx_fckv"] = ctx.enter_context(nc.sbuf_tensor("c_ifckv", [128, 11], U32))
        C["idx_fkr"] = ctx.enter_context(nc.sbuf_tensor("c_ifkr", [64, 1], U32))
        C["idx_h"] = ctx.enter_context(nc.sbuf_tensor("c_ih", [128, 8], U32))
        C["idx_t"] = ctx.enter_context(nc.sbuf_tensor("c_it", [128, 16], U32))
        C["idx_c"] = ctx.enter_context(nc.sbuf_tensor("c_ic", [128, 16], U32))
        ar = Arena(nc, ctx, 46592)
        rw = Rows(nc, ctx, 4096)
        p = Prog(nc, ctx)
        p.pb = [ctx.enter_context(nc.psum_tensor(f"pb{i}", [128, 512], F32))
                for i in range(8)]

        for nm in ("onc", "oncf", "onr", "gq", "gkv",
                   "idx_fcq", "idx_fckv", "idx_fkr", "idx_h", "idx_t", "idx_c"):
            src = {"onc": "ones_colb", "oncf": "ones_colf", "onr": "ones_row"}.get(nm, nm)
            key = "idx" if nm.startswith("idx") else nm
            p.op("sync", lambda e, nm=nm, src=src: e.dma_start(out=C[nm][:, :], in_=t[src]),
                 writes=[key], dma=True)
        p.op("sync", lambda e: e.dma_start(out=t["bar_in"], in_=C["onr"][0:1, :]),
             reads=["onr"], writes=["barin"], dma=True)

        def barrier(i):
            p.fence()
            p.cc(lambda e, i=i: e.collective_compute(
                "AllReduce", mybir.AluOpType.add, replica_groups=PAIRS,
                ins=[t["bar_in"].opt()], outs=[t[f"bar_out{i}"].opt()]))
            p.fence()

        for ph in phases:
            ar.reset(0)
            rw.reset()
            if ph == "AB":
                em_AB(p, t, C, ar, rw)
                barrier(0)
            elif ph == "C":
                em_C(p, t, C, ar, rw)
                p.fence()
            elif ph == "D":
                em_D(p, t, C, ar, rw)
                p.fence()
            elif ph == "E":
                em_E(p, t, C, ar, rw)
                barrier(1)
            elif ph == "F":
                em_F(p, t, C, ar, rw)
                p.fence()
            elif ph == "G0":
                em_G(p, t, C, ar, rw, "L0", t["x1"], t["x2"], scat=True)
                barrier(2)
            elif ph == "H":
                em_H(p, t, C, ar, rw)
                p.fence()
            elif ph == "I":
                em_I(p, t, C, ar, rw)
                barrier(3)
            elif ph == "J":
                em_J(p, t, C, ar, rw)
                p.fence()
            elif ph == "G1":
                em_G(p, t, C, ar, rw, "L1", t["x3"], t["xout"], scat=False)
        p.emit()
    return nc


def kernel(**ins):
    shd, cores = stage_inputs(ins)
    nc = build_fused()
    maps = []
    for c in range(NCORES):
        m = dict(shd)
        m.update(cores[c])
        maps.append(m)
    res = run_bass_kernel_spmd(nc, maps, core_ids=list(range(NCORES)))
    out = np.zeros((B, S, D), np.float32)
    for c in range(NCORES):
        b, r = c // 2, c % 2
        xo = np.asarray(res.results[c]["xout"])
        out[b, r * 2048:(r + 1) * 2048, :] = xo.T
    return out


# version 03
# speedup vs baseline: 1.3878x; 1.1562x over previous
import numpy as np
import ml_dtypes
import concourse.bass as bass
import concourse.mybir as mybir
from concourse.bass import IndirectOffsetOnAxis
from concourse.bass_utils import run_bass_kernel_spmd
from contextlib import ExitStack

F32 = mybir.dt.float32
BF16 = mybir.dt.bfloat16
U32 = mybir.dt.uint32
AX = mybir.AxisListType.X
ALU = mybir.AluOpType
AF = mybir.ActivationFunctionType
BF = ml_dtypes.bfloat16

NCORES = 8
B, S, D = 4, 4096, 2048
H, DH = 16, 128
QP = 1024
KVP = 1365
KVPP = 1408
INTER = 5632
EPS = 1e-5
SCALE = 1.0 / float(np.sqrt(np.float32(128.0)))
NEG = -1e9
T = B * S
TC = 2048
PAIRS = [[0, 1], [2, 3], [4, 5], [6, 7]]

ENGS = ("sync", "gpsimd", "tensor", "vector", "scalar")
DMA_POOL = 8


class Prog:

    def __init__(self, nc, ctx):
        self.nc = nc
        self.ops = []
        self.last_writer = {}
        self.readers = {}
        self.eng_count = {e: 0 for e in ENGS}
        self.sems = {e: ctx.enter_context(nc.semaphore(f"se_{e}")) for e in ENGS}
        self.dpool = {e: [ctx.enter_context(nc.semaphore(f"sd_{e}{i}"))
                          for i in range(DMA_POOL)] for e in ("sync", "gpsimd", "scalar")}
        self.dcount = {e: [0] * DMA_POOL for e in ("sync", "gpsimd", "scalar")}
        self.drr = {e: 0 for e in ("sync", "gpsimd", "scalar")}
        self.ccsems = [ctx.enter_context(nc.semaphore(f"scc{i}")) for i in range(8)]
        self.ncc = 0
        self.fence_deps = ()

    def op(self, engine, fn, reads=(), writes=(), dma=False, after=()):
        deps = set(after)
        deps.update(self.fence_deps)
        for k in reads:
            if k in self.last_writer:
                deps.add(self.last_writer[k])
            if k.startswith("pb"):
                for rd in self.readers.get(k, ()):
                    if self.ops[rd][0] != engine:
                        deps.add(rd)
        for k in writes:
            if k in self.last_writer:
                deps.add(self.last_writer[k])
            for rd in self.readers.get(k, ()):
                deps.add(rd)
        i = len(self.ops)
        if dma:
            idx = self.drr[engine] % DMA_POOL
            self.drr[engine] += 1
            before = self.dcount[engine][idx]
            self.dcount[engine][idx] = before + 16
            info = ("dma", idx, before)
            pos = 0
        else:
            info = None
            self.eng_count[engine] += 1
            pos = self.eng_count[engine]
        self.ops.append((engine, fn, deps, info, pos))
        for k in reads:
            self.readers.setdefault(k, []).append(i)
        for k in writes:
            self.last_writer[k] = i
            self.readers[k] = []
        return i

    def cc(self, fn):
        sem = self.ccsems[self.ncc]
        self.ncc += 1
        deps = set(self.fence_deps)
        i = len(self.ops)
        self.ops.append(("gpsimd", fn, deps, ("cc", sem), 0))
        return i

    def fence(self):
        deps = []
        last_eng = {}
        last_dma = {}
        last_cc = []
        for i, (e, fn, d, info, pos) in enumerate(self.ops):
            if info is None:
                last_eng[e] = i
            elif info[0] == "dma":
                last_dma[(e, info[1])] = i
            else:
                last_cc.append(i)
        deps.extend(last_eng.values())
        deps.extend(last_dma.values())
        deps.extend(last_cc[-4:])
        self.fence_deps = tuple(deps)

    def emit(self):
        nc = self.nc
        with nc.Block() as block:
            for ename in ENGS:
                self._emit_engine(block, ename)

    def _emit_engine(self, block, ename):
        prog = self

        def body(eng):
            wm = {}

            def wait(sem, val):
                if val > wm.get(id(sem), 0):
                    eng.wait_ge(sem, val)
                    wm[id(sem)] = val

            def do_waits(deps):
                for d in sorted(deps):
                    de, dfn, ddeps, dinfo, dpos = prog.ops[d]
                    if dinfo is not None and dinfo[0] == "dma":
                        wait(prog.dpool[de][dinfo[1]], dinfo[2] + 16)
                    elif dinfo is not None and dinfo[0] == "cc":
                        wait(dinfo[1], 1)
                    else:
                        if de == ename and ename == "tensor":
                            continue
                        wait(prog.sems[de], dpos)

            for i, (e, fn, deps, info, pos) in enumerate(prog.ops):
                if e != ename:
                    continue
                do_waits(deps)
                if info is not None and info[0] == "dma":
                    idx, before = info[1], info[2]
                    if before > 0:
                        wait(prog.dpool[ename][idx], before)
                    inst = fn(eng)
                    inst.then_inc(prog.dpool[ename][idx], 16)
                elif info is not None and info[0] == "cc":
                    inst = fn(eng)
                    inst.then_inc(info[1])
                else:
                    inst = fn(eng)
                    inst.then_inc(prog.sems[ename], 1)
            if ename == "sync":
                for q, pool in prog.dpool.items():
                    for idx, sem in enumerate(pool):
                        if prog.dcount[q][idx] > 0:
                            wait(sem, prog.dcount[q][idx])
                for i in range(prog.ncc):
                    wait(prog.ccsems[i], 1)

        getattr(block, ename)(body)


class Arena:

    def __init__(self, nc, ctx, cols_f32):
        self.t = ctx.enter_context(nc.sbuf_tensor("arena", [128, cols_f32], F32))
        self.cap = cols_f32
        self.off = 0
        self.peak = 0

    def reset(self, to=0):
        self.off = to

    def f32(self, parts, cols):
        o = self.off
        self.off += cols
        self.peak = max(self.peak, self.off)
        assert self.off <= self.cap, f"arena overflow {self.off}>{self.cap}"
        return self.t[0:parts, o:o + cols]

    def bf16(self, parts, cols):
        cf = (cols + 1) // 2
        o = self.off
        self.off += cf
        self.peak = max(self.peak, self.off)
        assert self.off <= self.cap, f"arena overflow {self.off}>{self.cap}"
        return self.t.bitcast(BF16)[0:parts, 2 * o:2 * o + cols]


class Rows:

    def __init__(self, nc, ctx, cols):
        self.t = ctx.enter_context(nc.sbuf_tensor("rows", [1, cols], F32))
        self.cap = cols
        self.off = 0

    def reset(self):
        self.off = 0

    def f32(self, cols):
        o = self.off
        self.off += cols
        assert self.off <= self.cap, "rows overflow"
        return self.t[0:1, o:o + cols]


def _f32(a):
    return np.ascontiguousarray(np.asarray(a, dtype=np.float32))


def _bf(a):
    return np.ascontiguousarray(np.asarray(a, dtype=np.float32).astype(BF))


def _rope_tables_np():
    freqs = (1.0 / (np.float32(10000.0) **
                    (np.arange(0, 128, 2, dtype=np.float32) / np.float32(128.0))))
    pos = np.arange(S, dtype=np.float32)
    emb = pos[:, None] * freqs[None, :32]
    return np.cos(emb).astype(np.float32).T.copy(), \
        np.sin(emb).astype(np.float32).T.copy()


def q_pack_perm8(r):
    perm = np.zeros(1024, np.int64)
    for c in range(4):
        for half in range(2):
            j = 2 * c + half
            h = 8 * r + j
            perm[c * 128 + half * 64: c * 128 + half * 64 + 64] = \
                h * 128 + np.arange(64)
    for g in range(2):
        for hh in range(4):
            j = 4 * g + hh
            h = 8 * r + j
            perm[(4 + g) * 128 + hh * 32:(4 + g) * 128 + hh * 32 + 32] = \
                h * 128 + 64 + 2 * np.arange(32)
    for g in range(2):
        for hh in range(4):
            j = 4 * g + hh
            h = 8 * r + j
            perm[(6 + g) * 128 + hh * 32:(6 + g) * 128 + hh * 32 + 32] = \
                h * 128 + 64 + 2 * np.arange(32) + 1
    return perm


def q_head_rows8(j):
    return ((j // 2) * 128 + (j % 2) * 64,
            (4 + j // 4) * 128 + (j % 4) * 32,
            (6 + j // 4) * 128 + (j % 4) * 32)


def stage_inputs(ins):
    sh = {}
    x = ins["L0_attn_x"]
    xf = _f32(x.reshape(T, D).T)

    sh["wq_down"] = _bf(ins["L0_attn_Wq_down"])
    wkv = np.asarray(ins["L0_attn_Wkv_down"], np.float32)
    lora = np.zeros((D, KVPP), np.float32)
    lora[:, :KVP] = wkv[:, :KVP]
    sh["wkv_lora"] = _bf(lora)
    wr = wkv[:, KVP:]
    sh["wkv_rope"] = _bf(np.concatenate([wr[:, 0::2], wr[:, 1::2]], axis=1))

    sh["gq"] = _f32(np.asarray(ins["L0_attn_gq"]).reshape(8, 128).T)
    gkv = np.zeros(KVPP, np.float32)
    gkv[:KVP] = np.asarray(ins["L0_attn_gkv"])
    sh["gkv"] = _f32(gkv.reshape(11, 128).T)
    sh["wo"] = _bf(ins["L0_attn_Wo"])

    cosT, sinT = _rope_tables_np()
    sh["cos4"] = _f32(np.tile(cosT, (4, 1)))
    sh["sin4"] = _f32(np.tile(sinT, (4, 1)))

    wkvu = np.asarray(ins["L0_attn_Wkv_up"], np.float32)
    wkvu_p = np.zeros((KVPP, 3072), np.float32)
    wkvu_p[:KVP] = wkvu
    per_head = wkvu_p.reshape(KVPP, H, 192)

    for L in ("L0", "L1"):
        gw = np.asarray(ins[f"{L}_ffn_gate_w"], np.float32)
        uw = np.asarray(ins[f"{L}_ffn_up_w"], np.float32)
        dw = np.asarray(ins[f"{L}_ffn_down_w"], np.float32)
        sh[f"{L}_gws"] = _bf(gw.reshape(44, 128, 16, 128).transpose(0, 3, 2, 1)
                              .reshape(44, 128, 2048))
        sh[f"{L}_uws"] = _bf(uw.reshape(44, 128, 16, 128).transpose(0, 3, 2, 1)
                              .reshape(44, 128, 2048))
        sh[f"{L}_dws"] = _bf(dw.reshape(16, 128, 44, 128).transpose(0, 3, 2, 1)
                              .reshape(16, 128, 5632))
        sh[f"{L}_gs"] = _f32(np.asarray(ins[f"{L}_ffn_gate_s"]).reshape(44, 128).T)
        sh[f"{L}_us"] = _f32(np.asarray(ins[f"{L}_ffn_up_s"]).reshape(44, 128).T)
        sh[f"{L}_ds"] = _f32(np.asarray(ins[f"{L}_ffn_down_s"]).reshape(16, 128).T)

    sh["ones_colb"] = np.ones((128, 1), BF)
    sh["ones_colf"] = np.ones((128, 1), np.float32)
    sh["ones_row"] = np.ones((1, 128), np.float32)
    sh["ones512"] = np.ones((1, 512), np.float32)
    sh["identd"] = np.eye(128).astype(BF)
    kk2 = np.arange(128)[:, None]
    qq2 = np.arange(512)[None, :]
    me = np.zeros((128, 4, 512), np.float32)
    for jj in range(4):
        me[:, jj, :] = np.where(jj * 128 + kk2 > qq2, NEG, 0.0)
    sh["maskE"] = _f32(me.reshape(128, 2048))
    ml = np.zeros((128, 384), np.float32)
    ml[64:128, 0:128] = NEG
    ml[0:64, 256:384] = NEG
    sh["maskL"] = _f32(ml)
    sh["io_row"] = _f32(np.tile(np.arange(64, dtype=np.float32)[None, :], (128, 1)))
    sh["io_pp"] = _f32(np.arange(64, dtype=np.float32).reshape(64, 1))

    wq1 = np.asarray(ins["L1_attn_Wq"], np.float32)
    wk1 = np.asarray(ins["L1_attn_Wk"], np.float32)
    wv1 = np.asarray(ins["L1_attn_Wv"], np.float32)
    bq1 = np.asarray(ins["L1_attn_bq"], np.float32)
    bk1 = np.asarray(ins["L1_attn_bk"], np.float32)
    bv1 = np.asarray(ins["L1_attn_bv"], np.float32)

    i128 = np.arange(128, dtype=np.uint32)[:, None]
    cores = []
    for c in range(NCORES):
        b, r = c // 2, c % 2
        pc = {}
        pc["x_f"] = _f32(xf[:, c * TC:(c + 1) * TC])
        pc["wq_up"] = _bf(np.asarray(ins["L0_attn_Wq_up"], np.float32)[:, q_pack_perm8(r)])
        hs = slice(8 * r, 8 * r + 8)
        pc["wkv_up_k"] = _bf(per_head[:, hs, :64].reshape(KVPP, 512))
        pc["wkv_up_v"] = _bf(per_head[:, hs, 64:].reshape(KVPP, 1024))
        ps = slice(r * 2048, r * 2048 + 2048)
        pc["cs_own"] = _f32(np.concatenate([cosT[:, ps], sinT[:, ps]], 0))
        pc["sc_own"] = _f32(np.concatenate([sinT[:, ps], cosT[:, ps]], 0))
        cs = slice(1024 * r, 1024 * r + 1024)
        pc["wq1"] = _bf(wq1[:, cs])
        pc["wk1"] = _bf(wk1[:, cs])
        pc["wv1"] = _bf(wv1[:, cs])
        pc["bq1"] = _f32(bq1[cs].reshape(1, 1024))
        pc["bqpp"] = _f32(bq1[cs].reshape(8, 128).T)
        pc["bkpp"] = _f32(bk1[cs].reshape(8, 128).T)
        pc["bk1"] = _f32(bk1[cs].reshape(1, 1024))
        pc["bv1"] = _f32(bv1[cs].reshape(1, 1024))
        pc["idx_fcq"] = (r * 1024 + np.arange(8, dtype=np.uint32)[None, :] * 128
                         + i128).astype(np.uint32)
        pc["idx_fckv"] = (r * 1408 + np.arange(11, dtype=np.uint32)[None, :] * 128
                          + i128).astype(np.uint32)
        pc["idx_fkr"] = (r * 64 + np.arange(64, dtype=np.uint32)[:, None]
                         ).astype(np.uint32)
        pc["idx_h"] = ((8 * r + np.arange(8, dtype=np.uint32)[None, :]) * 128
                       + i128).astype(np.uint32)
        pc["idx_t"] = (r * 2048 + np.arange(16, dtype=np.uint32)[None, :] * 128
                       + i128).astype(np.uint32)
        pc["idx_c"] = (2 * (np.arange(16, dtype=np.uint32)[None, :] * 128 + i128)
                       + r).astype(np.uint32)
        cores.append(pc)
    return sh, cores


def em_AB(p, t, C, ar, rw):
    xsb = ar.bf16(128, 16 * TC)
    for k in range(16):
        p.op("gpsimd", lambda e, k=k: e.dma_start(
            out=xsb[:, k * TC:(k + 1) * TC], in_=t["x_f"][k * 128:(k + 1) * 128, :]),
            writes=[f"x{k}"], dma=True)
    a0 = ar.off
    wsb = ar.bf16(128, 16 * QP)
    craw = ar.f32(128, 8 * 512)
    sqs = ar.bf16(128, 512)
    outsb = ar.bf16(128, 8 * 512)
    rowbuf = rw.f32(3 * 512)
    for k in range(16):
        p.op("sync", lambda e, k=k: e.dma_start(
            out=wsb[:, k * QP:(k + 1) * QP], in_=t["wq_down"][k * 128:(k + 1) * 128, :]),
            writes=[f"A:w{k}"], dma=True)
    NT = TC // 512

    def epiA(m):
        bank = m % 4
        cp = p.op("vector", lambda e, m=m, bank=bank: e.tensor_copy(
            craw[:, m * 512:(m + 1) * 512], p.pb[bank][:, :]),
            reads=[f"pb{bank}"], writes=[f"A:craw{m}"])
        p.op("scalar", lambda e, bank=bank: e.activation(
            sqs[:, :], p.pb[bank][:, :], AF.Square),
            reads=[f"pb{bank}"], writes=["A:sqs"], after=[cp])
        p.op("tensor", lambda e, m=m: e.matmul(
            p.pb[4][0:1, :], C["onc"][:, 0:1], sqs[:, :],
            start=(m == 0), stop=(m == 7)),
            reads=["A:sqs", "onc"],
            writes=["pb4"] if m in (0, 7) else [])

    for tt in range(NT):
        t0 = tt * 512
        for m in range(8):
            bank = m % 4
            for k in range(16):
                p.op("tensor", lambda e, k=k, m=m, t0=t0, bank=bank: e.matmul(
                    p.pb[bank][:, :],
                    wsb[:, k * QP + m * 128: k * QP + (m + 1) * 128],
                    xsb[:, k * TC + t0: k * TC + t0 + 512],
                    start=(k == 0), stop=(k == 15)),
                    reads=[f"x{k}", f"A:w{k}"],
                    writes=[f"pb{bank}"] if k in (0, 15) else [])
            if m > 0:
                epiA(m - 1)
        epiA(7)
        p.op("vector", lambda e: e.tensor_scalar(
            rowbuf[0:1, 0:512], p.pb[4][0:1, :], 1.0 / QP, EPS, ALU.mult, ALU.add),
            reads=["pb4"], writes=["A:row0"])
        p.op("scalar", lambda e: e.activation(
            rowbuf[0:1, 512:1024], rowbuf[0:1, 0:512], AF.Ln),
            reads=["A:row0"], writes=["A:row1"])
        p.op("scalar", lambda e: e.activation(
            rowbuf[0:1, 1024:1536], rowbuf[0:1, 512:1024], AF.Exp, scale=-0.5),
            reads=["A:row1"], writes=["A:row2"])
        p.op("tensor", lambda e: e.matmul(
            p.pb[5][:, :], C["onr"][0:1, 0:128], rowbuf[0:1, 1024:1536],
            start=True, stop=True),
            reads=["A:row2", "onr"], writes=["pb5"])
        for m in range(8):
            p.op("vector", lambda e, m=m: e.scalar_tensor_tensor(
                outsb[:, m * 512:(m + 1) * 512],
                craw[:, m * 512:(m + 1) * 512],
                C["gq"][:, m:m + 1], p.pb[5][:, :], ALU.mult, ALU.mult),
                reads=[f"A:craw{m}", "gq", "pb5"], writes=[f"A:out{m}"])
            p.op("gpsimd", lambda e, m=m, t0=t0: e.indirect_dma_start(
                t["cq_sh"][:, :],
                IndirectOffsetOnAxis(ap=C["idx_fcq"][:, m:m + 1], axis=0),
                outsb[:, m * 512:(m + 1) * 512], None, element_offset=t0),
                reads=[f"A:out{m}", "idx"], dma=True)
    p.fence()
    ar.reset(a0)
    wsb = ar.bf16(128, 16 * KVPP)
    wrs = ar.bf16(128, 16 * 64)
    css = ar.f32(64, 2048)
    scs = ar.f32(64, 2048)
    craw = ar.f32(128, 11 * 512)
    kraw = ar.f32(64, 2048)
    ksc1 = ar.f32(64, 1024)
    ksc2 = ar.f32(64, 1024)
    kout = ar.bf16(64, 2048)
    sqs = ar.bf16(128, 512)
    outsb = ar.bf16(128, 11 * 512)
    for k in range(16):
        p.op("sync", lambda e, k=k: e.dma_start(
            out=wsb[:, k * KVPP:(k + 1) * KVPP],
            in_=t["wkv_lora"][k * 128:(k + 1) * 128, :]),
            writes=[f"B:w{k}"], dma=True)
        p.op("sync", lambda e, k=k: e.dma_start(
            out=wrs[:, k * 64:(k + 1) * 64],
            in_=t["wkv_rope"][k * 128:(k + 1) * 128, :]),
            writes=[f"B:wr{k}"], dma=True)
    p.op("sync", lambda e: e.dma_start(out=css[:, :], in_=t["cs_own"]),
         writes=["B:cs"], dma=True)
    p.op("sync", lambda e: e.dma_start(out=scs[:, :], in_=t["sc_own"]),
         writes=["B:sc"], dma=True)
    rowbuf = rw.f32(3 * 512)
    def epiB(m):
        bank = m % 4
        cp = p.op("vector", lambda e, m=m, bank=bank: e.tensor_copy(
            craw[:, m * 512:(m + 1) * 512], p.pb[bank][:, :]),
            reads=[f"pb{bank}"], writes=[f"B:craw{m}"])
        p.op("scalar", lambda e, bank=bank: e.activation(
            sqs[:, :], p.pb[bank][:, :], AF.Square),
            reads=[f"pb{bank}"], writes=["B:sqs"], after=[cp])
        p.op("tensor", lambda e, m=m: e.matmul(
            p.pb[4][0:1, :], C["onc"][:, 0:1], sqs[:, :],
            start=(m == 0), stop=(m == 10)),
            reads=["B:sqs", "onc"],
            writes=["pb4"] if m in (0, 10) else [])

    for tt in range(NT):
        t0 = tt * 512
        for m in range(11):
            bank = m % 4
            for k in range(16):
                p.op("tensor", lambda e, k=k, m=m, t0=t0, bank=bank: e.matmul(
                    p.pb[bank][:, :],
                    wsb[:, k * KVPP + m * 128: k * KVPP + (m + 1) * 128],
                    xsb[:, k * TC + t0: k * TC + t0 + 512],
                    start=(k == 0), stop=(k == 15)),
                    reads=[f"x{k}", f"B:w{k}"],
                    writes=[f"pb{bank}"] if k in (0, 15) else [])
            if m > 0:
                epiB(m - 1)
        for k in range(16):
            p.op("tensor", lambda e, k=k, t0=t0: e.matmul(
                p.pb[5][0:64, :],
                wrs[:, k * 64:(k + 1) * 64],
                xsb[:, k * TC + t0: k * TC + t0 + 512],
                start=(k == 0), stop=(k == 15)),
                reads=[f"x{k}", f"B:wr{k}"],
                writes=["pb5"] if k in (0, 15) else [])
        epiB(10)
        p.op("vector", lambda e, t0=t0: e.tensor_copy(
            kraw[:, t0:t0 + 512], p.pb[5][0:64, :]),
            reads=["pb5"], writes=["B:kraw"])
        p.op("vector", lambda e: e.tensor_scalar(
            rowbuf[0:1, 0:512], p.pb[4][0:1, :], 1.0 / KVP, EPS, ALU.mult, ALU.add),
            reads=["pb4"], writes=["B:row0"])
        p.op("scalar", lambda e: e.activation(
            rowbuf[0:1, 512:1024], rowbuf[0:1, 0:512], AF.Ln),
            reads=["B:row0"], writes=["B:row1"])
        p.op("scalar", lambda e: e.activation(
            rowbuf[0:1, 1024:1536], rowbuf[0:1, 512:1024], AF.Exp, scale=-0.5),
            reads=["B:row1"], writes=["B:row2"])
        p.op("tensor", lambda e: e.matmul(
            p.pb[6][:, :], C["onr"][0:1, 0:128], rowbuf[0:1, 1024:1536],
            start=True, stop=True),
            reads=["B:row2", "onr"], writes=["pb6"])
        for m in range(11):
            p.op("vector", lambda e, m=m: e.scalar_tensor_tensor(
                outsb[:, m * 512:(m + 1) * 512],
                craw[:, m * 512:(m + 1) * 512],
                C["gkv"][:, m:m + 1], p.pb[6][:, :], ALU.mult, ALU.mult),
                reads=[f"B:craw{m}", "gkv", "pb6"], writes=[f"B:out{m}"])
            p.op("gpsimd", lambda e, m=m, t0=t0: e.indirect_dma_start(
                t["ckv_sh"][:, :],
                IndirectOffsetOnAxis(ap=C["idx_fckv"][:, m:m + 1], axis=0),
                outsb[:, m * 512:(m + 1) * 512], None, element_offset=t0),
                reads=[f"B:out{m}", "idx"], dma=True)
    for hh in range(2):
        c0 = hh * 1024
        cs_ = slice(c0, c0 + 1024)
        p.op("vector", lambda e, cs_=cs_: e.tensor_mul(
            ksc1[0:32, :], kraw[0:32, cs_], css[0:32, cs_]),
            reads=["B:kraw", "B:cs"], writes=["B:ksc1a"])
        p.op("vector", lambda e, cs_=cs_: e.tensor_mul(
            ksc1[32:64, :], kraw[32:64, cs_], css[32:64, cs_]),
            reads=["B:kraw", "B:cs"], writes=["B:ksc1b"])
        p.op("vector", lambda e, cs_=cs_: e.tensor_mul(
            ksc2[0:32, :], kraw[0:32, cs_], scs[0:32, cs_]),
            reads=["B:kraw", "B:sc"], writes=["B:ksc2a"])
        p.op("vector", lambda e, cs_=cs_: e.tensor_mul(
            ksc2[32:64, :], kraw[32:64, cs_], scs[32:64, cs_]),
            reads=["B:kraw", "B:sc"], writes=["B:ksc2b"])
        p.op("vector", lambda e, cs_=cs_: e.tensor_copy(kraw[0:32, cs_], ksc1[32:64, :]),
             reads=["B:ksc1b"], writes=["B:kraw"])
        p.op("vector", lambda e, cs_=cs_: e.tensor_sub(
            kout[0:32, cs_], ksc1[0:32, :], kraw[0:32, cs_]),
            reads=["B:ksc1a", "B:kraw"], writes=[f"B:ne{hh}"])
        p.op("vector", lambda e, cs_=cs_: e.tensor_copy(kraw[32:64, cs_], ksc2[0:32, :]),
             reads=["B:ksc2a"], writes=["B:kraw"])
        p.op("vector", lambda e, cs_=cs_: e.tensor_add(
            kout[32:64, cs_], ksc2[32:64, :], kraw[32:64, cs_]),
            reads=["B:ksc2b", "B:kraw"], writes=[f"B:no{hh}"])
    p.op("gpsimd", lambda e: e.indirect_dma_start(
        t["kr_sh"][:, :],
        IndirectOffsetOnAxis(ap=C["idx_fkr"][0:64, 0:1], axis=0),
        kout[0:64, :], None),
        reads=["B:ne0", "B:no0", "B:ne1", "B:no1", "idx"], dma=True)


def em_C(p, t, C, ar, rw):
    csb = ar.bf16(128, 8 * S)
    wsb = ar.bf16(128, 8 * QP)
    c4 = ar.f32(128, S)
    s4 = ar.f32(128, S)
    rraw = ar.f32(128, 4 * 512)
    rsc1 = ar.f32(128, 512)
    rsc2 = ar.f32(128, 512)
    outsb = ar.bf16(128, 8 * 512)
    for k in range(8):
        for pp in range(2):
            p.op("sync", lambda e, k=k, pp=pp: e.dma_start(
                out=csb[:, k * S + pp * 2048: k * S + pp * 2048 + 2048],
                in_=t["cq_sh"][pp * 1024 + k * 128: pp * 1024 + (k + 1) * 128, :]),
                writes=[f"C:c{k}"], dma=True)
        p.op("sync", lambda e, k=k: e.dma_start(
            out=wsb[:, k * QP:(k + 1) * QP],
            in_=t["wq_up"][k * 128:(k + 1) * 128, :]),
            writes=[f"C:w{k}"], dma=True)
    p.op("sync", lambda e: e.dma_start(out=c4[:, :], in_=t["cos4"]),
         writes=["C:cos"], dma=True)
    p.op("sync", lambda e: e.dma_start(out=s4[:, :], in_=t["sin4"]),
         writes=["C:sin"], dma=True)
    for tt in range(S // 512):
        t0 = tt * 512
        for m in range(8):
            bank = m % 4
            for k in range(8):
                p.op("tensor", lambda e, k=k, m=m, t0=t0, bank=bank: e.matmul(
                    p.pb[bank][:, :],
                    wsb[:, k * QP + m * 128: k * QP + (m + 1) * 128],
                    csb[:, k * S + t0: k * S + t0 + 512],
                    start=(k == 0), stop=(k == 7)),
                    reads=[f"C:c{k}", f"C:w{k}"],
                    writes=[f"pb{bank}"] if k in (0, 7) else [])
            if m < 4:
                p.op("vector", lambda e, m=m, bank=bank: e.tensor_copy(
                    outsb[:, m * 512:(m + 1) * 512], p.pb[bank][:, :]),
                    reads=[f"pb{bank}"], writes=[f"C:out{m}"])
            else:
                p.op("vector", lambda e, m=m, bank=bank: e.tensor_copy(
                    rraw[:, (m - 4) * 512:(m - 3) * 512], p.pb[bank][:, :]),
                    reads=[f"pb{bank}"], writes=[f"C:rr{m - 4}"])
        for g in range(2):
            ek, ok = f"C:rr{g}", f"C:rr{2 + g}"
            E = rraw[:, g * 512:(g + 1) * 512]
            O = rraw[:, (2 + g) * 512:(3 + g) * 512]
            p.op("vector", lambda e, E=E, t0=t0: e.tensor_mul(
                rsc1[:, :], E, c4[:, t0:t0 + 512]),
                reads=[ek, "C:cos"], writes=["C:rs1"])
            p.op("vector", lambda e, O=O, t0=t0: e.tensor_mul(
                rsc2[:, :], O, s4[:, t0:t0 + 512]),
                reads=[ok, "C:sin"], writes=["C:rs2"])
            p.op("vector", lambda e, g=g: e.tensor_sub(
                outsb[:, (4 + g) * 512:(5 + g) * 512], rsc1[:, :], rsc2[:, :]),
                reads=["C:rs1", "C:rs2"], writes=[f"C:out{4 + g}"])
            p.op("vector", lambda e, O=O, t0=t0: e.tensor_mul(
                rsc1[:, :], O, c4[:, t0:t0 + 512]),
                reads=[ok, "C:cos"], writes=["C:rs1"])
            p.op("vector", lambda e, E=E, t0=t0: e.tensor_mul(
                rsc2[:, :], E, s4[:, t0:t0 + 512]),
                reads=[ek, "C:sin"], writes=["C:rs2"])
            p.op("vector", lambda e, g=g: e.tensor_add(
                outsb[:, (6 + g) * 512:(7 + g) * 512], rsc1[:, :], rsc2[:, :]),
                reads=["C:rs1", "C:rs2"], writes=[f"C:out{6 + g}"])
        for m in range(8):
            p.op("sync", lambda e, m=m, t0=t0: e.dma_start(
                out=t["q0"][m * 128:(m + 1) * 128, t0:t0 + 512],
                in_=outsb[:, m * 512:(m + 1) * 512]),
                reads=[f"C:out{m}"], dma=True)


def em_D(p, t, C, ar, rw):
    csb = ar.bf16(128, 11 * S)
    wks = ar.bf16(128, 11 * 512)
    wvs = ar.bf16(128, 11 * 1024)
    outsb = ar.bf16(128, 4 * 512)
    vosb = ar.bf16(128, 2 * 512)
    for k in range(11):
        for pp in range(2):
            p.op("sync", lambda e, k=k, pp=pp: e.dma_start(
                out=csb[:, k * S + pp * 2048: k * S + pp * 2048 + 2048],
                in_=t["ckv_sh"][pp * 1408 + k * 128: pp * 1408 + (k + 1) * 128, :]),
                writes=[f"D:c{k}"], dma=True)
        p.op("sync", lambda e, k=k: e.dma_start(
            out=wks[:, k * 512:(k + 1) * 512],
            in_=t["wkv_up_k"][k * 128:(k + 1) * 128, :]),
            writes=[f"D:wk{k}"], dma=True)
        p.op("sync", lambda e, k=k: e.dma_start(
            out=wvs[:, k * 1024:(k + 1) * 1024],
            in_=t["wkv_up_v"][k * 128:(k + 1) * 128, :]),
            writes=[f"D:wv{k}"], dma=True)
    for tt in range(S // 512):
        t0 = tt * 512
        for m in range(4):
            bank = m % 4
            for k in range(11):
                p.op("tensor", lambda e, k=k, m=m, t0=t0, bank=bank: e.matmul(
                    p.pb[bank][:, :],
                    wks[:, k * 512 + m * 128: k * 512 + (m + 1) * 128],
                    csb[:, k * S + t0: k * S + t0 + 512],
                    start=(k == 0), stop=(k == 10)),
                    reads=[f"D:c{k}", f"D:wk{k}"],
                    writes=[f"pb{bank}"] if k in (0, 10) else [])
            p.op("vector", lambda e, m=m, bank=bank: e.tensor_copy(
                outsb[:, m * 512:(m + 1) * 512], p.pb[bank][:, :]),
                reads=[f"pb{bank}"], writes=[f"D:out{m}"])
            p.op("sync", lambda e, m=m, t0=t0: e.dma_start(
                out=t["knope"][m * 128:(m + 1) * 128, t0:t0 + 512],
                in_=outsb[:, m * 512:(m + 1) * 512]),
                reads=[f"D:out{m}"], dma=True)
    for ttile in range(S // 128):
        for v4 in range(2):
            bank = 4 + (2 * ttile + v4) % 4
            for k in range(11):
                p.op("tensor", lambda e, k=k, ttile=ttile, v4=v4, bank=bank: e.matmul(
                    p.pb[bank][:, :],
                    csb[:, k * S + ttile * 128: k * S + (ttile + 1) * 128],
                    wvs[:, k * 1024 + v4 * 512: k * 1024 + (v4 + 1) * 512],
                    start=(k == 0), stop=(k == 10)),
                    reads=[f"D:c{k}", f"D:wv{k}"],
                    writes=[f"pb{bank}"] if k in (0, 10) else [])
            p.op("vector", lambda e, v4=v4, bank=bank: e.tensor_copy(
                vosb[:, v4 * 512:(v4 + 1) * 512], p.pb[bank][:, :]),
                reads=[f"pb{bank}"], writes=[f"D:vo{v4}"])
            p.op("sync", lambda e, ttile=ttile, v4=v4: e.dma_start(
                out=t["vt"][ttile * 128:(ttile + 1) * 128, v4 * 512:(v4 + 1) * 512],
                in_=vosb[:, v4 * 512:(v4 + 1) * 512]),
                reads=[f"D:vo{v4}"], dma=True)


def _bound_c(p, q_key, q_sb, sq_sb, onc, srow, slot, pf):
    for c8 in range(8):
        p.op("scalar", lambda e, c8=c8: e.activation(
            sq_sb[:, 0:512], q_sb[:, c8 * 512:(c8 + 1) * 512], AF.Square),
            reads=[q_key], writes=[f"{pf}:bsq"])
        p.op("tensor", lambda e: e.matmul(
            p.pb[6][0:1, 0:512], onc[:, 0:1], sq_sb[:, 0:512],
            start=True, stop=True),
            reads=[f"{pf}:bsq", "onc"], writes=["pb6"])
        p.op("vector", lambda e, c8=c8, slot=slot: e.reduce_max(
            srow[0:1, 16 + slot * 8 + c8:17 + slot * 8 + c8],
            p.pb[6][0:1, 0:512], axis=AX),
            reads=["pb6"], writes=[f"{pf}:m8_{slot}_{c8}"])
    p.op("vector", lambda e, slot=slot: e.reduce_max(
        srow[0:1, slot:slot + 1],
        srow[0:1, 16 + slot * 8:16 + slot * 8 + 8], axis=AX),
        reads=[f"{pf}:m8_{slot}_{c8}" for c8 in range(8)], writes=[f"{pf}:mx{slot}"])


def _negc_from_bounds(p, C, srow, negc_col, pf, pr=0):
    p.op("vector", lambda e: e.tensor_mul(
        srow[0:1, 2:3], srow[0:1, 0:1], srow[0:1, 1:2]),
        reads=[f"{pf}:mx0", f"{pf}:mx1"], writes=[f"{pf}:prod"])
    p.op("scalar", lambda e: e.activation(
        srow[0:1, 3:4], srow[0:1, 2:3], AF.Ln),
        reads=[f"{pf}:prod"], writes=[f"{pf}:lnp"])
    p.op("scalar", lambda e: e.activation(
        srow[0:1, 4:5], srow[0:1, 3:4], AF.Exp, scale=0.5),
        reads=[f"{pf}:lnp"], writes=[f"{pf}:sqp"])
    p.op("vector", lambda e: e.tensor_scalar_mul(
        srow[0:1, 5:6], srow[0:1, 4:5], -SCALE),
        reads=[f"{pf}:sqp"], writes=[f"{pf}:negc_row"])
    p.op("tensor", lambda e: e.matmul(
        p.pb[6][:, 0:1], C["onr"][0:1, 0:128], srow[0:1, 5:6],
        start=True, stop=True),
        reads=[f"{pf}:negc_row", "onr"], writes=["pb6"])
    p.op("vector", lambda e: e.tensor_copy(negc_col[:, 0:1], p.pb[6][:, 0:1]),
         reads=["pb6"], writes=[f"{pf}:negc{pr}"])


def em_E(p, t, C, ar, rw):
    q_sb = ar.bf16(128, 2 * S)
    k_sb = ar.bf16(128, 2 * S)
    vt_sb = ar.bf16(128, 2 * S)
    msk = ar.f32(128, 2048)
    sq_sb = ar.bf16(128, 512)
    negc = ar.f32(128, 2)
    probs = ar.bf16(128, 2 * 512)
    rbc = ar.f32(128, 512)
    ctxo = ar.bf16(128, 512)
    srow = rw.f32(80)
    rrow = rw.f32(512)
    p.op("sync", lambda e: e.dma_start(out=msk[:, :], in_=t["maskE"]),
         writes=["E:msk"], dma=True)

    def loads(j):
        pr = j % 2
        o = pr * S
        nof, evf, odf = q_head_rows8(j)
        p.op("sync", lambda e, nof=nof, o=o: e.dma_start(
            out=q_sb[0:64, o:o + S], in_=t["q0"][nof:nof + 64, :]),
            writes=[f"E:q{pr}"], dma=True)
        p.op("sync", lambda e, evf=evf, o=o: e.dma_start(
            out=q_sb[64:96, o:o + S], in_=t["q0"][evf:evf + 32, :]),
            writes=[f"E:q{pr}"], dma=True)
        p.op("sync", lambda e, odf=odf, o=o: e.dma_start(
            out=q_sb[96:128, o:o + S], in_=t["q0"][odf:odf + 32, :]),
            writes=[f"E:q{pr}"], dma=True)
        p.op("sync", lambda e, j=j, o=o: e.dma_start(
            out=k_sb[0:64, o:o + S], in_=t["knope"][j * 64:(j + 1) * 64, :]),
            writes=[f"E:k{pr}"], dma=True)
        for pp in range(2):
            p.op("sync", lambda e, pp=pp, o=o: e.dma_start(
                out=k_sb[64:96, o + pp * 2048: o + (pp + 1) * 2048],
                in_=t["kr_sh"][pp * 64: pp * 64 + 32, :]),
                writes=[f"E:k{pr}"], dma=True)
            p.op("sync", lambda e, pp=pp, o=o: e.dma_start(
                out=k_sb[96:128, o + pp * 2048: o + (pp + 1) * 2048],
                in_=t["kr_sh"][pp * 64 + 32: pp * 64 + 64, :]),
                writes=[f"E:k{pr}"], dma=True)
        for g in range(8):
            p.op("sync", lambda e, j=j, g=g, o=o: e.dma_start(
                out=vt_sb[:, o + g * 512: o + (g + 1) * 512].rearrange(
                    "p (a d) -> p a d", d=128),
                in_=t["vt"][g * 512:(g + 1) * 512, j * 128:(j + 1) * 128].rearrange(
                    "(a p) d -> p a d", p=128)),
                writes=[f"E:v{pr}"], dma=True)

    loads(0)
    for j in range(8):
        pr = j % 2
        o = pr * S
        if j < 7:
            loads(j + 1)
        _bound_c(p, f"E:q{pr}", q_sb[:, o:o + S], sq_sb, C["onc"], srow, 0, "E")
        _bound_c(p, f"E:k{pr}", k_sb[:, o:o + S], sq_sb, C["onc"], srow, 1, "E")
        _negc_from_bounds(p, C, srow, negc[:, pr:pr + 1], "E", pr)

        def finkb(sl, kb, nkb, dn, pv):
            b0 = kb % 2
            jj = kb - 4 * sl
            if jj >= 0:
                p.op("vector", lambda e, jj=jj, b0=b0: e.tensor_add(
                    p.pb[b0][:, :], p.pb[b0][:, :], msk[:, jj * 512:(jj + 1) * 512]),
                    reads=[f"pb{b0}", "E:msk"], writes=[f"pb{b0}"])
            p.op("scalar", lambda e, b0=b0, pr=pr: e.activation(
                probs[:, b0 * 512:(b0 + 1) * 512], p.pb[b0][:, :], AF.Exp,
                bias=negc[:, pr:pr + 1], scale=SCALE),
                reads=[f"pb{b0}", f"E:negc{pr}"], writes=[f"E:probs{b0}"])
            p.op("tensor", lambda e, kb=kb, b0=b0, nkb=nkb, dn=dn: e.matmul(
                p.pb[dn][0:1, :], C["onc"][:, 0:1], probs[:, b0 * 512:(b0 + 1) * 512],
                start=(kb == 0), stop=(kb == nkb - 1)),
                reads=[f"E:probs{b0}", "onc"],
                writes=[f"pb{dn}"] if kb in (0, nkb - 1) else [])
            p.op("tensor", lambda e, kb=kb, b0=b0, nkb=nkb, pv=pv, o=o: e.matmul(
                p.pb[pv][:, :], vt_sb[:, o + kb * 128: o + (kb + 1) * 128],
                probs[:, b0 * 512:(b0 + 1) * 512],
                start=(kb == 0), stop=(kb == nkb - 1)),
                reads=[f"E:probs{b0}", f"E:v{pr}"],
                writes=[f"pb{pv}"] if kb in (0, nkb - 1) else [])

        def slepi(sl, dn, pv):
            p.op("vector", lambda e, dn=dn: e.reciprocal(rrow[0:1, :], p.pb[dn][0:1, :]),
                 reads=[f"pb{dn}"], writes=["E:rrow"])
            p.op("tensor", lambda e, dn=dn: e.matmul(
                p.pb[dn][:, :], C["onr"][0:1, 0:128], rrow[0:1, :],
                start=True, stop=True),
                reads=["E:rrow", "onr"], writes=[f"pb{dn}"])
            p.op("vector", lambda e, dn=dn: e.tensor_copy(rbc[:, :], p.pb[dn][:, :]),
                 reads=[f"pb{dn}"], writes=["E:rbc"])
            p.op("vector", lambda e, pv=pv: e.tensor_mul(ctxo[:, :], p.pb[pv][:, :], rbc[:, :]),
                 reads=[f"pb{pv}", "E:rbc"], writes=["E:ctxo"])
            p.op("gpsimd", lambda e, j=j, sl=sl: e.indirect_dma_start(
                t["ctx0_sh"][:, :],
                IndirectOffsetOnAxis(ap=C["idx_h"][:, j:j + 1], axis=0),
                ctxo[:, :], None, element_offset=sl * 512),
                reads=["E:ctxo", "idx"], dma=True)

        prev = None
        for sl in range(8):
            nkb = 4 * sl + 4
            dn = 4 + (sl % 2) * 2
            pv = 5 + (sl % 2) * 2
            for kb in range(nkb):
                b0 = kb % 2
                p.op("tensor", lambda e, sl=sl, kb=kb, b0=b0, o=o: e.matmul(
                    p.pb[b0][:, :],
                    k_sb[:, o + kb * 128: o + (kb + 1) * 128],
                    q_sb[:, o + sl * 512: o + (sl + 1) * 512],
                    start=True, stop=True),
                    reads=[f"E:k{pr}", f"E:q{pr}"], writes=[f"pb{b0}"])
                if kb == 0 and prev is not None:
                    slepi(*prev)
                if kb > 0:
                    finkb(sl, kb - 1, nkb, dn, pv)
            finkb(sl, nkb - 1, nkb, dn, pv)
            prev = (sl, dn, pv)
        slepi(*prev)


def _ln_apply(p, C, pf, u1, nchunk, dim, outfn, srow, t1, outs):
    A_ = srow[0:1, 0:512]
    Bm = srow[0:1, 512:1024]
    Cc = srow[0:1, 1024:1536]
    p.op("vector", lambda e: e.tensor_scalar(A_, p.pb[6][0:1, :], 1.0 / dim, None, ALU.mult),
         reads=["pb6"], writes=[f"{pf}:lnA"])
    p.op("vector", lambda e: e.tensor_mul(Bm, A_, A_), reads=[f"{pf}:lnA"],
         writes=[f"{pf}:lnB"])
    p.op("vector", lambda e: e.tensor_scalar(Cc, p.pb[7][0:1, :], 1.0 / dim, EPS,
                                             ALU.mult, ALU.add),
         reads=["pb7"], writes=[f"{pf}:lnC"])
    p.op("vector", lambda e: e.tensor_sub(Cc, Cc, Bm),
         reads=[f"{pf}:lnB", f"{pf}:lnC"], writes=[f"{pf}:lnC"])
    p.op("scalar", lambda e: e.activation(Cc, Cc, AF.Ln),
         reads=[f"{pf}:lnC"], writes=[f"{pf}:lnC"])
    p.op("scalar", lambda e: e.activation(Cc, Cc, AF.Exp, scale=-0.5),
         reads=[f"{pf}:lnC"], writes=[f"{pf}:lnC"])
    p.op("vector", lambda e: e.tensor_mul(Bm, A_, Cc),
         reads=[f"{pf}:lnA", f"{pf}:lnC"], writes=[f"{pf}:lnB"])
    p.op("vector", lambda e: e.tensor_scalar_mul(Bm, Bm, -1.0),
         reads=[f"{pf}:lnB"], writes=[f"{pf}:lnB"])
    p.op("tensor", lambda e: e.matmul(
        p.pb[4][:, :], C["onr"][0:1, 0:128], Cc, start=True, stop=True),
        reads=[f"{pf}:lnC", "onr"], writes=["pb4"])
    p.op("tensor", lambda e: e.matmul(
        p.pb[5][:, :], C["onr"][0:1, 0:128], Bm, start=True, stop=True),
        reads=[f"{pf}:lnB", "onr"], writes=["pb5"])
    for m in range(nchunk):
        p.op("vector", lambda e, m=m: e.tensor_mul(
            t1[:, :], u1[:, m * 512:(m + 1) * 512], p.pb[4][:, :]),
            reads=[f"{pf}:u1_{m}", "pb4"], writes=[f"{pf}:t1"])
        p.op("vector", lambda e: e.tensor_add(outs[:, :], t1[:, :], p.pb[5][:, :]),
             reads=[f"{pf}:t1", "pb5"], writes=[f"{pf}:outs"])
        outfn(m, outs)


def em_F(p, t, C, ar, rw):
    ctx0v = t["ctx0_sh"].rearrange("h (p w) -> (h p) w", p=2)
    csb = ar.bf16(128, 16 * TC)
    wsb = ar.bf16(128, 16 * D)
    xc = ar.f32(128, 2 * 512)
    u1 = ar.f32(128, 16 * 512)
    sq = ar.bf16(128, 512)
    t1 = ar.f32(128, 512)
    outs = ar.f32(128, 512)
    srow = rw.f32(3 * 512)
    for k in range(16):
        p.op("gpsimd", lambda e, k=k: e.indirect_dma_start(
            csb[:, k * TC:(k + 1) * TC], None,
            ctx0v[:, :], IndirectOffsetOnAxis(ap=C["idx_c"][:, k:k + 1], axis=0)),
            writes=[f"F:c{k}"], reads=["idx"], dma=True)
        p.op("sync", lambda e, k=k: e.dma_start(
            out=wsb[:, k * D:(k + 1) * D], in_=t["wo"][k * 128:(k + 1) * 128, :]),
            writes=[f"F:w{k}"], dma=True)
    def epiF(m):
        bank = m % 4
        xb = m % 2
        p.op("vector", lambda e, m=m, bank=bank, xb=xb: e.tensor_add(
            u1[:, m * 512:(m + 1) * 512], p.pb[bank][:, :],
            xc[:, xb * 512:(xb + 1) * 512]),
            reads=[f"pb{bank}", f"F:xc{xb}"], writes=[f"F:u1_{m}"])
        p.op("tensor", lambda e, m=m: e.matmul(
            p.pb[6][0:1, :], C["oncf"][:, 0:1], u1[:, m * 512:(m + 1) * 512],
            start=(m == 0), stop=(m == 15)),
            reads=[f"F:u1_{m}", "oncf"],
            writes=["pb6"] if m in (0, 15) else [])
        p.op("scalar", lambda e, m=m: e.activation(
            sq[:, :], u1[:, m * 512:(m + 1) * 512], AF.Square),
            reads=[f"F:u1_{m}"], writes=["F:sq"])
        p.op("tensor", lambda e, m=m: e.matmul(
            p.pb[7][0:1, :], C["onc"][:, 0:1], sq[:, :],
            start=(m == 0), stop=(m == 15)),
            reads=["F:sq", "onc"],
            writes=["pb7"] if m in (0, 15) else [])

    for tt in range(TC // 512):
        t0 = tt * 512
        for m in range(16):
            bank = m % 4
            xb = m % 2
            p.op("sync", lambda e, m=m, t0=t0, xb=xb: e.dma_start(
                out=xc[:, xb * 512:(xb + 1) * 512],
                in_=t["x_f"][m * 128:(m + 1) * 128, t0:t0 + 512]),
                writes=[f"F:xc{xb}"], dma=True)
            for k in range(16):
                p.op("tensor", lambda e, k=k, m=m, t0=t0, bank=bank: e.matmul(
                    p.pb[bank][:, :],
                    wsb[:, k * D + m * 128: k * D + (m + 1) * 128],
                    csb[:, k * TC + t0: k * TC + t0 + 512],
                    start=(k == 0), stop=(k == 15)),
                    reads=[f"F:c{k}", f"F:w{k}"],
                    writes=[f"pb{bank}"] if k in (0, 15) else [])
            if m > 0:
                epiF(m - 1)
        epiF(15)

        def outfn(m, o, t0=t0):
            p.op("sync", lambda e, m=m, t0=t0, o=o: e.dma_start(
                out=t["x1"][m * 128:(m + 1) * 128, t0:t0 + 512], in_=o[:, :]),
                reads=["F:outs"], dma=True)
        _ln_apply(p, C, "F", u1, 16, D, outfn, srow, t1, outs)


def em_G(p, t, C, ar, rw, L, xin, xout, scat):
    pf = f"G{L}"
    TH = TC // 2
    xb_sb = ar.bf16(128, 16 * TH)
    gw = ar.bf16(128, 2 * 2048)
    uw = ar.bf16(128, 2 * 2048)
    dw = ar.bf16(128, 2 * 5632)
    gssb = ar.f32(128, 44)
    ussb = ar.f32(128, 44)
    dssb = ar.f32(128, 16)
    g_sb = ar.f32(128, 1024)
    u_sb = ar.f32(128, 1024)
    hdn = ar.bf16(128, 44 * 1024)
    xc = ar.f32(128, 4 * 512)
    u2 = ar.f32(128, 2 * 512)
    sq = ar.bf16(128, 512)
    t1 = ar.f32(128, 512)
    outs = ar.f32(128, 512)
    outs_bf = ar.bf16(128, 512)
    srow = rw.f32(6 * 512)
    p.op("sync", lambda e: e.dma_start(out=gssb[:, :], in_=t[f"{L}_gs"]),
         writes=[f"{pf}:gs"], dma=True)
    p.op("sync", lambda e: e.dma_start(out=ussb[:, :], in_=t[f"{L}_us"]),
         writes=[f"{pf}:us"], dma=True)
    p.op("sync", lambda e: e.dma_start(out=dssb[:, :], in_=t[f"{L}_ds"]),
         writes=[f"{pf}:ds"], dma=True)
    for half in range(2):
        h0 = half * TH
        for k in range(16):
            p.op("gpsimd", lambda e, k=k, h0=h0: e.dma_start(
                out=xb_sb[:, k * TH:(k + 1) * TH],
                in_=xin[k * 128:(k + 1) * 128, h0:h0 + TH]),
                writes=[f"{pf}:xb{k}"], dma=True)
        for i in range(44):
            wsl = i % 2
            p.op("sync", lambda e, i=i, wsl=wsl: e.dma_start(
                out=gw[:, wsl * 2048:(wsl + 1) * 2048], in_=t[f"{L}_gws"][i]),
                writes=[f"{pf}:gw{wsl}"], dma=True)
            p.op("gpsimd", lambda e, i=i, wsl=wsl: e.dma_start(
                out=uw[:, wsl * 2048:(wsl + 1) * 2048], in_=t[f"{L}_uws"][i]),
                writes=[f"{pf}:uw{wsl}"], dma=True)
            for k in range(16):
                for t2 in range(2):
                    p.op("tensor", lambda e, k=k, t2=t2, wsl=wsl: e.matmul(
                        p.pb[t2][:, :],
                        gw[:, wsl * 2048 + k * 128: wsl * 2048 + (k + 1) * 128],
                        xb_sb[:, k * TH + t2 * 512: k * TH + t2 * 512 + 512],
                        start=(k == 0), stop=(k == 15)),
                        reads=[f"{pf}:gw{wsl}", f"{pf}:xb{k}"],
                        writes=[f"pb{t2}"] if k in (0, 15) else [])
            for k in range(16):
                for t2 in range(2):
                    p.op("tensor", lambda e, k=k, t2=t2, wsl=wsl: e.matmul(
                        p.pb[2 + t2][:, :],
                        uw[:, wsl * 2048 + k * 128: wsl * 2048 + (k + 1) * 128],
                        xb_sb[:, k * TH + t2 * 512: k * TH + t2 * 512 + 512],
                        start=(k == 0), stop=(k == 15)),
                        reads=[f"{pf}:uw{wsl}", f"{pf}:xb{k}"],
                        writes=[f"pb{2 + t2}"] if k in (0, 15) else [])
            for t2 in range(2):
                p.op("scalar", lambda e, i=i, t2=t2: e.activation(
                    g_sb[:, t2 * 512:(t2 + 1) * 512], p.pb[t2][:, :], AF.Silu,
                    scale=gssb[:, i:i + 1]),
                    reads=[f"pb{t2}", f"{pf}:gs"], writes=[f"{pf}:g{t2}"])
                p.op("vector", lambda e, i=i, t2=t2: e.tensor_scalar_mul(
                    u_sb[:, t2 * 512:(t2 + 1) * 512], p.pb[2 + t2][:, :],
                    ussb[:, i:i + 1]),
                    reads=[f"pb{2 + t2}", f"{pf}:us"], writes=[f"{pf}:u{t2}"])
            p.op("vector", lambda e, i=i: e.tensor_mul(
                hdn[:, i * 1024:(i + 1) * 1024], g_sb[:, :], u_sb[:, :]),
                reads=[f"{pf}:g0", f"{pf}:g1", f"{pf}:u0", f"{pf}:u1"],
                writes=[f"{pf}:hdn{i}"])
        def epiGd(f, h0):
            fx = (f % 2) * 2
            for t2 in range(2):
                bk = fx + t2
                p.op("vector", lambda e, f=f, t2=t2, fx=fx, bk=bk: e.scalar_tensor_tensor(
                    u2[:, t2 * 512:(t2 + 1) * 512], p.pb[bk][:, :],
                    dssb[:, f:f + 1], xc[:, (fx + t2) * 512:(fx + t2 + 1) * 512],
                    ALU.mult, ALU.add),
                    reads=[f"pb{bk}", f"{pf}:ds", f"{pf}:xc{fx + t2}"],
                    writes=[f"{pf}:u2_{t2}"])
                p.op("sync", lambda e, f=f, t2=t2, h0=h0: e.dma_start(
                    out=t["u2d"][f * 128:(f + 1) * 128, h0 + t2 * 512:h0 + t2 * 512 + 512],
                    in_=u2[:, t2 * 512:(t2 + 1) * 512]),
                    reads=[f"{pf}:u2_{t2}"], dma=True)
                p.op("tensor", lambda e, f=f, t2=t2: e.matmul(
                    p.pb[4 + t2][0:1, :], C["oncf"][:, 0:1], u2[:, t2 * 512:(t2 + 1) * 512],
                    start=(f == 0), stop=(f == 15)),
                    reads=[f"{pf}:u2_{t2}", "oncf"],
                    writes=[f"pb{4 + t2}"] if f in (0, 15) else [])
                p.op("scalar", lambda e, t2=t2: e.activation(
                    sq[:, :], u2[:, t2 * 512:(t2 + 1) * 512], AF.Square),
                    reads=[f"{pf}:u2_{t2}"], writes=[f"{pf}:sq"])
                p.op("tensor", lambda e, f=f, t2=t2: e.matmul(
                    p.pb[6 + t2][0:1, :], C["onc"][:, 0:1], sq[:, :],
                    start=(f == 0), stop=(f == 15)),
                    reads=[f"{pf}:sq", "onc"],
                    writes=[f"pb{6 + t2}"] if f in (0, 15) else [])

        for f in range(16):
            wsl = f % 2
            fx = (f % 2) * 2
            p.op("sync", lambda e, f=f, wsl=wsl: e.dma_start(
                out=dw[:, wsl * 5632:(wsl + 1) * 5632], in_=t[f"{L}_dws"][f]),
                writes=[f"{pf}:dw{wsl}"], dma=True)
            for t2 in range(2):
                p.op("sync", lambda e, f=f, t2=t2, h0=h0, fx=fx: e.dma_start(
                    out=xc[:, (fx + t2) * 512:(fx + t2 + 1) * 512],
                    in_=xin[f * 128:(f + 1) * 128, h0 + t2 * 512:h0 + t2 * 512 + 512]),
                    writes=[f"{pf}:xc{fx + t2}"], dma=True)
            for ik in range(44):
                for t2 in range(2):
                    bk = fx + t2
                    p.op("tensor", lambda e, ik=ik, t2=t2, wsl=wsl, bk=bk: e.matmul(
                        p.pb[bk][:, :],
                        dw[:, wsl * 5632 + ik * 128: wsl * 5632 + (ik + 1) * 128],
                        hdn[:, ik * 1024 + t2 * 512: ik * 1024 + t2 * 512 + 512],
                        start=(ik == 0), stop=(ik == 43)),
                        reads=[f"{pf}:dw{wsl}", f"{pf}:hdn{ik}"],
                        writes=[f"pb{bk}"] if ik in (0, 43) else [])
            if f > 0:
                epiGd(f - 1, h0)
        epiGd(15, h0)
        for t2 in range(2):
            base = 3 * t2
            A_ = srow[0:1, base * 512:(base + 1) * 512]
            Bm = srow[0:1, (base + 1) * 512:(base + 2) * 512]
            Cc = srow[0:1, (base + 2) * 512:(base + 3) * 512]
            p.op("vector", lambda e, A_=A_, t2=t2: e.tensor_scalar(
                A_, p.pb[4 + t2][0:1, :], 1.0 / D, None, ALU.mult),
                reads=[f"pb{4 + t2}"], writes=[f"{pf}:lnA{t2}"])
            p.op("vector", lambda e, A_=A_, Bm=Bm: e.tensor_mul(Bm, A_, A_),
                 reads=[f"{pf}:lnA{t2}"], writes=[f"{pf}:lnB{t2}"])
            p.op("vector", lambda e, Cc=Cc, t2=t2: e.tensor_scalar(
                Cc, p.pb[6 + t2][0:1, :], 1.0 / D, EPS, ALU.mult, ALU.add),
                reads=[f"pb{6 + t2}"], writes=[f"{pf}:lnC{t2}"])
            p.op("vector", lambda e, Cc=Cc, Bm=Bm: e.tensor_sub(Cc, Cc, Bm),
                 reads=[f"{pf}:lnB{t2}", f"{pf}:lnC{t2}"], writes=[f"{pf}:lnC{t2}"])
            p.op("scalar", lambda e, Cc=Cc: e.activation(Cc, Cc, AF.Ln),
                 reads=[f"{pf}:lnC{t2}"], writes=[f"{pf}:lnC{t2}"])
            p.op("scalar", lambda e, Cc=Cc: e.activation(Cc, Cc, AF.Exp, scale=-0.5),
                 reads=[f"{pf}:lnC{t2}"], writes=[f"{pf}:lnC{t2}"])
            p.op("vector", lambda e, A_=A_, Bm=Bm, Cc=Cc: e.tensor_mul(Bm, A_, Cc),
                 reads=[f"{pf}:lnA{t2}", f"{pf}:lnC{t2}"], writes=[f"{pf}:lnB{t2}"])
            p.op("vector", lambda e, Bm=Bm: e.tensor_scalar_mul(Bm, Bm, -1.0),
                 reads=[f"{pf}:lnB{t2}"], writes=[f"{pf}:lnB{t2}"])
            p.op("tensor", lambda e, Cc=Cc, t2=t2: e.matmul(
                p.pb[t2][:, :], C["onr"][0:1, 0:128], Cc, start=True, stop=True),
                reads=[f"{pf}:lnC{t2}", "onr"], writes=[f"pb{t2}"])
            p.op("tensor", lambda e, Bm=Bm, t2=t2: e.matmul(
                p.pb[2 + t2][:, :], C["onr"][0:1, 0:128], Bm, start=True, stop=True),
                reads=[f"{pf}:lnB{t2}", "onr"], writes=[f"pb{2 + t2}"])
            for f in range(16):
                p.op("sync", lambda e, f=f, t2=t2, h0=h0: e.dma_start(
                    out=u2[:, t2 * 512:(t2 + 1) * 512],
                    in_=t["u2d"][f * 128:(f + 1) * 128, h0 + t2 * 512:h0 + t2 * 512 + 512]),
                    writes=[f"{pf}:u2_{t2}"], dma=True)
                p.op("vector", lambda e, t2=t2: e.tensor_mul(
                    t1[:, :], u2[:, t2 * 512:(t2 + 1) * 512], p.pb[t2][:, :]),
                    reads=[f"{pf}:u2_{t2}", f"pb{t2}"], writes=[f"{pf}:t1"])
                p.op("vector", lambda e, t2=t2: e.tensor_add(
                    outs[:, :], t1[:, :], p.pb[2 + t2][:, :]),
                    reads=[f"{pf}:t1", f"pb{2 + t2}"], writes=[f"{pf}:outs"])
                p.op("sync", lambda e, f=f, t2=t2, h0=h0: e.dma_start(
                    out=xout[f * 128:(f + 1) * 128, h0 + t2 * 512:h0 + t2 * 512 + 512],
                    in_=outs[:, :]),
                    reads=[f"{pf}:outs"], dma=True)
                if scat:
                    p.op("vector", lambda e: e.tensor_copy(outs_bf[:, :], outs[:, :]),
                         reads=[f"{pf}:outs"], writes=[f"{pf}:obf"])
                    p.op("gpsimd", lambda e, f=f, t2=t2, h0=h0: e.indirect_dma_start(
                        t["x2_sh"][:, :],
                        IndirectOffsetOnAxis(ap=C["idx_t"][:, f:f + 1], axis=0),
                        outs_bf[:, :], None, element_offset=h0 + t2 * 512),
                        reads=[f"{pf}:obf", "idx"], dma=True)


def em_H(p, t, C, ar, rw):
    xsb = ar.bf16(128, 16 * S)
    wsb = ar.bf16(128, 16 * 1024)
    outsb = ar.bf16(128, 4 * 512)
    bpp = ar.f32(128, 16)
    bsb = rw.f32(3 * 1024)
    on5 = rw.f32(512)
    p.op("sync", lambda e: e.dma_start(out=on5[0:1, :], in_=t["ones512"]),
         writes=["H:on5"], dma=True)
    for k in range(16):
        for pp in range(2):
            p.op("sync", lambda e, k=k, pp=pp: e.dma_start(
                out=xsb[:, k * S + pp * 2048: k * S + pp * 2048 + 2048],
                in_=t["x2_sh"][pp * 2048 + k * 128: pp * 2048 + (k + 1) * 128, :]),
                writes=[f"H:x{k}"], dma=True)
    for nm, boff in (("bq1", 0), ("bk1", 1024), ("bv1", 2048)):
        p.op("sync", lambda e, nm=nm, boff=boff: e.dma_start(
            out=bsb[0:1, boff:boff + 1024], in_=t[nm]),
            writes=[f"H:{nm}"], dma=True)
    p.op("sync", lambda e: e.dma_start(out=bpp[:, 0:8], in_=t["bqpp"]),
         writes=["H:bpp"], dma=True)
    p.op("sync", lambda e: e.dma_start(out=bpp[:, 8:16], in_=t["bkpp"]),
         writes=["H:bpp"], dma=True)
    for wnm, bo, outd in (("wq1", 0, "q1"), ("wk1", 8, "k1")):
        for k in range(16):
            p.op("sync", lambda e, k=k, wnm=wnm: e.dma_start(
                out=wsb[:, k * 1024:(k + 1) * 1024],
                in_=t[wnm][k * 128:(k + 1) * 128, :]),
                writes=[f"H:w{k}"], dma=True)
        for tt in range(S // 512):
            t0 = tt * 512
            for m in range(8):
                bank = m % 4
                for k in range(16):
                    p.op("tensor", lambda e, k=k, m=m, t0=t0, bank=bank: e.matmul(
                        p.pb[bank][:, :],
                        wsb[:, k * 1024 + m * 128: k * 1024 + (m + 1) * 128],
                        xsb[:, k * S + t0: k * S + t0 + 512],
                        start=(k == 0), stop=(k == 15)),
                        reads=[f"H:x{k}", f"H:w{k}"],
                        writes=[f"pb{bank}"] if k in (0, 15) else [])
                p.op("vector", lambda e, m=m, bank=bank, bo=bo: e.tensor_scalar(
                    outsb[:, bank * 512:(bank + 1) * 512], p.pb[bank][:, :],
                    bpp[:, bo + m:bo + m + 1], None, ALU.add),
                    reads=[f"pb{bank}", "H:bpp"], writes=[f"H:o{bank}"])
                p.op("sync", lambda e, m=m, t0=t0, bank=bank, outd=outd: e.dma_start(
                    out=t[outd][m * 128:(m + 1) * 128, t0:t0 + 512],
                    in_=outsb[:, bank * 512:(bank + 1) * 512]),
                    reads=[f"H:o{bank}"], dma=True)
    for k in range(16):
        p.op("sync", lambda e, k=k: e.dma_start(
            out=wsb[:, k * 1024:(k + 1) * 1024],
            in_=t["wv1"][k * 128:(k + 1) * 128, :]),
            writes=[f"H:w{k}"], dma=True)
    for ttile in range(S // 128):
        for v4 in range(2):
            bank = 4 + (2 * ttile + v4) % 4
            p.op("tensor", lambda e, v4=v4, bank=bank: e.matmul(
                p.pb[bank][:, :], C["onr"][0:1, 0:128],
                bsb[0:1, 2048 + v4 * 512: 2048 + (v4 + 1) * 512],
                start=True, stop=False),
                reads=["H:bv1", "onr"], writes=[f"pb{bank}"])
            for k in range(16):
                p.op("tensor", lambda e, k=k, ttile=ttile, v4=v4, bank=bank: e.matmul(
                    p.pb[bank][:, :],
                    xsb[:, k * S + ttile * 128: k * S + (ttile + 1) * 128],
                    wsb[:, k * 1024 + v4 * 512: k * 1024 + (v4 + 1) * 512],
                    start=False, stop=(k == 15)),
                    reads=[f"H:x{k}", f"H:w{k}"],
                    writes=[f"pb{bank}"] if k == 15 else [])
            p.op("vector", lambda e, v4=v4, bank=bank: e.tensor_copy(
                outsb[:, v4 * 512:(v4 + 1) * 512], p.pb[bank][:, :]),
                reads=[f"pb{bank}"], writes=[f"H:o{v4}"])
            p.op("sync", lambda e, ttile=ttile, v4=v4: e.dma_start(
                out=t["v1t"][ttile * 128:(ttile + 1) * 128, v4 * 512:(v4 + 1) * 512],
                in_=outsb[:, v4 * 512:(v4 + 1) * 512]),
                reads=[f"H:o{v4}"], dma=True)


def em_I(p, t, C, ar, rw):
    q_sb = ar.bf16(128, S)
    k_sb = ar.bf16(128, S)
    vt_sb = ar.bf16(128, S)
    msk = ar.f32(128, 384)
    iorow = ar.f32(128, 64)
    iopp = ar.f32(64, 1)
    ident = ar.bf16(128, 128)
    sq_sb = ar.bf16(128, 512)
    negc = ar.f32(128, 1)
    na = ar.f32(8, S)
    nb = ar.f32(8, S)
    mx8 = ar.f32(8, 8)
    probs = ar.bf16(128, 384)
    lctx = ar.bf16(128, S)
    P_sb = ar.bf16(128, 64)
    qt_sb = ar.bf16(128, 128)
    qgT_sb = ar.bf16(64, 128)
    qg_sb = ar.bf16(128, 64)
    gp_sb = ar.bf16(128, 64)
    rg_pp = ar.f32(64, 1)
    gctx_sb = ar.bf16(64, 128)
    cb_sb = ar.f32(128, 512)
    pi_sb = ar.bf16(64, 512)
    rkcol = ar.f32(128, 1)
    t2_sb = ar.f32(128, 512)
    ctxo = ar.bf16(128, 512)
    nrow = ar.f32(1, S)
    rloc = ar.f32(1, S)
    srow = rw.f32(64)
    rgrow = rw.f32(64)
    rr_sb = rw.f32(512)
    mr_sb = rw.f32(512)
    cbrow = rw.f32(512)
    pbb6 = p.pb[6].bitcast(BF16)
    for (tl, dsrc, key) in [(msk, "maskL", "I:msk"), (iorow, "io_row", "I:iorow"),
                            (iopp, "io_pp", "I:iopp"), (ident, "identd", "I:ident")]:
        p.op("sync", lambda e, tl=tl, dsrc=dsrc: e.dma_start(out=tl[:, :], in_=t[dsrc]),
             writes=[key], dma=True)
    for j in range(8):
        p.op("sync", lambda e, j=j: e.dma_start(
            out=q_sb[:, :], in_=t["q1"][j * 128:(j + 1) * 128, :]),
            writes=["I:q"], dma=True)
        for c8 in range(8):
            p.op("scalar", lambda e, c8=c8: e.activation(
                sq_sb[:, :], q_sb[:, c8 * 512:(c8 + 1) * 512], AF.Square),
                reads=["I:q"], writes=["I:sq"])
            p.op("tensor", lambda e: e.matmul(
                p.pb[0][0:1, :], C["onc"][:, 0:1], sq_sb[:, :], start=True, stop=True),
                reads=["I:sq", "onc"], writes=["pb0"])
            p.op("vector", lambda e, c8=c8: e.tensor_copy(
                nrow[0:1, c8 * 512:(c8 + 1) * 512], p.pb[0][0:1, :]),
                reads=["pb0"], writes=["I:nrow"])
        p.op("sync", lambda e, j=j: e.dma_start(
            out=t["rankm"][j:j + 1, :], in_=nrow[0:1, :]),
            reads=["I:nrow"], writes=["I:na_d"], dma=True)
    p.op("sync", lambda e: e.dma_start(out=na[:, :], in_=t["rankm"]),
         reads=["I:na_d"], writes=["I:na"], dma=True)
    bufs = [na, nb]
    for rnd in range(8):
        cur, nxt = bufs[rnd % 2], bufs[(rnd + 1) % 2]
        ck, nk = ("I:na" if rnd % 2 == 0 else "I:nb"), ("I:nb" if rnd % 2 == 0 else "I:na")
        p.op("vector", lambda e, cur=cur: e.max(out=mx8[:, :], in_=cur[:, :]),
             reads=[ck], writes=["I:mx8"])
        if rnd == 7:
            p.op("vector", lambda e: e.memset(mx8[:, 6:8], -1.0),
                 reads=[], writes=["I:mx8"])
        p.op("vector", lambda e, cur=cur, nxt=nxt: e.match_replace(
            out=nxt[:, :], in_to_replace=mx8[:, :], in_values=cur[:, :],
            imm_value=-1.0),
            reads=[ck, "I:mx8"], writes=[nk])
    p.op("vector", lambda e: e.tensor_scalar(
        nb[:, :], na[:, :], -0.5, None, ALU.is_le),
        reads=["I:na"], writes=["I:nb"])
    p.op("vector", lambda e: e.memset(nb[:, 0:1], 1.0), reads=[], writes=["I:nb"])
    p.op("vector", lambda e: e.memset(nb[:, S - 1:S], 1.0), reads=[], writes=["I:nb"])
    p.op("vector", lambda e: e.tensor_tensor_scan(
        na[:, :], nb[:, :], nb[:, :], 0.0, ALU.add, ALU.bypass),
        reads=["I:nb"], writes=["I:na"])
    p.op("vector", lambda e: e.tensor_sub(na[:, :], na[:, :], nb[:, :]),
         reads=["I:na", "I:nb"], writes=["I:na"])
    p.op("sync", lambda e: e.dma_start(out=t["maskm"], in_=nb[:, :]),
         reads=["I:nb"], writes=["I:maskmD"], dma=True)
    p.op("vector", lambda e: e.tensor_scalar(
        nb[:, :], nb[:, :], -1e9, 1e9, ALU.mult, ALU.add),
        reads=["I:nb"], writes=["I:nb"])
    p.op("vector", lambda e: e.tensor_add(na[:, :], na[:, :], nb[:, :]),
         reads=["I:na", "I:nb"], writes=["I:na"])
    p.op("sync", lambda e: e.dma_start(out=t["rankm"], in_=na[:, :]),
         reads=["I:na"], writes=["I:rankmD"], dma=True)
    for j in range(8):
        p.op("sync", lambda e, j=j: e.dma_start(
            out=q_sb[:, :], in_=t["q1"][j * 128:(j + 1) * 128, :]),
            writes=["I:q"], dma=True)
        p.op("sync", lambda e, j=j: e.dma_start(
            out=k_sb[:, :], in_=t["k1"][j * 128:(j + 1) * 128, :]),
            writes=["I:k"], dma=True)
        for g in range(8):
            p.op("sync", lambda e, j=j, g=g: e.dma_start(
                out=vt_sb[:, g * 512:(g + 1) * 512].rearrange(
                    "p (a d) -> p a d", d=128),
                in_=t["v1t"][g * 512:(g + 1) * 512, j * 128:(j + 1) * 128].rearrange(
                    "(a p) d -> p a d", p=128)),
                writes=["I:v"], dma=True)
        _bound_c(p, "I:q", q_sb, sq_sb, C["onc"], srow, 0, "I")
        _bound_c(p, "I:k", k_sb, sq_sb, C["onc"], srow, 1, "I")
        _negc_from_bounds(p, C, srow, negc[:, 0:1], "I")
        NB = 32
        for tb in range(NB):
            qlo = max(0, (tb - 1) * 128)
            qhi = min(S, (tb + 2) * 128)
            Nq = qhi - qlo
            mlo = 128 if tb == 0 else 0
            p.op("tensor", lambda e, tb=tb, qlo=qlo, Nq=Nq: e.matmul(
                p.pb[0][:, 0:Nq], k_sb[:, tb * 128:(tb + 1) * 128],
                q_sb[:, qlo:qlo + Nq], start=True, stop=True),
                reads=["I:k", "I:q"], writes=["pb0"])
            p.op("vector", lambda e, Nq=Nq, mlo=mlo: e.tensor_add(
                p.pb[0][:, 0:Nq], p.pb[0][:, 0:Nq], msk[:, mlo:mlo + Nq]),
                reads=["pb0", "I:msk"], writes=["pb0"])
            p.op("scalar", lambda e, Nq=Nq: e.activation(
                probs[:, 0:Nq], p.pb[0][:, 0:Nq], AF.Exp,
                bias=negc[:, 0:1], scale=SCALE),
                reads=["pb0", "I:negc0"], writes=["I:probs"])
            for cg in (-1, 0, 1):
                qb = tb + cg
                if qb < 0 or qb >= NB:
                    continue
                col = qb * 128 - qlo
                first = (tb == max(0, qb - 1))
                last = (tb == min(NB - 1, qb + 1))
                cbank = 2 if qb % 3 == 0 else (3 if qb % 3 == 1 else 6)
                dbank = 1 if qb % 3 == 0 else (4 if qb % 3 == 1 else 5)
                p.op("tensor", lambda e, col=col, dbank=dbank, first=first,
                     last=last: e.matmul(
                    p.pb[dbank][0:1, 0:128], C["onc"][:, 0:1], probs[:, col:col + 128],
                    start=first, stop=last),
                    reads=["I:probs", "onc"],
                    writes=[f"pb{dbank}"] if (first or last) else [])
                p.op("tensor", lambda e, tb=tb, col=col, cbank=cbank, first=first,
                     last=last: e.matmul(
                    p.pb[cbank][:, 0:128], vt_sb[:, tb * 128:(tb + 1) * 128],
                    probs[:, col:col + 128], start=first, stop=last),
                    reads=["I:probs", "I:v"],
                    writes=[f"pb{cbank}"] if (first or last) else [])
            done = [tb - 1] if tb < NB - 1 else [tb - 1, tb]
            for qb in done:
                if qb < 0:
                    continue
                cbank = 2 if qb % 3 == 0 else (3 if qb % 3 == 1 else 6)
                dbank = 1 if qb % 3 == 0 else (4 if qb % 3 == 1 else 5)
                p.op("vector", lambda e, qb=qb, dbank=dbank: e.reciprocal(
                    rloc[0:1, qb * 128:(qb + 1) * 128], p.pb[dbank][0:1, 0:128]),
                    reads=[f"pb{dbank}"], writes=["I:rloc"])
                p.op("vector", lambda e, qb=qb, cbank=cbank: e.tensor_copy(
                    lctx[:, qb * 128:(qb + 1) * 128], p.pb[cbank][:, 0:128]),
                    reads=[f"pb{cbank}"], writes=["I:lctx"])
        for cc in range(32):
            p.op("sync", lambda e, j=j, cc=cc: e.dma_start(
                out=rkcol[:, 0:1], in_=t["rankm"][j:j + 1, cc * 128:(cc + 1) * 128]
                .rearrange("o p -> p o")),
                reads=["I:rankmD"], writes=["I:rkcol"], dma=True)
            p.op("vector", lambda e: e.tensor_scalar(
                P_sb[:, :], iorow[:, :], rkcol[:, 0:1], None, ALU.is_equal),
                reads=["I:iorow", "I:rkcol"], writes=["I:P"])
            p.op("tensor", lambda e, cc=cc: e.transpose(
                pbb6[:, 0:128], q_sb[:, cc * 128:(cc + 1) * 128], ident[:, :]),
                reads=["I:q", "I:ident"], writes=["pb6"])
            p.op("vector", lambda e: e.tensor_copy(qt_sb[:, :], pbb6[:, 0:128]),
                 reads=["pb6"], writes=["I:qt"])
            p.op("tensor", lambda e, cc=cc: e.matmul(
                p.pb[0][0:64, 0:128], P_sb[:, :], qt_sb[:, :],
                start=(cc == 0), stop=(cc == 31)),
                reads=["I:P", "I:qt"],
                writes=["pb0"] if cc in (0, 31) else [])
        p.op("vector", lambda e: e.tensor_copy(qgT_sb[:, :], p.pb[0][0:64, 0:128]),
             reads=["pb0"], writes=["I:qgT"])
        p.op("tensor", lambda e: e.transpose(
            pbb6[:, 0:64], qgT_sb[:, :], ident[0:64, 0:64]),
            reads=["I:qgT", "I:ident"], writes=["pb6"])
        p.op("vector", lambda e: e.tensor_copy(qg_sb[:, :], pbb6[:, 0:64]),
             reads=["pb6"], writes=["I:qg"])
        for kb in range(32):
            p.op("tensor", lambda e, kb=kb: e.matmul(
                p.pb[1][:, 0:64], k_sb[:, kb * 128:(kb + 1) * 128], qg_sb[:, :],
                start=True, stop=True),
                reads=["I:k", "I:qg"], writes=["pb1"])
            p.op("scalar", lambda e: e.activation(
                gp_sb[:, :], p.pb[1][:, 0:64], AF.Exp,
                bias=negc[:, 0:1], scale=SCALE),
                reads=["pb1", "I:negc0"], writes=["I:gp"])
            p.op("tensor", lambda e, kb=kb: e.matmul(
                p.pb[2][0:1, 0:64], C["onc"][:, 0:1], gp_sb[:, :],
                start=(kb == 0), stop=(kb == 31)),
                reads=["I:gp", "onc"],
                writes=["pb2"] if kb in (0, 31) else [])
            p.op("tensor", lambda e, kb=kb: e.matmul(
                p.pb[3][0:64, 0:128], gp_sb[:, :], vt_sb[:, kb * 128:(kb + 1) * 128],
                start=(kb == 0), stop=(kb == 31)),
                reads=["I:gp", "I:v"],
                writes=["pb3"] if kb in (0, 31) else [])
        p.op("vector", lambda e: e.reciprocal(rgrow[0:1, 0:64], p.pb[2][0:1, 0:64]),
             reads=["pb2"], writes=["I:rgrow"])
        p.op("tensor", lambda e: e.matmul(
            p.pb[5][0:64, 0:1], rgrow[0:1, 0:64], C["onr"][0:1, 0:1],
            start=True, stop=True),
            reads=["I:rgrow", "onr"], writes=["pb5"])
        p.op("vector", lambda e: e.tensor_copy(rg_pp[:, 0:1], p.pb[5][0:64, 0:1]),
             reads=["pb5"], writes=["I:rg"])
        p.op("vector", lambda e: e.tensor_scalar_mul(
            gctx_sb[:, :], p.pb[3][0:64, 0:128], rg_pp[:, 0:1]),
            reads=["pb3", "I:rg"], writes=["I:gctx"])
        for c8 in range(8):
            p.op("sync", lambda e, j=j, c8=c8: e.dma_start(
                out=rr_sb[0:1, :], in_=t["rankm"][j:j + 1, c8 * 512:(c8 + 1) * 512]),
                reads=["I:rankmD"], writes=["I:rr"], dma=True)
            p.op("sync", lambda e, j=j, c8=c8: e.dma_start(
                out=mr_sb[0:1, :], in_=t["maskm"][j:j + 1, c8 * 512:(c8 + 1) * 512]),
                reads=["I:maskmD"], writes=["I:mr"], dma=True)
            p.op("vector", lambda e: e.tensor_scalar(
                cbrow[0:1, :], mr_sb[0:1, :], -1.0, 1.0, ALU.mult, ALU.add),
                reads=["I:mr"], writes=["I:cbrow"])
            p.op("vector", lambda e, c8=c8: e.tensor_mul(
                cbrow[0:1, :], cbrow[0:1, :], rloc[0:1, c8 * 512:(c8 + 1) * 512]),
                reads=["I:cbrow", "I:rloc"], writes=["I:cbrow"])
            p.op("tensor", lambda e: e.matmul(
                p.pb[5][:, :], C["onr"][0:1, 0:128], cbrow[0:1, :], start=True, stop=True),
                reads=["I:cbrow", "onr"], writes=["pb5"])
            p.op("vector", lambda e: e.tensor_copy(cb_sb[:, :], p.pb[5][:, :]),
                 reads=["pb5"], writes=["I:cb"])
            p.op("tensor", lambda e: e.matmul(
                p.pb[7][0:64, :], C["onr"][0:1, 0:64], rr_sb[0:1, :],
                start=True, stop=True),
                reads=["I:rr", "onr"], writes=["pb7"])
            p.op("vector", lambda e: e.tensor_scalar(
                pi_sb[:, :], p.pb[7][0:64, :], iopp[:, 0:1], None, ALU.is_equal),
                reads=["pb7", "I:iopp"], writes=["I:pi"])
            p.op("tensor", lambda e: e.matmul(
                p.pb[4][:, :], gctx_sb[:, :], pi_sb[:, :], start=True, stop=True),
                reads=["I:gctx", "I:pi"], writes=["pb4"])
            p.op("vector", lambda e, c8=c8: e.tensor_mul(
                t2_sb[:, :], lctx[:, c8 * 512:(c8 + 1) * 512], cb_sb[:, :]),
                reads=["I:lctx", "I:cb"], writes=["I:t2"])
            p.op("vector", lambda e: e.tensor_add(ctxo[:, :], t2_sb[:, :], p.pb[4][:, :]),
                 reads=["I:t2", "pb4"], writes=["I:ctxo"])
            p.op("gpsimd", lambda e, j=j, c8=c8: e.indirect_dma_start(
                t["ctx1_sh"][:, :],
                IndirectOffsetOnAxis(ap=C["idx_h"][:, j:j + 1], axis=0),
                ctxo[:, :], None, element_offset=c8 * 512),
                reads=["I:ctxo", "idx"], dma=True)


def em_J(p, t, C, ar, rw):
    ctx1v = t["ctx1_sh"].rearrange("h (p w) -> (h p) w", p=2)
    a1sb = ar.bf16(128, 16 * TC)
    xc = ar.f32(128, 2 * 512)
    u1 = ar.f32(128, 16 * 512)
    sq = ar.bf16(128, 512)
    t1 = ar.f32(128, 512)
    outs = ar.f32(128, 512)
    srow = rw.f32(3 * 512)
    for k in range(16):
        p.op("gpsimd", lambda e, k=k: e.indirect_dma_start(
            a1sb[:, k * TC:(k + 1) * TC], None,
            ctx1v[:, :], IndirectOffsetOnAxis(ap=C["idx_c"][:, k:k + 1], axis=0)),
            writes=[f"J:a{k}"], reads=["idx"], dma=True)
    for tt in range(TC // 512):
        t0 = tt * 512
        for m in range(16):
            xb = m % 2
            p.op("sync", lambda e, m=m, t0=t0, xb=xb: e.dma_start(
                out=xc[:, xb * 512:(xb + 1) * 512],
                in_=t["x2"][m * 128:(m + 1) * 128, t0:t0 + 512]),
                writes=[f"J:xc{xb}"], dma=True)
            p.op("vector", lambda e, m=m, t0=t0: e.tensor_copy(
                u1[:, m * 512:(m + 1) * 512],
                a1sb[:, m * TC + t0: m * TC + t0 + 512]),
                reads=[f"J:a{m}"], writes=[f"J:u1_{m}"])
            p.op("vector", lambda e, m=m, xb=xb: e.tensor_add(
                u1[:, m * 512:(m + 1) * 512],
                u1[:, m * 512:(m + 1) * 512],
                xc[:, xb * 512:(xb + 1) * 512]),
                reads=[f"J:u1_{m}", f"J:xc{xb}"], writes=[f"J:u1_{m}"])
            p.op("tensor", lambda e, m=m: e.matmul(
                p.pb[6][0:1, :], C["oncf"][:, 0:1], u1[:, m * 512:(m + 1) * 512],
                start=(m == 0), stop=(m == 15)),
                reads=[f"J:u1_{m}", "oncf"],
                writes=["pb6"] if m in (0, 15) else [])
            p.op("scalar", lambda e, m=m: e.activation(
                sq[:, :], u1[:, m * 512:(m + 1) * 512], AF.Square),
                reads=[f"J:u1_{m}"], writes=["J:sq"])
            p.op("tensor", lambda e, m=m: e.matmul(
                p.pb[7][0:1, :], C["onc"][:, 0:1], sq[:, :],
                start=(m == 0), stop=(m == 15)),
                reads=["J:sq", "onc"],
                writes=["pb7"] if m in (0, 15) else [])

        def outfn(m, o, t0=t0):
            p.op("sync", lambda e, m=m, t0=t0, o=o: e.dma_start(
                out=t["x3"][m * 128:(m + 1) * 128, t0:t0 + 512], in_=o[:, :]),
                reads=["J:outs"], dma=True)
        _ln_apply(p, C, "J", u1, 16, D, outfn, srow, t1, outs)


PHASES_ALL = ("AB", "C", "D", "E", "F", "G0", "H", "I", "J", "G1")


def build_fused(upto="G1", dbg=False):
    nc = bass.Bass("TRN2", target_bir_lowering=False)
    phases = PHASES_ALL[:PHASES_ALL.index(upto) + 1]
    t = {}

    def ein(name, shape, dt=F32):
        t[name] = nc.dram_tensor(name, shape, dt, kind="ExternalInput").ap()

    ein("x_f", [D, TC]); ein("wq_down", [D, QP], BF16); ein("gq", [128, 8])
    ein("wkv_lora", [D, KVPP], BF16); ein("wkv_rope", [D, 64], BF16); ein("gkv", [128, 11])
    ein("wq_up", [QP, 1024], BF16); ein("wkv_up_k", [KVPP, 512], BF16)
    ein("wkv_up_v", [KVPP, 1024], BF16)
    ein("wo", [D, D], BF16)
    ein("cs_own", [64, TC]); ein("sc_own", [64, TC])
    ein("cos4", [128, S]); ein("sin4", [128, S])
    for L in ("L0", "L1"):
        ein(f"{L}_gws", [44, 128, 2048], BF16); ein(f"{L}_uws", [44, 128, 2048], BF16)
        ein(f"{L}_dws", [16, 128, 5632], BF16)
        ein(f"{L}_gs", [128, 44]); ein(f"{L}_us", [128, 44]); ein(f"{L}_ds", [128, 16])
    ein("wq1", [D, 1024], BF16); ein("wk1", [D, 1024], BF16)
    ein("wv1", [D, 1024], BF16)
    ein("bq1", [1, 1024]); ein("bk1", [1, 1024]); ein("bv1", [1, 1024])
    ein("bqpp", [128, 8]); ein("bkpp", [128, 8])
    ein("maskE", [128, 2048]); ein("maskL", [128, 384])
    ein("io_row", [128, 64]); ein("io_pp", [64, 1]); ein("identd", [128, 128], BF16)
    ein("ones_colb", [128, 1], BF16); ein("ones_colf", [128, 1])
    ein("ones_row", [1, 128]); ein("ones512", [1, 512])
    ein("idx_fcq", [128, 8], U32); ein("idx_fckv", [128, 11], U32)
    ein("idx_fkr", [64, 1], U32); ein("idx_h", [128, 8], U32)
    ein("idx_t", [128, 16], U32); ein("idx_c", [128, 16], U32)

    def sh_t(name, shape, dt=BF16):
        t[name] = nc.dram_tensor(name, shape, dt, kind="Internal",
                                 addr_space="Shared").ap()

    sh_t("cq_sh", [2 * 1024, 2048]); sh_t("ckv_sh", [2 * 1408, 2048])
    sh_t("kr_sh", [2 * 64, 2048])
    sh_t("ctx0_sh", [2048, S]); sh_t("x2_sh", [2 * 2048, 2048])
    sh_t("ctx1_sh", [2048, S])

    ikind = "ExternalOutput" if dbg else "Internal"

    def loc(name, shape, dt, kind):
        t[name] = nc.dram_tensor(name, shape, dt, kind=kind).ap()

    loc("q0", [1024, S], BF16, ikind)
    loc("knope", [512, S], BF16, ikind)
    loc("vt", [S, 1024], BF16, ikind)
    loc("x1", [D, TC], F32, ikind)
    loc("x2", [D, TC], F32, ikind)
    loc("x3", [D, TC], F32, ikind)
    loc("q1", [1024, S], BF16, ikind)
    loc("k1", [1024, S], BF16, ikind)
    loc("v1t", [S, 1024], BF16, ikind)
    loc("u2d", [D, TC], F32, "Internal")
    loc("rankm", [8, S], F32, "Internal")
    loc("maskm", [8, S], F32, "Internal")
    loc("bar_in", [1, 128], F32, "Internal")
    for i in range(4):
        loc(f"bar_out{i}", [1, 128], F32, "Internal")
    t["xout"] = nc.dram_tensor("xout", [D, TC], F32, kind="ExternalOutput").ap()

    with ExitStack() as ctx:
        C = {}
        C["onc"] = ctx.enter_context(nc.sbuf_tensor("c_onc", [128, 1], BF16))
        C["oncf"] = ctx.enter_context(nc.sbuf_tensor("c_oncf", [128, 1], F32))
        C["onr"] = ctx.enter_context(nc.sbuf_tensor("c_onr", [1, 128], F32))
        C["gq"] = ctx.enter_context(nc.sbuf_tensor("c_gq", [128, 8], F32))
        C["gkv"] = ctx.enter_context(nc.sbuf_tensor("c_gkv", [128, 11], F32))
        C["idx_fcq"] = ctx.enter_context(nc.sbuf_tensor("c_ifcq", [128, 8], U32))
        C["idx_fckv"] = ctx.enter_context(nc.sbuf_tensor("c_ifckv", [128, 11], U32))
        C["idx_fkr"] = ctx.enter_context(nc.sbuf_tensor("c_ifkr", [64, 1], U32))
        C["idx_h"] = ctx.enter_context(nc.sbuf_tensor("c_ih", [128, 8], U32))
        C["idx_t"] = ctx.enter_context(nc.sbuf_tensor("c_it", [128, 16], U32))
        C["idx_c"] = ctx.enter_context(nc.sbuf_tensor("c_ic", [128, 16], U32))
        ar = Arena(nc, ctx, 47232)
        rw = Rows(nc, ctx, 4096)
        p = Prog(nc, ctx)
        p.pb = [ctx.enter_context(nc.psum_tensor(f"pb{i}", [128, 512], F32))
                for i in range(8)]

        for nm in ("onc", "oncf", "onr", "gq", "gkv",
                   "idx_fcq", "idx_fckv", "idx_fkr", "idx_h", "idx_t", "idx_c"):
            src = {"onc": "ones_colb", "oncf": "ones_colf", "onr": "ones_row"}.get(nm, nm)
            key = "idx" if nm.startswith("idx") else nm
            p.op("sync", lambda e, nm=nm, src=src: e.dma_start(out=C[nm][:, :], in_=t[src]),
                 writes=[key], dma=True)
        p.op("sync", lambda e: e.dma_start(out=t["bar_in"], in_=C["onr"][0:1, :]),
             reads=["onr"], writes=["barin"], dma=True)

        def barrier(i):
            p.fence()
            p.cc(lambda e, i=i: e.collective_compute(
                "AllReduce", mybir.AluOpType.add, replica_groups=PAIRS,
                ins=[t["bar_in"].opt()], outs=[t[f"bar_out{i}"].opt()]))
            p.fence()

        for ph in phases:
            ar.reset(0)
            rw.reset()
            if ph == "AB":
                em_AB(p, t, C, ar, rw)
                barrier(0)
            elif ph == "C":
                em_C(p, t, C, ar, rw)
                p.fence()
            elif ph == "D":
                em_D(p, t, C, ar, rw)
                p.fence()
            elif ph == "E":
                em_E(p, t, C, ar, rw)
                barrier(1)
            elif ph == "F":
                em_F(p, t, C, ar, rw)
                p.fence()
            elif ph == "G0":
                em_G(p, t, C, ar, rw, "L0", t["x1"], t["x2"], scat=True)
                barrier(2)
            elif ph == "H":
                em_H(p, t, C, ar, rw)
                p.fence()
            elif ph == "I":
                em_I(p, t, C, ar, rw)
                barrier(3)
            elif ph == "J":
                em_J(p, t, C, ar, rw)
                p.fence()
            elif ph == "G1":
                em_G(p, t, C, ar, rw, "L1", t["x3"], t["xout"], scat=False)
        p.emit()
    return nc


def kernel(**ins):
    shd, cores = stage_inputs(ins)
    nc = build_fused()
    maps = []
    for c in range(NCORES):
        m = dict(shd)
        m.update(cores[c])
        maps.append(m)
    res = run_bass_kernel_spmd(nc, maps, core_ids=list(range(NCORES)))
    out = np.zeros((B, S, D), np.float32)
    for c in range(NCORES):
        b, r = c // 2, c % 2
        xo = np.asarray(res.results[c]["xout"])
        out[b, r * 2048:(r + 1) * 2048, :] = xo.T
    return out


# version 04
# speedup vs baseline: 1.4185x; 1.0221x over previous
import numpy as np
import ml_dtypes
import concourse.bass as bass
import concourse.mybir as mybir
from concourse.bass import IndirectOffsetOnAxis
from concourse.bass_utils import run_bass_kernel_spmd
from contextlib import ExitStack

F32 = mybir.dt.float32
BF16 = mybir.dt.bfloat16
U32 = mybir.dt.uint32
AX = mybir.AxisListType.X
ALU = mybir.AluOpType
AF = mybir.ActivationFunctionType
BF = ml_dtypes.bfloat16

NCORES = 8
B, S, D = 4, 4096, 2048
H, DH = 16, 128
QP = 1024
KVP = 1365
KVPP = 1408
INTER = 5632
EPS = 1e-5
SCALE = 1.0 / float(np.sqrt(np.float32(128.0)))
NEG = -1e9
T = B * S
TC = 2048
PAIRS = [[0, 1], [2, 3], [4, 5], [6, 7]]

ENGS = ("sync", "gpsimd", "tensor", "vector", "scalar")
DMA_POOL = 8


class Prog:

    def __init__(self, nc, ctx):
        self.nc = nc
        self.ops = []
        self.last_writer = {}
        self.readers = {}
        self.eng_count = {e: 0 for e in ENGS}
        self.sems = {e: ctx.enter_context(nc.semaphore(f"se_{e}")) for e in ENGS}
        self.dpool = {e: [ctx.enter_context(nc.semaphore(f"sd_{e}{i}"))
                          for i in range(DMA_POOL)] for e in ("sync", "gpsimd", "scalar")}
        self.dcount = {e: [0] * DMA_POOL for e in ("sync", "gpsimd", "scalar")}
        self.drr = {e: 0 for e in ("sync", "gpsimd", "scalar")}
        self.ccsems = [ctx.enter_context(nc.semaphore(f"scc{i}")) for i in range(8)]
        self.ncc = 0
        self.fence_deps = ()

    def op(self, engine, fn, reads=(), writes=(), dma=False, after=()):
        deps = set(after)
        deps.update(self.fence_deps)
        for k in reads:
            if k in self.last_writer:
                deps.add(self.last_writer[k])
            if k.startswith("pb"):
                for rd in self.readers.get(k, ()):
                    if self.ops[rd][0] != engine:
                        deps.add(rd)
        for k in writes:
            if k in self.last_writer:
                deps.add(self.last_writer[k])
            for rd in self.readers.get(k, ()):
                deps.add(rd)
        i = len(self.ops)
        if dma:
            idx = self.drr[engine] % DMA_POOL
            self.drr[engine] += 1
            before = self.dcount[engine][idx]
            self.dcount[engine][idx] = before + 16
            info = ("dma", idx, before)
            pos = 0
        else:
            info = None
            self.eng_count[engine] += 1
            pos = self.eng_count[engine]
        self.ops.append((engine, fn, deps, info, pos))
        for k in reads:
            self.readers.setdefault(k, []).append(i)
        for k in writes:
            self.last_writer[k] = i
            self.readers[k] = []
        return i

    def cc(self, fn):
        sem = self.ccsems[self.ncc]
        self.ncc += 1
        deps = set(self.fence_deps)
        i = len(self.ops)
        self.ops.append(("gpsimd", fn, deps, ("cc", sem), 0))
        return i

    def fence(self):
        deps = []
        last_eng = {}
        last_dma = {}
        last_cc = []
        for i, (e, fn, d, info, pos) in enumerate(self.ops):
            if info is None:
                last_eng[e] = i
            elif info[0] == "dma":
                last_dma[(e, info[1])] = i
            else:
                last_cc.append(i)
        deps.extend(last_eng.values())
        deps.extend(last_dma.values())
        deps.extend(last_cc[-4:])
        self.fence_deps = tuple(deps)

    def emit(self):
        nc = self.nc
        with nc.Block() as block:
            for ename in ENGS:
                self._emit_engine(block, ename)

    def _emit_engine(self, block, ename):
        prog = self

        def body(eng):
            wm = {}

            def wait(sem, val):
                if val > wm.get(id(sem), 0):
                    eng.wait_ge(sem, val)
                    wm[id(sem)] = val

            def do_waits(deps):
                for d in sorted(deps):
                    de, dfn, ddeps, dinfo, dpos = prog.ops[d]
                    if dinfo is not None and dinfo[0] == "dma":
                        wait(prog.dpool[de][dinfo[1]], dinfo[2] + 16)
                    elif dinfo is not None and dinfo[0] == "cc":
                        wait(dinfo[1], 1)
                    else:
                        if de == ename and ename == "tensor":
                            continue
                        wait(prog.sems[de], dpos)

            for i, (e, fn, deps, info, pos) in enumerate(prog.ops):
                if e != ename:
                    continue
                do_waits(deps)
                if info is not None and info[0] == "dma":
                    idx, before = info[1], info[2]
                    if before > 0:
                        wait(prog.dpool[ename][idx], before)
                    inst = fn(eng)
                    inst.then_inc(prog.dpool[ename][idx], 16)
                elif info is not None and info[0] == "cc":
                    inst = fn(eng)
                    inst.then_inc(info[1])
                else:
                    inst = fn(eng)
                    inst.then_inc(prog.sems[ename], 1)
            if ename == "sync":
                for q, pool in prog.dpool.items():
                    for idx, sem in enumerate(pool):
                        if prog.dcount[q][idx] > 0:
                            wait(sem, prog.dcount[q][idx])
                for i in range(prog.ncc):
                    wait(prog.ccsems[i], 1)

        getattr(block, ename)(body)


class Arena:

    def __init__(self, nc, ctx, cols_f32):
        self.t = ctx.enter_context(nc.sbuf_tensor("arena", [128, cols_f32], F32))
        self.cap = cols_f32
        self.off = 0
        self.peak = 0

    def reset(self, to=0):
        self.off = to

    def f32(self, parts, cols):
        o = self.off
        self.off += cols
        self.peak = max(self.peak, self.off)
        assert self.off <= self.cap, f"arena overflow {self.off}>{self.cap}"
        return self.t[0:parts, o:o + cols]

    def bf16(self, parts, cols):
        cf = (cols + 1) // 2
        o = self.off
        self.off += cf
        self.peak = max(self.peak, self.off)
        assert self.off <= self.cap, f"arena overflow {self.off}>{self.cap}"
        return self.t.bitcast(BF16)[0:parts, 2 * o:2 * o + cols]


class Rows:

    def __init__(self, nc, ctx, cols):
        self.t = ctx.enter_context(nc.sbuf_tensor("rows", [1, cols], F32))
        self.cap = cols
        self.off = 0

    def reset(self):
        self.off = 0

    def f32(self, cols):
        o = self.off
        self.off += cols
        assert self.off <= self.cap, "rows overflow"
        return self.t[0:1, o:o + cols]


def _f32(a):
    return np.ascontiguousarray(np.asarray(a, dtype=np.float32))


def _bf(a):
    return np.ascontiguousarray(np.asarray(a, dtype=np.float32).astype(BF))


def _rope_tables_np():
    freqs = (1.0 / (np.float32(10000.0) **
                    (np.arange(0, 128, 2, dtype=np.float32) / np.float32(128.0))))
    pos = np.arange(S, dtype=np.float32)
    emb = pos[:, None] * freqs[None, :32]
    return np.cos(emb).astype(np.float32).T.copy(), \
        np.sin(emb).astype(np.float32).T.copy()


def q_pack_perm8(r):
    perm = np.zeros(1024, np.int64)
    for c in range(4):
        for half in range(2):
            j = 2 * c + half
            h = 8 * r + j
            perm[c * 128 + half * 64: c * 128 + half * 64 + 64] = \
                h * 128 + np.arange(64)
    for g in range(2):
        for hh in range(4):
            j = 4 * g + hh
            h = 8 * r + j
            perm[(4 + g) * 128 + hh * 32:(4 + g) * 128 + hh * 32 + 32] = \
                h * 128 + 64 + 2 * np.arange(32)
    for g in range(2):
        for hh in range(4):
            j = 4 * g + hh
            h = 8 * r + j
            perm[(6 + g) * 128 + hh * 32:(6 + g) * 128 + hh * 32 + 32] = \
                h * 128 + 64 + 2 * np.arange(32) + 1
    return perm


def q_head_rows8(j):
    return ((j // 2) * 128 + (j % 2) * 64,
            (4 + j // 4) * 128 + (j % 4) * 32,
            (6 + j // 4) * 128 + (j % 4) * 32)


def stage_inputs(ins):
    sh = {}
    x = ins["L0_attn_x"]
    xf = _f32(x.reshape(T, D).T)

    sh["wq_down"] = _bf(ins["L0_attn_Wq_down"])
    wkv = np.asarray(ins["L0_attn_Wkv_down"], np.float32)
    lora = np.zeros((D, KVPP), np.float32)
    lora[:, :KVP] = wkv[:, :KVP]
    sh["wkv_lora"] = _bf(lora)
    wr = wkv[:, KVP:]
    sh["wkv_rope"] = _bf(np.concatenate([wr[:, 0::2], wr[:, 1::2]], axis=1))

    sh["gq"] = _f32(np.asarray(ins["L0_attn_gq"]).reshape(8, 128).T)
    gkv = np.zeros(KVPP, np.float32)
    gkv[:KVP] = np.asarray(ins["L0_attn_gkv"])
    sh["gkv"] = _f32(gkv.reshape(11, 128).T)
    sh["wo"] = _bf(ins["L0_attn_Wo"])

    cosT, sinT = _rope_tables_np()
    sh["cos4"] = _f32(np.tile(cosT, (4, 1)))
    sh["sin4"] = _f32(np.tile(sinT, (4, 1)))

    wkvu = np.asarray(ins["L0_attn_Wkv_up"], np.float32)
    wkvu_p = np.zeros((KVPP, 3072), np.float32)
    wkvu_p[:KVP] = wkvu
    per_head = wkvu_p.reshape(KVPP, H, 192)

    for L in ("L0", "L1"):
        gw = np.asarray(ins[f"{L}_ffn_gate_w"], np.float32)
        uw = np.asarray(ins[f"{L}_ffn_up_w"], np.float32)
        dw = np.asarray(ins[f"{L}_ffn_down_w"], np.float32)
        sh[f"{L}_gws"] = _bf(gw.reshape(44, 128, 16, 128).transpose(0, 3, 2, 1)
                              .reshape(44, 128, 2048))
        sh[f"{L}_uws"] = _bf(uw.reshape(44, 128, 16, 128).transpose(0, 3, 2, 1)
                              .reshape(44, 128, 2048))
        sh[f"{L}_dws"] = _bf(dw.reshape(16, 128, 44, 128).transpose(0, 3, 2, 1)
                              .reshape(16, 128, 5632))
        sh[f"{L}_gs"] = _f32(np.asarray(ins[f"{L}_ffn_gate_s"]).reshape(44, 128).T)
        sh[f"{L}_us"] = _f32(np.asarray(ins[f"{L}_ffn_up_s"]).reshape(44, 128).T)
        sh[f"{L}_ds"] = _f32(np.asarray(ins[f"{L}_ffn_down_s"]).reshape(16, 128).T)

    sh["ones_colb"] = np.ones((128, 1), BF)
    sh["ones_colf"] = np.ones((128, 1), np.float32)
    sh["ones_row"] = np.ones((1, 128), np.float32)
    sh["ones512"] = np.ones((1, 512), np.float32)
    sh["identd"] = np.eye(128).astype(BF)
    sh["ident8f"] = np.eye(8).astype(np.float32)
    kk2 = np.arange(128)[:, None]
    qq2 = np.arange(512)[None, :]
    me = np.zeros((128, 4, 512), np.float32)
    for jj in range(4):
        me[:, jj, :] = np.where(jj * 128 + kk2 > qq2, NEG, 0.0)
    sh["maskE"] = _f32(me.reshape(128, 2048))
    ml = np.zeros((128, 384), np.float32)
    ml[64:128, 0:128] = NEG
    ml[0:64, 256:384] = NEG
    sh["maskL"] = _f32(ml)
    sh["io_row"] = _f32(np.tile(np.arange(64, dtype=np.float32)[None, :], (128, 1)))
    sh["io_pp"] = _f32(np.arange(64, dtype=np.float32).reshape(64, 1))

    wq1 = np.asarray(ins["L1_attn_Wq"], np.float32)
    wk1 = np.asarray(ins["L1_attn_Wk"], np.float32)
    wv1 = np.asarray(ins["L1_attn_Wv"], np.float32)
    bq1 = np.asarray(ins["L1_attn_bq"], np.float32)
    bk1 = np.asarray(ins["L1_attn_bk"], np.float32)
    bv1 = np.asarray(ins["L1_attn_bv"], np.float32)

    i128 = np.arange(128, dtype=np.uint32)[:, None]
    cores = []
    for c in range(NCORES):
        b, r = c // 2, c % 2
        pc = {}
        pc["x_f"] = _f32(xf[:, c * TC:(c + 1) * TC])
        pc["wq_up"] = _bf(np.asarray(ins["L0_attn_Wq_up"], np.float32)[:, q_pack_perm8(r)])
        hs = slice(8 * r, 8 * r + 8)
        pc["wkv_up_k"] = _bf(per_head[:, hs, :64].reshape(KVPP, 512))
        pc["wkv_up_v"] = _bf(per_head[:, hs, 64:].reshape(KVPP, 1024))
        ps = slice(r * 2048, r * 2048 + 2048)
        pc["cs_own"] = _f32(np.concatenate([cosT[:, ps], sinT[:, ps]], 0))
        pc["sc_own"] = _f32(np.concatenate([sinT[:, ps], cosT[:, ps]], 0))
        cs = slice(1024 * r, 1024 * r + 1024)
        pc["wq1"] = _bf(wq1[:, cs])
        pc["wk1"] = _bf(wk1[:, cs])
        pc["wv1"] = _bf(wv1[:, cs])
        pc["bq1"] = _f32(bq1[cs].reshape(1, 1024))
        pc["bqpp"] = _f32(bq1[cs].reshape(8, 128).T)
        pc["bkpp"] = _f32(bk1[cs].reshape(8, 128).T)
        pc["bk1"] = _f32(bk1[cs].reshape(1, 1024))
        pc["bv1"] = _f32(bv1[cs].reshape(1, 1024))
        pc["idx_fcq"] = (r * 1024 + np.arange(8, dtype=np.uint32)[None, :] * 128
                         + i128).astype(np.uint32)
        pc["idx_fckv"] = (r * 1408 + np.arange(11, dtype=np.uint32)[None, :] * 128
                          + i128).astype(np.uint32)
        pc["idx_fkr"] = (r * 64 + np.arange(64, dtype=np.uint32)[:, None]
                         ).astype(np.uint32)
        pc["idx_h"] = ((8 * r + np.arange(8, dtype=np.uint32)[None, :]) * 128
                       + i128).astype(np.uint32)
        pc["idx_t"] = (r * 2048 + np.arange(16, dtype=np.uint32)[None, :] * 128
                       + i128).astype(np.uint32)
        pc["idx_c"] = (2 * (np.arange(16, dtype=np.uint32)[None, :] * 128 + i128)
                       + r).astype(np.uint32)
        cores.append(pc)
    return sh, cores


def em_AB(p, t, C, ar, rw):
    xsb = ar.bf16(128, 16 * TC)
    for k in range(16):
        p.op("gpsimd", lambda e, k=k: e.dma_start(
            out=xsb[:, k * TC:(k + 1) * TC], in_=t["x_f"][k * 128:(k + 1) * 128, :]),
            writes=[f"x{k}"], dma=True)
    a0 = ar.off
    wsb = ar.bf16(128, 16 * QP)
    craw = ar.f32(128, 8 * 512)
    sqs = ar.bf16(128, 512)
    outsb = ar.bf16(128, 8 * 512)
    rowbuf = rw.f32(3 * 512)
    for k in range(16):
        p.op("sync", lambda e, k=k: e.dma_start(
            out=wsb[:, k * QP:(k + 1) * QP], in_=t["wq_down"][k * 128:(k + 1) * 128, :]),
            writes=[f"A:w{k}"], dma=True)
    NT = TC // 512

    def epiA(m):
        bank = m % 4
        cp = p.op("vector", lambda e, m=m, bank=bank: e.tensor_copy(
            craw[:, m * 512:(m + 1) * 512], p.pb[bank][:, :]),
            reads=[f"pb{bank}"], writes=[f"A:craw{m}"])
        p.op("scalar", lambda e, bank=bank: e.activation(
            sqs[:, :], p.pb[bank][:, :], AF.Square),
            reads=[f"pb{bank}"], writes=["A:sqs"], after=[cp])
        p.op("tensor", lambda e, m=m: e.matmul(
            p.pb[4][0:1, :], C["onc"][:, 0:1], sqs[:, :],
            start=(m == 0), stop=(m == 7)),
            reads=["A:sqs", "onc"],
            writes=["pb4"] if m in (0, 7) else [])

    for tt in range(NT):
        t0 = tt * 512
        for m in range(8):
            bank = m % 4
            for k in range(16):
                p.op("tensor", lambda e, k=k, m=m, t0=t0, bank=bank: e.matmul(
                    p.pb[bank][:, :],
                    wsb[:, k * QP + m * 128: k * QP + (m + 1) * 128],
                    xsb[:, k * TC + t0: k * TC + t0 + 512],
                    start=(k == 0), stop=(k == 15)),
                    reads=[f"x{k}", f"A:w{k}"],
                    writes=[f"pb{bank}"] if k in (0, 15) else [])
            if m > 0:
                epiA(m - 1)
        epiA(7)
        p.op("vector", lambda e: e.tensor_scalar(
            rowbuf[0:1, 0:512], p.pb[4][0:1, :], 1.0 / QP, EPS, ALU.mult, ALU.add),
            reads=["pb4"], writes=["A:row0"])
        p.op("scalar", lambda e: e.activation(
            rowbuf[0:1, 512:1024], rowbuf[0:1, 0:512], AF.Ln),
            reads=["A:row0"], writes=["A:row1"])
        p.op("scalar", lambda e: e.activation(
            rowbuf[0:1, 1024:1536], rowbuf[0:1, 512:1024], AF.Exp, scale=-0.5),
            reads=["A:row1"], writes=["A:row2"])
        p.op("tensor", lambda e: e.matmul(
            p.pb[5][:, :], C["onr"][0:1, 0:128], rowbuf[0:1, 1024:1536],
            start=True, stop=True),
            reads=["A:row2", "onr"], writes=["pb5"])
        for m in range(8):
            p.op("vector", lambda e, m=m: e.scalar_tensor_tensor(
                outsb[:, m * 512:(m + 1) * 512],
                craw[:, m * 512:(m + 1) * 512],
                C["gq"][:, m:m + 1], p.pb[5][:, :], ALU.mult, ALU.mult),
                reads=[f"A:craw{m}", "gq", "pb5"], writes=[f"A:out{m}"])
            p.op("gpsimd", lambda e, m=m, t0=t0: e.indirect_dma_start(
                t["cq_sh"][:, :],
                IndirectOffsetOnAxis(ap=C["idx_fcq"][:, m:m + 1], axis=0),
                outsb[:, m * 512:(m + 1) * 512], None, element_offset=t0),
                reads=[f"A:out{m}", "idx"], dma=True)
    p.fence()
    ar.reset(a0)
    wsb = ar.bf16(128, 16 * KVPP)
    wrs = ar.bf16(128, 16 * 64)
    css = ar.f32(64, 2048)
    scs = ar.f32(64, 2048)
    craw = ar.f32(128, 11 * 512)
    kraw = ar.f32(64, 2048)
    ksc1 = ar.f32(64, 1024)
    ksc2 = ar.f32(64, 1024)
    kout = ar.bf16(64, 2048)
    sqs = ar.bf16(128, 512)
    outsb = ar.bf16(128, 11 * 512)
    for k in range(16):
        p.op("sync", lambda e, k=k: e.dma_start(
            out=wsb[:, k * KVPP:(k + 1) * KVPP],
            in_=t["wkv_lora"][k * 128:(k + 1) * 128, :]),
            writes=[f"B:w{k}"], dma=True)
        p.op("sync", lambda e, k=k: e.dma_start(
            out=wrs[:, k * 64:(k + 1) * 64],
            in_=t["wkv_rope"][k * 128:(k + 1) * 128, :]),
            writes=[f"B:wr{k}"], dma=True)
    p.op("sync", lambda e: e.dma_start(out=css[:, :], in_=t["cs_own"]),
         writes=["B:cs"], dma=True)
    p.op("sync", lambda e: e.dma_start(out=scs[:, :], in_=t["sc_own"]),
         writes=["B:sc"], dma=True)
    rowbuf = rw.f32(3 * 512)
    def epiB(m):
        bank = m % 4
        cp = p.op("vector", lambda e, m=m, bank=bank: e.tensor_copy(
            craw[:, m * 512:(m + 1) * 512], p.pb[bank][:, :]),
            reads=[f"pb{bank}"], writes=[f"B:craw{m}"])
        p.op("scalar", lambda e, bank=bank: e.activation(
            sqs[:, :], p.pb[bank][:, :], AF.Square),
            reads=[f"pb{bank}"], writes=["B:sqs"], after=[cp])
        p.op("tensor", lambda e, m=m: e.matmul(
            p.pb[4][0:1, :], C["onc"][:, 0:1], sqs[:, :],
            start=(m == 0), stop=(m == 10)),
            reads=["B:sqs", "onc"],
            writes=["pb4"] if m in (0, 10) else [])

    for tt in range(NT):
        t0 = tt * 512
        for m in range(11):
            bank = m % 4
            for k in range(16):
                p.op("tensor", lambda e, k=k, m=m, t0=t0, bank=bank: e.matmul(
                    p.pb[bank][:, :],
                    wsb[:, k * KVPP + m * 128: k * KVPP + (m + 1) * 128],
                    xsb[:, k * TC + t0: k * TC + t0 + 512],
                    start=(k == 0), stop=(k == 15)),
                    reads=[f"x{k}", f"B:w{k}"],
                    writes=[f"pb{bank}"] if k in (0, 15) else [])
            if m > 0:
                epiB(m - 1)
        for k in range(16):
            p.op("tensor", lambda e, k=k, t0=t0: e.matmul(
                p.pb[5][0:64, :],
                wrs[:, k * 64:(k + 1) * 64],
                xsb[:, k * TC + t0: k * TC + t0 + 512],
                start=(k == 0), stop=(k == 15)),
                reads=[f"x{k}", f"B:wr{k}"],
                writes=["pb5"] if k in (0, 15) else [])
        epiB(10)
        p.op("vector", lambda e, t0=t0: e.tensor_copy(
            kraw[:, t0:t0 + 512], p.pb[5][0:64, :]),
            reads=["pb5"], writes=["B:kraw"])
        p.op("vector", lambda e: e.tensor_scalar(
            rowbuf[0:1, 0:512], p.pb[4][0:1, :], 1.0 / KVP, EPS, ALU.mult, ALU.add),
            reads=["pb4"], writes=["B:row0"])
        p.op("scalar", lambda e: e.activation(
            rowbuf[0:1, 512:1024], rowbuf[0:1, 0:512], AF.Ln),
            reads=["B:row0"], writes=["B:row1"])
        p.op("scalar", lambda e: e.activation(
            rowbuf[0:1, 1024:1536], rowbuf[0:1, 512:1024], AF.Exp, scale=-0.5),
            reads=["B:row1"], writes=["B:row2"])
        p.op("tensor", lambda e: e.matmul(
            p.pb[6][:, :], C["onr"][0:1, 0:128], rowbuf[0:1, 1024:1536],
            start=True, stop=True),
            reads=["B:row2", "onr"], writes=["pb6"])
        for m in range(11):
            p.op("vector", lambda e, m=m: e.scalar_tensor_tensor(
                outsb[:, m * 512:(m + 1) * 512],
                craw[:, m * 512:(m + 1) * 512],
                C["gkv"][:, m:m + 1], p.pb[6][:, :], ALU.mult, ALU.mult),
                reads=[f"B:craw{m}", "gkv", "pb6"], writes=[f"B:out{m}"])
            p.op("gpsimd", lambda e, m=m, t0=t0: e.indirect_dma_start(
                t["ckv_sh"][:, :],
                IndirectOffsetOnAxis(ap=C["idx_fckv"][:, m:m + 1], axis=0),
                outsb[:, m * 512:(m + 1) * 512], None, element_offset=t0),
                reads=[f"B:out{m}", "idx"], dma=True)
    for hh in range(2):
        c0 = hh * 1024
        cs_ = slice(c0, c0 + 1024)
        p.op("vector", lambda e, cs_=cs_: e.tensor_mul(
            ksc1[0:32, :], kraw[0:32, cs_], css[0:32, cs_]),
            reads=["B:kraw", "B:cs"], writes=["B:ksc1a"])
        p.op("vector", lambda e, cs_=cs_: e.tensor_mul(
            ksc1[32:64, :], kraw[32:64, cs_], css[32:64, cs_]),
            reads=["B:kraw", "B:cs"], writes=["B:ksc1b"])
        p.op("vector", lambda e, cs_=cs_: e.tensor_mul(
            ksc2[0:32, :], kraw[0:32, cs_], scs[0:32, cs_]),
            reads=["B:kraw", "B:sc"], writes=["B:ksc2a"])
        p.op("vector", lambda e, cs_=cs_: e.tensor_mul(
            ksc2[32:64, :], kraw[32:64, cs_], scs[32:64, cs_]),
            reads=["B:kraw", "B:sc"], writes=["B:ksc2b"])
        p.op("vector", lambda e, cs_=cs_: e.tensor_copy(kraw[0:32, cs_], ksc1[32:64, :]),
             reads=["B:ksc1b"], writes=["B:kraw"])
        p.op("vector", lambda e, cs_=cs_: e.tensor_sub(
            kout[0:32, cs_], ksc1[0:32, :], kraw[0:32, cs_]),
            reads=["B:ksc1a", "B:kraw"], writes=[f"B:ne{hh}"])
        p.op("vector", lambda e, cs_=cs_: e.tensor_copy(kraw[32:64, cs_], ksc2[0:32, :]),
             reads=["B:ksc2a"], writes=["B:kraw"])
        p.op("vector", lambda e, cs_=cs_: e.tensor_add(
            kout[32:64, cs_], ksc2[32:64, :], kraw[32:64, cs_]),
            reads=["B:ksc2b", "B:kraw"], writes=[f"B:no{hh}"])
    p.op("gpsimd", lambda e: e.indirect_dma_start(
        t["kr_sh"][:, :],
        IndirectOffsetOnAxis(ap=C["idx_fkr"][0:64, 0:1], axis=0),
        kout[0:64, :], None),
        reads=["B:ne0", "B:no0", "B:ne1", "B:no1", "idx"], dma=True)


def em_C(p, t, C, ar, rw):
    csb = ar.bf16(128, 8 * S)
    wsb = ar.bf16(128, 8 * QP)
    c4 = ar.f32(128, S)
    s4 = ar.f32(128, S)
    rraw = ar.f32(128, 4 * 512)
    rsc1 = ar.f32(128, 512)
    rsc2 = ar.f32(128, 512)
    outsb = ar.bf16(128, 8 * 512)
    for k in range(8):
        for pp in range(2):
            p.op("sync", lambda e, k=k, pp=pp: e.dma_start(
                out=csb[:, k * S + pp * 2048: k * S + pp * 2048 + 2048],
                in_=t["cq_sh"][pp * 1024 + k * 128: pp * 1024 + (k + 1) * 128, :]),
                writes=[f"C:c{k}"], dma=True)
        p.op("sync", lambda e, k=k: e.dma_start(
            out=wsb[:, k * QP:(k + 1) * QP],
            in_=t["wq_up"][k * 128:(k + 1) * 128, :]),
            writes=[f"C:w{k}"], dma=True)
    p.op("sync", lambda e: e.dma_start(out=c4[:, :], in_=t["cos4"]),
         writes=["C:cos"], dma=True)
    p.op("sync", lambda e: e.dma_start(out=s4[:, :], in_=t["sin4"]),
         writes=["C:sin"], dma=True)
    for tt in range(S // 512):
        t0 = tt * 512
        for m in range(8):
            bank = m % 4
            for k in range(8):
                p.op("tensor", lambda e, k=k, m=m, t0=t0, bank=bank: e.matmul(
                    p.pb[bank][:, :],
                    wsb[:, k * QP + m * 128: k * QP + (m + 1) * 128],
                    csb[:, k * S + t0: k * S + t0 + 512],
                    start=(k == 0), stop=(k == 7)),
                    reads=[f"C:c{k}", f"C:w{k}"],
                    writes=[f"pb{bank}"] if k in (0, 7) else [])
            if m < 4:
                p.op("vector", lambda e, m=m, bank=bank: e.tensor_copy(
                    outsb[:, m * 512:(m + 1) * 512], p.pb[bank][:, :]),
                    reads=[f"pb{bank}"], writes=[f"C:out{m}"])
            else:
                p.op("vector", lambda e, m=m, bank=bank: e.tensor_copy(
                    rraw[:, (m - 4) * 512:(m - 3) * 512], p.pb[bank][:, :]),
                    reads=[f"pb{bank}"], writes=[f"C:rr{m - 4}"])
        for g in range(2):
            ek, ok = f"C:rr{g}", f"C:rr{2 + g}"
            E = rraw[:, g * 512:(g + 1) * 512]
            O = rraw[:, (2 + g) * 512:(3 + g) * 512]
            p.op("vector", lambda e, E=E, t0=t0: e.tensor_mul(
                rsc1[:, :], E, c4[:, t0:t0 + 512]),
                reads=[ek, "C:cos"], writes=["C:rs1"])
            p.op("vector", lambda e, O=O, t0=t0: e.tensor_mul(
                rsc2[:, :], O, s4[:, t0:t0 + 512]),
                reads=[ok, "C:sin"], writes=["C:rs2"])
            p.op("vector", lambda e, g=g: e.tensor_sub(
                outsb[:, (4 + g) * 512:(5 + g) * 512], rsc1[:, :], rsc2[:, :]),
                reads=["C:rs1", "C:rs2"], writes=[f"C:out{4 + g}"])
            p.op("vector", lambda e, O=O, t0=t0: e.tensor_mul(
                rsc1[:, :], O, c4[:, t0:t0 + 512]),
                reads=[ok, "C:cos"], writes=["C:rs1"])
            p.op("vector", lambda e, E=E, t0=t0: e.tensor_mul(
                rsc2[:, :], E, s4[:, t0:t0 + 512]),
                reads=[ek, "C:sin"], writes=["C:rs2"])
            p.op("vector", lambda e, g=g: e.tensor_add(
                outsb[:, (6 + g) * 512:(7 + g) * 512], rsc1[:, :], rsc2[:, :]),
                reads=["C:rs1", "C:rs2"], writes=[f"C:out{6 + g}"])
        for m in range(8):
            p.op("sync", lambda e, m=m, t0=t0: e.dma_start(
                out=t["q0"][m * 128:(m + 1) * 128, t0:t0 + 512],
                in_=outsb[:, m * 512:(m + 1) * 512]),
                reads=[f"C:out{m}"], dma=True)


def em_D(p, t, C, ar, rw):
    csb = ar.bf16(128, 11 * S)
    wks = ar.bf16(128, 11 * 512)
    wvs = ar.bf16(128, 11 * 1024)
    outsb = ar.bf16(128, 4 * 512)
    vosb = ar.bf16(128, 2 * 512)
    for k in range(11):
        for pp in range(2):
            p.op("sync", lambda e, k=k, pp=pp: e.dma_start(
                out=csb[:, k * S + pp * 2048: k * S + pp * 2048 + 2048],
                in_=t["ckv_sh"][pp * 1408 + k * 128: pp * 1408 + (k + 1) * 128, :]),
                writes=[f"D:c{k}"], dma=True)
        p.op("sync", lambda e, k=k: e.dma_start(
            out=wks[:, k * 512:(k + 1) * 512],
            in_=t["wkv_up_k"][k * 128:(k + 1) * 128, :]),
            writes=[f"D:wk{k}"], dma=True)
        p.op("sync", lambda e, k=k: e.dma_start(
            out=wvs[:, k * 1024:(k + 1) * 1024],
            in_=t["wkv_up_v"][k * 128:(k + 1) * 128, :]),
            writes=[f"D:wv{k}"], dma=True)
    for tt in range(S // 512):
        t0 = tt * 512
        for m in range(4):
            bank = m % 4
            for k in range(11):
                p.op("tensor", lambda e, k=k, m=m, t0=t0, bank=bank: e.matmul(
                    p.pb[bank][:, :],
                    wks[:, k * 512 + m * 128: k * 512 + (m + 1) * 128],
                    csb[:, k * S + t0: k * S + t0 + 512],
                    start=(k == 0), stop=(k == 10)),
                    reads=[f"D:c{k}", f"D:wk{k}"],
                    writes=[f"pb{bank}"] if k in (0, 10) else [])
            p.op("vector", lambda e, m=m, bank=bank: e.tensor_copy(
                outsb[:, m * 512:(m + 1) * 512], p.pb[bank][:, :]),
                reads=[f"pb{bank}"], writes=[f"D:out{m}"])
            p.op("sync", lambda e, m=m, t0=t0: e.dma_start(
                out=t["knope"][m * 128:(m + 1) * 128, t0:t0 + 512],
                in_=outsb[:, m * 512:(m + 1) * 512]),
                reads=[f"D:out{m}"], dma=True)
    for ttile in range(S // 128):
        for v4 in range(2):
            bank = 4 + (2 * ttile + v4) % 4
            for k in range(11):
                p.op("tensor", lambda e, k=k, ttile=ttile, v4=v4, bank=bank: e.matmul(
                    p.pb[bank][:, :],
                    csb[:, k * S + ttile * 128: k * S + (ttile + 1) * 128],
                    wvs[:, k * 1024 + v4 * 512: k * 1024 + (v4 + 1) * 512],
                    start=(k == 0), stop=(k == 10)),
                    reads=[f"D:c{k}", f"D:wv{k}"],
                    writes=[f"pb{bank}"] if k in (0, 10) else [])
            p.op("vector", lambda e, v4=v4, bank=bank: e.tensor_copy(
                vosb[:, v4 * 512:(v4 + 1) * 512], p.pb[bank][:, :]),
                reads=[f"pb{bank}"], writes=[f"D:vo{v4}"])
            p.op("sync", lambda e, ttile=ttile, v4=v4: e.dma_start(
                out=t["vt"][ttile * 128:(ttile + 1) * 128, v4 * 512:(v4 + 1) * 512],
                in_=vosb[:, v4 * 512:(v4 + 1) * 512]),
                reads=[f"D:vo{v4}"], dma=True)


def _bound_c(p, q_key, q_sb, sq_sb, onc, srow, slot, pf):
    for c8 in range(8):
        p.op("scalar", lambda e, c8=c8: e.activation(
            sq_sb[:, 0:512], q_sb[:, c8 * 512:(c8 + 1) * 512], AF.Square),
            reads=[q_key], writes=[f"{pf}:bsq"])
        p.op("tensor", lambda e: e.matmul(
            p.pb[6][0:1, 0:512], onc[:, 0:1], sq_sb[:, 0:512],
            start=True, stop=True),
            reads=[f"{pf}:bsq", "onc"], writes=["pb6"])
        p.op("vector", lambda e, c8=c8, slot=slot: e.reduce_max(
            srow[0:1, 16 + slot * 8 + c8:17 + slot * 8 + c8],
            p.pb[6][0:1, 0:512], axis=AX),
            reads=["pb6"], writes=[f"{pf}:m8_{slot}_{c8}"])
    p.op("vector", lambda e, slot=slot: e.reduce_max(
        srow[0:1, slot:slot + 1],
        srow[0:1, 16 + slot * 8:16 + slot * 8 + 8], axis=AX),
        reads=[f"{pf}:m8_{slot}_{c8}" for c8 in range(8)], writes=[f"{pf}:mx{slot}"])


def _negc_from_bounds(p, C, srow, negc_col, pf, pr=0):
    p.op("vector", lambda e: e.tensor_mul(
        srow[0:1, 2:3], srow[0:1, 0:1], srow[0:1, 1:2]),
        reads=[f"{pf}:mx0", f"{pf}:mx1"], writes=[f"{pf}:prod"])
    p.op("scalar", lambda e: e.activation(
        srow[0:1, 3:4], srow[0:1, 2:3], AF.Ln),
        reads=[f"{pf}:prod"], writes=[f"{pf}:lnp"])
    p.op("scalar", lambda e: e.activation(
        srow[0:1, 4:5], srow[0:1, 3:4], AF.Exp, scale=0.5),
        reads=[f"{pf}:lnp"], writes=[f"{pf}:sqp"])
    p.op("vector", lambda e: e.tensor_scalar_mul(
        srow[0:1, 5:6], srow[0:1, 4:5], -SCALE),
        reads=[f"{pf}:sqp"], writes=[f"{pf}:negc_row"])
    p.op("tensor", lambda e: e.matmul(
        p.pb[6][:, 0:1], C["onr"][0:1, 0:128], srow[0:1, 5:6],
        start=True, stop=True),
        reads=[f"{pf}:negc_row", "onr"], writes=["pb6"])
    p.op("vector", lambda e: e.tensor_copy(negc_col[:, 0:1], p.pb[6][:, 0:1]),
         reads=["pb6"], writes=[f"{pf}:negc{pr}"])


def em_E(p, t, C, ar, rw):
    q_sb = ar.bf16(128, 2 * S)
    k_sb = ar.bf16(128, 2 * S)
    vt_sb = ar.bf16(128, 2 * S)
    msk = ar.f32(128, 2048)
    sq_sb = ar.bf16(128, 512)
    negc = ar.f32(128, 2)
    probs = ar.bf16(128, 2 * 512)
    rbc = ar.f32(128, 512)
    ctxo = ar.bf16(128, 512)
    srow = rw.f32(80)
    rrow = rw.f32(512)
    p.op("sync", lambda e: e.dma_start(out=msk[:, :], in_=t["maskE"]),
         writes=["E:msk"], dma=True)

    def loads(j):
        pr = j % 2
        o = pr * S
        nof, evf, odf = q_head_rows8(j)
        p.op("sync", lambda e, nof=nof, o=o: e.dma_start(
            out=q_sb[0:64, o:o + S], in_=t["q0"][nof:nof + 64, :]),
            writes=[f"E:q{pr}"], dma=True)
        p.op("sync", lambda e, evf=evf, o=o: e.dma_start(
            out=q_sb[64:96, o:o + S], in_=t["q0"][evf:evf + 32, :]),
            writes=[f"E:q{pr}"], dma=True)
        p.op("sync", lambda e, odf=odf, o=o: e.dma_start(
            out=q_sb[96:128, o:o + S], in_=t["q0"][odf:odf + 32, :]),
            writes=[f"E:q{pr}"], dma=True)
        p.op("sync", lambda e, j=j, o=o: e.dma_start(
            out=k_sb[0:64, o:o + S], in_=t["knope"][j * 64:(j + 1) * 64, :]),
            writes=[f"E:k{pr}"], dma=True)
        for pp in range(2):
            p.op("sync", lambda e, pp=pp, o=o: e.dma_start(
                out=k_sb[64:96, o + pp * 2048: o + (pp + 1) * 2048],
                in_=t["kr_sh"][pp * 64: pp * 64 + 32, :]),
                writes=[f"E:k{pr}"], dma=True)
            p.op("sync", lambda e, pp=pp, o=o: e.dma_start(
                out=k_sb[96:128, o + pp * 2048: o + (pp + 1) * 2048],
                in_=t["kr_sh"][pp * 64 + 32: pp * 64 + 64, :]),
                writes=[f"E:k{pr}"], dma=True)
        for g in range(8):
            p.op("sync", lambda e, j=j, g=g, o=o: e.dma_start(
                out=vt_sb[:, o + g * 512: o + (g + 1) * 512].rearrange(
                    "p (a d) -> p a d", d=128),
                in_=t["vt"][g * 512:(g + 1) * 512, j * 128:(j + 1) * 128].rearrange(
                    "(a p) d -> p a d", p=128)),
                writes=[f"E:v{pr}"], dma=True)

    loads(0)
    for j in range(8):
        pr = j % 2
        o = pr * S
        if j < 7:
            loads(j + 1)
        _bound_c(p, f"E:q{pr}", q_sb[:, o:o + S], sq_sb, C["onc"], srow, 0, "E")
        _bound_c(p, f"E:k{pr}", k_sb[:, o:o + S], sq_sb, C["onc"], srow, 1, "E")
        _negc_from_bounds(p, C, srow, negc[:, pr:pr + 1], "E", pr)

        def finkb(sl, kb, nkb, dn, pv):
            b0 = kb % 2
            jj = kb - 4 * sl
            if jj >= 0:
                p.op("vector", lambda e, jj=jj, b0=b0: e.tensor_add(
                    p.pb[b0][:, :], p.pb[b0][:, :], msk[:, jj * 512:(jj + 1) * 512]),
                    reads=[f"pb{b0}", "E:msk"], writes=[f"pb{b0}"])
            p.op("scalar", lambda e, b0=b0, pr=pr: e.activation(
                probs[:, b0 * 512:(b0 + 1) * 512], p.pb[b0][:, :], AF.Exp,
                bias=negc[:, pr:pr + 1], scale=SCALE),
                reads=[f"pb{b0}", f"E:negc{pr}"], writes=[f"E:probs{b0}"])
            p.op("tensor", lambda e, kb=kb, b0=b0, nkb=nkb, dn=dn: e.matmul(
                p.pb[dn][0:1, :], C["onc"][:, 0:1], probs[:, b0 * 512:(b0 + 1) * 512],
                start=(kb == 0), stop=(kb == nkb - 1)),
                reads=[f"E:probs{b0}", "onc"],
                writes=[f"pb{dn}"] if kb in (0, nkb - 1) else [])
            p.op("tensor", lambda e, kb=kb, b0=b0, nkb=nkb, pv=pv, o=o: e.matmul(
                p.pb[pv][:, :], vt_sb[:, o + kb * 128: o + (kb + 1) * 128],
                probs[:, b0 * 512:(b0 + 1) * 512],
                start=(kb == 0), stop=(kb == nkb - 1)),
                reads=[f"E:probs{b0}", f"E:v{pr}"],
                writes=[f"pb{pv}"] if kb in (0, nkb - 1) else [])

        def slepi(sl, dn, pv):
            p.op("vector", lambda e, dn=dn: e.reciprocal(rrow[0:1, :], p.pb[dn][0:1, :]),
                 reads=[f"pb{dn}"], writes=["E:rrow"])
            p.op("tensor", lambda e, dn=dn: e.matmul(
                p.pb[dn][:, :], C["onr"][0:1, 0:128], rrow[0:1, :],
                start=True, stop=True),
                reads=["E:rrow", "onr"], writes=[f"pb{dn}"])
            p.op("vector", lambda e, dn=dn: e.tensor_copy(rbc[:, :], p.pb[dn][:, :]),
                 reads=[f"pb{dn}"], writes=["E:rbc"])
            p.op("vector", lambda e, pv=pv: e.tensor_mul(ctxo[:, :], p.pb[pv][:, :], rbc[:, :]),
                 reads=[f"pb{pv}", "E:rbc"], writes=["E:ctxo"])
            p.op("gpsimd", lambda e, j=j, sl=sl: e.indirect_dma_start(
                t["ctx0_sh"][:, :],
                IndirectOffsetOnAxis(ap=C["idx_h"][:, j:j + 1], axis=0),
                ctxo[:, :], None, element_offset=sl * 512),
                reads=["E:ctxo", "idx"], dma=True)

        prev = None
        for sl in range(8):
            nkb = 4 * sl + 4
            dn = 4 + (sl % 2) * 2
            pv = 5 + (sl % 2) * 2
            for kb in range(nkb):
                b0 = kb % 2
                p.op("tensor", lambda e, sl=sl, kb=kb, b0=b0, o=o: e.matmul(
                    p.pb[b0][:, :],
                    k_sb[:, o + kb * 128: o + (kb + 1) * 128],
                    q_sb[:, o + sl * 512: o + (sl + 1) * 512],
                    start=True, stop=True),
                    reads=[f"E:k{pr}", f"E:q{pr}"], writes=[f"pb{b0}"])
                if kb == 0 and prev is not None:
                    slepi(*prev)
                if kb > 0:
                    finkb(sl, kb - 1, nkb, dn, pv)
            finkb(sl, nkb - 1, nkb, dn, pv)
            prev = (sl, dn, pv)
        slepi(*prev)


def _ln_apply(p, C, pf, u1, nchunk, dim, outfn, srow, t1, outs):
    A_ = srow[0:1, 0:512]
    Bm = srow[0:1, 512:1024]
    Cc = srow[0:1, 1024:1536]
    p.op("vector", lambda e: e.tensor_scalar(A_, p.pb[6][0:1, :], 1.0 / dim, None, ALU.mult),
         reads=["pb6"], writes=[f"{pf}:lnA"])
    p.op("vector", lambda e: e.tensor_mul(Bm, A_, A_), reads=[f"{pf}:lnA"],
         writes=[f"{pf}:lnB"])
    p.op("vector", lambda e: e.tensor_scalar(Cc, p.pb[7][0:1, :], 1.0 / dim, EPS,
                                             ALU.mult, ALU.add),
         reads=["pb7"], writes=[f"{pf}:lnC"])
    p.op("vector", lambda e: e.tensor_sub(Cc, Cc, Bm),
         reads=[f"{pf}:lnB", f"{pf}:lnC"], writes=[f"{pf}:lnC"])
    p.op("scalar", lambda e: e.activation(Cc, Cc, AF.Ln),
         reads=[f"{pf}:lnC"], writes=[f"{pf}:lnC"])
    p.op("scalar", lambda e: e.activation(Cc, Cc, AF.Exp, scale=-0.5),
         reads=[f"{pf}:lnC"], writes=[f"{pf}:lnC"])
    p.op("vector", lambda e: e.tensor_mul(Bm, A_, Cc),
         reads=[f"{pf}:lnA", f"{pf}:lnC"], writes=[f"{pf}:lnB"])
    p.op("vector", lambda e: e.tensor_scalar_mul(Bm, Bm, -1.0),
         reads=[f"{pf}:lnB"], writes=[f"{pf}:lnB"])
    p.op("tensor", lambda e: e.matmul(
        p.pb[4][:, :], C["onr"][0:1, 0:128], Cc, start=True, stop=True),
        reads=[f"{pf}:lnC", "onr"], writes=["pb4"])
    p.op("tensor", lambda e: e.matmul(
        p.pb[5][:, :], C["onr"][0:1, 0:128], Bm, start=True, stop=True),
        reads=[f"{pf}:lnB", "onr"], writes=["pb5"])
    for m in range(nchunk):
        p.op("vector", lambda e, m=m: e.tensor_mul(
            t1[:, :], u1[:, m * 512:(m + 1) * 512], p.pb[4][:, :]),
            reads=[f"{pf}:u1_{m}", "pb4"], writes=[f"{pf}:t1"])
        p.op("vector", lambda e: e.tensor_add(outs[:, :], t1[:, :], p.pb[5][:, :]),
             reads=[f"{pf}:t1", "pb5"], writes=[f"{pf}:outs"])
        outfn(m, outs)


def em_F(p, t, C, ar, rw):
    ctx0v = t["ctx0_sh"].rearrange("h (p w) -> (h p) w", p=2)
    csb = ar.bf16(128, 16 * TC)
    wsb = ar.bf16(128, 16 * D)
    xc = ar.f32(128, 2 * 512)
    u1 = ar.f32(128, 16 * 512)
    sq = ar.bf16(128, 512)
    t1 = ar.f32(128, 512)
    outs = ar.f32(128, 512)
    srow = rw.f32(3 * 512)
    for k in range(16):
        p.op("gpsimd", lambda e, k=k: e.indirect_dma_start(
            csb[:, k * TC:(k + 1) * TC], None,
            ctx0v[:, :], IndirectOffsetOnAxis(ap=C["idx_c"][:, k:k + 1], axis=0)),
            writes=[f"F:c{k}"], reads=["idx"], dma=True)
        p.op("sync", lambda e, k=k: e.dma_start(
            out=wsb[:, k * D:(k + 1) * D], in_=t["wo"][k * 128:(k + 1) * 128, :]),
            writes=[f"F:w{k}"], dma=True)
    def epiF(m):
        bank = m % 4
        xb = m % 2
        p.op("vector", lambda e, m=m, bank=bank, xb=xb: e.tensor_add(
            u1[:, m * 512:(m + 1) * 512], p.pb[bank][:, :],
            xc[:, xb * 512:(xb + 1) * 512]),
            reads=[f"pb{bank}", f"F:xc{xb}"], writes=[f"F:u1_{m}"])
        p.op("tensor", lambda e, m=m: e.matmul(
            p.pb[6][0:1, :], C["oncf"][:, 0:1], u1[:, m * 512:(m + 1) * 512],
            start=(m == 0), stop=(m == 15)),
            reads=[f"F:u1_{m}", "oncf"],
            writes=["pb6"] if m in (0, 15) else [])
        p.op("scalar", lambda e, m=m: e.activation(
            sq[:, :], u1[:, m * 512:(m + 1) * 512], AF.Square),
            reads=[f"F:u1_{m}"], writes=["F:sq"])
        p.op("tensor", lambda e, m=m: e.matmul(
            p.pb[7][0:1, :], C["onc"][:, 0:1], sq[:, :],
            start=(m == 0), stop=(m == 15)),
            reads=["F:sq", "onc"],
            writes=["pb7"] if m in (0, 15) else [])

    for tt in range(TC // 512):
        t0 = tt * 512
        for m in range(16):
            bank = m % 4
            xb = m % 2
            p.op("sync", lambda e, m=m, t0=t0, xb=xb: e.dma_start(
                out=xc[:, xb * 512:(xb + 1) * 512],
                in_=t["x_f"][m * 128:(m + 1) * 128, t0:t0 + 512]),
                writes=[f"F:xc{xb}"], dma=True)
            for k in range(16):
                p.op("tensor", lambda e, k=k, m=m, t0=t0, bank=bank: e.matmul(
                    p.pb[bank][:, :],
                    wsb[:, k * D + m * 128: k * D + (m + 1) * 128],
                    csb[:, k * TC + t0: k * TC + t0 + 512],
                    start=(k == 0), stop=(k == 15)),
                    reads=[f"F:c{k}", f"F:w{k}"],
                    writes=[f"pb{bank}"] if k in (0, 15) else [])
            if m > 0:
                epiF(m - 1)
        epiF(15)

        def outfn(m, o, t0=t0):
            p.op("sync", lambda e, m=m, t0=t0, o=o: e.dma_start(
                out=t["x1"][m * 128:(m + 1) * 128, t0:t0 + 512], in_=o[:, :]),
                reads=["F:outs"], dma=True)
        _ln_apply(p, C, "F", u1, 16, D, outfn, srow, t1, outs)


def em_G(p, t, C, ar, rw, L, xin, xout, scat):
    pf = f"G{L}"
    TH = TC // 2
    xb_sb = ar.bf16(128, 16 * TH)
    gw = ar.bf16(128, 2 * 2048)
    uw = ar.bf16(128, 2 * 2048)
    dw = ar.bf16(128, 2 * 5632)
    gssb = ar.f32(128, 44)
    ussb = ar.f32(128, 44)
    dssb = ar.f32(128, 16)
    g_sb = ar.f32(128, 1024)
    u_sb = ar.f32(128, 1024)
    hdn = ar.bf16(128, 44 * 1024)
    xc = ar.f32(128, 4 * 512)
    u2 = ar.f32(128, 2 * 512)
    sq = ar.bf16(128, 512)
    t1 = ar.f32(128, 512)
    outs = ar.f32(128, 512)
    outs_bf = ar.bf16(128, 512)
    srow = rw.f32(6 * 512)
    p.op("sync", lambda e: e.dma_start(out=gssb[:, :], in_=t[f"{L}_gs"]),
         writes=[f"{pf}:gs"], dma=True)
    p.op("sync", lambda e: e.dma_start(out=ussb[:, :], in_=t[f"{L}_us"]),
         writes=[f"{pf}:us"], dma=True)
    p.op("sync", lambda e: e.dma_start(out=dssb[:, :], in_=t[f"{L}_ds"]),
         writes=[f"{pf}:ds"], dma=True)
    for half in range(2):
        h0 = half * TH
        for k in range(16):
            p.op("gpsimd", lambda e, k=k, h0=h0: e.dma_start(
                out=xb_sb[:, k * TH:(k + 1) * TH],
                in_=xin[k * 128:(k + 1) * 128, h0:h0 + TH]),
                writes=[f"{pf}:xb{k}"], dma=True)
        for i in range(44):
            wsl = i % 2
            p.op("scalar", lambda e, i=i, wsl=wsl: e.dma_start(
                out=gw[:, wsl * 2048:(wsl + 1) * 2048], in_=t[f"{L}_gws"][i]),
                writes=[f"{pf}:gw{wsl}"], dma=True)
            p.op("gpsimd", lambda e, i=i, wsl=wsl: e.dma_start(
                out=uw[:, wsl * 2048:(wsl + 1) * 2048], in_=t[f"{L}_uws"][i]),
                writes=[f"{pf}:uw{wsl}"], dma=True)
            for k in range(16):
                for t2 in range(2):
                    p.op("tensor", lambda e, k=k, t2=t2, wsl=wsl: e.matmul(
                        p.pb[t2][:, :],
                        gw[:, wsl * 2048 + k * 128: wsl * 2048 + (k + 1) * 128],
                        xb_sb[:, k * TH + t2 * 512: k * TH + t2 * 512 + 512],
                        start=(k == 0), stop=(k == 15)),
                        reads=[f"{pf}:gw{wsl}", f"{pf}:xb{k}"],
                        writes=[f"pb{t2}"] if k in (0, 15) else [])
            for k in range(16):
                for t2 in range(2):
                    p.op("tensor", lambda e, k=k, t2=t2, wsl=wsl: e.matmul(
                        p.pb[2 + t2][:, :],
                        uw[:, wsl * 2048 + k * 128: wsl * 2048 + (k + 1) * 128],
                        xb_sb[:, k * TH + t2 * 512: k * TH + t2 * 512 + 512],
                        start=(k == 0), stop=(k == 15)),
                        reads=[f"{pf}:uw{wsl}", f"{pf}:xb{k}"],
                        writes=[f"pb{2 + t2}"] if k in (0, 15) else [])
            for t2 in range(2):
                p.op("scalar", lambda e, i=i, t2=t2: e.activation(
                    g_sb[:, t2 * 512:(t2 + 1) * 512], p.pb[t2][:, :], AF.Silu,
                    scale=gssb[:, i:i + 1]),
                    reads=[f"pb{t2}", f"{pf}:gs"], writes=[f"{pf}:g{t2}"])
                p.op("vector", lambda e, i=i, t2=t2: e.tensor_scalar_mul(
                    u_sb[:, t2 * 512:(t2 + 1) * 512], p.pb[2 + t2][:, :],
                    ussb[:, i:i + 1]),
                    reads=[f"pb{2 + t2}", f"{pf}:us"], writes=[f"{pf}:u{t2}"])
            p.op("vector", lambda e, i=i: e.tensor_mul(
                hdn[:, i * 1024:(i + 1) * 1024], g_sb[:, :], u_sb[:, :]),
                reads=[f"{pf}:g0", f"{pf}:g1", f"{pf}:u0", f"{pf}:u1"],
                writes=[f"{pf}:hdn{i}"])
        def epiGd(f, h0):
            fx = (f % 2) * 2
            for t2 in range(2):
                bk = fx + t2
                p.op("vector", lambda e, f=f, t2=t2, fx=fx, bk=bk: e.scalar_tensor_tensor(
                    u2[:, t2 * 512:(t2 + 1) * 512], p.pb[bk][:, :],
                    dssb[:, f:f + 1], xc[:, (fx + t2) * 512:(fx + t2 + 1) * 512],
                    ALU.mult, ALU.add),
                    reads=[f"pb{bk}", f"{pf}:ds", f"{pf}:xc{fx + t2}"],
                    writes=[f"{pf}:u2_{t2}"])
                p.op("gpsimd", lambda e, f=f, t2=t2, h0=h0: e.dma_start(
                    out=t["u2d"][f * 128:(f + 1) * 128, h0 + t2 * 512:h0 + t2 * 512 + 512],
                    in_=u2[:, t2 * 512:(t2 + 1) * 512]),
                    reads=[f"{pf}:u2_{t2}"], dma=True)
                p.op("tensor", lambda e, f=f, t2=t2: e.matmul(
                    p.pb[4 + t2][0:1, :], C["oncf"][:, 0:1], u2[:, t2 * 512:(t2 + 1) * 512],
                    start=(f == 0), stop=(f == 15)),
                    reads=[f"{pf}:u2_{t2}", "oncf"],
                    writes=[f"pb{4 + t2}"] if f in (0, 15) else [])
                p.op("scalar", lambda e, t2=t2: e.activation(
                    sq[:, :], u2[:, t2 * 512:(t2 + 1) * 512], AF.Square),
                    reads=[f"{pf}:u2_{t2}"], writes=[f"{pf}:sq"])
                p.op("tensor", lambda e, f=f, t2=t2: e.matmul(
                    p.pb[6 + t2][0:1, :], C["onc"][:, 0:1], sq[:, :],
                    start=(f == 0), stop=(f == 15)),
                    reads=[f"{pf}:sq", "onc"],
                    writes=[f"pb{6 + t2}"] if f in (0, 15) else [])

        for f in range(16):
            wsl = f % 2
            fx = (f % 2) * 2
            p.op("sync", lambda e, f=f, wsl=wsl: e.dma_start(
                out=dw[:, wsl * 5632:(wsl + 1) * 5632], in_=t[f"{L}_dws"][f]),
                writes=[f"{pf}:dw{wsl}"], dma=True)
            for t2 in range(2):
                p.op("sync", lambda e, f=f, t2=t2, h0=h0, fx=fx: e.dma_start(
                    out=xc[:, (fx + t2) * 512:(fx + t2 + 1) * 512],
                    in_=xin[f * 128:(f + 1) * 128, h0 + t2 * 512:h0 + t2 * 512 + 512]),
                    writes=[f"{pf}:xc{fx + t2}"], dma=True)
            for ik in range(44):
                for t2 in range(2):
                    bk = fx + t2
                    p.op("tensor", lambda e, ik=ik, t2=t2, wsl=wsl, bk=bk: e.matmul(
                        p.pb[bk][:, :],
                        dw[:, wsl * 5632 + ik * 128: wsl * 5632 + (ik + 1) * 128],
                        hdn[:, ik * 1024 + t2 * 512: ik * 1024 + t2 * 512 + 512],
                        start=(ik == 0), stop=(ik == 43)),
                        reads=[f"{pf}:dw{wsl}", f"{pf}:hdn{ik}"],
                        writes=[f"pb{bk}"] if ik in (0, 43) else [])
            if f > 0:
                epiGd(f - 1, h0)
        epiGd(15, h0)
        for t2 in range(2):
            base = 3 * t2
            A_ = srow[0:1, base * 512:(base + 1) * 512]
            Bm = srow[0:1, (base + 1) * 512:(base + 2) * 512]
            Cc = srow[0:1, (base + 2) * 512:(base + 3) * 512]
            p.op("vector", lambda e, A_=A_, t2=t2: e.tensor_scalar(
                A_, p.pb[4 + t2][0:1, :], 1.0 / D, None, ALU.mult),
                reads=[f"pb{4 + t2}"], writes=[f"{pf}:lnA{t2}"])
            p.op("vector", lambda e, A_=A_, Bm=Bm: e.tensor_mul(Bm, A_, A_),
                 reads=[f"{pf}:lnA{t2}"], writes=[f"{pf}:lnB{t2}"])
            p.op("vector", lambda e, Cc=Cc, t2=t2: e.tensor_scalar(
                Cc, p.pb[6 + t2][0:1, :], 1.0 / D, EPS, ALU.mult, ALU.add),
                reads=[f"pb{6 + t2}"], writes=[f"{pf}:lnC{t2}"])
            p.op("vector", lambda e, Cc=Cc, Bm=Bm: e.tensor_sub(Cc, Cc, Bm),
                 reads=[f"{pf}:lnB{t2}", f"{pf}:lnC{t2}"], writes=[f"{pf}:lnC{t2}"])
            p.op("scalar", lambda e, Cc=Cc: e.activation(Cc, Cc, AF.Ln),
                 reads=[f"{pf}:lnC{t2}"], writes=[f"{pf}:lnC{t2}"])
            p.op("scalar", lambda e, Cc=Cc: e.activation(Cc, Cc, AF.Exp, scale=-0.5),
                 reads=[f"{pf}:lnC{t2}"], writes=[f"{pf}:lnC{t2}"])
            p.op("vector", lambda e, A_=A_, Bm=Bm, Cc=Cc: e.tensor_mul(Bm, A_, Cc),
                 reads=[f"{pf}:lnA{t2}", f"{pf}:lnC{t2}"], writes=[f"{pf}:lnB{t2}"])
            p.op("vector", lambda e, Bm=Bm: e.tensor_scalar_mul(Bm, Bm, -1.0),
                 reads=[f"{pf}:lnB{t2}"], writes=[f"{pf}:lnB{t2}"])
            p.op("tensor", lambda e, Cc=Cc, t2=t2: e.matmul(
                p.pb[4 + t2][:, :], C["onr"][0:1, 0:128], Cc, start=True, stop=True),
                reads=[f"{pf}:lnC{t2}", "onr"], writes=[f"pb{4 + t2}"])
            p.op("tensor", lambda e, Bm=Bm, t2=t2: e.matmul(
                p.pb[6 + t2][:, :], C["onr"][0:1, 0:128], Bm, start=True, stop=True),
                reads=[f"{pf}:lnB{t2}", "onr"], writes=[f"pb{6 + t2}"])
            for f in range(16):
                p.op("gpsimd", lambda e, f=f, t2=t2, h0=h0: e.dma_start(
                    out=u2[:, t2 * 512:(t2 + 1) * 512],
                    in_=t["u2d"][f * 128:(f + 1) * 128, h0 + t2 * 512:h0 + t2 * 512 + 512]),
                    writes=[f"{pf}:u2_{t2}"], dma=True)
                p.op("vector", lambda e, t2=t2: e.tensor_mul(
                    t1[:, :], u2[:, t2 * 512:(t2 + 1) * 512], p.pb[4 + t2][:, :]),
                    reads=[f"{pf}:u2_{t2}", f"pb{4 + t2}"], writes=[f"{pf}:t1"])
                p.op("vector", lambda e, t2=t2: e.tensor_add(
                    outs[:, :], t1[:, :], p.pb[6 + t2][:, :]),
                    reads=[f"{pf}:t1", f"pb{6 + t2}"], writes=[f"{pf}:outs"])
                p.op("sync", lambda e, f=f, t2=t2, h0=h0: e.dma_start(
                    out=xout[f * 128:(f + 1) * 128, h0 + t2 * 512:h0 + t2 * 512 + 512],
                    in_=outs[:, :]),
                    reads=[f"{pf}:outs"], dma=True)
                if scat:
                    p.op("vector", lambda e: e.tensor_copy(outs_bf[:, :], outs[:, :]),
                         reads=[f"{pf}:outs"], writes=[f"{pf}:obf"])
                    p.op("gpsimd", lambda e, f=f, t2=t2, h0=h0: e.indirect_dma_start(
                        t["x2_sh"][:, :],
                        IndirectOffsetOnAxis(ap=C["idx_t"][:, f:f + 1], axis=0),
                        outs_bf[:, :], None, element_offset=h0 + t2 * 512),
                        reads=[f"{pf}:obf", "idx"], dma=True)


def em_H(p, t, C, ar, rw):
    xsb = ar.bf16(128, 16 * S)
    wsb = ar.bf16(128, 16 * 1024)
    outsb = ar.bf16(128, 4 * 512)
    bpp = ar.f32(128, 16)
    bsb = rw.f32(3 * 1024)
    on5 = rw.f32(512)
    p.op("sync", lambda e: e.dma_start(out=on5[0:1, :], in_=t["ones512"]),
         writes=["H:on5"], dma=True)
    for k in range(16):
        for pp in range(2):
            p.op("sync", lambda e, k=k, pp=pp: e.dma_start(
                out=xsb[:, k * S + pp * 2048: k * S + pp * 2048 + 2048],
                in_=t["x2_sh"][pp * 2048 + k * 128: pp * 2048 + (k + 1) * 128, :]),
                writes=[f"H:x{k}"], dma=True)
    for nm, boff in (("bq1", 0), ("bk1", 1024), ("bv1", 2048)):
        p.op("sync", lambda e, nm=nm, boff=boff: e.dma_start(
            out=bsb[0:1, boff:boff + 1024], in_=t[nm]),
            writes=[f"H:{nm}"], dma=True)
    p.op("sync", lambda e: e.dma_start(out=bpp[:, 0:8], in_=t["bqpp"]),
         writes=["H:bpp"], dma=True)
    p.op("sync", lambda e: e.dma_start(out=bpp[:, 8:16], in_=t["bkpp"]),
         writes=["H:bpp"], dma=True)
    for wnm, bo, outd in (("wq1", 0, "q1"), ("wk1", 8, "k1")):
        for k in range(16):
            p.op("sync", lambda e, k=k, wnm=wnm: e.dma_start(
                out=wsb[:, k * 1024:(k + 1) * 1024],
                in_=t[wnm][k * 128:(k + 1) * 128, :]),
                writes=[f"H:w{k}"], dma=True)
        for tt in range(S // 512):
            t0 = tt * 512
            for m in range(8):
                bank = m % 4
                for k in range(16):
                    p.op("tensor", lambda e, k=k, m=m, t0=t0, bank=bank: e.matmul(
                        p.pb[bank][:, :],
                        wsb[:, k * 1024 + m * 128: k * 1024 + (m + 1) * 128],
                        xsb[:, k * S + t0: k * S + t0 + 512],
                        start=(k == 0), stop=(k == 15)),
                        reads=[f"H:x{k}", f"H:w{k}"],
                        writes=[f"pb{bank}"] if k in (0, 15) else [])
                p.op("vector", lambda e, m=m, bank=bank, bo=bo: e.tensor_scalar(
                    outsb[:, bank * 512:(bank + 1) * 512], p.pb[bank][:, :],
                    bpp[:, bo + m:bo + m + 1], None, ALU.add),
                    reads=[f"pb{bank}", "H:bpp"], writes=[f"H:o{bank}"])
                p.op("sync", lambda e, m=m, t0=t0, bank=bank, outd=outd: e.dma_start(
                    out=t[outd][m * 128:(m + 1) * 128, t0:t0 + 512],
                    in_=outsb[:, bank * 512:(bank + 1) * 512]),
                    reads=[f"H:o{bank}"], dma=True)
    for wnm, brow, outd in (("wv1", 2048, "v1t"), ("wq1", 0, "q1t")):
        for k in range(16):
            p.op("sync", lambda e, k=k, wnm=wnm: e.dma_start(
                out=wsb[:, k * 1024:(k + 1) * 1024],
                in_=t[wnm][k * 128:(k + 1) * 128, :]),
                writes=[f"H:w{k}"], dma=True)
        for ttile in range(S // 128):
            for v4 in range(2):
                bank = 4 + (2 * ttile + v4) % 4
                p.op("tensor", lambda e, v4=v4, bank=bank, brow=brow: e.matmul(
                    p.pb[bank][:, :], C["onr"][0:1, 0:128],
                    bsb[0:1, brow + v4 * 512: brow + (v4 + 1) * 512],
                    start=True, stop=False),
                    reads=["H:bv1", "H:bq1", "onr"], writes=[f"pb{bank}"])
                for k in range(16):
                    p.op("tensor", lambda e, k=k, ttile=ttile, v4=v4, bank=bank: e.matmul(
                        p.pb[bank][:, :],
                        xsb[:, k * S + ttile * 128: k * S + (ttile + 1) * 128],
                        wsb[:, k * 1024 + v4 * 512: k * 1024 + (v4 + 1) * 512],
                        start=False, stop=(k == 15)),
                        reads=[f"H:x{k}", f"H:w{k}"],
                        writes=[f"pb{bank}"] if k == 15 else [])
                p.op("vector", lambda e, v4=v4, bank=bank: e.tensor_copy(
                    outsb[:, v4 * 512:(v4 + 1) * 512], p.pb[bank][:, :]),
                    reads=[f"pb{bank}"], writes=[f"H:o{v4}"])
                p.op("sync", lambda e, ttile=ttile, v4=v4, outd=outd: e.dma_start(
                    out=t[outd][ttile * 128:(ttile + 1) * 128, v4 * 512:(v4 + 1) * 512],
                    in_=outsb[:, v4 * 512:(v4 + 1) * 512]),
                    reads=[f"H:o{v4}"], dma=True)


def em_I(p, t, C, ar, rw):
    q_sb = ar.bf16(128, S)
    k_sb = ar.bf16(128, S)
    vt_sb = ar.bf16(128, S)
    qt_all = ar.bf16(128, S)
    msk = ar.f32(128, 384)
    iorow = ar.f32(128, 64)
    iopp = ar.f32(64, 1)
    ident = ar.bf16(128, 128)
    idf8 = ar.f32(8, 8)
    sq_sb = ar.bf16(128, 512)
    negc = ar.f32(128, 1)
    na = ar.f32(8, S)
    nb = ar.f32(8, S)
    mx8 = ar.f32(8, 8)
    rkall = ar.f32(128, 256)
    probs = ar.bf16(128, 384)
    lctx = ar.bf16(128, S)
    P_sb = ar.bf16(128, 128)
    qgT_sb = ar.bf16(64, 128)
    qg_sb = ar.bf16(128, 64)
    gp_sb = ar.bf16(128, 64)
    rg_pp = ar.f32(64, 1)
    gctx_sb = ar.bf16(64, 128)
    cb_sb = ar.f32(128, 512)
    pi_sb = ar.bf16(64, 512)
    t2_sb = ar.f32(128, 512)
    ctxo = ar.bf16(128, 512)
    nrow = ar.f32(1, S)
    rloc = ar.f32(1, S)
    srow = rw.f32(64)
    rgrow = rw.f32(64)
    rr_sb = rw.f32(512)
    mr_sb = rw.f32(512)
    cbrow = rw.f32(512)
    pbb6 = p.pb[6].bitcast(BF16)
    for (tl, dsrc, key) in [(msk, "maskL", "I:msk"), (iorow, "io_row", "I:iorow"),
                            (iopp, "io_pp", "I:iopp"), (ident, "identd", "I:ident"),
                            (idf8, "ident8f", "I:id8")]:
        p.op("sync", lambda e, tl=tl, dsrc=dsrc: e.dma_start(out=tl[:, :], in_=t[dsrc]),
             writes=[key], dma=True)
    for j in range(8):
        p.op("sync", lambda e, j=j: e.dma_start(
            out=q_sb[:, :], in_=t["q1"][j * 128:(j + 1) * 128, :]),
            writes=["I:q"], dma=True)
        for c8 in range(8):
            p.op("scalar", lambda e, c8=c8: e.activation(
                sq_sb[:, :], q_sb[:, c8 * 512:(c8 + 1) * 512], AF.Square),
                reads=["I:q"], writes=["I:sq"])
            p.op("tensor", lambda e: e.matmul(
                p.pb[0][0:1, :], C["onc"][:, 0:1], sq_sb[:, :], start=True, stop=True),
                reads=["I:sq", "onc"], writes=["pb0"])
            p.op("vector", lambda e, c8=c8: e.tensor_copy(
                nrow[0:1, c8 * 512:(c8 + 1) * 512], p.pb[0][0:1, :]),
                reads=["pb0"], writes=["I:nrow"])
        p.op("sync", lambda e, j=j: e.dma_start(
            out=t["rankm"][j:j + 1, :], in_=nrow[0:1, :]),
            reads=["I:nrow"], writes=["I:na_d"], dma=True)
    p.op("sync", lambda e: e.dma_start(out=na[:, :], in_=t["rankm"]),
         reads=["I:na_d"], writes=["I:na"], dma=True)
    bufs = [na, nb]
    for rnd in range(8):
        cur, nxt = bufs[rnd % 2], bufs[(rnd + 1) % 2]
        ck, nk = ("I:na" if rnd % 2 == 0 else "I:nb"), ("I:nb" if rnd % 2 == 0 else "I:na")
        p.op("vector", lambda e, cur=cur: e.max(out=mx8[:, :], in_=cur[:, :]),
             reads=[ck], writes=["I:mx8"])
        if rnd == 7:
            p.op("vector", lambda e: e.memset(mx8[:, 6:8], -1.0),
                 reads=[], writes=["I:mx8"])
        p.op("vector", lambda e, cur=cur, nxt=nxt: e.match_replace(
            out=nxt[:, :], in_to_replace=mx8[:, :], in_values=cur[:, :],
            imm_value=-1.0),
            reads=[ck, "I:mx8"], writes=[nk])
    p.op("vector", lambda e: e.tensor_scalar(
        nb[:, :], na[:, :], -0.5, None, ALU.is_le),
        reads=["I:na"], writes=["I:nb"])
    p.op("vector", lambda e: e.memset(nb[:, 0:1], 1.0), reads=[], writes=["I:nb"])
    p.op("vector", lambda e: e.memset(nb[:, S - 1:S], 1.0), reads=[], writes=["I:nb"])
    p.op("vector", lambda e: e.tensor_tensor_scan(
        na[:, :], nb[:, :], nb[:, :], 0.0, ALU.add, ALU.bypass),
        reads=["I:nb"], writes=["I:na"])
    p.op("vector", lambda e: e.tensor_sub(na[:, :], na[:, :], nb[:, :]),
         reads=["I:na", "I:nb"], writes=["I:na"])
    p.op("sync", lambda e: e.dma_start(out=t["maskm"], in_=nb[:, :]),
         reads=["I:nb"], writes=["I:maskmD"], dma=True)
    p.op("vector", lambda e: e.tensor_scalar(
        nb[:, :], nb[:, :], -1e9, 1e9, ALU.mult, ALU.add),
        reads=["I:nb"], writes=["I:nb"])
    p.op("vector", lambda e: e.tensor_add(na[:, :], na[:, :], nb[:, :]),
         reads=["I:na", "I:nb"], writes=["I:na"])
    p.op("sync", lambda e: e.dma_start(out=t["rankm"], in_=na[:, :]),
         reads=["I:na"], writes=["I:rankmD"], dma=True)
    for c in range(32):
        p.op("tensor", lambda e, c=c: e.transpose(
            p.pb[6][:, 0:8], na[0:8, c * 128:(c + 1) * 128], idf8[0:8, 0:8]),
            reads=["I:na", "I:id8"], writes=["pb6"])
        p.op("vector", lambda e, c=c: e.tensor_copy(
            rkall[:, c * 8:(c + 1) * 8], p.pb[6][:, 0:8]),
            reads=["pb6"], writes=["I:rkall"])
    NB = 32
    for j in range(8):
        p.op("sync", lambda e, j=j: e.dma_start(
            out=q_sb[:, :], in_=t["q1"][j * 128:(j + 1) * 128, :]),
            writes=["I:q"], dma=True)
        p.op("sync", lambda e, j=j: e.dma_start(
            out=k_sb[:, :], in_=t["k1"][j * 128:(j + 1) * 128, :]),
            writes=["I:k"], dma=True)
        for g in range(8):
            p.op("sync", lambda e, j=j, g=g: e.dma_start(
                out=vt_sb[:, g * 512:(g + 1) * 512].rearrange(
                    "p (a d) -> p a d", d=128),
                in_=t["v1t"][g * 512:(g + 1) * 512, j * 128:(j + 1) * 128].rearrange(
                    "(a p) d -> p a d", p=128)),
                writes=["I:v"], dma=True)
            p.op("sync", lambda e, j=j, g=g: e.dma_start(
                out=qt_all[:, g * 512:(g + 1) * 512].rearrange(
                    "p (a d) -> p a d", d=128),
                in_=t["q1t"][g * 512:(g + 1) * 512, j * 128:(j + 1) * 128].rearrange(
                    "(a p) d -> p a d", p=128)),
                writes=["I:qt"], dma=True)
        _bound_c(p, "I:q", q_sb, sq_sb, C["onc"], srow, 0, "I")
        _bound_c(p, "I:k", k_sb, sq_sb, C["onc"], srow, 1, "I")
        _negc_from_bounds(p, C, srow, negc[:, 0:1], "I")

        def qb0(tb):
            return 0 if tb % 2 == 0 else 7

        def epi_local(tb):
            bqk = qb0(tb)
            qlo = max(0, (tb - 1) * 128)
            qhi = min(S, (tb + 2) * 128)
            Nq = qhi - qlo
            mlo = 128 if tb == 0 else 0
            p.op("vector", lambda e, Nq=Nq, mlo=mlo, bqk=bqk: e.tensor_add(
                p.pb[bqk][:, 0:Nq], p.pb[bqk][:, 0:Nq], msk[:, mlo:mlo + Nq]),
                reads=[f"pb{bqk}", "I:msk"], writes=[f"pb{bqk}"])
            p.op("scalar", lambda e, Nq=Nq, bqk=bqk: e.activation(
                probs[:, 0:Nq], p.pb[bqk][:, 0:Nq], AF.Exp,
                bias=negc[:, 0:1], scale=SCALE),
                reads=[f"pb{bqk}", "I:negc0"], writes=["I:probs"])
            for cg in (-1, 0, 1):
                qb = tb + cg
                if qb < 0 or qb >= NB:
                    continue
                col = qb * 128 - qlo
                first = (tb == max(0, qb - 1))
                last = (tb == min(NB - 1, qb + 1))
                cbank = 2 if qb % 3 == 0 else (3 if qb % 3 == 1 else 6)
                dbank = 1 if qb % 3 == 0 else (4 if qb % 3 == 1 else 5)
                p.op("tensor", lambda e, col=col, dbank=dbank, first=first,
                     last=last: e.matmul(
                    p.pb[dbank][0:1, 0:128], C["onc"][:, 0:1], probs[:, col:col + 128],
                    start=first, stop=last),
                    reads=["I:probs", "onc"],
                    writes=[f"pb{dbank}"] if (first or last) else [])
                p.op("tensor", lambda e, tb=tb, col=col, cbank=cbank, first=first,
                     last=last: e.matmul(
                    p.pb[cbank][:, 0:128], vt_sb[:, tb * 128:(tb + 1) * 128],
                    probs[:, col:col + 128], start=first, stop=last),
                    reads=["I:probs", "I:v"],
                    writes=[f"pb{cbank}"] if (first or last) else [])
            done = [tb - 1] if tb < NB - 1 else [tb - 1, tb]
            for qb in done:
                if qb < 0:
                    continue
                cbank = 2 if qb % 3 == 0 else (3 if qb % 3 == 1 else 6)
                dbank = 1 if qb % 3 == 0 else (4 if qb % 3 == 1 else 5)
                p.op("vector", lambda e, qb=qb, dbank=dbank: e.reciprocal(
                    rloc[0:1, qb * 128:(qb + 1) * 128], p.pb[dbank][0:1, 0:128]),
                    reads=[f"pb{dbank}"], writes=["I:rloc"])
                p.op("vector", lambda e, qb=qb, cbank=cbank: e.tensor_copy(
                    lctx[:, qb * 128:(qb + 1) * 128], p.pb[cbank][:, 0:128]),
                    reads=[f"pb{cbank}"], writes=["I:lctx"])

        for tb in range(NB):
            qlo = max(0, (tb - 1) * 128)
            qhi = min(S, (tb + 2) * 128)
            Nq = qhi - qlo
            bqk = qb0(tb)
            p.op("tensor", lambda e, tb=tb, qlo=qlo, Nq=Nq, bqk=bqk: e.matmul(
                p.pb[bqk][:, 0:Nq], k_sb[:, tb * 128:(tb + 1) * 128],
                q_sb[:, qlo:qlo + Nq], start=True, stop=True),
                reads=["I:k", "I:q"], writes=[f"pb{bqk}"])
            if tb > 0:
                epi_local(tb - 1)
        epi_local(NB - 1)
        for c in range(32):
            pp = c % 2
            p.op("vector", lambda e, c=c, pp=pp, j=j: e.tensor_scalar(
                P_sb[:, pp * 64:(pp + 1) * 64], iorow[:, :],
                rkall[:, c * 8 + j:c * 8 + j + 1], None, ALU.is_equal),
                reads=["I:iorow", "I:rkall"], writes=[f"I:P{pp}"])
            p.op("tensor", lambda e, c=c, pp=pp: e.matmul(
                p.pb[0][0:64, 0:128], P_sb[:, pp * 64:(pp + 1) * 64],
                qt_all[:, c * 128:(c + 1) * 128],
                start=(c == 0), stop=(c == 31)),
                reads=[f"I:P{pp}", "I:qt"],
                writes=["pb0"] if c in (0, 31) else [])
        p.op("vector", lambda e: e.tensor_copy(qgT_sb[:, :], p.pb[0][0:64, 0:128]),
             reads=["pb0"], writes=["I:qgT"])
        p.op("tensor", lambda e: e.transpose(
            pbb6[:, 0:64], qgT_sb[:, :], ident[0:64, 0:64]),
            reads=["I:qgT", "I:ident"], writes=["pb6"])
        p.op("vector", lambda e: e.tensor_copy(qg_sb[:, :], pbb6[:, 0:64]),
             reads=["pb6"], writes=["I:qg"])
        for kb in range(32):
            p.op("tensor", lambda e, kb=kb: e.matmul(
                p.pb[1][:, 0:64], k_sb[:, kb * 128:(kb + 1) * 128], qg_sb[:, :],
                start=True, stop=True),
                reads=["I:k", "I:qg"], writes=["pb1"])
            p.op("scalar", lambda e: e.activation(
                gp_sb[:, :], p.pb[1][:, 0:64], AF.Exp,
                bias=negc[:, 0:1], scale=SCALE),
                reads=["pb1", "I:negc0"], writes=["I:gp"])
            p.op("tensor", lambda e, kb=kb: e.matmul(
                p.pb[2][0:1, 0:64], C["onc"][:, 0:1], gp_sb[:, :],
                start=(kb == 0), stop=(kb == 31)),
                reads=["I:gp", "onc"],
                writes=["pb2"] if kb in (0, 31) else [])
            p.op("tensor", lambda e, kb=kb: e.matmul(
                p.pb[3][0:64, 0:128], gp_sb[:, :], vt_sb[:, kb * 128:(kb + 1) * 128],
                start=(kb == 0), stop=(kb == 31)),
                reads=["I:gp", "I:v"],
                writes=["pb3"] if kb in (0, 31) else [])
        p.op("vector", lambda e: e.reciprocal(rgrow[0:1, 0:64], p.pb[2][0:1, 0:64]),
             reads=["pb2"], writes=["I:rgrow"])
        p.op("tensor", lambda e: e.matmul(
            p.pb[5][0:64, 0:1], rgrow[0:1, 0:64], C["onr"][0:1, 0:1],
            start=True, stop=True),
            reads=["I:rgrow", "onr"], writes=["pb5"])
        p.op("vector", lambda e: e.tensor_copy(rg_pp[:, 0:1], p.pb[5][0:64, 0:1]),
             reads=["pb5"], writes=["I:rg"])
        p.op("vector", lambda e: e.tensor_scalar_mul(
            gctx_sb[:, :], p.pb[3][0:64, 0:128], rg_pp[:, 0:1]),
            reads=["pb3", "I:rg"], writes=["I:gctx"])
        for c8 in range(8):
            p.op("sync", lambda e, j=j, c8=c8: e.dma_start(
                out=rr_sb[0:1, :], in_=t["rankm"][j:j + 1, c8 * 512:(c8 + 1) * 512]),
                reads=["I:rankmD"], writes=["I:rr"], dma=True)
            p.op("sync", lambda e, j=j, c8=c8: e.dma_start(
                out=mr_sb[0:1, :], in_=t["maskm"][j:j + 1, c8 * 512:(c8 + 1) * 512]),
                reads=["I:maskmD"], writes=["I:mr"], dma=True)
            p.op("vector", lambda e: e.tensor_scalar(
                cbrow[0:1, :], mr_sb[0:1, :], -1.0, 1.0, ALU.mult, ALU.add),
                reads=["I:mr"], writes=["I:cbrow"])
            p.op("vector", lambda e, c8=c8: e.tensor_mul(
                cbrow[0:1, :], cbrow[0:1, :], rloc[0:1, c8 * 512:(c8 + 1) * 512]),
                reads=["I:cbrow", "I:rloc"], writes=["I:cbrow"])
            p.op("tensor", lambda e: e.matmul(
                p.pb[5][:, :], C["onr"][0:1, 0:128], cbrow[0:1, :], start=True, stop=True),
                reads=["I:cbrow", "onr"], writes=["pb5"])
            p.op("vector", lambda e: e.tensor_copy(cb_sb[:, :], p.pb[5][:, :]),
                 reads=["pb5"], writes=["I:cb"])
            p.op("tensor", lambda e: e.matmul(
                p.pb[7][0:64, :], C["onr"][0:1, 0:64], rr_sb[0:1, :],
                start=True, stop=True),
                reads=["I:rr", "onr"], writes=["pb7"])
            p.op("vector", lambda e: e.tensor_scalar(
                pi_sb[:, :], p.pb[7][0:64, :], iopp[:, 0:1], None, ALU.is_equal),
                reads=["pb7", "I:iopp"], writes=["I:pi"])
            p.op("tensor", lambda e: e.matmul(
                p.pb[4][:, :], gctx_sb[:, :], pi_sb[:, :], start=True, stop=True),
                reads=["I:gctx", "I:pi"], writes=["pb4"])
            p.op("vector", lambda e, c8=c8: e.tensor_mul(
                t2_sb[:, :], lctx[:, c8 * 512:(c8 + 1) * 512], cb_sb[:, :]),
                reads=["I:lctx", "I:cb"], writes=["I:t2"])
            p.op("vector", lambda e: e.tensor_add(ctxo[:, :], t2_sb[:, :], p.pb[4][:, :]),
                 reads=["I:t2", "pb4"], writes=["I:ctxo"])
            p.op("gpsimd", lambda e, j=j, c8=c8: e.indirect_dma_start(
                t["ctx1_sh"][:, :],
                IndirectOffsetOnAxis(ap=C["idx_h"][:, j:j + 1], axis=0),
                ctxo[:, :], None, element_offset=c8 * 512),
                reads=["I:ctxo", "idx"], dma=True)


def em_J(p, t, C, ar, rw):
    ctx1v = t["ctx1_sh"].rearrange("h (p w) -> (h p) w", p=2)
    a1sb = ar.bf16(128, 16 * TC)
    xc = ar.f32(128, 2 * 512)
    u1 = ar.f32(128, 16 * 512)
    sq = ar.bf16(128, 512)
    t1 = ar.f32(128, 512)
    outs = ar.f32(128, 512)
    srow = rw.f32(3 * 512)
    for k in range(16):
        p.op("gpsimd", lambda e, k=k: e.indirect_dma_start(
            a1sb[:, k * TC:(k + 1) * TC], None,
            ctx1v[:, :], IndirectOffsetOnAxis(ap=C["idx_c"][:, k:k + 1], axis=0)),
            writes=[f"J:a{k}"], reads=["idx"], dma=True)
    for tt in range(TC // 512):
        t0 = tt * 512
        for m in range(16):
            xb = m % 2
            p.op("sync", lambda e, m=m, t0=t0, xb=xb: e.dma_start(
                out=xc[:, xb * 512:(xb + 1) * 512],
                in_=t["x2"][m * 128:(m + 1) * 128, t0:t0 + 512]),
                writes=[f"J:xc{xb}"], dma=True)
            p.op("vector", lambda e, m=m, t0=t0: e.tensor_copy(
                u1[:, m * 512:(m + 1) * 512],
                a1sb[:, m * TC + t0: m * TC + t0 + 512]),
                reads=[f"J:a{m}"], writes=[f"J:u1_{m}"])
            p.op("vector", lambda e, m=m, xb=xb: e.tensor_add(
                u1[:, m * 512:(m + 1) * 512],
                u1[:, m * 512:(m + 1) * 512],
                xc[:, xb * 512:(xb + 1) * 512]),
                reads=[f"J:u1_{m}", f"J:xc{xb}"], writes=[f"J:u1_{m}"])
            p.op("tensor", lambda e, m=m: e.matmul(
                p.pb[6][0:1, :], C["oncf"][:, 0:1], u1[:, m * 512:(m + 1) * 512],
                start=(m == 0), stop=(m == 15)),
                reads=[f"J:u1_{m}", "oncf"],
                writes=["pb6"] if m in (0, 15) else [])
            p.op("scalar", lambda e, m=m: e.activation(
                sq[:, :], u1[:, m * 512:(m + 1) * 512], AF.Square),
                reads=[f"J:u1_{m}"], writes=["J:sq"])
            p.op("tensor", lambda e, m=m: e.matmul(
                p.pb[7][0:1, :], C["onc"][:, 0:1], sq[:, :],
                start=(m == 0), stop=(m == 15)),
                reads=["J:sq", "onc"],
                writes=["pb7"] if m in (0, 15) else [])

        def outfn(m, o, t0=t0):
            p.op("sync", lambda e, m=m, t0=t0, o=o: e.dma_start(
                out=t["x3"][m * 128:(m + 1) * 128, t0:t0 + 512], in_=o[:, :]),
                reads=["J:outs"], dma=True)
        _ln_apply(p, C, "J", u1, 16, D, outfn, srow, t1, outs)


PHASES_ALL = ("AB", "C", "D", "E", "F", "G0", "H", "I", "J", "G1")


def build_fused(upto="G1", dbg=False):
    nc = bass.Bass("TRN2", target_bir_lowering=False)
    phases = PHASES_ALL[:PHASES_ALL.index(upto) + 1]
    t = {}

    def ein(name, shape, dt=F32):
        t[name] = nc.dram_tensor(name, shape, dt, kind="ExternalInput").ap()

    ein("x_f", [D, TC]); ein("wq_down", [D, QP], BF16); ein("gq", [128, 8])
    ein("wkv_lora", [D, KVPP], BF16); ein("wkv_rope", [D, 64], BF16); ein("gkv", [128, 11])
    ein("wq_up", [QP, 1024], BF16); ein("wkv_up_k", [KVPP, 512], BF16)
    ein("wkv_up_v", [KVPP, 1024], BF16)
    ein("wo", [D, D], BF16)
    ein("cs_own", [64, TC]); ein("sc_own", [64, TC])
    ein("cos4", [128, S]); ein("sin4", [128, S])
    for L in ("L0", "L1"):
        ein(f"{L}_gws", [44, 128, 2048], BF16); ein(f"{L}_uws", [44, 128, 2048], BF16)
        ein(f"{L}_dws", [16, 128, 5632], BF16)
        ein(f"{L}_gs", [128, 44]); ein(f"{L}_us", [128, 44]); ein(f"{L}_ds", [128, 16])
    ein("wq1", [D, 1024], BF16); ein("wk1", [D, 1024], BF16)
    ein("wv1", [D, 1024], BF16)
    ein("bq1", [1, 1024]); ein("bk1", [1, 1024]); ein("bv1", [1, 1024])
    ein("bqpp", [128, 8]); ein("bkpp", [128, 8])
    ein("maskE", [128, 2048]); ein("maskL", [128, 384])
    ein("io_row", [128, 64]); ein("io_pp", [64, 1]); ein("identd", [128, 128], BF16)
    ein("ident8f", [8, 8])
    ein("ones_colb", [128, 1], BF16); ein("ones_colf", [128, 1])
    ein("ones_row", [1, 128]); ein("ones512", [1, 512])
    ein("idx_fcq", [128, 8], U32); ein("idx_fckv", [128, 11], U32)
    ein("idx_fkr", [64, 1], U32); ein("idx_h", [128, 8], U32)
    ein("idx_t", [128, 16], U32); ein("idx_c", [128, 16], U32)

    def sh_t(name, shape, dt=BF16):
        t[name] = nc.dram_tensor(name, shape, dt, kind="Internal",
                                 addr_space="Shared").ap()

    sh_t("cq_sh", [2 * 1024, 2048]); sh_t("ckv_sh", [2 * 1408, 2048])
    sh_t("kr_sh", [2 * 64, 2048])
    sh_t("ctx0_sh", [2048, S]); sh_t("x2_sh", [2 * 2048, 2048])
    sh_t("ctx1_sh", [2048, S])

    ikind = "ExternalOutput" if dbg else "Internal"

    def loc(name, shape, dt, kind):
        t[name] = nc.dram_tensor(name, shape, dt, kind=kind).ap()

    loc("q0", [1024, S], BF16, ikind)
    loc("knope", [512, S], BF16, ikind)
    loc("vt", [S, 1024], BF16, ikind)
    loc("x1", [D, TC], F32, ikind)
    loc("x2", [D, TC], F32, ikind)
    loc("x3", [D, TC], F32, ikind)
    loc("q1", [1024, S], BF16, ikind)
    loc("k1", [1024, S], BF16, ikind)
    loc("v1t", [S, 1024], BF16, ikind)
    loc("q1t", [S, 1024], BF16, "Internal")
    loc("u2d", [D, TC], F32, "Internal")
    loc("rankm", [8, S], F32, "Internal")
    loc("maskm", [8, S], F32, "Internal")
    loc("bar_in", [1, 128], F32, "Internal")
    for i in range(4):
        loc(f"bar_out{i}", [1, 128], F32, "Internal")
    t["xout"] = nc.dram_tensor("xout", [D, TC], F32, kind="ExternalOutput").ap()

    with ExitStack() as ctx:
        C = {}
        C["onc"] = ctx.enter_context(nc.sbuf_tensor("c_onc", [128, 1], BF16))
        C["oncf"] = ctx.enter_context(nc.sbuf_tensor("c_oncf", [128, 1], F32))
        C["onr"] = ctx.enter_context(nc.sbuf_tensor("c_onr", [1, 128], F32))
        C["gq"] = ctx.enter_context(nc.sbuf_tensor("c_gq", [128, 8], F32))
        C["gkv"] = ctx.enter_context(nc.sbuf_tensor("c_gkv", [128, 11], F32))
        C["idx_fcq"] = ctx.enter_context(nc.sbuf_tensor("c_ifcq", [128, 8], U32))
        C["idx_fckv"] = ctx.enter_context(nc.sbuf_tensor("c_ifckv", [128, 11], U32))
        C["idx_fkr"] = ctx.enter_context(nc.sbuf_tensor("c_ifkr", [64, 1], U32))
        C["idx_h"] = ctx.enter_context(nc.sbuf_tensor("c_ih", [128, 8], U32))
        C["idx_t"] = ctx.enter_context(nc.sbuf_tensor("c_it", [128, 16], U32))
        C["idx_c"] = ctx.enter_context(nc.sbuf_tensor("c_ic", [128, 16], U32))
        ar = Arena(nc, ctx, 47232)
        rw = Rows(nc, ctx, 4096)
        p = Prog(nc, ctx)
        p.pb = [ctx.enter_context(nc.psum_tensor(f"pb{i}", [128, 512], F32))
                for i in range(8)]

        for nm in ("onc", "oncf", "onr", "gq", "gkv",
                   "idx_fcq", "idx_fckv", "idx_fkr", "idx_h", "idx_t", "idx_c"):
            src = {"onc": "ones_colb", "oncf": "ones_colf", "onr": "ones_row"}.get(nm, nm)
            key = "idx" if nm.startswith("idx") else nm
            p.op("sync", lambda e, nm=nm, src=src: e.dma_start(out=C[nm][:, :], in_=t[src]),
                 writes=[key], dma=True)
        p.op("sync", lambda e: e.dma_start(out=t["bar_in"], in_=C["onr"][0:1, :]),
             reads=["onr"], writes=["barin"], dma=True)

        def barrier(i):
            p.fence()
            p.cc(lambda e, i=i: e.collective_compute(
                "AllReduce", mybir.AluOpType.add, replica_groups=PAIRS,
                ins=[t["bar_in"].opt()], outs=[t[f"bar_out{i}"].opt()]))
            p.fence()

        for ph in phases:
            ar.reset(0)
            rw.reset()
            if ph == "AB":
                em_AB(p, t, C, ar, rw)
                barrier(0)
            elif ph == "C":
                em_C(p, t, C, ar, rw)
                p.fence()
            elif ph == "D":
                em_D(p, t, C, ar, rw)
                p.fence()
            elif ph == "E":
                em_E(p, t, C, ar, rw)
                barrier(1)
            elif ph == "F":
                em_F(p, t, C, ar, rw)
                p.fence()
            elif ph == "G0":
                em_G(p, t, C, ar, rw, "L0", t["x1"], t["x2"], scat=True)
                barrier(2)
            elif ph == "H":
                em_H(p, t, C, ar, rw)
                p.fence()
            elif ph == "I":
                em_I(p, t, C, ar, rw)
                barrier(3)
            elif ph == "J":
                em_J(p, t, C, ar, rw)
                p.fence()
            elif ph == "G1":
                em_G(p, t, C, ar, rw, "L1", t["x3"], t["xout"], scat=False)
        p.emit()
    return nc


def kernel(**ins):
    shd, cores = stage_inputs(ins)
    nc = build_fused()
    maps = []
    for c in range(NCORES):
        m = dict(shd)
        m.update(cores[c])
        maps.append(m)
    res = run_bass_kernel_spmd(nc, maps, core_ids=list(range(NCORES)))
    out = np.zeros((B, S, D), np.float32)
    for c in range(NCORES):
        b, r = c // 2, c % 2
        xo = np.asarray(res.results[c]["xout"])
        out[b, r * 2048:(r + 1) * 2048, :] = xo.T
    return out


# version 06
# speedup vs baseline: 1.4533x; 1.0245x over previous
import numpy as np
import ml_dtypes
import concourse.bass as bass
import concourse.mybir as mybir
from concourse.bass import IndirectOffsetOnAxis
from concourse.bass_utils import run_bass_kernel_spmd
from contextlib import ExitStack

F32 = mybir.dt.float32
BF16 = mybir.dt.bfloat16
U32 = mybir.dt.uint32
AX = mybir.AxisListType.X
ALU = mybir.AluOpType
AF = mybir.ActivationFunctionType
BF = ml_dtypes.bfloat16

NCORES = 8
B, S, D = 4, 4096, 2048
H, DH = 16, 128
QP = 1024
KVP = 1365
KVPP = 1408
INTER = 5632
EPS = 1e-5
SCALE = 1.0 / float(np.sqrt(np.float32(128.0)))
NEG = -1e9
T = B * S
TC = 2048
PAIRS = [[0, 1], [2, 3], [4, 5], [6, 7]]

ENGS = ("sync", "gpsimd", "tensor", "vector", "scalar")
DMA_POOL = 8


class Prog:

    def __init__(self, nc, ctx):
        self.nc = nc
        self.ops = []
        self.last_writer = {}
        self.readers = {}
        self.eng_count = {e: 0 for e in ENGS}
        self.sems = {e: ctx.enter_context(nc.semaphore(f"se_{e}")) for e in ENGS}
        self.dpool = {e: [ctx.enter_context(nc.semaphore(f"sd_{e}{i}"))
                          for i in range(DMA_POOL)] for e in ("sync", "gpsimd", "scalar")}
        self.dcount = {e: [0] * DMA_POOL for e in ("sync", "gpsimd", "scalar")}
        self.drr = {e: 0 for e in ("sync", "gpsimd", "scalar")}
        self.ccsems = [ctx.enter_context(nc.semaphore(f"scc{i}")) for i in range(8)]
        self.ncc = 0
        self.fence_deps = ()

    def op(self, engine, fn, reads=(), writes=(), dma=False, after=()):
        deps = set(after)
        deps.update(self.fence_deps)
        for k in reads:
            if k in self.last_writer:
                deps.add(self.last_writer[k])
            if k.startswith("pb"):
                for rd in self.readers.get(k, ()):
                    if self.ops[rd][0] != engine:
                        deps.add(rd)
        for k in writes:
            if k in self.last_writer:
                deps.add(self.last_writer[k])
            for rd in self.readers.get(k, ()):
                deps.add(rd)
        i = len(self.ops)
        if dma:
            idx = self.drr[engine] % DMA_POOL
            self.drr[engine] += 1
            before = self.dcount[engine][idx]
            self.dcount[engine][idx] = before + 16
            info = ("dma", idx, before)
            pos = 0
        else:
            info = None
            self.eng_count[engine] += 1
            pos = self.eng_count[engine]
        self.ops.append((engine, fn, deps, info, pos))
        for k in reads:
            self.readers.setdefault(k, []).append(i)
        for k in writes:
            self.last_writer[k] = i
            self.readers[k] = []
        return i

    def cc(self, fn):
        sem = self.ccsems[self.ncc]
        self.ncc += 1
        deps = set(self.fence_deps)
        i = len(self.ops)
        self.ops.append(("gpsimd", fn, deps, ("cc", sem), 0))
        return i

    def fence(self):
        deps = []
        last_eng = {}
        last_dma = {}
        last_cc = []
        for i, (e, fn, d, info, pos) in enumerate(self.ops):
            if info is None:
                last_eng[e] = i
            elif info[0] == "dma":
                last_dma[(e, info[1])] = i
            else:
                last_cc.append(i)
        deps.extend(last_eng.values())
        deps.extend(last_dma.values())
        deps.extend(last_cc[-4:])
        self.fence_deps = tuple(deps)

    def emit(self):
        nc = self.nc
        with nc.Block() as block:
            for ename in ENGS:
                self._emit_engine(block, ename)

    def _emit_engine(self, block, ename):
        prog = self

        def body(eng):
            wm = {}

            def wait(sem, val):
                if val > wm.get(id(sem), 0):
                    eng.wait_ge(sem, val)
                    wm[id(sem)] = val

            def do_waits(deps):
                for d in sorted(deps):
                    de, dfn, ddeps, dinfo, dpos = prog.ops[d]
                    if dinfo is not None and dinfo[0] == "dma":
                        wait(prog.dpool[de][dinfo[1]], dinfo[2] + 16)
                    elif dinfo is not None and dinfo[0] == "cc":
                        wait(dinfo[1], 1)
                    else:
                        if de == ename and ename == "tensor":
                            continue
                        wait(prog.sems[de], dpos)

            for i, (e, fn, deps, info, pos) in enumerate(prog.ops):
                if e != ename:
                    continue
                do_waits(deps)
                if info is not None and info[0] == "dma":
                    idx, before = info[1], info[2]
                    if before > 0:
                        wait(prog.dpool[ename][idx], before)
                    inst = fn(eng)
                    inst.then_inc(prog.dpool[ename][idx], 16)
                elif info is not None and info[0] == "cc":
                    inst = fn(eng)
                    inst.then_inc(info[1])
                else:
                    inst = fn(eng)
                    inst.then_inc(prog.sems[ename], 1)
            if ename == "sync":
                for q, pool in prog.dpool.items():
                    for idx, sem in enumerate(pool):
                        if prog.dcount[q][idx] > 0:
                            wait(sem, prog.dcount[q][idx])
                for i in range(prog.ncc):
                    wait(prog.ccsems[i], 1)

        getattr(block, ename)(body)


class Arena:

    def __init__(self, nc, ctx, cols_f32):
        self.t = ctx.enter_context(nc.sbuf_tensor("arena", [128, cols_f32], F32))
        self.cap = cols_f32
        self.off = 0
        self.peak = 0

    def reset(self, to=0):
        self.off = to

    def f32(self, parts, cols):
        o = self.off
        self.off += cols
        self.peak = max(self.peak, self.off)
        assert self.off <= self.cap, f"arena overflow {self.off}>{self.cap}"
        return self.t[0:parts, o:o + cols]

    def bf16(self, parts, cols):
        cf = (cols + 1) // 2
        o = self.off
        self.off += cf
        self.peak = max(self.peak, self.off)
        assert self.off <= self.cap, f"arena overflow {self.off}>{self.cap}"
        return self.t.bitcast(BF16)[0:parts, 2 * o:2 * o + cols]


class Rows:

    def __init__(self, nc, ctx, cols):
        self.t = ctx.enter_context(nc.sbuf_tensor("rows", [1, cols], F32))
        self.cap = cols
        self.off = 0

    def reset(self):
        self.off = 0

    def f32(self, cols):
        o = self.off
        self.off += cols
        assert self.off <= self.cap, "rows overflow"
        return self.t[0:1, o:o + cols]


def _f32(a):
    return np.ascontiguousarray(np.asarray(a, dtype=np.float32))


def _bf(a):
    return np.ascontiguousarray(np.asarray(a, dtype=np.float32).astype(BF))


def _rope_tables_np():
    freqs = (1.0 / (np.float32(10000.0) **
                    (np.arange(0, 128, 2, dtype=np.float32) / np.float32(128.0))))
    pos = np.arange(S, dtype=np.float32)
    emb = pos[:, None] * freqs[None, :32]
    return np.cos(emb).astype(np.float32).T.copy(), \
        np.sin(emb).astype(np.float32).T.copy()


def q_pack_perm8(r):
    perm = np.zeros(1024, np.int64)
    for c in range(4):
        for half in range(2):
            j = 2 * c + half
            h = 8 * r + j
            perm[c * 128 + half * 64: c * 128 + half * 64 + 64] = \
                h * 128 + np.arange(64)
    for g in range(2):
        for hh in range(4):
            j = 4 * g + hh
            h = 8 * r + j
            perm[(4 + g) * 128 + hh * 32:(4 + g) * 128 + hh * 32 + 32] = \
                h * 128 + 64 + 2 * np.arange(32)
    for g in range(2):
        for hh in range(4):
            j = 4 * g + hh
            h = 8 * r + j
            perm[(6 + g) * 128 + hh * 32:(6 + g) * 128 + hh * 32 + 32] = \
                h * 128 + 64 + 2 * np.arange(32) + 1
    return perm


def q_head_rows8(j):
    return ((j // 2) * 128 + (j % 2) * 64,
            (4 + j // 4) * 128 + (j % 4) * 32,
            (6 + j // 4) * 128 + (j % 4) * 32)


def stage_inputs(ins):
    sh = {}
    x = ins["L0_attn_x"]
    xf = _f32(x.reshape(T, D).T)

    sh["wq_down"] = _bf(ins["L0_attn_Wq_down"])
    wkv = np.asarray(ins["L0_attn_Wkv_down"], np.float32)
    lora = np.zeros((D, KVPP), np.float32)
    lora[:, :KVP] = wkv[:, :KVP]
    sh["wkv_lora"] = _bf(lora)
    wr = wkv[:, KVP:]
    sh["wkv_rope"] = _bf(np.concatenate([wr[:, 0::2], wr[:, 1::2]], axis=1))

    sh["gq"] = _f32(np.asarray(ins["L0_attn_gq"]).reshape(8, 128).T)
    gkv = np.zeros(KVPP, np.float32)
    gkv[:KVP] = np.asarray(ins["L0_attn_gkv"])
    sh["gkv"] = _f32(gkv.reshape(11, 128).T)
    sh["wo"] = _bf(ins["L0_attn_Wo"])

    cosT, sinT = _rope_tables_np()
    sh["cos4"] = _f32(np.tile(cosT, (4, 1)))
    sh["sin4"] = _f32(np.tile(sinT, (4, 1)))

    wkvu = np.asarray(ins["L0_attn_Wkv_up"], np.float32)
    wkvu_p = np.zeros((KVPP, 3072), np.float32)
    wkvu_p[:KVP] = wkvu
    per_head = wkvu_p.reshape(KVPP, H, 192)

    for L in ("L0", "L1"):
        gw = np.asarray(ins[f"{L}_ffn_gate_w"], np.float32)
        uw = np.asarray(ins[f"{L}_ffn_up_w"], np.float32)
        dw = np.asarray(ins[f"{L}_ffn_down_w"], np.float32)
        sh[f"{L}_gws"] = _bf(gw.reshape(44, 128, 16, 128).transpose(0, 3, 2, 1)
                              .reshape(44, 128, 2048))
        sh[f"{L}_uws"] = _bf(uw.reshape(44, 128, 16, 128).transpose(0, 3, 2, 1)
                              .reshape(44, 128, 2048))
        sh[f"{L}_dws"] = _bf(dw.reshape(16, 128, 44, 128).transpose(0, 3, 2, 1)
                              .reshape(16, 128, 5632))
        sh[f"{L}_gs"] = _f32(np.asarray(ins[f"{L}_ffn_gate_s"]).reshape(44, 128).T)
        sh[f"{L}_us"] = _f32(np.asarray(ins[f"{L}_ffn_up_s"]).reshape(44, 128).T)
        sh[f"{L}_ds"] = _f32(np.asarray(ins[f"{L}_ffn_down_s"]).reshape(16, 128).T)

    sh["ones_colb"] = np.ones((128, 1), BF)
    sh["ones_colf"] = np.ones((128, 1), np.float32)
    sh["ones_row"] = np.ones((1, 128), np.float32)
    sh["ones512"] = np.ones((1, 512), np.float32)
    sh["identd"] = np.eye(128).astype(BF)
    sh["ident8f"] = np.eye(8).astype(np.float32)
    kk2 = np.arange(128)[:, None]
    qq2 = np.arange(512)[None, :]
    me = np.zeros((128, 4, 512), np.float32)
    for jj in range(4):
        me[:, jj, :] = np.where(jj * 128 + kk2 > qq2, NEG, 0.0)
    sh["maskE"] = _f32(me.reshape(128, 2048))
    ml = np.zeros((128, 384), np.float32)
    ml[64:128, 0:128] = NEG
    ml[0:64, 256:384] = NEG
    sh["maskL"] = _f32(ml)
    sh["io_row"] = _f32(np.tile(np.arange(64, dtype=np.float32)[None, :], (128, 1)))
    sh["io_pp"] = _f32(np.arange(64, dtype=np.float32).reshape(64, 1))

    wq1 = np.asarray(ins["L1_attn_Wq"], np.float32)
    wk1 = np.asarray(ins["L1_attn_Wk"], np.float32)
    wv1 = np.asarray(ins["L1_attn_Wv"], np.float32)
    bq1 = np.asarray(ins["L1_attn_bq"], np.float32)
    bk1 = np.asarray(ins["L1_attn_bk"], np.float32)
    bv1 = np.asarray(ins["L1_attn_bv"], np.float32)

    i128 = np.arange(128, dtype=np.uint32)[:, None]
    cores = []
    for c in range(NCORES):
        b, r = c // 2, c % 2
        pc = {}
        pc["x_f"] = _f32(xf[:, c * TC:(c + 1) * TC])
        pc["wq_up"] = _bf(np.asarray(ins["L0_attn_Wq_up"], np.float32)[:, q_pack_perm8(r)])
        hs = slice(8 * r, 8 * r + 8)
        pc["wkv_up_k"] = _bf(per_head[:, hs, :64].reshape(KVPP, 512))
        pc["wkv_up_v"] = _bf(per_head[:, hs, 64:].reshape(KVPP, 1024))
        ps = slice(r * 2048, r * 2048 + 2048)
        pc["cs_own"] = _f32(np.concatenate([cosT[:, ps], sinT[:, ps]], 0))
        pc["sc_own"] = _f32(np.concatenate([sinT[:, ps], cosT[:, ps]], 0))
        cs = slice(1024 * r, 1024 * r + 1024)
        pc["wq1"] = _bf(wq1[:, cs])
        pc["wk1"] = _bf(wk1[:, cs])
        pc["wv1"] = _bf(wv1[:, cs])
        pc["bq1"] = _f32(bq1[cs].reshape(1, 1024))
        pc["bqpp"] = _f32(bq1[cs].reshape(8, 128).T)
        pc["bkpp"] = _f32(bk1[cs].reshape(8, 128).T)
        pc["bk1"] = _f32(bk1[cs].reshape(1, 1024))
        pc["bv1"] = _f32(bv1[cs].reshape(1, 1024))
        pc["idx_fcq"] = (r * 1024 + np.arange(8, dtype=np.uint32)[None, :] * 128
                         + i128).astype(np.uint32)
        pc["idx_fckv"] = (r * 1408 + np.arange(11, dtype=np.uint32)[None, :] * 128
                          + i128).astype(np.uint32)
        pc["idx_fkr"] = (r * 64 + np.arange(64, dtype=np.uint32)[:, None]
                         ).astype(np.uint32)
        pc["idx_h"] = ((8 * r + np.arange(8, dtype=np.uint32)[None, :]) * 128
                       + i128).astype(np.uint32)
        pc["idx_t"] = (r * 2048 + np.arange(16, dtype=np.uint32)[None, :] * 128
                       + i128).astype(np.uint32)
        pc["idx_c"] = (2 * (np.arange(16, dtype=np.uint32)[None, :] * 128 + i128)
                       + r).astype(np.uint32)
        cores.append(pc)
    return sh, cores


def em_AB(p, t, C, ar, rw):
    xsb = ar.bf16(128, 16 * TC)
    for tt in range(4):
        for k in range(16):
            p.op("gpsimd", lambda e, k=k, tt=tt: e.dma_start(
                out=xsb[:, k * TC + tt * 512: k * TC + (tt + 1) * 512],
                in_=t["x_f"][k * 128:(k + 1) * 128, tt * 512:(tt + 1) * 512]),
                writes=[f"x{k}_{tt}"], dma=True)
    a0 = ar.off
    wsb = ar.bf16(128, 16 * QP)
    craw = ar.f32(128, 8 * 512)
    sqs = ar.bf16(128, 512)
    outsb = ar.bf16(128, 8 * 512)
    rowbuf = rw.f32(3 * 512)
    for k in range(16):
        p.op("sync", lambda e, k=k: e.dma_start(
            out=wsb[:, k * QP:(k + 1) * QP], in_=t["wq_down"][k * 128:(k + 1) * 128, :]),
            writes=[f"A:w{k}"], dma=True)
    NT = TC // 512

    def epiA(m):
        bank = m % 4
        cp = p.op("vector", lambda e, m=m, bank=bank: e.tensor_copy(
            craw[:, m * 512:(m + 1) * 512], p.pb[bank][:, :]),
            reads=[f"pb{bank}"], writes=[f"A:craw{m}"])
        p.op("scalar", lambda e, bank=bank: e.activation(
            sqs[:, :], p.pb[bank][:, :], AF.Square),
            reads=[f"pb{bank}"], writes=["A:sqs"], after=[cp])
        p.op("tensor", lambda e, m=m: e.matmul(
            p.pb[4][0:1, :], C["onc"][:, 0:1], sqs[:, :],
            start=(m == 0), stop=(m == 7)),
            reads=["A:sqs", "onc"],
            writes=["pb4"] if m in (0, 7) else [])

    for tt in range(NT):
        t0 = tt * 512
        for m in range(8):
            bank = m % 4
            for k in range(16):
                p.op("tensor", lambda e, k=k, m=m, t0=t0, bank=bank: e.matmul(
                    p.pb[bank][:, :],
                    wsb[:, k * QP + m * 128: k * QP + (m + 1) * 128],
                    xsb[:, k * TC + t0: k * TC + t0 + 512],
                    start=(k == 0), stop=(k == 15)),
                    reads=[f"x{k}_{tt}", f"A:w{k}"],
                    writes=[f"pb{bank}"] if k in (0, 15) else [])
            if m > 0:
                epiA(m - 1)
        epiA(7)
        p.op("vector", lambda e: e.tensor_scalar(
            rowbuf[0:1, 0:512], p.pb[4][0:1, :], 1.0 / QP, EPS, ALU.mult, ALU.add),
            reads=["pb4"], writes=["A:row0"])
        p.op("scalar", lambda e: e.activation(
            rowbuf[0:1, 512:1024], rowbuf[0:1, 0:512], AF.Ln),
            reads=["A:row0"], writes=["A:row1"])
        p.op("scalar", lambda e: e.activation(
            rowbuf[0:1, 1024:1536], rowbuf[0:1, 512:1024], AF.Exp, scale=-0.5),
            reads=["A:row1"], writes=["A:row2"])
        p.op("tensor", lambda e: e.matmul(
            p.pb[5][:, :], C["onr"][0:1, 0:128], rowbuf[0:1, 1024:1536],
            start=True, stop=True),
            reads=["A:row2", "onr"], writes=["pb5"])
        for m in range(8):
            p.op("vector", lambda e, m=m: e.scalar_tensor_tensor(
                outsb[:, m * 512:(m + 1) * 512],
                craw[:, m * 512:(m + 1) * 512],
                C["gq"][:, m:m + 1], p.pb[5][:, :], ALU.mult, ALU.mult),
                reads=[f"A:craw{m}", "gq", "pb5"], writes=[f"A:out{m}"])
            p.op("gpsimd", lambda e, m=m, t0=t0: e.indirect_dma_start(
                t["cq_sh"][:, :],
                IndirectOffsetOnAxis(ap=C["idx_fcq"][:, m:m + 1], axis=0),
                outsb[:, m * 512:(m + 1) * 512], None, element_offset=t0),
                reads=[f"A:out{m}", "idx"], dma=True)
    p.fence()
    ar.reset(a0)
    wsb = ar.bf16(128, 16 * KVPP)
    wrs = ar.bf16(128, 16 * 64)
    css = ar.f32(64, 2048)
    scs = ar.f32(64, 2048)
    craw = ar.f32(128, 11 * 512)
    kraw = ar.f32(64, 2048)
    ksc1 = ar.f32(64, 1024)
    ksc2 = ar.f32(64, 1024)
    kout = ar.bf16(64, 2048)
    sqs = ar.bf16(128, 512)
    outsb = ar.bf16(128, 11 * 512)
    for k in range(16):
        p.op("sync", lambda e, k=k: e.dma_start(
            out=wsb[:, k * KVPP:(k + 1) * KVPP],
            in_=t["wkv_lora"][k * 128:(k + 1) * 128, :]),
            writes=[f"B:w{k}"], dma=True)
        p.op("sync", lambda e, k=k: e.dma_start(
            out=wrs[:, k * 64:(k + 1) * 64],
            in_=t["wkv_rope"][k * 128:(k + 1) * 128, :]),
            writes=[f"B:wr{k}"], dma=True)
    p.op("sync", lambda e: e.dma_start(out=css[:, :], in_=t["cs_own"]),
         writes=["B:cs"], dma=True)
    p.op("sync", lambda e: e.dma_start(out=scs[:, :], in_=t["sc_own"]),
         writes=["B:sc"], dma=True)
    rowbuf = rw.f32(3 * 512)
    def epiB(m):
        bank = m % 4
        cp = p.op("vector", lambda e, m=m, bank=bank: e.tensor_copy(
            craw[:, m * 512:(m + 1) * 512], p.pb[bank][:, :]),
            reads=[f"pb{bank}"], writes=[f"B:craw{m}"])
        p.op("scalar", lambda e, bank=bank: e.activation(
            sqs[:, :], p.pb[bank][:, :], AF.Square),
            reads=[f"pb{bank}"], writes=["B:sqs"], after=[cp])
        p.op("tensor", lambda e, m=m: e.matmul(
            p.pb[4][0:1, :], C["onc"][:, 0:1], sqs[:, :],
            start=(m == 0), stop=(m == 10)),
            reads=["B:sqs", "onc"],
            writes=["pb4"] if m in (0, 10) else [])

    for tt in range(NT):
        t0 = tt * 512
        for m in range(11):
            bank = m % 4
            for k in range(16):
                p.op("tensor", lambda e, k=k, m=m, t0=t0, bank=bank: e.matmul(
                    p.pb[bank][:, :],
                    wsb[:, k * KVPP + m * 128: k * KVPP + (m + 1) * 128],
                    xsb[:, k * TC + t0: k * TC + t0 + 512],
                    start=(k == 0), stop=(k == 15)),
                    reads=[f"x{k}_{tt}", f"B:w{k}"],
                    writes=[f"pb{bank}"] if k in (0, 15) else [])
            if m > 0:
                epiB(m - 1)
        for k in range(16):
            p.op("tensor", lambda e, k=k, t0=t0: e.matmul(
                p.pb[5][0:64, :],
                wrs[:, k * 64:(k + 1) * 64],
                xsb[:, k * TC + t0: k * TC + t0 + 512],
                start=(k == 0), stop=(k == 15)),
                reads=[f"x{k}_{tt}", f"B:wr{k}"],
                writes=["pb5"] if k in (0, 15) else [])
        epiB(10)
        p.op("vector", lambda e, t0=t0: e.tensor_copy(
            kraw[:, t0:t0 + 512], p.pb[5][0:64, :]),
            reads=["pb5"], writes=["B:kraw"])
        p.op("vector", lambda e: e.tensor_scalar(
            rowbuf[0:1, 0:512], p.pb[4][0:1, :], 1.0 / KVP, EPS, ALU.mult, ALU.add),
            reads=["pb4"], writes=["B:row0"])
        p.op("scalar", lambda e: e.activation(
            rowbuf[0:1, 512:1024], rowbuf[0:1, 0:512], AF.Ln),
            reads=["B:row0"], writes=["B:row1"])
        p.op("scalar", lambda e: e.activation(
            rowbuf[0:1, 1024:1536], rowbuf[0:1, 512:1024], AF.Exp, scale=-0.5),
            reads=["B:row1"], writes=["B:row2"])
        p.op("tensor", lambda e: e.matmul(
            p.pb[6][:, :], C["onr"][0:1, 0:128], rowbuf[0:1, 1024:1536],
            start=True, stop=True),
            reads=["B:row2", "onr"], writes=["pb6"])
        for m in range(11):
            p.op("vector", lambda e, m=m: e.scalar_tensor_tensor(
                outsb[:, m * 512:(m + 1) * 512],
                craw[:, m * 512:(m + 1) * 512],
                C["gkv"][:, m:m + 1], p.pb[6][:, :], ALU.mult, ALU.mult),
                reads=[f"B:craw{m}", "gkv", "pb6"], writes=[f"B:out{m}"])
            p.op("gpsimd", lambda e, m=m, t0=t0: e.indirect_dma_start(
                t["ckv_sh"][:, :],
                IndirectOffsetOnAxis(ap=C["idx_fckv"][:, m:m + 1], axis=0),
                outsb[:, m * 512:(m + 1) * 512], None, element_offset=t0),
                reads=[f"B:out{m}", "idx"], dma=True)
    for hh in range(2):
        c0 = hh * 1024
        cs_ = slice(c0, c0 + 1024)
        p.op("vector", lambda e, cs_=cs_: e.tensor_mul(
            ksc1[0:32, :], kraw[0:32, cs_], css[0:32, cs_]),
            reads=["B:kraw", "B:cs"], writes=["B:ksc1a"])
        p.op("vector", lambda e, cs_=cs_: e.tensor_mul(
            ksc1[32:64, :], kraw[32:64, cs_], css[32:64, cs_]),
            reads=["B:kraw", "B:cs"], writes=["B:ksc1b"])
        p.op("vector", lambda e, cs_=cs_: e.tensor_mul(
            ksc2[0:32, :], kraw[0:32, cs_], scs[0:32, cs_]),
            reads=["B:kraw", "B:sc"], writes=["B:ksc2a"])
        p.op("vector", lambda e, cs_=cs_: e.tensor_mul(
            ksc2[32:64, :], kraw[32:64, cs_], scs[32:64, cs_]),
            reads=["B:kraw", "B:sc"], writes=["B:ksc2b"])
        p.op("vector", lambda e, cs_=cs_: e.tensor_copy(kraw[0:32, cs_], ksc1[32:64, :]),
             reads=["B:ksc1b"], writes=["B:kraw"])
        p.op("vector", lambda e, cs_=cs_: e.tensor_sub(
            kout[0:32, cs_], ksc1[0:32, :], kraw[0:32, cs_]),
            reads=["B:ksc1a", "B:kraw"], writes=[f"B:ne{hh}"])
        p.op("vector", lambda e, cs_=cs_: e.tensor_copy(kraw[32:64, cs_], ksc2[0:32, :]),
             reads=["B:ksc2a"], writes=["B:kraw"])
        p.op("vector", lambda e, cs_=cs_: e.tensor_add(
            kout[32:64, cs_], ksc2[32:64, :], kraw[32:64, cs_]),
            reads=["B:ksc2b", "B:kraw"], writes=[f"B:no{hh}"])
    p.op("gpsimd", lambda e: e.indirect_dma_start(
        t["kr_sh"][:, :],
        IndirectOffsetOnAxis(ap=C["idx_fkr"][0:64, 0:1], axis=0),
        kout[0:64, :], None),
        reads=["B:ne0", "B:no0", "B:ne1", "B:no1", "idx"], dma=True)


def em_C(p, t, C, ar, rw):
    csb = ar.bf16(128, 8 * S)
    wsb = ar.bf16(128, 8 * QP)
    c4 = ar.f32(128, S)
    s4 = ar.f32(128, S)
    rraw = ar.f32(128, 4 * 512)
    rsc1 = ar.f32(128, 512)
    rsc2 = ar.f32(128, 512)
    outsb = ar.bf16(128, 8 * 512)
    for k in range(8):
        for pp in range(2):
            p.op("sync", lambda e, k=k, pp=pp: e.dma_start(
                out=csb[:, k * S + pp * 2048: k * S + pp * 2048 + 2048],
                in_=t["cq_sh"][pp * 1024 + k * 128: pp * 1024 + (k + 1) * 128, :]),
                writes=[f"C:c{k}"], dma=True)
        p.op("sync", lambda e, k=k: e.dma_start(
            out=wsb[:, k * QP:(k + 1) * QP],
            in_=t["wq_up"][k * 128:(k + 1) * 128, :]),
            writes=[f"C:w{k}"], dma=True)
    p.op("sync", lambda e: e.dma_start(out=c4[:, :], in_=t["cos4"]),
         writes=["C:cos"], dma=True)
    p.op("sync", lambda e: e.dma_start(out=s4[:, :], in_=t["sin4"]),
         writes=["C:sin"], dma=True)
    for tt in range(S // 512):
        t0 = tt * 512
        for m in range(8):
            bank = m % 4
            for k in range(8):
                p.op("tensor", lambda e, k=k, m=m, t0=t0, bank=bank: e.matmul(
                    p.pb[bank][:, :],
                    wsb[:, k * QP + m * 128: k * QP + (m + 1) * 128],
                    csb[:, k * S + t0: k * S + t0 + 512],
                    start=(k == 0), stop=(k == 7)),
                    reads=[f"C:c{k}", f"C:w{k}"],
                    writes=[f"pb{bank}"] if k in (0, 7) else [])
            if m < 4:
                p.op("vector", lambda e, m=m, bank=bank: e.tensor_copy(
                    outsb[:, m * 512:(m + 1) * 512], p.pb[bank][:, :]),
                    reads=[f"pb{bank}"], writes=[f"C:out{m}"])
            else:
                p.op("vector", lambda e, m=m, bank=bank: e.tensor_copy(
                    rraw[:, (m - 4) * 512:(m - 3) * 512], p.pb[bank][:, :]),
                    reads=[f"pb{bank}"], writes=[f"C:rr{m - 4}"])
        for g in range(2):
            ek, ok = f"C:rr{g}", f"C:rr{2 + g}"
            E = rraw[:, g * 512:(g + 1) * 512]
            O = rraw[:, (2 + g) * 512:(3 + g) * 512]
            p.op("vector", lambda e, E=E, t0=t0: e.tensor_mul(
                rsc1[:, :], E, c4[:, t0:t0 + 512]),
                reads=[ek, "C:cos"], writes=["C:rs1"])
            p.op("vector", lambda e, O=O, t0=t0: e.tensor_mul(
                rsc2[:, :], O, s4[:, t0:t0 + 512]),
                reads=[ok, "C:sin"], writes=["C:rs2"])
            p.op("vector", lambda e, g=g: e.tensor_sub(
                outsb[:, (4 + g) * 512:(5 + g) * 512], rsc1[:, :], rsc2[:, :]),
                reads=["C:rs1", "C:rs2"], writes=[f"C:out{4 + g}"])
            p.op("vector", lambda e, O=O, t0=t0: e.tensor_mul(
                rsc1[:, :], O, c4[:, t0:t0 + 512]),
                reads=[ok, "C:cos"], writes=["C:rs1"])
            p.op("vector", lambda e, E=E, t0=t0: e.tensor_mul(
                rsc2[:, :], E, s4[:, t0:t0 + 512]),
                reads=[ek, "C:sin"], writes=["C:rs2"])
            p.op("vector", lambda e, g=g: e.tensor_add(
                outsb[:, (6 + g) * 512:(7 + g) * 512], rsc1[:, :], rsc2[:, :]),
                reads=["C:rs1", "C:rs2"], writes=[f"C:out{6 + g}"])
        for m in range(8):
            p.op("sync", lambda e, m=m, t0=t0: e.dma_start(
                out=t["q0"][m * 128:(m + 1) * 128, t0:t0 + 512],
                in_=outsb[:, m * 512:(m + 1) * 512]),
                reads=[f"C:out{m}"], dma=True)


def em_D(p, t, C, ar, rw):
    csb = ar.bf16(128, 11 * S)
    wks = ar.bf16(128, 11 * 512)
    wvs = ar.bf16(128, 11 * 1024)
    outsb = ar.bf16(128, 4 * 512)
    vosb = ar.bf16(128, 2 * 512)
    for k in range(11):
        for pp in range(2):
            p.op("sync", lambda e, k=k, pp=pp: e.dma_start(
                out=csb[:, k * S + pp * 2048: k * S + pp * 2048 + 2048],
                in_=t["ckv_sh"][pp * 1408 + k * 128: pp * 1408 + (k + 1) * 128, :]),
                writes=[f"D:c{k}"], dma=True)
        p.op("sync", lambda e, k=k: e.dma_start(
            out=wks[:, k * 512:(k + 1) * 512],
            in_=t["wkv_up_k"][k * 128:(k + 1) * 128, :]),
            writes=[f"D:wk{k}"], dma=True)
        p.op("sync", lambda e, k=k: e.dma_start(
            out=wvs[:, k * 1024:(k + 1) * 1024],
            in_=t["wkv_up_v"][k * 128:(k + 1) * 128, :]),
            writes=[f"D:wv{k}"], dma=True)
    for tt in range(S // 512):
        t0 = tt * 512
        for m in range(4):
            bank = m % 4
            for k in range(11):
                p.op("tensor", lambda e, k=k, m=m, t0=t0, bank=bank: e.matmul(
                    p.pb[bank][:, :],
                    wks[:, k * 512 + m * 128: k * 512 + (m + 1) * 128],
                    csb[:, k * S + t0: k * S + t0 + 512],
                    start=(k == 0), stop=(k == 10)),
                    reads=[f"D:c{k}", f"D:wk{k}"],
                    writes=[f"pb{bank}"] if k in (0, 10) else [])
            p.op("vector", lambda e, m=m, bank=bank: e.tensor_copy(
                outsb[:, m * 512:(m + 1) * 512], p.pb[bank][:, :]),
                reads=[f"pb{bank}"], writes=[f"D:out{m}"])
            p.op("sync", lambda e, m=m, t0=t0: e.dma_start(
                out=t["knope"][m * 128:(m + 1) * 128, t0:t0 + 512],
                in_=outsb[:, m * 512:(m + 1) * 512]),
                reads=[f"D:out{m}"], dma=True)
    for ttile in range(S // 128):
        for v4 in range(2):
            bank = 4 + (2 * ttile + v4) % 4
            for k in range(11):
                p.op("tensor", lambda e, k=k, ttile=ttile, v4=v4, bank=bank: e.matmul(
                    p.pb[bank][:, :],
                    csb[:, k * S + ttile * 128: k * S + (ttile + 1) * 128],
                    wvs[:, k * 1024 + v4 * 512: k * 1024 + (v4 + 1) * 512],
                    start=(k == 0), stop=(k == 10)),
                    reads=[f"D:c{k}", f"D:wv{k}"],
                    writes=[f"pb{bank}"] if k in (0, 10) else [])
            p.op("vector", lambda e, v4=v4, bank=bank: e.tensor_copy(
                vosb[:, v4 * 512:(v4 + 1) * 512], p.pb[bank][:, :]),
                reads=[f"pb{bank}"], writes=[f"D:vo{v4}"])
            p.op("sync", lambda e, ttile=ttile, v4=v4: e.dma_start(
                out=t["vt"][ttile * 128:(ttile + 1) * 128, v4 * 512:(v4 + 1) * 512],
                in_=vosb[:, v4 * 512:(v4 + 1) * 512]),
                reads=[f"D:vo{v4}"], dma=True)


def _bound_c(p, q_key, q_sb, sq_sb, onc, srow, slot, pf):
    for c8 in range(8):
        p.op("scalar", lambda e, c8=c8: e.activation(
            sq_sb[:, 0:512], q_sb[:, c8 * 512:(c8 + 1) * 512], AF.Square),
            reads=[q_key], writes=[f"{pf}:bsq"])
        p.op("tensor", lambda e: e.matmul(
            p.pb[6][0:1, 0:512], onc[:, 0:1], sq_sb[:, 0:512],
            start=True, stop=True),
            reads=[f"{pf}:bsq", "onc"], writes=["pb6"])
        p.op("vector", lambda e, c8=c8, slot=slot: e.reduce_max(
            srow[0:1, 16 + slot * 8 + c8:17 + slot * 8 + c8],
            p.pb[6][0:1, 0:512], axis=AX),
            reads=["pb6"], writes=[f"{pf}:m8_{slot}_{c8}"])
    p.op("vector", lambda e, slot=slot: e.reduce_max(
        srow[0:1, slot:slot + 1],
        srow[0:1, 16 + slot * 8:16 + slot * 8 + 8], axis=AX),
        reads=[f"{pf}:m8_{slot}_{c8}" for c8 in range(8)], writes=[f"{pf}:mx{slot}"])


def _negc_from_bounds(p, C, srow, negc_col, pf, pr=0):
    p.op("vector", lambda e: e.tensor_mul(
        srow[0:1, 2:3], srow[0:1, 0:1], srow[0:1, 1:2]),
        reads=[f"{pf}:mx0", f"{pf}:mx1"], writes=[f"{pf}:prod"])
    p.op("scalar", lambda e: e.activation(
        srow[0:1, 3:4], srow[0:1, 2:3], AF.Ln),
        reads=[f"{pf}:prod"], writes=[f"{pf}:lnp"])
    p.op("scalar", lambda e: e.activation(
        srow[0:1, 4:5], srow[0:1, 3:4], AF.Exp, scale=0.5),
        reads=[f"{pf}:lnp"], writes=[f"{pf}:sqp"])
    p.op("vector", lambda e: e.tensor_scalar_mul(
        srow[0:1, 5:6], srow[0:1, 4:5], -SCALE),
        reads=[f"{pf}:sqp"], writes=[f"{pf}:negc_row"])
    p.op("tensor", lambda e: e.matmul(
        p.pb[6][:, 0:1], C["onr"][0:1, 0:128], srow[0:1, 5:6],
        start=True, stop=True),
        reads=[f"{pf}:negc_row", "onr"], writes=["pb6"])
    p.op("vector", lambda e: e.tensor_copy(negc_col[:, 0:1], p.pb[6][:, 0:1]),
         reads=["pb6"], writes=[f"{pf}:negc{pr}"])


def em_E(p, t, C, ar, rw):
    q_sb = ar.bf16(128, 2 * S)
    k_sb = ar.bf16(128, 2 * S)
    vt_sb = ar.bf16(128, 2 * S)
    msk = ar.f32(128, 2048)
    sq_sb = ar.bf16(128, 512)
    negc = ar.f32(128, 2)
    probs = ar.bf16(128, 2 * 512)
    rbc = ar.f32(128, 512)
    ctxo = ar.bf16(128, 512)
    srow = rw.f32(80)
    rrow = rw.f32(512)
    p.op("sync", lambda e: e.dma_start(out=msk[:, :], in_=t["maskE"]),
         writes=["E:msk"], dma=True)

    def loads(j):
        pr = j % 2
        o = pr * S
        nof, evf, odf = q_head_rows8(j)
        p.op("sync", lambda e, nof=nof, o=o: e.dma_start(
            out=q_sb[0:64, o:o + S], in_=t["q0"][nof:nof + 64, :]),
            writes=[f"E:q{pr}"], dma=True)
        p.op("sync", lambda e, evf=evf, o=o: e.dma_start(
            out=q_sb[64:96, o:o + S], in_=t["q0"][evf:evf + 32, :]),
            writes=[f"E:q{pr}"], dma=True)
        p.op("sync", lambda e, odf=odf, o=o: e.dma_start(
            out=q_sb[96:128, o:o + S], in_=t["q0"][odf:odf + 32, :]),
            writes=[f"E:q{pr}"], dma=True)
        p.op("sync", lambda e, j=j, o=o: e.dma_start(
            out=k_sb[0:64, o:o + S], in_=t["knope"][j * 64:(j + 1) * 64, :]),
            writes=[f"E:k{pr}"], dma=True)
        for pp in range(2):
            p.op("sync", lambda e, pp=pp, o=o: e.dma_start(
                out=k_sb[64:96, o + pp * 2048: o + (pp + 1) * 2048],
                in_=t["kr_sh"][pp * 64: pp * 64 + 32, :]),
                writes=[f"E:k{pr}"], dma=True)
            p.op("sync", lambda e, pp=pp, o=o: e.dma_start(
                out=k_sb[96:128, o + pp * 2048: o + (pp + 1) * 2048],
                in_=t["kr_sh"][pp * 64 + 32: pp * 64 + 64, :]),
                writes=[f"E:k{pr}"], dma=True)
        for g in range(8):
            p.op("sync", lambda e, j=j, g=g, o=o: e.dma_start(
                out=vt_sb[:, o + g * 512: o + (g + 1) * 512].rearrange(
                    "p (a d) -> p a d", d=128),
                in_=t["vt"][g * 512:(g + 1) * 512, j * 128:(j + 1) * 128].rearrange(
                    "(a p) d -> p a d", p=128)),
                writes=[f"E:v{pr}"], dma=True)

    loads(0)
    for j in range(8):
        pr = j % 2
        o = pr * S
        if j < 7:
            loads(j + 1)
        _bound_c(p, f"E:q{pr}", q_sb[:, o:o + S], sq_sb, C["onc"], srow, 0, "E")
        _bound_c(p, f"E:k{pr}", k_sb[:, o:o + S], sq_sb, C["onc"], srow, 1, "E")
        _negc_from_bounds(p, C, srow, negc[:, pr:pr + 1], "E", pr)

        def finkb(sl, kb, nkb, dn, pv):
            b0 = kb % 2
            jj = kb - 4 * sl
            if jj >= 0:
                p.op("vector", lambda e, jj=jj, b0=b0: e.tensor_add(
                    p.pb[b0][:, :], p.pb[b0][:, :], msk[:, jj * 512:(jj + 1) * 512]),
                    reads=[f"pb{b0}", "E:msk"], writes=[f"pb{b0}"])
            p.op("scalar", lambda e, b0=b0, pr=pr: e.activation(
                probs[:, b0 * 512:(b0 + 1) * 512], p.pb[b0][:, :], AF.Exp,
                bias=negc[:, pr:pr + 1], scale=SCALE),
                reads=[f"pb{b0}", f"E:negc{pr}"], writes=[f"E:probs{b0}"])
            p.op("tensor", lambda e, kb=kb, b0=b0, nkb=nkb, dn=dn: e.matmul(
                p.pb[dn][0:1, :], C["onc"][:, 0:1], probs[:, b0 * 512:(b0 + 1) * 512],
                start=(kb == 0), stop=(kb == nkb - 1)),
                reads=[f"E:probs{b0}", "onc"],
                writes=[f"pb{dn}"] if kb in (0, nkb - 1) else [])
            p.op("tensor", lambda e, kb=kb, b0=b0, nkb=nkb, pv=pv, o=o: e.matmul(
                p.pb[pv][:, :], vt_sb[:, o + kb * 128: o + (kb + 1) * 128],
                probs[:, b0 * 512:(b0 + 1) * 512],
                start=(kb == 0), stop=(kb == nkb - 1)),
                reads=[f"E:probs{b0}", f"E:v{pr}"],
                writes=[f"pb{pv}"] if kb in (0, nkb - 1) else [])

        def slepi(sl, dn, pv):
            p.op("vector", lambda e, dn=dn: e.reciprocal(rrow[0:1, :], p.pb[dn][0:1, :]),
                 reads=[f"pb{dn}"], writes=["E:rrow"])
            p.op("tensor", lambda e, dn=dn: e.matmul(
                p.pb[dn][:, :], C["onr"][0:1, 0:128], rrow[0:1, :],
                start=True, stop=True),
                reads=["E:rrow", "onr"], writes=[f"pb{dn}"])
            p.op("vector", lambda e, dn=dn: e.tensor_copy(rbc[:, :], p.pb[dn][:, :]),
                 reads=[f"pb{dn}"], writes=["E:rbc"])
            p.op("vector", lambda e, pv=pv: e.tensor_mul(ctxo[:, :], p.pb[pv][:, :], rbc[:, :]),
                 reads=[f"pb{pv}", "E:rbc"], writes=["E:ctxo"])
            p.op("gpsimd", lambda e, j=j, sl=sl: e.indirect_dma_start(
                t["ctx0_sh"][:, :],
                IndirectOffsetOnAxis(ap=C["idx_h"][:, j:j + 1], axis=0),
                ctxo[:, :], None, element_offset=sl * 512),
                reads=["E:ctxo", "idx"], dma=True)

        prev = None
        for sl in range(8):
            nkb = 4 * sl + 4
            dn = 4 + (sl % 2) * 2
            pv = 5 + (sl % 2) * 2
            for kb in range(nkb):
                b0 = kb % 2
                p.op("tensor", lambda e, sl=sl, kb=kb, b0=b0, o=o: e.matmul(
                    p.pb[b0][:, :],
                    k_sb[:, o + kb * 128: o + (kb + 1) * 128],
                    q_sb[:, o + sl * 512: o + (sl + 1) * 512],
                    start=True, stop=True),
                    reads=[f"E:k{pr}", f"E:q{pr}"], writes=[f"pb{b0}"])
                if kb == 0 and prev is not None:
                    slepi(*prev)
                if kb > 0:
                    finkb(sl, kb - 1, nkb, dn, pv)
            finkb(sl, nkb - 1, nkb, dn, pv)
            prev = (sl, dn, pv)
        slepi(*prev)


def _ln_apply(p, C, pf, u1, nchunk, dim, outfn, srow, t1, outs):
    A_ = srow[0:1, 0:512]
    Bm = srow[0:1, 512:1024]
    Cc = srow[0:1, 1024:1536]
    p.op("vector", lambda e: e.tensor_scalar(A_, p.pb[6][0:1, :], 1.0 / dim, None, ALU.mult),
         reads=["pb6"], writes=[f"{pf}:lnA"])
    p.op("vector", lambda e: e.tensor_mul(Bm, A_, A_), reads=[f"{pf}:lnA"],
         writes=[f"{pf}:lnB"])
    p.op("vector", lambda e: e.tensor_scalar(Cc, p.pb[7][0:1, :], 1.0 / dim, EPS,
                                             ALU.mult, ALU.add),
         reads=["pb7"], writes=[f"{pf}:lnC"])
    p.op("vector", lambda e: e.tensor_sub(Cc, Cc, Bm),
         reads=[f"{pf}:lnB", f"{pf}:lnC"], writes=[f"{pf}:lnC"])
    p.op("scalar", lambda e: e.activation(Cc, Cc, AF.Ln),
         reads=[f"{pf}:lnC"], writes=[f"{pf}:lnC"])
    p.op("scalar", lambda e: e.activation(Cc, Cc, AF.Exp, scale=-0.5),
         reads=[f"{pf}:lnC"], writes=[f"{pf}:lnC"])
    p.op("vector", lambda e: e.tensor_mul(Bm, A_, Cc),
         reads=[f"{pf}:lnA", f"{pf}:lnC"], writes=[f"{pf}:lnB"])
    p.op("vector", lambda e: e.tensor_scalar_mul(Bm, Bm, -1.0),
         reads=[f"{pf}:lnB"], writes=[f"{pf}:lnB"])
    p.op("tensor", lambda e: e.matmul(
        p.pb[4][:, :], C["onr"][0:1, 0:128], Cc, start=True, stop=True),
        reads=[f"{pf}:lnC", "onr"], writes=["pb4"])
    p.op("tensor", lambda e: e.matmul(
        p.pb[5][:, :], C["onr"][0:1, 0:128], Bm, start=True, stop=True),
        reads=[f"{pf}:lnB", "onr"], writes=["pb5"])
    for m in range(nchunk):
        p.op("vector", lambda e, m=m: e.tensor_mul(
            t1[:, :], u1[:, m * 512:(m + 1) * 512], p.pb[4][:, :]),
            reads=[f"{pf}:u1_{m}", "pb4"], writes=[f"{pf}:t1"])
        p.op("vector", lambda e: e.tensor_add(outs[:, :], t1[:, :], p.pb[5][:, :]),
             reads=[f"{pf}:t1", "pb5"], writes=[f"{pf}:outs"])
        outfn(m, outs)


def em_F(p, t, C, ar, rw):
    ctx0v = t["ctx0_sh"].rearrange("h (p w) -> (h p) w", p=2)
    csb = ar.bf16(128, 16 * TC)
    wsb = ar.bf16(128, 16 * D)
    xc = ar.f32(128, 2 * 512)
    u1 = ar.f32(128, 16 * 512)
    sq = ar.bf16(128, 512)
    t1 = ar.f32(128, 512)
    outs = ar.f32(128, 512)
    srow = rw.f32(3 * 512)
    for k in range(16):
        p.op("gpsimd", lambda e, k=k: e.indirect_dma_start(
            csb[:, k * TC:(k + 1) * TC], None,
            ctx0v[:, :], IndirectOffsetOnAxis(ap=C["idx_c"][:, k:k + 1], axis=0)),
            writes=[f"F:c{k}"], reads=["idx"], dma=True)
        p.op("sync", lambda e, k=k: e.dma_start(
            out=wsb[:, k * D:(k + 1) * D], in_=t["wo"][k * 128:(k + 1) * 128, :]),
            writes=[f"F:w{k}"], dma=True)
    def epiF(m):
        bank = m % 4
        xb = m % 2
        p.op("vector", lambda e, m=m, bank=bank, xb=xb: e.tensor_add(
            u1[:, m * 512:(m + 1) * 512], p.pb[bank][:, :],
            xc[:, xb * 512:(xb + 1) * 512]),
            reads=[f"pb{bank}", f"F:xc{xb}"], writes=[f"F:u1_{m}"])
        p.op("tensor", lambda e, m=m: e.matmul(
            p.pb[6][0:1, :], C["oncf"][:, 0:1], u1[:, m * 512:(m + 1) * 512],
            start=(m == 0), stop=(m == 15)),
            reads=[f"F:u1_{m}", "oncf"],
            writes=["pb6"] if m in (0, 15) else [])
        p.op("scalar", lambda e, m=m: e.activation(
            sq[:, :], u1[:, m * 512:(m + 1) * 512], AF.Square),
            reads=[f"F:u1_{m}"], writes=["F:sq"])
        p.op("tensor", lambda e, m=m: e.matmul(
            p.pb[7][0:1, :], C["onc"][:, 0:1], sq[:, :],
            start=(m == 0), stop=(m == 15)),
            reads=["F:sq", "onc"],
            writes=["pb7"] if m in (0, 15) else [])

    for tt in range(TC // 512):
        t0 = tt * 512
        for m in range(16):
            bank = m % 4
            xb = m % 2
            p.op("sync", lambda e, m=m, t0=t0, xb=xb: e.dma_start(
                out=xc[:, xb * 512:(xb + 1) * 512],
                in_=t["x_f"][m * 128:(m + 1) * 128, t0:t0 + 512]),
                writes=[f"F:xc{xb}"], dma=True)
            for k in range(16):
                p.op("tensor", lambda e, k=k, m=m, t0=t0, bank=bank: e.matmul(
                    p.pb[bank][:, :],
                    wsb[:, k * D + m * 128: k * D + (m + 1) * 128],
                    csb[:, k * TC + t0: k * TC + t0 + 512],
                    start=(k == 0), stop=(k == 15)),
                    reads=[f"F:c{k}", f"F:w{k}"],
                    writes=[f"pb{bank}"] if k in (0, 15) else [])
            if m > 0:
                epiF(m - 1)
        epiF(15)

        def outfn(m, o, t0=t0):
            p.op("sync", lambda e, m=m, t0=t0, o=o: e.dma_start(
                out=t["x1"][m * 128:(m + 1) * 128, t0:t0 + 512], in_=o[:, :]),
                reads=["F:outs"], dma=True)
        _ln_apply(p, C, "F", u1, 16, D, outfn, srow, t1, outs)


def em_G(p, t, C, ar, rw, L, xin, xout, scat):
    pf = f"G{L}"
    TH = TC // 2
    xb_sb = ar.bf16(128, 16 * TH)
    gw = ar.bf16(128, 2 * 2048)
    uw = ar.bf16(128, 2 * 2048)
    dw = ar.bf16(128, 2 * 5632)
    gssb = ar.f32(128, 44)
    ussb = ar.f32(128, 44)
    dssb = ar.f32(128, 16)
    g_sb = ar.f32(128, 1024)
    u_sb = ar.f32(128, 1024)
    hdn = ar.bf16(128, 44 * 1024)
    xc = ar.f32(128, 4 * 512)
    u2 = ar.f32(128, 4 * 512)
    sq = ar.bf16(128, 512)
    t1 = ar.f32(128, 512)
    outs = ar.f32(128, 512)
    outs_bf = ar.bf16(128, 512)
    srow = rw.f32(6 * 512)
    p.op("sync", lambda e: e.dma_start(out=gssb[:, :], in_=t[f"{L}_gs"]),
         writes=[f"{pf}:gs"], dma=True)
    p.op("sync", lambda e: e.dma_start(out=ussb[:, :], in_=t[f"{L}_us"]),
         writes=[f"{pf}:us"], dma=True)
    p.op("sync", lambda e: e.dma_start(out=dssb[:, :], in_=t[f"{L}_ds"]),
         writes=[f"{pf}:ds"], dma=True)
    for half in range(2):
        h0 = half * TH
        for k in range(16):
            p.op("gpsimd", lambda e, k=k, h0=h0: e.dma_start(
                out=xb_sb[:, k * TH:(k + 1) * TH],
                in_=xin[k * 128:(k + 1) * 128, h0:h0 + TH]),
                writes=[f"{pf}:xb{k}"], dma=True)
        for i in range(44):
            wsl = i % 2
            p.op("scalar", lambda e, i=i, wsl=wsl: e.dma_start(
                out=gw[:, wsl * 2048:(wsl + 1) * 2048], in_=t[f"{L}_gws"][i]),
                writes=[f"{pf}:gw{wsl}"], dma=True)
            p.op("gpsimd", lambda e, i=i, wsl=wsl: e.dma_start(
                out=uw[:, wsl * 2048:(wsl + 1) * 2048], in_=t[f"{L}_uws"][i]),
                writes=[f"{pf}:uw{wsl}"], dma=True)
            for k in range(16):
                for t2 in range(2):
                    p.op("tensor", lambda e, k=k, t2=t2, wsl=wsl: e.matmul(
                        p.pb[t2][:, :],
                        gw[:, wsl * 2048 + k * 128: wsl * 2048 + (k + 1) * 128],
                        xb_sb[:, k * TH + t2 * 512: k * TH + t2 * 512 + 512],
                        start=(k == 0), stop=(k == 15)),
                        reads=[f"{pf}:gw{wsl}", f"{pf}:xb{k}"],
                        writes=[f"pb{t2}"] if k in (0, 15) else [])
            for k in range(16):
                for t2 in range(2):
                    p.op("tensor", lambda e, k=k, t2=t2, wsl=wsl: e.matmul(
                        p.pb[2 + t2][:, :],
                        uw[:, wsl * 2048 + k * 128: wsl * 2048 + (k + 1) * 128],
                        xb_sb[:, k * TH + t2 * 512: k * TH + t2 * 512 + 512],
                        start=(k == 0), stop=(k == 15)),
                        reads=[f"{pf}:uw{wsl}", f"{pf}:xb{k}"],
                        writes=[f"pb{2 + t2}"] if k in (0, 15) else [])
            for t2 in range(2):
                p.op("scalar", lambda e, i=i, t2=t2: e.activation(
                    g_sb[:, t2 * 512:(t2 + 1) * 512], p.pb[t2][:, :], AF.Silu,
                    scale=gssb[:, i:i + 1]),
                    reads=[f"pb{t2}", f"{pf}:gs"], writes=[f"{pf}:g{t2}"])
                p.op("vector", lambda e, i=i, t2=t2: e.tensor_scalar_mul(
                    u_sb[:, t2 * 512:(t2 + 1) * 512], p.pb[2 + t2][:, :],
                    ussb[:, i:i + 1]),
                    reads=[f"pb{2 + t2}", f"{pf}:us"], writes=[f"{pf}:u{t2}"])
            p.op("vector", lambda e, i=i: e.tensor_mul(
                hdn[:, i * 1024:(i + 1) * 1024], g_sb[:, :], u_sb[:, :]),
                reads=[f"{pf}:g0", f"{pf}:g1", f"{pf}:u0", f"{pf}:u1"],
                writes=[f"{pf}:hdn{i}"])
        def epiGd(f, h0):
            fx = (f % 2) * 2
            for t2 in range(2):
                bk = fx + t2
                p.op("vector", lambda e, f=f, t2=t2, fx=fx, bk=bk: e.scalar_tensor_tensor(
                    u2[:, (fx + t2) * 512:(fx + t2 + 1) * 512], p.pb[bk][:, :],
                    dssb[:, f:f + 1], xc[:, (fx + t2) * 512:(fx + t2 + 1) * 512],
                    ALU.mult, ALU.add),
                    reads=[f"pb{bk}", f"{pf}:ds", f"{pf}:xc{fx + t2}"],
                    writes=[f"{pf}:u2_{fx + t2}"])
                p.op("scalar", lambda e, f=f, t2=t2, h0=h0, fx=fx: e.dma_start(
                    out=t["u2d"][f * 128:(f + 1) * 128, h0 + t2 * 512:h0 + t2 * 512 + 512],
                    in_=u2[:, (fx + t2) * 512:(fx + t2 + 1) * 512]),
                    reads=[f"{pf}:u2_{fx + t2}"], dma=True)
                p.op("tensor", lambda e, f=f, t2=t2, fx=fx: e.matmul(
                    p.pb[4 + t2][0:1, :], C["oncf"][:, 0:1],
                    u2[:, (fx + t2) * 512:(fx + t2 + 1) * 512],
                    start=(f == 0), stop=(f == 15)),
                    reads=[f"{pf}:u2_{fx + t2}", "oncf"],
                    writes=[f"pb{4 + t2}"] if f in (0, 15) else [])
                p.op("scalar", lambda e, t2=t2, fx=fx: e.activation(
                    sq[:, :], u2[:, (fx + t2) * 512:(fx + t2 + 1) * 512], AF.Square),
                    reads=[f"{pf}:u2_{fx + t2}"], writes=[f"{pf}:sq"])
                p.op("tensor", lambda e, f=f, t2=t2: e.matmul(
                    p.pb[6 + t2][0:1, :], C["onc"][:, 0:1], sq[:, :],
                    start=(f == 0), stop=(f == 15)),
                    reads=[f"{pf}:sq", "onc"],
                    writes=[f"pb{6 + t2}"] if f in (0, 15) else [])

        for f in range(16):
            wsl = f % 2
            fx = (f % 2) * 2
            p.op("sync", lambda e, f=f, wsl=wsl: e.dma_start(
                out=dw[:, wsl * 5632:(wsl + 1) * 5632], in_=t[f"{L}_dws"][f]),
                writes=[f"{pf}:dw{wsl}"], dma=True)
            for t2 in range(2):
                p.op("sync", lambda e, f=f, t2=t2, h0=h0, fx=fx: e.dma_start(
                    out=xc[:, (fx + t2) * 512:(fx + t2 + 1) * 512],
                    in_=xin[f * 128:(f + 1) * 128, h0 + t2 * 512:h0 + t2 * 512 + 512]),
                    writes=[f"{pf}:xc{fx + t2}"], dma=True)
            for ik in range(44):
                for t2 in range(2):
                    bk = fx + t2
                    p.op("tensor", lambda e, ik=ik, t2=t2, wsl=wsl, bk=bk: e.matmul(
                        p.pb[bk][:, :],
                        dw[:, wsl * 5632 + ik * 128: wsl * 5632 + (ik + 1) * 128],
                        hdn[:, ik * 1024 + t2 * 512: ik * 1024 + t2 * 512 + 512],
                        start=(ik == 0), stop=(ik == 43)),
                        reads=[f"{pf}:dw{wsl}", f"{pf}:hdn{ik}"],
                        writes=[f"pb{bk}"] if ik in (0, 43) else [])
            if f > 0:
                epiGd(f - 1, h0)
        epiGd(15, h0)
        for t2 in range(2):
            base = 3 * t2
            A_ = srow[0:1, base * 512:(base + 1) * 512]
            Bm = srow[0:1, (base + 1) * 512:(base + 2) * 512]
            Cc = srow[0:1, (base + 2) * 512:(base + 3) * 512]
            p.op("vector", lambda e, A_=A_, t2=t2: e.tensor_scalar(
                A_, p.pb[4 + t2][0:1, :], 1.0 / D, None, ALU.mult),
                reads=[f"pb{4 + t2}"], writes=[f"{pf}:lnA{t2}"])
            p.op("vector", lambda e, A_=A_, Bm=Bm: e.tensor_mul(Bm, A_, A_),
                 reads=[f"{pf}:lnA{t2}"], writes=[f"{pf}:lnB{t2}"])
            p.op("vector", lambda e, Cc=Cc, t2=t2: e.tensor_scalar(
                Cc, p.pb[6 + t2][0:1, :], 1.0 / D, EPS, ALU.mult, ALU.add),
                reads=[f"pb{6 + t2}"], writes=[f"{pf}:lnC{t2}"])
            p.op("vector", lambda e, Cc=Cc, Bm=Bm: e.tensor_sub(Cc, Cc, Bm),
                 reads=[f"{pf}:lnB{t2}", f"{pf}:lnC{t2}"], writes=[f"{pf}:lnC{t2}"])
            p.op("scalar", lambda e, Cc=Cc: e.activation(Cc, Cc, AF.Ln),
                 reads=[f"{pf}:lnC{t2}"], writes=[f"{pf}:lnC{t2}"])
            p.op("scalar", lambda e, Cc=Cc: e.activation(Cc, Cc, AF.Exp, scale=-0.5),
                 reads=[f"{pf}:lnC{t2}"], writes=[f"{pf}:lnC{t2}"])
            p.op("vector", lambda e, A_=A_, Bm=Bm, Cc=Cc: e.tensor_mul(Bm, A_, Cc),
                 reads=[f"{pf}:lnA{t2}", f"{pf}:lnC{t2}"], writes=[f"{pf}:lnB{t2}"])
            p.op("vector", lambda e, Bm=Bm: e.tensor_scalar_mul(Bm, Bm, -1.0),
                 reads=[f"{pf}:lnB{t2}"], writes=[f"{pf}:lnB{t2}"])
            p.op("tensor", lambda e, Cc=Cc, t2=t2: e.matmul(
                p.pb[4 + t2][:, :], C["onr"][0:1, 0:128], Cc, start=True, stop=True),
                reads=[f"{pf}:lnC{t2}", "onr"], writes=[f"pb{4 + t2}"])
            p.op("tensor", lambda e, Bm=Bm, t2=t2: e.matmul(
                p.pb[6 + t2][:, :], C["onr"][0:1, 0:128], Bm, start=True, stop=True),
                reads=[f"{pf}:lnB{t2}", "onr"], writes=[f"pb{6 + t2}"])
            for f in range(16):
                fx = (f % 2) * 2
                p.op("scalar", lambda e, f=f, t2=t2, h0=h0, fx=fx: e.dma_start(
                    out=u2[:, (fx + t2) * 512:(fx + t2 + 1) * 512],
                    in_=t["u2d"][f * 128:(f + 1) * 128, h0 + t2 * 512:h0 + t2 * 512 + 512]),
                    writes=[f"{pf}:u2_{fx + t2}"], dma=True)
                p.op("vector", lambda e, t2=t2, fx=fx: e.tensor_mul(
                    t1[:, :], u2[:, (fx + t2) * 512:(fx + t2 + 1) * 512], p.pb[4 + t2][:, :]),
                    reads=[f"{pf}:u2_{fx + t2}", f"pb{4 + t2}"], writes=[f"{pf}:t1"])
                p.op("vector", lambda e, t2=t2: e.tensor_add(
                    outs[:, :], t1[:, :], p.pb[6 + t2][:, :]),
                    reads=[f"{pf}:t1", f"pb{6 + t2}"], writes=[f"{pf}:outs"])
                p.op("sync", lambda e, f=f, t2=t2, h0=h0: e.dma_start(
                    out=xout[f * 128:(f + 1) * 128, h0 + t2 * 512:h0 + t2 * 512 + 512],
                    in_=outs[:, :]),
                    reads=[f"{pf}:outs"], dma=True)
                if scat:
                    p.op("vector", lambda e: e.tensor_copy(outs_bf[:, :], outs[:, :]),
                         reads=[f"{pf}:outs"], writes=[f"{pf}:obf"])
                    p.op("gpsimd", lambda e, f=f, t2=t2, h0=h0: e.indirect_dma_start(
                        t["x2_sh"][:, :],
                        IndirectOffsetOnAxis(ap=C["idx_t"][:, f:f + 1], axis=0),
                        outs_bf[:, :], None, element_offset=h0 + t2 * 512),
                        reads=[f"{pf}:obf", "idx"], dma=True)


def em_H(p, t, C, ar, rw):
    xsb = ar.bf16(128, 16 * S)
    wsb = ar.bf16(128, 16 * 1024)
    outsb = ar.bf16(128, 4 * 512)
    bpp = ar.f32(128, 16)
    bsb = rw.f32(3 * 1024)
    on5 = rw.f32(512)
    p.op("sync", lambda e: e.dma_start(out=on5[0:1, :], in_=t["ones512"]),
         writes=["H:on5"], dma=True)
    for k in range(16):
        for pp in range(2):
            p.op("sync", lambda e, k=k, pp=pp: e.dma_start(
                out=xsb[:, k * S + pp * 2048: k * S + pp * 2048 + 2048],
                in_=t["x2_sh"][pp * 2048 + k * 128: pp * 2048 + (k + 1) * 128, :]),
                writes=[f"H:x{k}"], dma=True)
    for nm, boff in (("bq1", 0), ("bk1", 1024), ("bv1", 2048)):
        p.op("sync", lambda e, nm=nm, boff=boff: e.dma_start(
            out=bsb[0:1, boff:boff + 1024], in_=t[nm]),
            writes=[f"H:{nm}"], dma=True)
    p.op("sync", lambda e: e.dma_start(out=bpp[:, 0:8], in_=t["bqpp"]),
         writes=["H:bpp"], dma=True)
    p.op("sync", lambda e: e.dma_start(out=bpp[:, 8:16], in_=t["bkpp"]),
         writes=["H:bpp"], dma=True)
    for wnm, bo, outd in (("wq1", 0, "q1"), ("wk1", 8, "k1")):
        for k in range(16):
            p.op("sync", lambda e, k=k, wnm=wnm: e.dma_start(
                out=wsb[:, k * 1024:(k + 1) * 1024],
                in_=t[wnm][k * 128:(k + 1) * 128, :]),
                writes=[f"H:w{k}"], dma=True)
        for tt in range(S // 512):
            t0 = tt * 512
            for m in range(8):
                bank = m % 4
                for k in range(16):
                    p.op("tensor", lambda e, k=k, m=m, t0=t0, bank=bank: e.matmul(
                        p.pb[bank][:, :],
                        wsb[:, k * 1024 + m * 128: k * 1024 + (m + 1) * 128],
                        xsb[:, k * S + t0: k * S + t0 + 512],
                        start=(k == 0), stop=(k == 15)),
                        reads=[f"H:x{k}", f"H:w{k}"],
                        writes=[f"pb{bank}"] if k in (0, 15) else [])
                p.op("vector", lambda e, m=m, bank=bank, bo=bo: e.tensor_scalar(
                    outsb[:, bank * 512:(bank + 1) * 512], p.pb[bank][:, :],
                    bpp[:, bo + m:bo + m + 1], None, ALU.add),
                    reads=[f"pb{bank}", "H:bpp"], writes=[f"H:o{bank}"])
                p.op("sync", lambda e, m=m, t0=t0, bank=bank, outd=outd: e.dma_start(
                    out=t[outd][m * 128:(m + 1) * 128, t0:t0 + 512],
                    in_=outsb[:, bank * 512:(bank + 1) * 512]),
                    reads=[f"H:o{bank}"], dma=True)
    for wnm, brow, outd in (("wv1", 2048, "v1t"), ("wq1", 0, "q1t")):
        for k in range(16):
            p.op("sync", lambda e, k=k, wnm=wnm: e.dma_start(
                out=wsb[:, k * 1024:(k + 1) * 1024],
                in_=t[wnm][k * 128:(k + 1) * 128, :]),
                writes=[f"H:w{k}"], dma=True)
        for ttile in range(S // 128):
            for v4 in range(2):
                bank = 4 + (2 * ttile + v4) % 4
                p.op("tensor", lambda e, v4=v4, bank=bank, brow=brow: e.matmul(
                    p.pb[bank][:, :], C["onr"][0:1, 0:128],
                    bsb[0:1, brow + v4 * 512: brow + (v4 + 1) * 512],
                    start=True, stop=False),
                    reads=["H:bv1", "H:bq1", "onr"], writes=[f"pb{bank}"])
                for k in range(16):
                    p.op("tensor", lambda e, k=k, ttile=ttile, v4=v4, bank=bank: e.matmul(
                        p.pb[bank][:, :],
                        xsb[:, k * S + ttile * 128: k * S + (ttile + 1) * 128],
                        wsb[:, k * 1024 + v4 * 512: k * 1024 + (v4 + 1) * 512],
                        start=False, stop=(k == 15)),
                        reads=[f"H:x{k}", f"H:w{k}"],
                        writes=[f"pb{bank}"] if k == 15 else [])
                p.op("vector", lambda e, v4=v4, bank=bank: e.tensor_copy(
                    outsb[:, v4 * 512:(v4 + 1) * 512], p.pb[bank][:, :]),
                    reads=[f"pb{bank}"], writes=[f"H:o{v4}"])
                p.op("sync", lambda e, ttile=ttile, v4=v4, outd=outd: e.dma_start(
                    out=t[outd][ttile * 128:(ttile + 1) * 128, v4 * 512:(v4 + 1) * 512],
                    in_=outsb[:, v4 * 512:(v4 + 1) * 512]),
                    reads=[f"H:o{v4}"], dma=True)


def em_I(p, t, C, ar, rw):
    q_sb = ar.bf16(128, 2 * S)
    k_sb = ar.bf16(128, 2 * S)
    vt_sb = ar.bf16(128, 2 * S)
    qt_all = ar.bf16(128, 2 * S)
    msk = ar.f32(128, 384)
    iorow = ar.f32(128, 64)
    iopp = ar.f32(64, 1)
    ident = ar.bf16(128, 128)
    idf8 = ar.f32(8, 8)
    sq_sb = ar.bf16(128, 512)
    negc = ar.f32(128, 1)
    na = ar.f32(8, S)
    nb = ar.f32(8, S)
    mx8 = ar.f32(8, 8)
    rkall = ar.f32(128, 256)
    probs = ar.bf16(128, 384)
    lctx = ar.bf16(128, S)
    P_sb = ar.bf16(128, 128)
    qgT_sb = ar.bf16(64, 128)
    qg_sb = ar.bf16(128, 64)
    gp_sb = ar.bf16(128, 64)
    rg_pp = ar.f32(64, 1)
    gctx_sb = ar.bf16(64, 128)
    cb_sb = ar.f32(128, 2 * 512)
    pi_sb = ar.bf16(64, 2 * 512)
    t2_sb = ar.f32(128, 2 * 512)
    ctxo = ar.bf16(128, 2 * 512)
    nrow = ar.f32(1, S)
    rloc = ar.f32(1, S)
    mra = ar.f32(1, S)
    srow = rw.f32(64)
    rgrow = rw.f32(64)
    rr_sb = rw.f32(512)
    mr_sb = rw.f32(512)
    cbrow = rw.f32(1024)
    pbb6 = p.pb[6].bitcast(BF16)
    for (tl, dsrc, key) in [(msk, "maskL", "I:msk"), (iorow, "io_row", "I:iorow"),
                            (iopp, "io_pp", "I:iopp"), (ident, "identd", "I:ident"),
                            (idf8, "ident8f", "I:id8")]:
        p.op("sync", lambda e, tl=tl, dsrc=dsrc: e.dma_start(out=tl[:, :], in_=t[dsrc]),
             writes=[key], dma=True)
    for j in range(8):
        p.op("sync", lambda e, j=j: e.dma_start(
            out=q_sb[:, 0:S], in_=t["q1"][j * 128:(j + 1) * 128, :]),
            writes=["I:q0"], dma=True)
        for c8 in range(8):
            p.op("scalar", lambda e, c8=c8: e.activation(
                sq_sb[:, :], q_sb[:, c8 * 512:(c8 + 1) * 512], AF.Square),
                reads=["I:q0"], writes=["I:sq"])
            p.op("tensor", lambda e: e.matmul(
                p.pb[0][0:1, :], C["onc"][:, 0:1], sq_sb[:, :], start=True, stop=True),
                reads=["I:sq", "onc"], writes=["pb0"])
            p.op("vector", lambda e, c8=c8: e.tensor_copy(
                nrow[0:1, c8 * 512:(c8 + 1) * 512], p.pb[0][0:1, :]),
                reads=["pb0"], writes=["I:nrow"])
        p.op("sync", lambda e, j=j: e.dma_start(
            out=t["rankm"][j:j + 1, :], in_=nrow[0:1, :]),
            reads=["I:nrow"], writes=["I:na_d"], dma=True)
    p.op("sync", lambda e: e.dma_start(out=na[:, :], in_=t["rankm"]),
         reads=["I:na_d"], writes=["I:na"], dma=True)
    bufs = [na, nb]
    for rnd in range(8):
        cur, nxt = bufs[rnd % 2], bufs[(rnd + 1) % 2]
        ck, nk = ("I:na" if rnd % 2 == 0 else "I:nb"), ("I:nb" if rnd % 2 == 0 else "I:na")
        p.op("vector", lambda e, cur=cur: e.max(out=mx8[:, :], in_=cur[:, :]),
             reads=[ck], writes=["I:mx8"])
        if rnd == 7:
            p.op("vector", lambda e: e.memset(mx8[:, 6:8], -1.0),
                 reads=[], writes=["I:mx8"])
        p.op("vector", lambda e, cur=cur, nxt=nxt: e.match_replace(
            out=nxt[:, :], in_to_replace=mx8[:, :], in_values=cur[:, :],
            imm_value=-1.0),
            reads=[ck, "I:mx8"], writes=[nk])
    p.op("vector", lambda e: e.tensor_scalar(
        nb[:, :], na[:, :], -0.5, None, ALU.is_le),
        reads=["I:na"], writes=["I:nb"])
    p.op("vector", lambda e: e.memset(nb[:, 0:1], 1.0), reads=[], writes=["I:nb"])
    p.op("vector", lambda e: e.memset(nb[:, S - 1:S], 1.0), reads=[], writes=["I:nb"])
    p.op("vector", lambda e: e.tensor_tensor_scan(
        na[:, :], nb[:, :], nb[:, :], 0.0, ALU.add, ALU.bypass),
        reads=["I:nb"], writes=["I:na"])
    p.op("vector", lambda e: e.tensor_sub(na[:, :], na[:, :], nb[:, :]),
         reads=["I:na", "I:nb"], writes=["I:na"])
    p.op("sync", lambda e: e.dma_start(out=t["maskm"], in_=nb[:, :]),
         reads=["I:nb"], writes=["I:maskmD"], dma=True)
    p.op("vector", lambda e: e.tensor_scalar(
        nb[:, :], nb[:, :], -1e9, 1e9, ALU.mult, ALU.add),
        reads=["I:nb"], writes=["I:nb"])
    p.op("vector", lambda e: e.tensor_add(na[:, :], na[:, :], nb[:, :]),
         reads=["I:na", "I:nb"], writes=["I:na"])
    p.op("sync", lambda e: e.dma_start(out=t["rankm"], in_=na[:, :]),
         reads=["I:na"], writes=["I:rankmD"], dma=True)
    for c in range(32):
        p.op("tensor", lambda e, c=c: e.transpose(
            p.pb[6][:, 0:8], na[0:8, c * 128:(c + 1) * 128], idf8[0:8, 0:8]),
            reads=["I:na", "I:id8"], writes=["pb6"])
        p.op("vector", lambda e, c=c: e.tensor_copy(
            rkall[:, c * 8:(c + 1) * 8], p.pb[6][:, 0:8]),
            reads=["pb6"], writes=["I:rkall"])
    NB = 32

    def loads(j):
        pr = j % 2
        o = pr * S
        p.op("sync", lambda e, j=j, o=o: e.dma_start(
            out=q_sb[:, o:o + S], in_=t["q1"][j * 128:(j + 1) * 128, :]),
            writes=[f"I:q{pr}"], dma=True)
        p.op("sync", lambda e, j=j, o=o: e.dma_start(
            out=k_sb[:, o:o + S], in_=t["k1"][j * 128:(j + 1) * 128, :]),
            writes=[f"I:k{pr}"], dma=True)
        for g in range(8):
            p.op("sync", lambda e, j=j, g=g, o=o: e.dma_start(
                out=vt_sb[:, o + g * 512: o + (g + 1) * 512].rearrange(
                    "p (a d) -> p a d", d=128),
                in_=t["v1t"][g * 512:(g + 1) * 512, j * 128:(j + 1) * 128].rearrange(
                    "(a p) d -> p a d", p=128)),
                writes=[f"I:v{pr}"], dma=True)
            p.op("sync", lambda e, j=j, g=g, o=o: e.dma_start(
                out=qt_all[:, o + g * 512: o + (g + 1) * 512].rearrange(
                    "p (a d) -> p a d", d=128),
                in_=t["q1t"][g * 512:(g + 1) * 512, j * 128:(j + 1) * 128].rearrange(
                    "(a p) d -> p a d", p=128)),
                writes=[f"I:qt{pr}"], dma=True)

    loads(0)
    for j in range(8):
        pr = j % 2
        o = pr * S
        if j < 7:
            loads(j + 1)
        _bound_c(p, f"I:q{pr}", q_sb[:, o:o + S], sq_sb, C["onc"], srow, 0, "I")
        _bound_c(p, f"I:k{pr}", k_sb[:, o:o + S], sq_sb, C["onc"], srow, 1, "I")
        _negc_from_bounds(p, C, srow, negc[:, 0:1], "I")

        def qb0(tb):
            return 0 if tb % 2 == 0 else 7

        def epi_local(tb):
            bqk = qb0(tb)
            qlo = max(0, (tb - 1) * 128)
            qhi = min(S, (tb + 2) * 128)
            Nq = qhi - qlo
            mlo = 128 if tb == 0 else 0
            del qhi
            p.op("vector", lambda e, Nq=Nq, mlo=mlo, bqk=bqk: e.tensor_add(
                p.pb[bqk][:, 0:Nq], p.pb[bqk][:, 0:Nq], msk[:, mlo:mlo + Nq]),
                reads=[f"pb{bqk}", "I:msk"], writes=[f"pb{bqk}"])
            p.op("scalar", lambda e, Nq=Nq, bqk=bqk: e.activation(
                probs[:, 0:Nq], p.pb[bqk][:, 0:Nq], AF.Exp,
                bias=negc[:, 0:1], scale=SCALE),
                reads=[f"pb{bqk}", "I:negc0"], writes=["I:probs"])
            for cg in (-1, 0, 1):
                qb = tb + cg
                if qb < 0 or qb >= NB:
                    continue
                col = qb * 128 - qlo
                first = (tb == max(0, qb - 1))
                last = (tb == min(NB - 1, qb + 1))
                cbank = 2 if qb % 3 == 0 else (3 if qb % 3 == 1 else 6)
                dbank = 1 if qb % 3 == 0 else (4 if qb % 3 == 1 else 5)
                p.op("tensor", lambda e, col=col, dbank=dbank, first=first,
                     last=last: e.matmul(
                    p.pb[dbank][0:1, 0:128], C["onc"][:, 0:1], probs[:, col:col + 128],
                    start=first, stop=last),
                    reads=["I:probs", "onc"],
                    writes=[f"pb{dbank}"] if (first or last) else [])
                p.op("tensor", lambda e, tb=tb, col=col, cbank=cbank, first=first,
                     last=last, o=o: e.matmul(
                    p.pb[cbank][:, 0:128], vt_sb[:, o + tb * 128: o + (tb + 1) * 128],
                    probs[:, col:col + 128], start=first, stop=last),
                    reads=["I:probs", f"I:v{pr}"],
                    writes=[f"pb{cbank}"] if (first or last) else [])
            done = [tb - 1] if tb < NB - 1 else [tb - 1, tb]
            for qb in done:
                if qb < 0:
                    continue
                cbank = 2 if qb % 3 == 0 else (3 if qb % 3 == 1 else 6)
                dbank = 1 if qb % 3 == 0 else (4 if qb % 3 == 1 else 5)
                p.op("vector", lambda e, qb=qb, dbank=dbank: e.reciprocal(
                    rloc[0:1, qb * 128:(qb + 1) * 128], p.pb[dbank][0:1, 0:128]),
                    reads=[f"pb{dbank}"], writes=["I:rloc"])
                p.op("vector", lambda e, qb=qb, cbank=cbank: e.tensor_copy(
                    lctx[:, qb * 128:(qb + 1) * 128], p.pb[cbank][:, 0:128]),
                    reads=[f"pb{cbank}"], writes=["I:lctx"])

        for tb in range(NB):
            qlo = max(0, (tb - 1) * 128)
            qhi = min(S, (tb + 2) * 128)
            Nq = qhi - qlo
            bqk = qb0(tb)
            p.op("tensor", lambda e, tb=tb, qlo=qlo, Nq=Nq, bqk=bqk, o=o: e.matmul(
                p.pb[bqk][:, 0:Nq], k_sb[:, o + tb * 128: o + (tb + 1) * 128],
                q_sb[:, o + qlo: o + qlo + Nq], start=True, stop=True),
                reads=[f"I:k{pr}", f"I:q{pr}"], writes=[f"pb{bqk}"])
            if tb > 0:
                epi_local(tb - 1)
        epi_local(NB - 1)
        for c in range(32):
            pp = c % 2
            p.op("vector", lambda e, c=c, pp=pp, j=j: e.tensor_scalar(
                P_sb[:, pp * 64:(pp + 1) * 64], iorow[:, :],
                rkall[:, c * 8 + j:c * 8 + j + 1], None, ALU.is_equal),
                reads=["I:iorow", "I:rkall"], writes=[f"I:P{pp}"])
            p.op("tensor", lambda e, c=c, pp=pp, o=o: e.matmul(
                p.pb[0][0:64, 0:128], P_sb[:, pp * 64:(pp + 1) * 64],
                qt_all[:, o + c * 128: o + (c + 1) * 128],
                start=(c == 0), stop=(c == 31)),
                reads=[f"I:P{pp}", f"I:qt{pr}"],
                writes=["pb0"] if c in (0, 31) else [])
        p.op("vector", lambda e: e.tensor_copy(qgT_sb[:, :], p.pb[0][0:64, 0:128]),
             reads=["pb0"], writes=["I:qgT"])
        p.op("tensor", lambda e: e.transpose(
            pbb6[:, 0:64], qgT_sb[:, :], ident[0:64, 0:64]),
            reads=["I:qgT", "I:ident"], writes=["pb6"])
        p.op("vector", lambda e: e.tensor_copy(qg_sb[:, :], pbb6[:, 0:64]),
             reads=["pb6"], writes=["I:qg"])
        for kb in range(32):
            p.op("tensor", lambda e, kb=kb, o=o: e.matmul(
                p.pb[1][:, 0:64], k_sb[:, o + kb * 128: o + (kb + 1) * 128], qg_sb[:, :],
                start=True, stop=True),
                reads=[f"I:k{pr}", "I:qg"], writes=["pb1"])
            p.op("scalar", lambda e: e.activation(
                gp_sb[:, :], p.pb[1][:, 0:64], AF.Exp,
                bias=negc[:, 0:1], scale=SCALE),
                reads=["pb1", "I:negc0"], writes=["I:gp"])
            p.op("tensor", lambda e, kb=kb: e.matmul(
                p.pb[2][0:1, 0:64], C["onc"][:, 0:1], gp_sb[:, :],
                start=(kb == 0), stop=(kb == 31)),
                reads=["I:gp", "onc"],
                writes=["pb2"] if kb in (0, 31) else [])
            p.op("tensor", lambda e, kb=kb, o=o: e.matmul(
                p.pb[3][0:64, 0:128], gp_sb[:, :], vt_sb[:, o + kb * 128: o + (kb + 1) * 128],
                start=(kb == 0), stop=(kb == 31)),
                reads=["I:gp", f"I:v{pr}"],
                writes=["pb3"] if kb in (0, 31) else [])
        p.op("vector", lambda e: e.reciprocal(rgrow[0:1, 0:64], p.pb[2][0:1, 0:64]),
             reads=["pb2"], writes=["I:rgrow"])
        p.op("tensor", lambda e: e.matmul(
            p.pb[5][0:64, 0:1], rgrow[0:1, 0:64], C["onr"][0:1, 0:1],
            start=True, stop=True),
            reads=["I:rgrow", "onr"], writes=["pb5"])
        p.op("vector", lambda e: e.tensor_copy(rg_pp[:, 0:1], p.pb[5][0:64, 0:1]),
             reads=["pb5"], writes=["I:rg"])
        p.op("vector", lambda e: e.tensor_scalar_mul(
            gctx_sb[:, :], p.pb[3][0:64, 0:128], rg_pp[:, 0:1]),
            reads=["pb3", "I:rg"], writes=["I:gctx"])
        p.op("sync", lambda e, j=j: e.dma_start(
            out=nrow[0:1, :], in_=t["rankm"][j:j + 1, :]),
            reads=["I:rankmD"], writes=["I:nrow"], dma=True)
        p.op("sync", lambda e, j=j: e.dma_start(
            out=mra[0:1, :], in_=t["maskm"][j:j + 1, :]),
            reads=["I:maskmD"], writes=["I:mra"], dma=True)
        for c8 in range(8):
            cp = c8 % 2
            co = cp * 512
            bb5 = 5 if cp == 0 else 2
            bb7 = 7 if cp == 0 else 1
            bb4 = 4 if cp == 0 else 3
            p.op("vector", lambda e, c8=c8, co=co, cp=cp: e.tensor_scalar(
                cbrow[0:1, co:co + 512], mra[0:1, c8 * 512:(c8 + 1) * 512], -1.0, 1.0,
                ALU.mult, ALU.add),
                reads=["I:mra"], writes=[f"I:cbrow{cp}"])
            p.op("vector", lambda e, c8=c8, co=co, cp=cp: e.tensor_mul(
                cbrow[0:1, co:co + 512], cbrow[0:1, co:co + 512],
                rloc[0:1, c8 * 512:(c8 + 1) * 512]),
                reads=[f"I:cbrow{cp}", "I:rloc"], writes=[f"I:cbrow{cp}"])
            p.op("tensor", lambda e, co=co, cp=cp, bb5=bb5: e.matmul(
                p.pb[bb5][:, :], C["onr"][0:1, 0:128], cbrow[0:1, co:co + 512],
                start=True, stop=True),
                reads=[f"I:cbrow{cp}", "onr"], writes=[f"pb{bb5}"])
            p.op("vector", lambda e, co=co, cp=cp, bb5=bb5: e.tensor_copy(
                cb_sb[:, co:co + 512], p.pb[bb5][:, :]),
                reads=[f"pb{bb5}"], writes=[f"I:cb{cp}"])
            p.op("tensor", lambda e, c8=c8, bb7=bb7: e.matmul(
                p.pb[bb7][0:64, :], C["onr"][0:1, 0:64],
                nrow[0:1, c8 * 512:(c8 + 1) * 512],
                start=True, stop=True),
                reads=["I:nrow", "onr"], writes=[f"pb{bb7}"])
            p.op("vector", lambda e, co=co, cp=cp, bb7=bb7: e.tensor_scalar(
                pi_sb[:, co:co + 512], p.pb[bb7][0:64, :], iopp[:, 0:1], None,
                ALU.is_equal),
                reads=[f"pb{bb7}", "I:iopp"], writes=[f"I:pi{cp}"])
            p.op("tensor", lambda e, co=co, cp=cp, bb4=bb4: e.matmul(
                p.pb[bb4][:, :], gctx_sb[:, :], pi_sb[:, co:co + 512],
                start=True, stop=True),
                reads=["I:gctx", f"I:pi{cp}"], writes=[f"pb{bb4}"])
            p.op("vector", lambda e, c8=c8, co=co, cp=cp: e.tensor_mul(
                t2_sb[:, co:co + 512], lctx[:, c8 * 512:(c8 + 1) * 512],
                cb_sb[:, co:co + 512]),
                reads=["I:lctx", f"I:cb{cp}"], writes=[f"I:t2{cp}"])
            p.op("vector", lambda e, co=co, cp=cp, bb4=bb4: e.tensor_add(
                ctxo[:, co:co + 512], t2_sb[:, co:co + 512], p.pb[bb4][:, :]),
                reads=[f"I:t2{cp}", f"pb{bb4}"], writes=[f"I:ctxo{cp}"])
            p.op("gpsimd", lambda e, j=j, c8=c8, co=co, cp=cp: e.indirect_dma_start(
                t["ctx1_sh"][:, :],
                IndirectOffsetOnAxis(ap=C["idx_h"][:, j:j + 1], axis=0),
                ctxo[:, co:co + 512], None, element_offset=c8 * 512),
                reads=[f"I:ctxo{cp}", "idx"], dma=True)


def em_J(p, t, C, ar, rw):
    ctx1v = t["ctx1_sh"].rearrange("h (p w) -> (h p) w", p=2)
    a1sb = ar.bf16(128, 16 * TC)
    xc = ar.f32(128, 2 * 512)
    u1 = ar.f32(128, 16 * 512)
    sq = ar.bf16(128, 512)
    t1 = ar.f32(128, 512)
    outs = ar.f32(128, 512)
    srow = rw.f32(3 * 512)
    for k in range(16):
        p.op("gpsimd", lambda e, k=k: e.indirect_dma_start(
            a1sb[:, k * TC:(k + 1) * TC], None,
            ctx1v[:, :], IndirectOffsetOnAxis(ap=C["idx_c"][:, k:k + 1], axis=0)),
            writes=[f"J:a{k}"], reads=["idx"], dma=True)
    def epiJ(m):
        p.op("tensor", lambda e, m=m: e.matmul(
            p.pb[6][0:1, :], C["oncf"][:, 0:1], u1[:, m * 512:(m + 1) * 512],
            start=(m == 0), stop=(m == 15)),
            reads=[f"J:u1_{m}", "oncf"],
            writes=["pb6"] if m in (0, 15) else [])
        p.op("scalar", lambda e, m=m: e.activation(
            sq[:, :], u1[:, m * 512:(m + 1) * 512], AF.Square),
            reads=[f"J:u1_{m}"], writes=["J:sq"])
        p.op("tensor", lambda e, m=m: e.matmul(
            p.pb[7][0:1, :], C["onc"][:, 0:1], sq[:, :],
            start=(m == 0), stop=(m == 15)),
            reads=["J:sq", "onc"],
            writes=["pb7"] if m in (0, 15) else [])

    for tt in range(TC // 512):
        t0 = tt * 512
        for m in range(16):
            xb = m % 2
            p.op("sync", lambda e, m=m, t0=t0, xb=xb: e.dma_start(
                out=xc[:, xb * 512:(xb + 1) * 512],
                in_=t["x2"][m * 128:(m + 1) * 128, t0:t0 + 512]),
                writes=[f"J:xc{xb}"], dma=True)
            p.op("vector", lambda e, m=m, t0=t0: e.tensor_copy(
                u1[:, m * 512:(m + 1) * 512],
                a1sb[:, m * TC + t0: m * TC + t0 + 512]),
                reads=[f"J:a{m}"], writes=[f"J:u1_{m}"])
            p.op("vector", lambda e, m=m, xb=xb: e.tensor_add(
                u1[:, m * 512:(m + 1) * 512],
                u1[:, m * 512:(m + 1) * 512],
                xc[:, xb * 512:(xb + 1) * 512]),
                reads=[f"J:u1_{m}", f"J:xc{xb}"], writes=[f"J:u1_{m}"])
            if m > 0:
                epiJ(m - 1)
        epiJ(15)

        def outfn(m, o, t0=t0):
            p.op("sync", lambda e, m=m, t0=t0, o=o: e.dma_start(
                out=t["x3"][m * 128:(m + 1) * 128, t0:t0 + 512], in_=o[:, :]),
                reads=["J:outs"], dma=True)
        _ln_apply(p, C, "J", u1, 16, D, outfn, srow, t1, outs)


PHASES_ALL = ("AB", "C", "D", "E", "F", "G0", "H", "I", "J", "G1")


def build_fused(upto="G1", dbg=False):
    nc = bass.Bass("TRN2", target_bir_lowering=False)
    phases = PHASES_ALL[:PHASES_ALL.index(upto) + 1]
    t = {}

    def ein(name, shape, dt=F32):
        t[name] = nc.dram_tensor(name, shape, dt, kind="ExternalInput").ap()

    ein("x_f", [D, TC]); ein("wq_down", [D, QP], BF16); ein("gq", [128, 8])
    ein("wkv_lora", [D, KVPP], BF16); ein("wkv_rope", [D, 64], BF16); ein("gkv", [128, 11])
    ein("wq_up", [QP, 1024], BF16); ein("wkv_up_k", [KVPP, 512], BF16)
    ein("wkv_up_v", [KVPP, 1024], BF16)
    ein("wo", [D, D], BF16)
    ein("cs_own", [64, TC]); ein("sc_own", [64, TC])
    ein("cos4", [128, S]); ein("sin4", [128, S])
    for L in ("L0", "L1"):
        ein(f"{L}_gws", [44, 128, 2048], BF16); ein(f"{L}_uws", [44, 128, 2048], BF16)
        ein(f"{L}_dws", [16, 128, 5632], BF16)
        ein(f"{L}_gs", [128, 44]); ein(f"{L}_us", [128, 44]); ein(f"{L}_ds", [128, 16])
    ein("wq1", [D, 1024], BF16); ein("wk1", [D, 1024], BF16)
    ein("wv1", [D, 1024], BF16)
    ein("bq1", [1, 1024]); ein("bk1", [1, 1024]); ein("bv1", [1, 1024])
    ein("bqpp", [128, 8]); ein("bkpp", [128, 8])
    ein("maskE", [128, 2048]); ein("maskL", [128, 384])
    ein("io_row", [128, 64]); ein("io_pp", [64, 1]); ein("identd", [128, 128], BF16)
    ein("ident8f", [8, 8])
    ein("ones_colb", [128, 1], BF16); ein("ones_colf", [128, 1])
    ein("ones_row", [1, 128]); ein("ones512", [1, 512])
    ein("idx_fcq", [128, 8], U32); ein("idx_fckv", [128, 11], U32)
    ein("idx_fkr", [64, 1], U32); ein("idx_h", [128, 8], U32)
    ein("idx_t", [128, 16], U32); ein("idx_c", [128, 16], U32)

    def sh_t(name, shape, dt=BF16):
        t[name] = nc.dram_tensor(name, shape, dt, kind="Internal",
                                 addr_space="Shared").ap()

    sh_t("cq_sh", [2 * 1024, 2048]); sh_t("ckv_sh", [2 * 1408, 2048])
    sh_t("kr_sh", [2 * 64, 2048])
    sh_t("ctx0_sh", [2048, S]); sh_t("x2_sh", [2 * 2048, 2048])
    sh_t("ctx1_sh", [2048, S])

    ikind = "ExternalOutput" if dbg else "Internal"

    def loc(name, shape, dt, kind):
        t[name] = nc.dram_tensor(name, shape, dt, kind=kind).ap()

    loc("q0", [1024, S], BF16, ikind)
    loc("knope", [512, S], BF16, ikind)
    loc("vt", [S, 1024], BF16, ikind)
    loc("x1", [D, TC], F32, ikind)
    loc("x2", [D, TC], F32, ikind)
    loc("x3", [D, TC], F32, ikind)
    loc("q1", [1024, S], BF16, ikind)
    loc("k1", [1024, S], BF16, ikind)
    loc("v1t", [S, 1024], BF16, ikind)
    loc("q1t", [S, 1024], BF16, "Internal")
    loc("u2d", [D, TC], F32, "Internal")
    loc("rankm", [8, S], F32, "Internal")
    loc("maskm", [8, S], F32, "Internal")
    loc("bar_in", [1, 128], F32, "Internal")
    for i in range(4):
        loc(f"bar_out{i}", [1, 128], F32, "Internal")
    t["xout"] = nc.dram_tensor("xout", [D, TC], F32, kind="ExternalOutput").ap()

    with ExitStack() as ctx:
        C = {}
        C["onc"] = ctx.enter_context(nc.sbuf_tensor("c_onc", [128, 1], BF16))
        C["oncf"] = ctx.enter_context(nc.sbuf_tensor("c_oncf", [128, 1], F32))
        C["onr"] = ctx.enter_context(nc.sbuf_tensor("c_onr", [1, 128], F32))
        C["gq"] = ctx.enter_context(nc.sbuf_tensor("c_gq", [128, 8], F32))
        C["gkv"] = ctx.enter_context(nc.sbuf_tensor("c_gkv", [128, 11], F32))
        C["idx_fcq"] = ctx.enter_context(nc.sbuf_tensor("c_ifcq", [128, 8], U32))
        C["idx_fckv"] = ctx.enter_context(nc.sbuf_tensor("c_ifckv", [128, 11], U32))
        C["idx_fkr"] = ctx.enter_context(nc.sbuf_tensor("c_ifkr", [64, 1], U32))
        C["idx_h"] = ctx.enter_context(nc.sbuf_tensor("c_ih", [128, 8], U32))
        C["idx_t"] = ctx.enter_context(nc.sbuf_tensor("c_it", [128, 16], U32))
        C["idx_c"] = ctx.enter_context(nc.sbuf_tensor("c_ic", [128, 16], U32))
        ar = Arena(nc, ctx, 48384)
        rw = Rows(nc, ctx, 4096)
        p = Prog(nc, ctx)
        p.pb = [ctx.enter_context(nc.psum_tensor(f"pb{i}", [128, 512], F32))
                for i in range(8)]

        for nm in ("onc", "oncf", "onr", "gq", "gkv",
                   "idx_fcq", "idx_fckv", "idx_fkr", "idx_h", "idx_t", "idx_c"):
            src = {"onc": "ones_colb", "oncf": "ones_colf", "onr": "ones_row"}.get(nm, nm)
            key = "idx" if nm.startswith("idx") else nm
            p.op("sync", lambda e, nm=nm, src=src: e.dma_start(out=C[nm][:, :], in_=t[src]),
                 writes=[key], dma=True)
        p.op("sync", lambda e: e.dma_start(out=t["bar_in"], in_=C["onr"][0:1, :]),
             reads=["onr"], writes=["barin"], dma=True)

        def barrier(i):
            p.fence()
            p.cc(lambda e, i=i: e.collective_compute(
                "AllReduce", mybir.AluOpType.add, replica_groups=PAIRS,
                ins=[t["bar_in"].opt()], outs=[t[f"bar_out{i}"].opt()]))
            p.fence()

        for ph in phases:
            ar.reset(0)
            rw.reset()
            if ph == "AB":
                em_AB(p, t, C, ar, rw)
                barrier(0)
            elif ph == "C":
                em_C(p, t, C, ar, rw)
                p.fence()
            elif ph == "D":
                em_D(p, t, C, ar, rw)
                p.fence()
            elif ph == "E":
                em_E(p, t, C, ar, rw)
                barrier(1)
            elif ph == "F":
                em_F(p, t, C, ar, rw)
                p.fence()
            elif ph == "G0":
                em_G(p, t, C, ar, rw, "L0", t["x1"], t["x2"], scat=True)
                barrier(2)
            elif ph == "H":
                em_H(p, t, C, ar, rw)
                p.fence()
            elif ph == "I":
                em_I(p, t, C, ar, rw)
                barrier(3)
            elif ph == "J":
                em_J(p, t, C, ar, rw)
                p.fence()
            elif ph == "G1":
                em_G(p, t, C, ar, rw, "L1", t["x3"], t["xout"], scat=False)
        p.emit()
    return nc


def kernel(**ins):
    shd, cores = stage_inputs(ins)
    nc = build_fused()
    maps = []
    for c in range(NCORES):
        m = dict(shd)
        m.update(cores[c])
        maps.append(m)
    res = run_bass_kernel_spmd(nc, maps, core_ids=list(range(NCORES)))
    out = np.zeros((B, S, D), np.float32)
    for c in range(NCORES):
        b, r = c // 2, c % 2
        xo = np.asarray(res.results[c]["xout"])
        out[b, r * 2048:(r + 1) * 2048, :] = xo.T
    return out
